# version 03
# baseline (speedup 1.0000x reference)
import numpy as np
from contextlib import ExitStack
import concourse.bass as bass
import concourse.mybir as mybir
from concourse.bass_utils import run_bass_kernel_spmd

F32 = mybir.dt.float32
BF16 = mybir.dt.bfloat16
AF = mybir.ActivationFunctionType
ALU = mybir.AluOpType
AX = mybir.AxisListType

NCORES = 8
T = 4096
D = 1024
ENGINES = ("tensor", "vector", "scalar", "gpsimd", "sync")
NEG_BIG = -1.0e30


def bf16(x):
    import ml_dtypes
    return np.asarray(x, dtype=ml_dtypes.bfloat16)


class Prog:
    def __init__(self, name="prog"):
        self.nc = bass.Bass("TRN2", target_bir_lowering=False, debug=False)
        self.stack = ExitStack()
        self.ops = []
        self.sem = {}
        self.cnt = {}
        self.bar_n = 0
        self.bar_dma_seen = {}
        self._uid = 0
        for e in ENGINES:
            self.new_sem("bar_" + e)
        for q in ("sync", "gpsimd", "scalar", "vector"):
            self.new_sem("dma_" + q)
            self.bar_dma_seen[q] = 0
        for s in ("t0", "t1", "v0", "v1", "s0", "s1", "g0"):
            self.new_sem(s)
        self.ps = [self.stack.enter_context(
            self.nc.psum_tensor(f"ps{i}", [128, 512], F32)) for i in range(7)]
        self.psT = self.stack.enter_context(
            self.nc.psum_tensor("psT", [128, 1024], BF16))
        self.arenas = {}
        self.marks = {}
        self.cvals = {}

    def uid(self):
        self._uid += 1
        return self._uid

    def new_sem(self, name):
        h = self.stack.enter_context(self.nc.semaphore(name))
        self.sem[name] = h
        self.cnt[name] = 0
        return h

    def arena(self, name, cols, dtype=F32):
        t = self.stack.enter_context(self.nc.sbuf_tensor(name, [128, cols], dtype))
        self.arenas[name] = [t, 0, cols]
        return t

    def take(self, name, cols, rows=128):
        a = self.arenas[name]
        off = (a[1] + 7) & ~7
        assert off + cols <= a[2], f"arena {name} overflow: {off}+{cols} > {a[2]}"
        a[1] = off + cols
        return a[0][0:rows, off:off + cols]

    def alias(self, name, target):
        self.arenas[name] = self.arenas[target]

    def mark(self, name):
        self.marks[name] = {k: v[1] for k, v in self.arenas.items()}

    def reset(self, name):
        for k, v in self.marks[name].items():
            self.arenas[k][1] = v

    def constv(self, value):
        if value in self.cvals:
            return value
        tile = self.take("cst", 8)
        self.v("memset", tile[:, 0:1], value)
        self.nc.const_aps.register(F32, value, tile[:, 0:1])
        self.cvals[value] = tile
        return value

    def dram(self, name, shape, dtype=F32, kind="Internal"):
        return self.nc.dram_tensor(name, list(shape), dtype, kind=kind).ap()

    def inp(self, name, shape, dtype=F32):
        return self.dram(name, shape, dtype, kind="ExternalInput")

    def out(self, name, shape, dtype=F32):
        return self.dram(name, shape, dtype, kind="ExternalOutput")

    AUTODRAIN = ("vector", "scalar", "gpsimd")

    def op(self, engine, method, *args, inc=None, nodrain=False, **kwargs):
        if inc is not None:
            sname, n = inc
            self.cnt[sname] += n
        drain = (engine in self.AUTODRAIN and not nodrain
                 and method != "dma_start")

        def fn(eng, method=method, args=args, kwargs=kwargs, inc=inc, drain=drain):
            r = getattr(eng, method)(*args, **kwargs)
            if inc is not None:
                r.then_inc(self.sem[inc[0]], inc[1])
            if drain:
                eng.drain()
        self.ops.append((engine, fn))
        if inc is not None:
            return self.cnt[inc[0]]

    def dma(self, queue, out_ap, in_ap, **kw):
        return self.op(queue, "dma_start", inc=("dma_" + queue, 16),
                       out=out_ap, in_=in_ap, **kw)

    def wait(self, engine, semname, val):
        self.ops.append((engine,
                         lambda eng, s=semname, v=val: eng.wait_ge(self.sem[s], v)))

    def wait_dma(self, engine, queue, val=None):
        self.wait(engine, "dma_" + queue, self.cnt["dma_" + queue] if val is None else val)

    def inc(self, engine, semname, n=1, drain=True):
        self.cnt[semname] += n

        def fn(eng, s=semname, n=n, drain=drain):
            if drain:
                eng.drain().then_inc(self.sem[s], n)
            else:
                eng.nop().then_inc(self.sem[s], n)
        self.ops.append((engine, fn))
        return self.cnt[semname]

    def barrier(self):
        for q in ("sync", "gpsimd", "scalar", "vector"):
            c = self.cnt["dma_" + q]
            if c > self.bar_dma_seen[q]:
                self.wait(q, "dma_" + q, c)
                self.bar_dma_seen[q] = c
        self.bar_n += 1
        for e in ENGINES:
            self.inc(e, "bar_" + e, 1)
        for e in ENGINES:
            for o in ENGINES:
                if o != e:
                    self.wait(e, "bar_" + o, self.bar_n)

    def mm(self, ps, lhsT, rhs, start=True, stop=True, inc=None):
        return self.op("tensor", "matmul", ps, lhsT, rhs,
                       start=start, stop=stop, inc=inc)

    def trans(self, ps, in_, ident, inc=None):
        return self.op("tensor", "transpose", ps, in_, ident, inc=inc)

    def act(self, out, in_, func, bias=0.0, scale=1.0, accum_out=None, inc=None,
            eng="scalar"):
        return self.op(eng, "activation", out, in_, func,
                       bias=bias, scale=scale, accum_out=accum_out, inc=inc)

    def v(self, method, *args, inc=None, eng="vector", **kw):
        return self.op(eng, method, *args, inc=inc, **kw)

    def build(self):
        with self.nc.Block() as block:
            for ename in ENGINES:
                def make_runner(ename):
                    def runner(eng):
                        for e, fn in self.ops:
                            if e == ename:
                                fn(eng)
                    return runner
                getattr(block, ename)(make_runner(ename))
        self.stack.close()
        return self.nc


def emit_ln(p, x_tiles_f32, xb_tiles, width, consts, out_bf_tiles, psA, psB,
            t_tiles=None, out_f32_tiles=None, gcol=None, bcol=None, eps=1e-5):
    nch = len(x_tiles_f32)
    N = nch * 128
    for i in range(nch):
        p.act(xb_tiles[i][:, :width], x_tiles_f32[i][:, :width], AF.Copy)
    c_s = p.inc("scalar", "s0")
    p.wait("tensor", "s0", c_s)
    for i in range(nch):
        p.mm(psA[0:1, :width], consts["ones_col"][:, 0:1], xb_tiles[i][:, :width],
             start=(i == 0), stop=(i == nch - 1))
    c_t = p.inc("tensor", "t0")
    p.wait("scalar", "t0", c_t)
    p.act(consts["rowbf"][0:1, :width], psA[0:1, :width], AF.Copy, scale=1.0 / N)
    c_s = p.inc("scalar", "s0")
    p.wait("tensor", "s0", c_s)
    p.mm(psB[:, :width], consts["ones_row"], consts["rowbf"][0:1, :width])
    c_t = p.inc("tensor", "t0")
    p.wait("scalar", "t0", c_t)
    p.act(consts["rep_bf"][:, :width], psB[:, :width], AF.Copy)
    c_s = p.inc("scalar", "s0")
    p.wait("vector", "s0", c_s)
    tt = t_tiles if t_tiles is not None else x_tiles_f32
    for i in range(nch):
        p.v("tensor_tensor", tt[i][:, :width], x_tiles_f32[i][:, :width],
            consts["rep_bf"][:, :width], ALU.subtract)
    c_v = p.inc("vector", "v0")
    p.wait("scalar", "v0", c_v)
    for i in range(nch):
        p.act(xb_tiles[i][:, :width], tt[i][:, :width], AF.Square)
    c_s = p.inc("scalar", "s0")
    p.wait("tensor", "s0", c_s)
    for i in range(nch):
        p.mm(psA[0:1, :width], consts["ones_col"][:, 0:1], xb_tiles[i][:, :width],
             start=(i == 0), stop=(i == nch - 1))
    c_t = p.inc("tensor", "t0")
    p.wait("scalar", "t0", c_t)
    p.act(consts["row_a"][0:1, :width], psA[0:1, :width], AF.Ln,
          scale=1.0 / N, bias=eps)
    p.act(consts["rowbf"][0:1, :width], consts["row_a"][0:1, :width], AF.Exp,
          scale=-0.5)
    c_s = p.inc("scalar", "s0")
    p.wait("tensor", "s0", c_s)
    p.mm(psB[:, :width], consts["ones_row"], consts["rowbf"][0:1, :width])
    c_t = p.inc("tensor", "t0")
    p.wait("scalar", "t0", c_t)
    p.act(consts["rep2_bf"][:, :width], psB[:, :width], AF.Copy)
    c_s = p.inc("scalar", "s0")
    p.wait("vector", "s0", c_s)
    for i in range(nch):
        p.v("tensor_tensor", out_bf_tiles[i][:, :width], tt[i][:, :width],
            consts["rep2_bf"][:, :width], ALU.mult)
    if out_f32_tiles is not None:
        for i in range(nch):
            p.v("tensor_tensor", out_f32_tiles[i][:, :width], tt[i][:, :width],
                consts["rep2_bf"][:, :width], ALU.mult)
    if gcol is not None:
        for i in range(nch):
            p.v("tensor_scalar", out_bf_tiles[i][:, :width],
                out_bf_tiles[i][:, :width], gcol[i], None, ALU.mult)
            p.v("tensor_scalar", out_bf_tiles[i][:, :width],
                out_bf_tiles[i][:, :width], bcol[i], None, ALU.add)


def sec_moe(p, xin, xout, W, pref="moe", debug=False):
    nc = p.nc
    NS = 8
    SW = T // NS
    E, DH = 8, 2048

    kind = "ExternalOutput" if debug else "Internal"
    xn_hbm = p.dram(pref + "_xn", (D, T), BF16, kind=kind)
    y_hbm = p.dram(pref + "_y", (D, T), F32, kind=kind)
    cw_dbg = p.out(pref + "_cw", (8, T), F32) if debug else None

    consts = std_consts(p)

    gw = p.take("wbf", 64)
    for k in range(8):
        p.dma("gpsimd", gw[:, 8 * k:8 * k + 8], W["gate_W"][128 * k:128 * k + 128, :])

    cw_rows_b = p.take("cbf", T, rows=8)
    B2t = p.take("wbf", 1024, rows=8)
    p.dma("gpsimd", B2t[0:8, :], W["B2"][:, :])
    esel = p.take("wbf", 1024, rows=8)
    p.dma("gpsimd", esel[0:8, :], W["esel"][:, :])
    identf = p.take("cf32", 128)
    p.dma("sync", identf[:, :], W["ident_f32"][:, :])
    p.barrier()

    p.mark(pref + "s1")
    for s in range(NS):
        p.reset(pref + "s1")
        xf = [p.take("af32", SW) for _ in range(8)]
        xb = [p.take("abf", SW) for _ in range(8)]
        for i in range(8):
            p.dma("sync", xf[i][:, :], xin[128 * i:128 * i + 128, SW * s:SW * s + SW])
        p.barrier()
        emit_ln(p, xf, xb, SW, consts, xb, p.ps[0], p.ps[1])
        c_v = p.inc("vector", "v0")
        p.wait("gpsimd", "v0", c_v)
        for i in range(8):
            p.dma("gpsimd", xn_hbm[128 * i:128 * i + 128, SW * s:SW * s + SW],
                  xb[i][:, :])
        p.wait("tensor", "v0", c_v)
        lg = p.take("af32", 8 * 4)
        pcw = p.take("af32", 8 * 4)
        for tc in range(4):
            for k in range(8):
                p.mm(p.ps[2][:, 8 * tc:8 * tc + 8],
                     xb[k][:, 128 * tc:128 * tc + 128], gw[:, 8 * k:8 * k + 8],
                     start=(k == 0), stop=(k == 7))
        c_t = p.inc("tensor", "t0")
        p.wait("vector", "t0", c_t)
        p.v("tensor_copy", lg[:, :], p.ps[2][:, 0:32])
        mx = p.take("af32", 4)
        sm = p.take("af32", 4)
        for tc in range(4):
            p.v("reduce_max", mx[:, tc:tc + 1], lg[:, 8 * tc:8 * tc + 8], axis=AX.X,
                negate=True)
        c_v = p.inc("vector", "v0")
        p.wait("scalar", "v0", c_v)
        for tc in range(4):
            p.act(pcw[:, 8 * tc:8 * tc + 8], lg[:, 8 * tc:8 * tc + 8], AF.Exp,
                  bias=mx[:, tc:tc + 1], accum_out=sm[:, tc:tc + 1])
        c_s = p.inc("scalar", "s0")
        p.wait("vector", "s0", c_s)
        m1 = p.take("af32", 4)
        m2 = p.take("af32", 4)
        t1 = p.take("af32", 8 * 4)
        w12 = p.take("af32", 8 * 4)
        rs = p.take("af32", 4)
        for tc in range(4):
            a = lg[:, 8 * tc:8 * tc + 8]
            pp = pcw[:, 8 * tc:8 * tc + 8]
            tt = t1[:, 8 * tc:8 * tc + 8]
            ww = w12[:, 8 * tc:8 * tc + 8]
            p.v("reduce_max", m1[:, tc:tc + 1], pp, axis=AX.X)
            p.v("tensor_scalar", ww, pp, m1[:, tc:tc + 1], None, ALU.is_ge)
            p.v("scalar_tensor_tensor", tt, ww, NEG_BIG, pp, ALU.mult, ALU.add)
            p.v("reduce_max", m2[:, tc:tc + 1], tt, axis=AX.X)
            p.v("tensor_scalar", tt, tt, m2[:, tc:tc + 1], None, ALU.is_ge)
            p.v("tensor_tensor", ww, ww, tt, ALU.add)
            p.v("tensor_tensor", ww, ww, pp, ALU.mult)
            p.v("tensor_tensor", rs[:, tc:tc + 1], m1[:, tc:tc + 1],
                m2[:, tc:tc + 1], ALU.add)
            p.v("scalar_tensor_tensor", rs[:, tc:tc + 1], sm[:, tc:tc + 1], 1e-8,
                rs[:, tc:tc + 1], ALU.mult, ALU.add)
            p.v("reciprocal", rs[:, tc:tc + 1], rs[:, tc:tc + 1])
            p.v("tensor_scalar", ww, ww, rs[:, tc:tc + 1], None, ALU.mult)
        c_v = p.inc("vector", "v0")
        p.wait("tensor", "v0", c_v)
        for tc in range(4):
            p.trans(p.ps[3][0:8, 128 * tc:128 * tc + 128],
                    w12[:, 8 * tc:8 * tc + 8], identf[:, :])
        c_t = p.inc("tensor", "t0")
        p.wait("scalar", "t0", c_t)
        p.act(cw_rows_b[0:8, SW * s:SW * s + SW], p.ps[3][0:8, 0:SW], AF.Copy)
        if debug:
            cwf = p.take("af32", SW, rows=8)
            p.act(cwf[0:8, :], p.ps[3][0:8, 0:SW], AF.Copy)
            c_s = p.inc("scalar", "s0")
            p.wait("gpsimd", "s0", c_s)
            p.dma("gpsimd", cw_dbg[0:8, SW * s:SW * s + SW], cwf[0:8, :])
        p.barrier()
    p.reset(pref + "s1")

    p.mark(pref + "s2")
    for s in range(NS):
        p.reset(pref + "s2")
        xb = [p.take("abf", SW) for _ in range(8)]
        yf = [p.take("af32", SW) for _ in range(8)]
        for i in range(8):
            p.dma("sync", xb[i][:, :], xn_hbm[128 * i:128 * i + 128, SW * s:SW * s + SW])
        p.wait_dma("tensor", "sync")
        for i in range(8):
            p.mm(p.ps[i % 4][:, 0:SW], B2t[0:8, 128 * i:128 * i + 128],
                 cw_rows_b[0:8, SW * s:SW * s + SW],
                 start=True, stop=True)
            if i % 4 == 3 or i == 7:
                c_t = p.inc("tensor", "t0")
                p.wait("vector", "t0", c_t)
                for j in range(4 * (i // 4), i + 1):
                    p.v("tensor_tensor", yf[j][:, :],
                        p.ps[j % 4][:, 0:SW], xb[j][:, :], ALU.add)
                c_v = p.inc("vector", "v0")
                p.wait("tensor", "v0", c_v)
        p.wait("gpsimd", "v0", p.cnt["v0"])
        for j in range(8):
            p.dma("gpsimd", y_hbm[128 * j:128 * j + 128, SW * s:SW * s + SW],
                  yf[j][:, :])
        p.barrier()
    p.reset(pref + "s2")

    b1c = p.take("wf32", 16)
    p.mark(pref + "s3")
    for e in range(E):
        p.reset(pref + "s3")
        w1 = [p.take("wbf", DH) for _ in range(8)]
        w2 = [p.take("wbf", D) for _ in range(16)]
        p.dma("gpsimd", b1c[:, :], W["b1r"][e])
        for k in range(8):
            p.dma("gpsimd", w1[k][:, :], W["W1"][e, 128 * k:128 * k + 128, :])
        for k in range(16):
            p.dma("gpsimd", w2[k][:, :], W["W2"][e, 128 * k:128 * k + 128, :])
        p.barrier()
        for s in range(NS):
            p.mark(pref + "s3s")
            xb = [p.take("abf", SW) for _ in range(8)]
            yf = [p.take("af32", SW) for _ in range(8)]
            p.wait_dma("sync", "gpsimd")
            for i in range(8):
                p.dma("sync", xb[i][:, :],
                      xn_hbm[128 * i:128 * i + 128, SW * s:SW * s + SW])
                p.dma("sync", yf[i][:, :],
                      y_hbm[128 * i:128 * i + 128, SW * s:SW * s + SW])
            cwrep = p.take("abf", SW)
            p.wait_dma("tensor", "sync")
            p.mm(p.ps[6][:, 0:SW], esel[0:8, 128 * e:128 * e + 128],
                 cw_rows_b[0:8, SW * s:SW * s + SW])
            c_t = p.inc("tensor", "t0")
            p.wait("scalar", "t0", c_t)
            p.act(cwrep[:, :], p.ps[6][:, 0:SW], AF.Copy)
            c_s = p.inc("scalar", "s0")
            p.wait("vector", "s0", c_s)
            p.wait("tensor", "s0", c_s)
            ht = [p.take("abf", SW) for _ in range(16)]
            hr = p.take("abf", SW)
            for mc in range(16):
                pb = p.ps[mc % 2]
                for k in range(8):
                    p.mm(pb[:, 0:SW], w1[k][:, 128 * mc:128 * mc + 128], xb[k][:, :],
                         start=(k == 0), stop=(k == 7))
                c_t = p.inc("tensor", "t0")
                p.wait("scalar", "t0", c_t)
                p.act(hr[:, :], pb[:, 0:SW], AF.Relu, bias=b1c[:, mc:mc + 1])
                c_s = p.inc("scalar", "s0")
                p.wait("vector", "s0", c_s)
                p.v("tensor_tensor", ht[mc][:, :], hr[:, :], cwrep[:, :], ALU.mult)
                c_v = p.inc("vector", "v0")
                p.wait("scalar", "v0", c_v)
                p.wait("tensor", "v0", c_v)
            for half in range(2):
                for dm in range(4):
                    j = 4 * half + dm
                    for mc in range(16):
                        p.mm(p.ps[2 + dm][:, 0:SW],
                             w2[mc][:, 128 * j:128 * j + 128], ht[mc][:, :],
                             start=(mc == 0), stop=(mc == 15))
                c_t = p.inc("tensor", "t0")
                p.wait("vector", "t0", c_t)
                for dm in range(4):
                    j = 4 * half + dm
                    p.v("tensor_tensor", yf[j][:, :],
                        yf[j][:, :], p.ps[2 + dm][:, 0:SW], ALU.add)
                c_v = p.inc("vector", "v0")
                p.wait("tensor", "v0", c_v)
            p.wait("gpsimd", "v0", p.cnt["v0"])
            for j in range(8):
                p.dma("gpsimd", y_hbm[128 * j:128 * j + 128, SW * s:SW * s + SW],
                      yf[j][:, :])
            p.reset(pref + "s3s")
        p.barrier()
    p.reset(pref + "s3")

    p.mark(pref + "s4")
    for s in range(NS):
        p.reset(pref + "s4")
        yv = [p.take("af32", SW) for _ in range(8)]
        yb = [p.take("abf", SW) for _ in range(8)]
        ob = [p.take("abf", SW) for _ in range(8)]
        xf = [p.take("af32", SW) for _ in range(8)]
        for i in range(8):
            p.dma("sync", xf[i][:, :], xin[128 * i:128 * i + 128, SW * s:SW * s + SW])
            p.dma("sync", yv[i][:, :], y_hbm[128 * i:128 * i + 128, SW * s:SW * s + SW])
        p.barrier()
        emit_ln(p, yv, yb, SW, consts, ob, p.ps[0], p.ps[1])
        c_v = p.inc("vector", "v0")
        p.wait("vector", "v0", c_v)
        for i in range(8):
            p.v("tensor_tensor", xf[i][:, :], xf[i][:, :], ob[i][:, :], ALU.add)
        c_v = p.inc("vector", "v0")
        p.wait("gpsimd", "v0", c_v)
        for i in range(8):
            p.dma("gpsimd", xout[128 * i:128 * i + 128, SW * s:SW * s + SW],
                  xf[i][:, :])
        p.barrier()


def host_moe_weights(inputs):
    W1 = np.ascontiguousarray(inputs["L1_ffn_W1"])
    W2 = np.ascontiguousarray(inputs["L1_ffn_W2"])
    b1 = inputs["L1_ffn_b1"]
    b1r = np.ascontiguousarray(b1.reshape(8, 16, 128).transpose(0, 2, 1))
    B2 = np.ascontiguousarray(inputs["L1_ffn_b2"])
    gate_W = np.ascontiguousarray(inputs["L1_ffn_gate_W"])
    ident = np.eye(128, dtype=np.float32)
    esel = np.zeros((8, 1024), dtype=np.float32)
    for e in range(8):
        esel[e, 128 * e:128 * e + 128] = 1.0
    return dict(W1=W1, W2=W2, b1r=b1r, B2=B2, gate_W=gate_W, ident_f32=ident,
                esel=esel)


def build_moe_prog(debug=False):
    p = Prog("moe")
    p.arena("cst", 128, F32)
    p.arena("cbf", 10240, BF16)
    p.arena("cf32", 1024, F32)
    p.arena("wbf", 8 * 2048 + 16 * 1024 + 4096, BF16)
    p.arena("wf32", 64, F32)
    p.arena("abf", 15360, BF16)
    p.arena("af32", 18 * 512, F32)
    xin = p.inp("xin", (D, T), F32)
    xout = p.out("xout", (D, T), F32)
    W = {
        "gate_W": p.inp("gate_W", (1024, 8), F32),
        "W1": p.inp("W1", (8, 1024, 2048), F32),
        "b1r": p.inp("b1r", (8, 128, 16), F32),
        "W2": p.inp("W2", (8, 2048, 1024), F32),
        "B2": p.inp("B2", (8, 1024), F32),
        "esel": p.inp("esel", (8, 1024), F32),
        "ident_f32": p.inp("ident_f32", (128, 128), F32),
    }
    sec_moe(p, xin, xout, W, debug=debug)
    return p.build()


def std_consts(p):
    if hasattr(p, "_stdconsts"):
        return p._stdconsts
    consts = {
        "ones_col": p.take("cbf", 8),
        "ones_row": p.take("cbf", 128, rows=1),
        "rowbf": p.take("cbf", 1024, rows=1),
        "row_a": p.take("cf32", 512, rows=1),
        "rep_bf": p.take("cbf", 512),
        "rep2_bf": p.take("cbf", 512),
    }
    p.v("memset", consts["ones_col"][:, 0:1], 1.0)
    p.v("memset", consts["ones_row"][0:1, :], 1.0)
    p.constv(1e-5)
    p.constv(1.0)
    p._stdconsts = consts
    return consts


def sec_gla(p, xin, xout, W, pref="gla", debug=False):
    NS = 8
    SW = T // NS
    NH, HDK, HDV, CH = 4, 256, 512, 64
    kind = "ExternalOutput" if debug else "Internal"

    xn_hbm = p.dram(pref + "_xn", (D, T), BF16, kind=kind)
    q_hbm = p.dram(pref + "_q", (D, T), BF16, kind=kind)
    k_hbm = p.dram(pref + "_k", (D, T), BF16, kind=kind)
    gkt_hbm = p.dram(pref + "_gkt", (D, T), F32, kind=kind)
    v_hbm = p.dram(pref + "_v", (T, 2048), BF16, kind=kind)
    sg_hbm = p.dram(pref + "_sg", (2048, T), BF16, kind=kind)
    og_hbm = p.dram(pref + "_og", (2048, T), BF16, kind=kind)
    op_hbm = p.dram(pref + "_op", (2048, T), BF16, kind=kind) if debug else None
    if debug:
        dbg_sT = p.out(pref + "_dbg_sT", (64, 64), BF16)
        dbg_q = p.out(pref + "_dbg_q", (128, 64), BF16)
        dbg_k = p.out(pref + "_dbg_k", (128, 64), BF16)
        dbg_b = p.out(pref + "_dbg_b", (128, 64), F32)
        dbg_eb = p.out(pref + "_dbg_eb", (128, 64), BF16)
        dbg_qraw = p.out(pref + "_dbg_qraw", (128, 64), BF16)

    consts = std_consts(p)
    consts["maskT"] = p.take("cf32", 64, rows=64)
    p.dma("sync", consts["maskT"][0:64, :], W["maskT"][:, :])
    identb = p.take("cbf", 128)
    p.dma("sync", identb[:, :], W["ident_bf"][:, :])
    zer = p.take("cf32", 64)
    p.v("memset", zer[:, :], 0.0)

    bgk = p.take("cf32", 8)
    nbgk = p.take("cf32", 8)
    gnw = p.take("cf32", 4)
    p.dma("sync", bgk[:, :], W["bgk2r"][:, :])
    p.dma("sync", gnw[:, :], W["gnwr"][:, :])
    p.wait_dma("vector", "sync")
    p.v("tensor_scalar_mul", nbgk[:, :], bgk[:, :], -1.0)

    p.mark(pref + "w")
    wq = [p.take("wbf", 1024) for _ in range(8)]
    wk = [p.take("wbf", 1024) for _ in range(8)]
    wgk1 = p.take("wbf", 128)
    wgk2 = p.take("wbf", 1024, rows=16)
    for k in range(8):
        r = slice(128 * k, 128 * k + 128)
        p.dma("gpsimd", wq[k][:, :], W["Wq"][r, :])
        p.dma("gpsimd", wk[k][:, :], W["Wk"][r, :])
        p.dma("gpsimd", wgk1[:, 16 * k:16 * k + 16], W["Wgk1"][r, :])
    p.dma("gpsimd", wgk2[0:16, :], W["Wgk2"][:, :])

    p.barrier()
    p.mark(pref + "p1")
    for s in range(NS):
        p.reset(pref + "p1")
        cs = slice(SW * s, SW * s + SW)
        xf = [p.take("af32", SW) for _ in range(8)]
        xb = [p.take("abf", SW) for _ in range(8)]
        for i in range(8):
            p.dma("sync", xf[i][:, :], xin[128 * i:128 * i + 128, cs])
        p.barrier()
        emit_ln(p, xf, xb, SW, consts, xb, p.ps[0], p.ps[1])
        c_v = p.inc("vector", "v0")
        p.wait("gpsimd", "v0", c_v)
        for i in range(8):
            p.dma("gpsimd", xn_hbm[128 * i:128 * i + 128, cs], xb[i][:, :])
        p.wait("tensor", "v0", c_v)
        ev = p.take("abf", SW)
        for (wt, dst, scl) in ((wq, q_hbm, 1.0 / 16.0), (wk, k_hbm, 1.0)):
            for mc in range(8):
                pb = p.ps[2 + mc % 2]
                for kc in range(8):
                    p.mm(pb[:, 0:SW], wt[kc][:, 128 * mc:128 * mc + 128],
                         xb[kc][:, :], start=(kc == 0), stop=(kc == 7))
                c_t = p.inc("tensor", "t0")
                p.wait("scalar", "t0", c_t)
                p.act(ev[:, :], pb[:, 0:SW], AF.Copy, scale=scl)
                c_s = p.inc("scalar", "s0")
                p.wait("gpsimd", "s0", c_s)
                p.dma("gpsimd", dst[128 * mc:128 * mc + 128, cs], ev[:, :])
                p.wait("tensor", "dma_gpsimd", p.cnt["dma_gpsimd"])
        t1b = p.take("abf", SW, rows=16)
        for kc in range(8):
            p.mm(p.ps[4][0:16, 0:SW], wgk1[:, 16 * kc:16 * kc + 16],
                 xb[kc][:, :], start=(kc == 0), stop=(kc == 7))
        c_t = p.inc("tensor", "t0")
        p.wait("scalar", "t0", c_t)
        p.act(t1b[0:16, :], p.ps[4][0:16, 0:SW], AF.Copy)
        c_s = p.inc("scalar", "s0")
        p.wait("tensor", "s0", c_s)
        sa = p.take("af32", SW)
        sb = p.take("af32", SW)
        gk_o = p.take("af32", SW)
        for mc in range(8):
            pb = p.ps[2 + mc % 2]
            p.mm(pb[:, 0:SW], wgk2[0:16, 128 * mc:128 * mc + 128], t1b[0:16, :])
            c_t = p.inc("tensor", "t0")
            p.wait("scalar", "t0", c_t)
            bcol = bgk[:, mc:mc + 1]
            p.act(sa[:, :], pb[:, 0:SW], AF.Abs, bias=bcol)
            p.act(sa[:, :], sa[:, :], AF.Exp, scale=-1.0)
            p.act(sa[:, :], sa[:, :], AF.Ln, bias=1.0)
            p.act(sb[:, :], pb[:, 0:SW], AF.Relu, scale=-1.0,
                  bias=nbgk[:, mc:mc + 1])
            c_s = p.inc("scalar", "s0")
            p.wait("vector", "s0", c_s)
            p.v("tensor_tensor", gk_o[:, :], sa[:, :], sb[:, :], ALU.add)
            c_v = p.inc("vector", "v0")
            p.wait("gpsimd", "v0", c_v)
            p.dma("gpsimd", gkt_hbm[128 * mc:128 * mc + 128, cs], gk_o[:, :])
            p.wait("scalar", "dma_gpsimd", p.cnt["dma_gpsimd"])
            p.wait("tensor", "dma_gpsimd", p.cnt["dma_gpsimd"])
        p.barrier()
    p.reset(pref + "p1")
    p.reset(pref + "w")

    wv = [p.take("wbf", 2048) for _ in range(8)]
    wg = [p.take("wbf", 2048) for _ in range(8)]
    for k in range(8):
        r = slice(128 * k, 128 * k + 128)
        p.dma("gpsimd", wv[k][:, :], W["Wv"][r, :])
        p.dma("gpsimd", wg[k][:, :], W["Wg"][r, :])
    p.barrier()
    p.mark(pref + "p1b")
    for s in range(NS):
        p.reset(pref + "p1b")
        cs = slice(SW * s, SW * s + SW)
        xb = [p.take("abf", SW) for _ in range(8)]
        for i in range(8):
            p.dma("sync", xb[i][:, :], xn_hbm[128 * i:128 * i + 128, cs])
        p.barrier()
        ev = p.take("abf", SW)
        for mc in range(16):
            pb = p.ps[2 + mc % 2]
            for kc in range(8):
                p.mm(pb[:, 0:SW], wg[kc][:, 128 * mc:128 * mc + 128],
                     xb[kc][:, :], start=(kc == 0), stop=(kc == 7))
            c_t = p.inc("tensor", "t0")
            p.wait("scalar", "t0", c_t)
            p.act(ev[:, :], pb[:, 0:SW], AF.Silu)
            c_s = p.inc("scalar", "s0")
            p.wait("gpsimd", "s0", c_s)
            p.dma("gpsimd", sg_hbm[128 * mc:128 * mc + 128, cs], ev[:, :])
            p.wait("tensor", "dma_gpsimd", p.cnt["dma_gpsimd"])
        vb = p.take("abf", 512)
        for tc in range(4):
            for dvc in range(4):
                pb = p.ps[2 + (tc * 4 + dvc) % 2]
                for kc in range(8):
                    p.mm(pb[:, 0:512], xb[kc][:, 128 * tc:128 * tc + 128],
                         wv[kc][:, 512 * dvc:512 * dvc + 512],
                         start=(kc == 0), stop=(kc == 7))
                c_t = p.inc("tensor", "t0")
                p.wait("scalar", "t0", c_t)
                p.act(vb[:, :], pb[:, 0:512], AF.Copy)
                c_s = p.inc("scalar", "s0")
                p.wait("gpsimd", "s0", c_s)
                p.dma("gpsimd",
                      v_hbm[SW * s + 128 * tc:SW * s + 128 * tc + 128,
                            512 * dvc:512 * dvc + 512], vb[:, :])
                p.wait("tensor", "dma_gpsimd", p.cnt["dma_gpsimd"])
        p.barrier()
    p.reset(pref + "p1b")
    p.reset(pref + "w")

    p.mark(pref + "p2")
    NG = 8
    for h in range(NH):
        p.reset(pref + "p2")
        hr = slice(256 * h, 256 * h + 256)
        S0 = p.take("sf32", 512)
        S1 = p.take("sf32", 512)
        Sb0 = p.take("abf", 512)
        Sb1 = p.take("abf", 512)
        o_head = [p.take("obf", T) for _ in range(4)]
        for tl in (S0, S1):
            p.v("memset", tl[:, :], 0.0)
        for tl in (Sb0, Sb1):
            p.v("memset", tl[:, :], 0.0)
        p.mark(pref + "p2g")
        for cg in range(NG):
            p.reset(pref + "p2g")
            gs = slice(SW * cg, SW * cg + SW)
            qg_t = [p.take("abf", SW) for _ in range(2)]
            kg_t = [p.take("abf", SW) for _ in range(2)]
            gk_t = [p.take("af32", SW) for _ in range(2)]
            vg_t = [p.take("abf", 512, rows=64) for _ in range(8)]
            for d2 in range(2):
                rr = slice(256 * h + 128 * d2, 256 * h + 128 * d2 + 128)
                p.dma("sync", qg_t[d2][:, :], q_hbm[rr, gs])
                p.dma("sync", kg_t[d2][:, :], k_hbm[rr, gs])
                p.dma("sync", gk_t[d2][:, :], gkt_hbm[rr, gs])
            for tc in range(8):
                p.dma("sync", vg_t[tc][0:64, :],
                      v_hbm[SW * cg + 64 * tc:SW * cg + 64 * tc + 64,
                            512 * h:512 * h + 512])
            p.barrier()
            for c in range(8):
                ci = slice(64 * c, 64 * c + 64)
                vt = vg_t[c]
                vrows = slice(0, 64)
                bb = [p.take("af32", 64) for _ in range(2)]
                eb = p.take("abf", 64)
                qgc = [p.take("abf", 64) for _ in range(2)]
                kic = [p.take("abf", 64) for _ in range(2)]
                ksc = [p.take("abf", 64) for _ in range(2)]
                blc = p.take("af32", 8)
                sT = p.take("abf", 64, rows=64)
                kst = p.take("abf", 256, rows=64)
                for d2 in range(2):
                    p.v("tensor_tensor_scan", bb[d2][:, :], gk_t[d2][:, ci],
                        zer[:, :], 0.0, ALU.add, ALU.add)
                c_v = p.inc("vector", "v0")
                p.wait("scalar", "v0", c_v)
                for d2 in range(2):
                    p.act(blc[:, d2:d2 + 1], bb[d2][:, 63:64], AF.Copy,
                          scale=-1.0 / 16.0)
                    p.act(blc[:, 2 + d2:3 + d2], bb[d2][:, 63:64], AF.Exp,
                          scale=-1.0 / 16.0)
                c_s = p.inc("scalar", "s0")
                p.wait("vector", "s0", c_s)
                for d2 in range(2):
                    p.act(eb[:, :], bb[d2][:, :], AF.Exp, scale=-1.0 / 16.0,
                          eng="scalar")
                    c_s = p.inc("scalar", "s0")
                    p.wait("vector", "s0", c_s)
                    p.v("tensor_tensor", qgc[d2][:, :], qg_t[d2][:, ci], eb[:, :],
                        ALU.mult)
                    c_v = p.inc("vector", "v0")
                    p.wait("scalar", "v0", c_v)
                    p.act(eb[:, :], bb[d2][:, :], AF.Exp, scale=1.0 / 16.0)
                    c_s = p.inc("scalar", "s0")
                    p.wait("vector", "s0", c_s)
                    p.v("tensor_tensor", kic[d2][:, :], kg_t[d2][:, ci], eb[:, :],
                        ALU.mult)
                    c_v = p.inc("vector", "v0")
                    p.wait("scalar", "v0", c_v)
                    p.act(eb[:, :], bb[d2][:, :], AF.Exp, scale=1.0 / 16.0,
                          bias=blc[:, d2:d2 + 1])
                    c_s = p.inc("scalar", "s0")
                    p.wait("vector", "s0", c_s)
                    p.v("tensor_tensor", ksc[d2][:, :], kg_t[d2][:, ci], eb[:, :],
                        ALU.mult)
                    c_v = p.inc("vector", "v0")
                    p.wait("scalar", "v0", c_v)
                c_v = p.inc("vector", "v0")
                p.wait("tensor", "v0", c_v)
                for d2 in range(2):
                    p.mm(p.ps[4][0:64, 0:64], kic[d2][:, :], qgc[d2][:, :],
                         start=(d2 == 0), stop=(d2 == 1))
                c_t = p.inc("tensor", "t0")
                p.wait("vector", "t0", c_t)
                p.v("tensor_tensor", sT[0:64, :], p.ps[4][0:64, 0:64],
                    consts["maskT"][0:64, :], ALU.mult)
                c_v = p.inc("vector", "v0")
                if debug and h == 0 and cg == 0 and c == 0:
                    p.wait("gpsimd", "v0", c_v)
                    p.dma("gpsimd", dbg_sT[0:64, :], sT[0:64, :])
                    p.dma("gpsimd", dbg_q[:, :], qgc[0][:, :])
                    p.dma("gpsimd", dbg_k[:, :], kic[0][:, :])
                    p.dma("gpsimd", dbg_b[:, :], bb[0][:, :])
                    p.dma("gpsimd", dbg_eb[:, :], eb[:, :])
                    p.dma("gpsimd", dbg_qraw[:, :], qg_t[0][:, 0:64])
                    p.wait_dma("vector", "gpsimd")
                p.wait("tensor", "v0", c_v)
                for dvc in range(4):
                    p.mm(p.ps[dvc][:, 0:64],
                         vt[vrows, 128 * dvc:128 * dvc + 128], sT[0:64, :],
                         start=True, stop=False)
                    for d2, Sb in ((0, Sb0), (1, Sb1)):
                        p.mm(p.ps[dvc][:, 0:64],
                             Sb[:, 128 * dvc:128 * dvc + 128], qgc[d2][:, :],
                             start=False, stop=(d2 == 1))
                c_t = p.inc("tensor", "t0")
                p.wait("scalar", "t0", c_t)
                for dvc in range(4):
                    p.act(o_head[dvc][:, SW * cg + 64 * c:SW * cg + 64 * c + 64],
                          p.ps[dvc][:, 0:64], AF.Copy)
                c_s = p.inc("scalar", "s0")
                p.wait("tensor", "s0", c_s)
                for d2 in range(2):
                    p.trans(p.psT[0:64, 128 * d2:128 * d2 + 128], ksc[d2][:, :],
                            identb[:, :])
                c_t = p.inc("tensor", "t0")
                p.wait("scalar", "t0", c_t)
                p.act(kst[0:64, :], p.psT[0:64, 0:256], AF.Copy)
                c_s = p.inc("scalar", "s0")
                p.wait("tensor", "s0", c_s)
                for d2 in range(2):
                    p.mm(p.ps[5 + d2][:, :], kst[0:64, 128 * d2:128 * d2 + 128],
                         vt[vrows, :], start=True, stop=True)
                c_t = p.inc("tensor", "t0")
                p.wait("vector", "t0", c_t)
                for d2, Sf in ((0, S0), (1, S1)):
                    p.v("scalar_tensor_tensor", Sf[:, :], Sf[:, :],
                        blc[:, 2 + d2:3 + d2], p.ps[5 + d2][:, :],
                        ALU.mult, ALU.add)
                c_v = p.inc("vector", "v0")
                p.wait("scalar", "v0", c_v)
                for d2, (Sf, Sb) in ((0, (S0, Sb0)), (1, (S1, Sb1))):
                    p.act(Sb[:, :], Sf[:, :], AF.Copy)
                c_s = p.inc("scalar", "s0")
                p.wait("tensor", "s0", c_s)
                p.wait("vector", "s0", c_s)
            p.barrier()
        p.mark(pref + "p3")
        for s in range(NS):
            p.reset(pref + "p3")
            cs = slice(SW * s, SW * s + SW)
            if debug:
                for dvc in range(4):
                    p.dma("gpsimd", op_hbm[512 * h + 128 * dvc:512 * h + 128 * dvc + 128, cs],
                          o_head[dvc][:, cs])
            sqb = p.take("abf", SW)
            sgb = [p.take("abf", SW) for _ in range(4)]
            ob = p.take("abf", SW)
            for dvc in range(4):
                p.dma("sync", sgb[dvc][:, :],
                      sg_hbm[512 * h + 128 * dvc:512 * h + 128 * dvc + 128, cs])
            p.barrier()
            for dvc in range(4):
                p.act(sqb[:, :], o_head[dvc][:, cs], AF.Square)
                c_s = p.inc("scalar", "s0")
                p.wait("tensor", "s0", c_s)
                p.mm(p.ps[0][0:1, 0:SW], consts["ones_col"][:, 0:1], sqb[:, :],
                     start=(dvc == 0), stop=(dvc == 3))
                c_t = p.inc("tensor", "t0")
                p.wait("scalar", "t0", c_t)
            p.act(consts["row_a"][0:1, :SW], p.ps[0][0:1, 0:SW], AF.Ln,
                  scale=1.0 / 512.0, bias=1e-5)
            p.act(consts["rowbf"][0:1, :SW], consts["row_a"][0:1, :SW], AF.Exp,
                  scale=-0.5)
            c_s = p.inc("scalar", "s0")
            p.wait("tensor", "s0", c_s)
            p.mm(p.ps[1][:, 0:SW], consts["ones_row"], consts["rowbf"][0:1, :SW])
            c_t = p.inc("tensor", "t0")
            p.wait("vector", "t0", c_t)
            for dvc in range(4):
                p.v("tensor_tensor", ob[:, :], o_head[dvc][:, cs], p.ps[1][:, 0:SW],
                    ALU.mult)
                c_v = p.inc("vector", "v0")
                p.wait("scalar", "v0", c_v)
                p.act(ob[:, :], ob[:, :], AF.Copy, scale=gnw[:, dvc:dvc + 1])
                c_s = p.inc("scalar", "s0")
                p.wait("vector", "s0", c_s)
                p.v("tensor_tensor", ob[:, :], ob[:, :], sgb[dvc][:, :], ALU.mult)
                c_v = p.inc("vector", "v0")
                p.wait("gpsimd", "v0", c_v)
                p.dma("gpsimd", og_hbm[512 * h + 128 * dvc:512 * h + 128 * dvc + 128,
                                       cs], ob[:, :])
                p.wait("vector", "dma_gpsimd", p.cnt["dma_gpsimd"])
                p.wait("scalar", "dma_gpsimd", p.cnt["dma_gpsimd"])
            p.barrier()
        p.reset(pref + "p3")
    p.reset(pref + "p2")

    p.mark(pref + "p4w")
    wo = [p.take("wbf", 1024) for _ in range(16)]
    for k in range(16):
        p.dma("gpsimd", wo[k][:, :], W["Wo"][128 * k:128 * k + 128, :])
    p.barrier()
    p.mark(pref + "p4")
    for s in range(NS):
        p.reset(pref + "p4")
        cs = slice(SW * s, SW * s + SW)
        ogb = [p.take("abf", SW) for _ in range(16)]
        yv = [p.take("af32", SW) for _ in range(8)]
        yb = [p.take("abf", SW) for _ in range(8)]
        obb = [p.take("abf", SW) for _ in range(8)]
        xf = [p.take("af32", SW) for _ in range(8)]
        for k in range(16):
            p.dma("sync", ogb[k][:, :], og_hbm[128 * k:128 * k + 128, cs])
        for i in range(8):
            p.dma("sync", xf[i][:, :], xin[128 * i:128 * i + 128, cs])
        p.barrier()
        for half in range(2):
            for dm in range(4):
                mc = 4 * half + dm
                for kc in range(16):
                    p.mm(p.ps[dm][:, 0:SW], wo[kc][:, 128 * mc:128 * mc + 128],
                         ogb[kc][:, :], start=(kc == 0), stop=(kc == 15))
            c_t = p.inc("tensor", "t0")
            p.wait("scalar", "t0", c_t)
            for dm in range(4):
                mc = 4 * half + dm
                p.act(yv[mc][:, :], p.ps[dm][:, 0:SW], AF.Copy)
            c_s = p.inc("scalar", "s0")
            p.wait("tensor", "s0", c_s)
        p.wait("vector", "s0", p.cnt["s0"])
        emit_ln(p, yv, yb, SW, consts, obb, p.ps[4], p.ps[5])
        c_v = p.inc("vector", "v0")
        p.wait("vector", "v0", c_v)
        for i in range(8):
            p.v("tensor_tensor", xf[i][:, :], xf[i][:, :], obb[i][:, :], ALU.add)
        c_v = p.inc("vector", "v0")
        p.wait("gpsimd", "v0", c_v)
        for i in range(8):
            p.dma("gpsimd", xout[128 * i:128 * i + 128, cs], xf[i][:, :])
        p.barrier()
    p.reset(pref + "p4")
    p.reset(pref + "p4w")


def host_gla_weights(inputs):
    gnwr = np.zeros((128, 4), dtype=np.float32)
    gnw = inputs["L1_attn_g_norm_w"]
    for c in range(4):
        gnwr[:, c] = gnw[128 * c:128 * c + 128]
    bgk = inputs["L1_attn_bgk2"]
    bgk2r = np.ascontiguousarray(bgk.reshape(8, 128).T)
    identb = bf16(np.eye(128, dtype=np.float32))
    maskT = np.zeros((64, 64), dtype=np.float32)
    for j in range(64):
        maskT[j, j:] = 1.0
    return {
        "Wq": np.ascontiguousarray(inputs["L1_attn_Wq"]),
        "Wk": np.ascontiguousarray(inputs["L1_attn_Wk"]),
        "Wv": np.ascontiguousarray(inputs["L1_attn_Wv"]),
        "Wg": np.ascontiguousarray(inputs["L1_attn_Wg"]),
        "Wgk1": np.ascontiguousarray(inputs["L1_attn_Wgk1"]),
        "Wgk2": np.ascontiguousarray(inputs["L1_attn_Wgk2"]),
        "Wo": np.ascontiguousarray(inputs["L1_attn_Wo"]),
        "bgk2r": bgk2r,
        "gnwr": gnwr,
        "ident_bf": identb,
        "maskT": maskT,
    }


def build_gla_prog(debug=False):
    p = Prog("gla")
    p.arena("cst", 128, F32)
    p.arena("cbf", 11264, BF16)
    p.arena("cf32", 1152, F32)
    p.arena("wbf", 33280, BF16)
    p.arena("abf", 18432, BF16)
    p.arena("af32", 19 * 512, F32)
    p.arena("sf32", 1024, F32)
    p.arena("obf", 4 * T, BF16)
    xin = p.inp("xin", (D, T), F32)
    xout = p.out("xout", (D, T), F32)
    W = {
        "Wq": p.inp("Wq", (1024, 1024), F32),
        "Wk": p.inp("Wk", (1024, 1024), F32),
        "Wv": p.inp("Wv", (1024, 2048), F32),
        "Wg": p.inp("Wg", (1024, 2048), F32),
        "Wgk1": p.inp("Wgk1", (1024, 16), F32),
        "Wgk2": p.inp("Wgk2", (16, 1024), F32),
        "Wo": p.inp("Wo", (2048, 1024), F32),
        "bgk2r": p.inp("bgk2r", (128, 8), F32),
        "gnwr": p.inp("gnwr", (128, 4), F32),
        "ident_bf": p.inp("ident_bf", (128, 128), BF16),
        "maskT": p.inp("maskT", (64, 64), F32),
    }
    sec_gla(p, xin, xout, W, debug=debug)
    return p.build()


def sec_mamba(p, xin, xout, W, pref="mmb", debug=False):
    NS = 8
    SW = T // NS
    DI = 2048
    kind = "ExternalOutput" if debug else "Internal"

    xn_hbm = p.dram(pref + "_xn", (D, T), BF16, kind=kind)
    u_hbm = p.dram(pref + "_u", (DI, T), BF16, kind=kind)
    sz_hbm = p.dram(pref + "_sz", (DI, T), BF16, kind=kind)
    yh_hbm = p.dram(pref + "_yh", (DI, T), BF16, kind=kind)
    if debug:
        dt_dbg = p.out(pref + "_dt", (DI, T), F32)
        dbl_dbg = p.out(pref + "_dbl", (128, T), F32)
        d_hda = p.out(pref + "_hda", (128, 512), F32)
        d_hb = p.out(pref + "_hb", (128, 512), F32)
        d_hh = p.out(pref + "_hh", (128, 512), F32)
        d_Br = p.out(pref + "_Br", (128, 512), F32)
        d_Cr = p.out(pref + "_Cr", (128, 512), F32)

    consts = std_consts(p)
    p.constv(1.0)

    cwv = p.take("cf32", 64)
    cbv = p.take("cf32", 16)
    bdt = p.take("cf32", 16)
    dsk = p.take("cf32", 16)
    Atl = p.take("cf32", 256)
    bcsel = p.take("cbf", 4096, rows=32)
    p.dma("sync", cwv[:, :], W["conv_wr"][:, :])
    p.dma("sync", cbv[:, :], W["conv_br"][:, :])
    p.dma("sync", bdt[:, :], W["b_dtr"][:, :])
    p.dma("sync", dsk[:, :], W["D_skipr"][:, :])
    p.dma("sync", Atl[:, :], W["A_logr"][:, :])
    p.dma("gpsimd", bcsel[0:32, :], W["bcsel"][:, :])
    p.barrier()
    p.act(Atl[:, :], Atl[:, :], AF.Exp)
    c_s = p.inc("scalar", "s0")
    p.wait("vector", "s0", c_s)
    p.v("tensor_scalar_mul", Atl[:, :], Atl[:, :], -1.0)
    p.barrier()

    dbl = p.take("dbl", T, rows=96)

    p.mark(pref + "w1")
    winT = [p.take("wbf", 2048) for _ in range(8)]
    wxT = [p.take("wbf", 96) for _ in range(16)]
    for k in range(8):
        p.dma("gpsimd", winT[k][:, :], W["W_inT"][128 * k:128 * k + 128, 0:2048])
    for k in range(16):
        p.dma("gpsimd", wxT[k][:, :], W["W_xprojT"][128 * k:128 * k + 128, :])
    ptail = p.take("af32", 48)
    p.v("memset", ptail[:, :], 0.0)
    p.barrier()
    p.mark(pref + "m1")
    for s in range(NS):
        p.reset(pref + "m1")
        cs = slice(SW * s, SW * s + SW)
        xf = [p.take("af32", SW) for _ in range(8)]
        xb = [p.take("abf", SW) for _ in range(8)]
        for i in range(8):
            p.dma("sync", xf[i][:, :], xin[128 * i:128 * i + 128, cs])
        p.barrier()
        emit_ln(p, xf, xb, SW, consts, xb, p.ps[0], p.ps[1])
        c_v = p.inc("vector", "v0")
        p.wait("tensor", "v0", c_v)
        p.wait("gpsimd", "v0", c_v)
        for i in range(8):
            p.dma("gpsimd", xn_hbm[128 * i:128 * i + 128, cs], xb[i][:, :])
        ub = [p.take("abf", SW) for _ in range(16)]
        xi = p.take("af32", 520)
        xc = p.take("af32", SW)
        for mc in range(16):
            pb = p.ps[2 + mc % 2]
            for kc in range(8):
                p.mm(pb[:, 0:SW], winT[kc][:, 128 * mc:128 * mc + 128],
                     xb[kc][:, :], start=(kc == 0), stop=(kc == 7))
            c_t = p.inc("tensor", "t0")
            p.wait("scalar", "t0", c_t)
            cb = mc
            p.act(xi[:, 3:3 + SW], pb[:, 0:SW], AF.Copy)
            p.act(xi[:, 0:3], ptail[:, 3 * cb:3 * cb + 3], AF.Copy)
            p.act(ptail[:, 3 * cb:3 * cb + 3], xi[:, SW:SW + 3], AF.Copy)
            p.act(xc[:, :], xi[:, 0:SW], AF.Copy, scale=cwv[:, 4 * cb:4 * cb + 1])
            c_s = p.inc("scalar", "s0")
            p.wait("vector", "s0", c_s)
            for k in (1, 2, 3):
                p.v("scalar_tensor_tensor", xc[:, :], xi[:, k:k + SW],
                    cwv[:, 4 * cb + k:4 * cb + k + 1], xc[:, :],
                    ALU.mult, ALU.add)
            c_v = p.inc("vector", "v0")
            p.wait("scalar", "v0", c_v)
            p.act(ub[cb][:, :], xc[:, :], AF.Silu, bias=cbv[:, cb:cb + 1])
            c_s = p.inc("scalar", "s0")
            p.wait("gpsimd", "s0", c_s)
            p.dma("gpsimd", u_hbm[128 * cb:128 * cb + 128, cs], ub[cb][:, :])
            p.wait("tensor", "s0", c_s)
        for kc in range(16):
            p.mm(p.ps[4][0:96, 0:SW], wxT[kc][:, :], ub[kc][:, :],
                 start=(kc == 0), stop=(kc == 15))
        c_t = p.inc("tensor", "t0")
        p.wait("scalar", "t0", c_t)
        p.act(dbl[0:96, cs], p.ps[4][0:96, 0:SW], AF.Copy)
        c_s = p.inc("scalar", "s0")
        p.barrier()
    p.reset(pref + "m1")
    p.reset(pref + "w1")

    wz = [p.take("wbf", 2048) for _ in range(8)]
    for k in range(8):
        p.dma("gpsimd", wz[k][:, :], W["W_inT"][128 * k:128 * k + 128, 2048:4096])
    p.barrier()
    p.mark(pref + "m1b")
    for s in range(NS):
        p.reset(pref + "m1b")
        cs = slice(SW * s, SW * s + SW)
        xb = [p.take("abf", SW) for _ in range(8)]
        ev = p.take("abf", SW)
        for i in range(8):
            p.dma("sync", xb[i][:, :], xn_hbm[128 * i:128 * i + 128, cs])
        p.barrier()
        for mc in range(16):
            pb = p.ps[2 + mc % 2]
            for kc in range(8):
                p.mm(pb[:, 0:SW], wz[kc][:, 128 * mc:128 * mc + 128],
                     xb[kc][:, :], start=(kc == 0), stop=(kc == 7))
            c_t = p.inc("tensor", "t0")
            p.wait("scalar", "t0", c_t)
            p.act(ev[:, :], pb[:, 0:SW], AF.Silu)
            c_s = p.inc("scalar", "s0")
            p.wait("gpsimd", "s0", c_s)
            p.dma("gpsimd", sz_hbm[128 * mc:128 * mc + 128, cs], ev[:, :])
            p.wait("tensor", "dma_gpsimd", p.cnt["dma_gpsimd"])
            p.wait("scalar", "dma_gpsimd", p.cnt["dma_gpsimd"])
        p.barrier()
    p.reset(pref + "m1b")
    p.reset(pref + "w1")
    if debug:
        dbf = p.take("af32", T, rows=96)
        p.act(dbf[0:96, :], dbl[0:96, :], AF.Copy)
        c_s = p.inc("scalar", "s0")
        p.wait("gpsimd", "s0", c_s)
        p.dma("gpsimd", dbl_dbg[0:96, :], dbf[0:96, :])
        p.barrier()
        p.reset(pref + "w1")

    wdtT = p.take("wbf", 2048, rows=64)
    p.dma("gpsimd", wdtT[0:64, :], W["W_dtT"][:, :])
    hcar = p.take("cf32", 256)
    p.v("memset", hcar[:, :], 0.0)
    zer = p.take("cf32", SW)
    p.v("memset", zer[:, :], 0.0)
    p.barrier()
    p.mark(pref + "m2")
    for tc in range(NS):
        p.reset(pref + "m2")
        cs = slice(SW * tc, SW * tc + SW)
        ub = [p.take("abf", SW) for _ in range(16)]
        for cb in range(16):
            p.dma("sync", ub[cb][:, :], u_hbm[128 * cb:128 * cb + 128, cs])
        dt_t = [p.take("abf", SW) for _ in range(16)]
        dtu = [p.take("abf", SW) for _ in range(16)]
        sa = p.take("af32", SW)
        sb2 = p.take("af32", SW)
        p.wait_dma("tensor", "sync")
        for cb in range(16):
            pb = p.ps[2 + cb % 2]
            p.mm(pb[:, 0:SW], wdtT[0:64, 128 * cb:128 * cb + 128],
                 dbl[0:64, cs], start=True, stop=True)
            c_t = p.inc("tensor", "t0")
            p.wait("scalar", "t0", c_t)
            bcol = bdt[:, cb:cb + 1]
            p.act(sa[:, :], pb[:, 0:SW], AF.Abs, bias=bcol)
            p.act(sa[:, :], sa[:, :], AF.Exp, scale=-1.0)
            p.act(sa[:, :], sa[:, :], AF.Ln, bias=1.0)
            p.act(sb2[:, :], pb[:, 0:SW], AF.Relu, bias=bcol)
            c_s = p.inc("scalar", "s0")
            p.wait("vector", "s0", c_s)
            p.v("tensor_tensor", dt_t[cb][:, :], sa[:, :], sb2[:, :], ALU.add)
            p.v("tensor_tensor", dtu[cb][:, :], dt_t[cb][:, :], ub[cb][:, :],
                ALU.mult)
            c_v = p.inc("vector", "v0")
            p.wait("scalar", "v0", c_v)
            p.wait("tensor", "v0", c_v)
            if debug:
                p.wait("gpsimd", "v0", c_v)
                p.dma("gpsimd", dt_dbg[128 * cb:128 * cb + 128, cs],
                      dt_t[cb][:, :])
        bc0 = p.take("abf", SW, rows=32)
        p.act(bc0[0:32, :], dbl[64:96, cs], AF.Copy)
        c_s = p.inc("scalar", "s0")
        p.wait("tensor", "s0", c_s)
        y_t = [p.take("af32", SW) for _ in range(16)]
        for cb in range(16):
            p.v("memset", y_t[cb][:, :], 0.0)
        c_v = p.inc("vector", "v0")
        hda = p.take("af32", SW)
        hb = p.take("af32", SW)
        hh = p.take("af32", SW)
        ht2 = p.take("af32", SW)
        Brep = p.take("abf", SW)
        Crep = p.take("af32", SW)
        for s in range(16):
            p.wait("tensor", "v0", p.cnt["v0"])
            p.mm(p.ps[4][:, 0:SW], bcsel[0:32, 128 * s:128 * s + 128],
                 bc0[0:32, :], start=True, stop=True)
            p.mm(p.ps[5][:, 0:SW], bcsel[0:32, 2048 + 128 * s:2048 + 128 * s + 128],
                 bc0[0:32, :], start=True, stop=True)
            c_t = p.inc("tensor", "t0")
            p.wait("scalar", "t0", c_t)
            p.act(Brep[:, :], p.ps[4][:, 0:SW], AF.Copy)
            p.act(Crep[:, :], p.ps[5][:, 0:SW], AF.Copy)
            c_s = p.inc("scalar", "s0")
            p.wait("vector", "s0", c_s)
            for cb in range(16):
                p.act(hda[:, :], dt_t[cb][:, :], AF.Exp,
                      scale=Atl[:, 16 * cb + s:16 * cb + s + 1])
                c_s = p.inc("scalar", "s0")
                p.wait("vector", "s0", c_s)
                p.v("tensor_tensor", hb[:, :], dtu[cb][:, :], Brep[:, :], ALU.mult)
                p.v("tensor_tensor_scan", hh[:, :], hda[:, :], hb[:, :],
                    hcar[:, 16 * s + cb:16 * s + cb + 1], ALU.mult, ALU.add)
                p.v("tensor_tensor", ht2[:, :], hh[:, :], Crep[:, :], ALU.mult)
                p.v("tensor_tensor", y_t[cb][:, :], y_t[cb][:, :], ht2[:, :],
                    ALU.add)
                c_v = p.inc("vector", "v0")
                p.wait("scalar", "v0", c_v)
                if debug and tc == 0 and s == 0 and cb == 0:
                    p.wait("gpsimd", "v0", c_v)
                    p.dma("gpsimd", d_hda[:, :], hda[:, :])
                    p.dma("gpsimd", d_hb[:, :], hb[:, :])
                    p.dma("gpsimd", d_hh[:, :], hh[:, :])
                    p.dma("gpsimd", d_Br[:, :], Brep[:, :])
                    p.dma("gpsimd", d_Cr[:, :], Crep[:, :])
                    p.wait("scalar", "dma_gpsimd", p.cnt["dma_gpsimd"])
                    p.wait("vector", "dma_gpsimd", p.cnt["dma_gpsimd"])
                p.act(hcar[:, 16 * s + cb:16 * s + cb + 1], hh[:, 511:512],
                      AF.Copy)
                c_s = p.inc("scalar", "s0")
                p.wait("vector", "s0", c_s)
        if debug and tc == 0:
            p.wait("gpsimd", "v0", p.cnt["v0"])
            p.dma("gpsimd", d_hb[:, :], y_t[0][:, :])
            p.wait("vector", "dma_gpsimd", p.cnt["dma_gpsimd"])
        szb = p.take("abf", SW)
        yhb = p.take("abf", SW)
        for cb in range(16):
            if cb > 0:
                p.wait("sync", "v0", p.cnt["v0"])
            p.dma("sync", szb[:, :], sz_hbm[128 * cb:128 * cb + 128, cs])
            p.act(sa[:, :], ub[cb][:, :], AF.Copy, scale=dsk[:, cb:cb + 1])
            c_s = p.inc("scalar", "s0")
            p.wait("vector", "s0", c_s)
            p.wait_dma("vector", "sync")
            p.v("tensor_tensor", y_t[cb][:, :], y_t[cb][:, :], sa[:, :], ALU.add)
            p.v("tensor_tensor", yhb[:, :], y_t[cb][:, :], szb[:, :], ALU.mult)
            c_v = p.inc("vector", "v0")
            p.wait("gpsimd", "v0", c_v)
            p.dma("gpsimd", yh_hbm[128 * cb:128 * cb + 128, cs], yhb[:, :])
            p.wait("scalar", "dma_gpsimd", p.cnt["dma_gpsimd"])
            p.wait("vector", "dma_gpsimd", p.cnt["dma_gpsimd"])
        p.barrier()
    p.reset(pref + "m2")

    woT = [p.take("wbf", 1024) for _ in range(16)]
    for k in range(16):
        p.dma("gpsimd", woT[k][:, :], W["W_outT"][128 * k:128 * k + 128, :])
    p.barrier()
    p.mark(pref + "m3")
    for s in range(NS):
        p.reset(pref + "m3")
        cs = slice(SW * s, SW * s + SW)
        yhb = [p.take("abf", SW) for _ in range(16)]
        yv = [p.take("af32", SW) for _ in range(8)]
        yb = [p.take("abf", SW) for _ in range(8)]
        obb = [p.take("abf", SW) for _ in range(8)]
        xf = [p.take("af32", SW) for _ in range(8)]
        for k in range(16):
            p.dma("sync", yhb[k][:, :], yh_hbm[128 * k:128 * k + 128, cs])
        for i in range(8):
            p.dma("sync", xf[i][:, :], xin[128 * i:128 * i + 128, cs])
        p.barrier()
        for half in range(2):
            for dm in range(4):
                mc = 4 * half + dm
                for kc in range(16):
                    p.mm(p.ps[dm][:, 0:SW], woT[kc][:, 128 * mc:128 * mc + 128],
                         yhb[kc][:, :], start=(kc == 0), stop=(kc == 15))
            c_t = p.inc("tensor", "t0")
            p.wait("scalar", "t0", c_t)
            for dm in range(4):
                mc = 4 * half + dm
                p.act(yv[mc][:, :], p.ps[dm][:, 0:SW], AF.Copy)
            c_s = p.inc("scalar", "s0")
            p.wait("tensor", "s0", c_s)
        p.wait("vector", "s0", p.cnt["s0"])
        emit_ln(p, yv, yb, SW, consts, obb, p.ps[4], p.ps[5])
        c_v = p.inc("vector", "v0")
        p.wait("vector", "v0", c_v)
        for i in range(8):
            p.v("tensor_tensor", xf[i][:, :], xf[i][:, :], obb[i][:, :], ALU.add)
        c_v = p.inc("vector", "v0")
        p.wait("gpsimd", "v0", c_v)
        for i in range(8):
            p.dma("gpsimd", xout[128 * i:128 * i + 128, cs], xf[i][:, :])
        p.barrier()
    p.reset(pref + "m3")


def host_mamba_weights(inputs):
    conv_w = inputs["L0_attn_conv_w"]
    conv_wr = np.ascontiguousarray(
        conv_w.reshape(16, 128, 4).transpose(1, 0, 2).reshape(128, 64))
    def cols(v):
        return np.ascontiguousarray(v.reshape(16, 128).T)
    A_logr = np.ascontiguousarray(
        inputs["L0_attn_A_log"].reshape(16, 128, 16).transpose(1, 0, 2)
        .reshape(128, 256))
    bcsel = np.zeros((32, 4096), dtype=np.float32)
    for s in range(16):
        bcsel[s, 128 * s:128 * s + 128] = 1.0
        bcsel[16 + s, 2048 + 128 * s:2048 + 128 * s + 128] = 1.0
    return {
        "W_inT": np.ascontiguousarray(inputs["L0_attn_W_in"].T),
        "W_xprojT": np.ascontiguousarray(inputs["L0_attn_W_xproj"].T),
        "W_dtT": np.ascontiguousarray(inputs["L0_attn_W_dt"].T),
        "W_outT": np.ascontiguousarray(inputs["L0_attn_W_out"].T),
        "conv_wr": conv_wr,
        "conv_br": cols(inputs["L0_attn_conv_b"]),
        "b_dtr": cols(inputs["L0_attn_b_dt"]),
        "D_skipr": cols(inputs["L0_attn_D_skip"]),
        "A_logr": A_logr,
        "bcsel": bcsel,
    }


def build_mamba_prog(debug=False):
    p = Prog("mmb")
    p.arena("cst", 128, F32)
    p.arena("cbf", 10240, BF16)
    p.arena("cf32", 2304, F32)
    p.arena("dbl", T, BF16)
    p.arena("wbf", 18432, BF16)
    p.arena("abf", 27648, BF16)
    p.arena("af32", 13312, F32)
    xin = p.inp("xin", (D, T), F32)
    xout = p.out("xout", (D, T), F32)
    W = {
        "W_inT": p.inp("W_inT", (1024, 4096), F32),
        "W_xprojT": p.inp("W_xprojT", (2048, 96), F32),
        "W_dtT": p.inp("W_dtT", (64, 2048), F32),
        "W_outT": p.inp("W_outT", (2048, 1024), F32),
        "conv_wr": p.inp("conv_wr", (128, 64), F32),
        "conv_br": p.inp("conv_br", (128, 16), F32),
        "b_dtr": p.inp("b_dtr", (128, 16), F32),
        "D_skipr": p.inp("D_skipr", (128, 16), F32),
        "A_logr": p.inp("A_logr", (128, 256), F32),
        "bcsel": p.inp("bcsel", (32, 4096), F32),
    }
    sec_mamba(p, xin, xout, W, debug=debug)
    return p.build()


def sec_l0ffn(p, xin, xout, W, pref="trf", debug=False, depth=6,
              do_attn=True, do_ffn=True):
    NS = 8
    SW = T // NS
    DEPTH = depth
    DIH = 2816
    kind = "ExternalOutput" if debug else "Internal"

    ra = p.dram(pref + "_ra", (D, T), F32, kind=kind)
    rb = p.dram(pref + "_rb", (D, T), F32, kind=kind)
    xnf_hbm = p.dram(pref + "_xnf", (D, T), BF16, kind=kind)
    hf_hbm = p.dram(pref + "_hf", (DIH, T), BF16, kind=kind)

    consts = std_consts(p)
    identb = p.take("cbf", 128)
    p.dma("sync", identb[:, :], W["ident_bf"][:, :])
    maskadd = p.take("cf32", 384)
    p.dma("sync", maskadd[:, :], W["maskadd"][:, :])
    maskmulT = p.take("cbf", 384)
    p.dma("gpsimd", maskmulT[:, :], W["maskmulT"][:, :])
    ones16 = p.take("cbf", 128)
    p.dma("gpsimd", ones16[:, :], W["ones16"][:, :])
    sel16 = p.take("cbf", 1024, rows=16)
    p.dma("gpsimd", sel16[0:16, :], W["sel16"][:, :])
    hsel = p.take("cbf", 2048, rows=16)
    p.dma("gpsimd", hsel[0:16, :], W["hsel"][:, :])
    qsc = p.take("cf32", 24)
    p.dma("sync", qsc[:, :], W["qk_scales"][:, :])
    lgb = p.take("cf32", 96)
    p.dma("sync", lgb[:, :], W["ln_gb"][:, :])
    b2c = p.take("cf32", 48)
    p.dma("sync", b2c[:, :], W["b2r"][:, :])
    b1c = p.take("cf32", 264)
    p.dma("sync", b1c[:, :], W["b1r"][:, :])
    p.barrier()

    p.mark(pref + "i0")
    for s in range(NS):
        p.reset(pref + "i0")
        cs = slice(SW * s, SW * s + SW)
        xf = [p.take("af32", SW) for _ in range(8)]
        xb = [p.take("abf", SW) for _ in range(8)]
        of = [p.take("af32", SW) for _ in range(8)]
        for i in range(8):
            p.dma("sync", xf[i][:, :], xin[128 * i:128 * i + 128, cs])
        p.barrier()
        emit_ln(p, xf, xb, SW, consts, xb, p.ps[0], p.ps[1],
                out_f32_tiles=of)
        c_v = p.inc("vector", "v0")
        p.wait("gpsimd", "v0", c_v)
        for i in range(8):
            p.dma("gpsimd", ra[128 * i:128 * i + 128, cs], of[i][:, :])
        p.barrier()
    p.reset(pref + "i0")

    cur, nxt = ra, rb
    for li in range(DEPTH):
        if do_attn:
            attn_layer(p, li, cur, nxt, W, consts, identb, maskadd, maskmulT,
                       ones16, sel16, hsel, qsc, pref)
            cur, nxt = nxt, cur
        if do_ffn:
            ffn_layer(p, li, cur, nxt, W, consts, lgb, b1c, b2c, xnf_hbm, hf_hbm,
                      pref)
            cur, nxt = nxt, cur

    p.mark(pref + "f0")
    for s in range(NS):
        p.reset(pref + "f0")
        cs = slice(SW * s, SW * s + SW)
        yv = [p.take("af32", SW) for _ in range(8)]
        yb = [p.take("abf", SW) for _ in range(8)]
        ob = [p.take("abf", SW) for _ in range(8)]
        xf = [p.take("af32", SW) for _ in range(8)]
        for i in range(8):
            p.dma("sync", xf[i][:, :], xin[128 * i:128 * i + 128, cs])
            p.dma("sync", yv[i][:, :], cur[128 * i:128 * i + 128, cs])
        p.barrier()
        emit_ln(p, yv, yb, SW, consts, ob, p.ps[0], p.ps[1])
        c_v = p.inc("vector", "v0")
        p.wait("vector", "v0", c_v)
        for i in range(8):
            p.v("tensor_tensor", xf[i][:, :], xf[i][:, :], ob[i][:, :], ALU.add)
        c_v = p.inc("vector", "v0")
        p.wait("gpsimd", "v0", c_v)
        for i in range(8):
            p.dma("gpsimd", xout[128 * i:128 * i + 128, cs], xf[i][:, :])
        p.barrier()
    p.reset(pref + "f0")


def load_shifted(p, tiles, src, s, SW):
    cs = slice(SW * s, SW * s + SW)
    for i in range(4):
        p.dma("sync", tiles[i][:, :], src[128 * i:128 * i + 128, cs])
    for i in range(4, 8):
        r = slice(128 * i, 128 * i + 128)
        if s == 0:
            p.v("memset", tiles[i][:, 0:1], 0.0)
            p.dma("sync", tiles[i][:, 1:SW], src[r, 0:SW - 1])
        else:
            p.dma("sync", tiles[i][:, :], src[r, SW * s - 1:SW * s + SW - 1])


def attn_layer(p, li, cur, nxt, W, consts, identb, maskadd, maskmulT,
               ones16, sel16, hsel, qsc, pref):
    NS = 8
    SW = T // NS
    p.mark(pref + "aw")
    wqkv = [p.take("wbf", 3072) for _ in range(8)]
    wo = [p.take("wbf", 1024) for _ in range(8)]
    for k in range(8):
        p.dma("gpsimd", wqkv[k][:, :], W["Wqkv"][li, 128 * k:128 * k + 128, :])
        p.dma("gpsimd", wo[k][:, :], W["Wo"][li, 128 * k:128 * k + 128, :])
    qsp = qsc[:, 4 * li:4 * li + 1]
    qsrh = qsc[:, 4 * li + 1:4 * li + 2]
    ksp = qsc[:, 4 * li + 2:4 * li + 3]
    ksrh = qsc[:, 4 * li + 3:4 * li + 4]
    ktail = [p.take("abf", 256) for _ in range(8)]
    vtail = [p.take("abf", 1024) for _ in range(2)]
    p.barrier()
    p.mark(pref + "as")
    for s in range(NS):
        p.reset(pref + "as")
        cs = slice(SW * s, SW * s + SW)
        ws = [p.take("af32", SW) for _ in range(8)]
        wres = [p.take("af32", SW) for _ in range(8)]
        xb = [p.take("abf", SW) for _ in range(8)]
        load_shifted(p, ws, cur, s, SW)
        for i in range(8):
            p.dma("sync", wres[i][:, :], cur[128 * i:128 * i + 128, cs])
        tb_f = p.take("af32", SW)
        Aq = p.take("abf", SW)
        Bq = p.take("abf", SW)
        Ak = p.take("abf", SW)
        Bk = p.take("abf", SW)
        p.barrier()
        for i in range(8):
            p.act(xb[i][:, :], ws[i][:, :], AF.Copy)
        c_s = p.inc("scalar", "s0")
        for (tb, hb, col, scl) in ((Aq, "cs2p", qsp, 8.0), (Bq, "snq", qsrh, 8.0),
                                   (Ak, "ci2p", ksp, 1.0), (Bk, "snk", ksrh, 1.0)):
            p.dma("sync", tb_f[:, :], W[hb][:, cs])
            p.wait_dma("scalar", "sync")
            p.act(tb[:, :], tb_f[:, :], AF.Copy, scale=col)
            c_s = p.inc("scalar", "s0")
            if scl != 1.0:
                p.wait("vector", "s0", c_s)
                p.v("tensor_scalar_mul", tb[:, :], tb[:, :], scl)
                c_v = p.inc("vector", "v0")
                p.wait("sync", "v0", c_v)
            else:
                p.wait("sync", "s0", c_s)
        p.wait("tensor", "s0", p.cnt["s0"])
        qt = [p.take("abf", SW) for _ in range(8)]
        kt = [p.take("abf", SW) for _ in range(8)]
        vt = [p.take("abf", 1024) for _ in range(4)]
        qraw = p.take("af32", SW)
        q2b = p.take("abf", SW)
        tmp = p.take("af32", SW)
        t1 = p.take("abf", SW)
        t2 = p.take("abf", SW)
        nq = p.take("af32", 512, rows=16)
        nqb = p.take("abf", 512, rows=16)
        for part, wcol0, tiles, Atb, Btb in ((0, 0, qt, Aq, Bq),
                                             (1, 1024, kt, Ak, Bk)):
            for mc in range(8):
                pb = p.ps[2 + mc % 2]
                for kc in range(8):
                    p.mm(pb[:, 0:SW],
                         wqkv[kc][:, wcol0 + 128 * mc:wcol0 + 128 * mc + 128],
                         xb[kc][:, :], start=(kc == 0), stop=(kc == 7))
                c_t = p.inc("tensor", "t0")
                p.wait("scalar", "t0", c_t)
                p.act(qraw[:, :], pb[:, 0:SW], AF.Copy)
                p.act(q2b[:, :], pb[:, 0:SW], AF.Square)
                c_s = p.inc("scalar", "s0")
                p.wait("tensor", "s0", c_s)
                p.mm(p.ps[4][0:16, 0:SW], ones16[:, 16 * mc:16 * mc + 16],
                     q2b[:, :], start=(mc == 0), stop=(mc == 7))
                c_t = p.inc("tensor", "t0")
                p.wait("scalar", "t0", c_t)
                p.act(tiles[mc][:, :], qraw[:, :], AF.Copy)
                c_s = p.inc("scalar", "s0")
                p.wait("tensor", "s0", c_s)
            p.wait("scalar", "t0", p.cnt["t0"])
            p.act(nq[0:16, :], p.ps[4][0:16, 0:SW], AF.Sqrt)
            c_s = p.inc("scalar", "s0")
            p.wait("vector", "s0", c_s)
            p.v("tensor_scalar_max", nq[0:16, :], nq[0:16, :], 1e-12)
            p.v("reciprocal", nq[0:16, :], nq[0:16, :])
            c_v = p.inc("vector", "v0")
            p.wait("scalar", "v0", c_v)
            p.act(nqb[0:16, :], nq[0:16, :], AF.Copy)
            c_s = p.inc("scalar", "s0")
            p.wait("tensor", "s0", c_s)
            for mc in range(8):
                p.mm(p.ps[4][:, 0:SW], sel16[0:16, 128 * mc:128 * mc + 128],
                     nqb[0:16, :], start=True, stop=True)
                c_t = p.inc("tensor", "t0")
                p.wait("vector", "t0", c_t)
                p.v("tensor_tensor", tmp[:, :], tiles[mc][:, :], p.ps[4][:, 0:SW],
                    ALU.mult)
                c_v = p.inc("vector", "v0")
                p.wait("scalar", "v0", c_v)
                p.act(qraw[0:32, :], tmp[32:64, :], AF.Copy)
                p.act(qraw[32:64, :], tmp[0:32, :], AF.Copy)
                p.act(qraw[64:96, :], tmp[96:128, :], AF.Copy)
                p.act(qraw[96:128, :], tmp[64:96, :], AF.Copy)
                c_s = p.inc("scalar", "s0")
                p.wait("vector", "s0", c_s)
                p.v("tensor_tensor", t1[:, :], tmp[:, :], Atb[:, :], ALU.mult)
                p.v("tensor_tensor", t2[:, :], qraw[:, :], Btb[:, :], ALU.mult)
                p.v("tensor_tensor", tiles[mc][:, :], t1[:, :], t2[:, :], ALU.add)
                c_v = p.inc("vector", "v0")
                p.wait("scalar", "v0", c_v)
                p.wait("tensor", "v0", c_v)
        for tc in range(4):
            for nv in range(2):
                pb = p.ps[2 + nv]
                for kc in range(8):
                    p.mm(pb[:, 0:SW],
                         xb[kc][:, 128 * tc:128 * tc + 128],
                         wqkv[kc][:, 2048 + 512 * nv:2048 + 512 * nv + 512],
                         start=(kc == 0), stop=(kc == 7))
            c_t = p.inc("tensor", "t0")
            p.wait("scalar", "t0", c_t)
            for nv in range(2):
                p.act(vt[tc][:, 512 * nv:512 * nv + 512], p.ps[2 + nv][:, 0:SW],
                      AF.Copy)
            c_s = p.inc("scalar", "s0")
            p.wait("tensor", "s0", c_s)
        ot = [p.take("abf", SW) for _ in range(8)]
        mcols = p.take("af32", 16)
        zcols = p.take("af32", 16)
        mbb = p.take("abf", 16)
        mbf = p.take("af32", 16)
        ccols = p.take("af32", 16)
        ccb = p.take("abf", 16)
        mrows = p.take("abf", 128, rows=16)
        crows = p.take("abf", 128, rows=16)
        Sm = p.take("af32", 384)
        Pb = p.take("abf", 384)
        PT = p.take("abf", 384)
        crep = p.take("abf", 128, rows=64)
        for qb in range(4):
            qbg = 4 * s + qb
            nk = min(384, 128 * (qbg + 1))
            koff = 128 * (qbg + 1) - nk
            mask_c0 = 384 - nk
            def ksegs(h):
                base = 64 * (h % 2)
                segs = []
                kx = koff - SW * s
                left = nk
                while left > 0:
                    if kx < 0:
                        n = min(-kx, left)
                        segs.append((ktail[h // 2][base:base + 64,
                                                   256 + kx:256 + kx + n], n))
                    else:
                        tcix = kx // 512
                        n = min(512 - kx % 512, left)
                        segs.append((kt[h // 2][base:base + 64,
                                                kx % 512:kx % 512 + n], n))
                    kx += n
                    left -= n
                return segs
            for h in range(16):
                base = 64 * (h % 2)
                lq = qt[h // 2][base:base + 64, 128 * qb:128 * qb + 128]
                col = 0
                kx = koff - SW * s
                left = nk
                first = True
                while left > 0:
                    if kx < 0:
                        n = min(-kx, left)
                        rhs = ktail[h // 2][base:base + 64, 256 + kx:256 + kx + n]
                    else:
                        n = min(512 - kx % 512, left)
                        rhs = kt[h // 2][base:base + 64, kx % 512:kx % 512 + n]
                    p.mm(p.ps[5][:, col:col + n], lq, rhs,
                         start=True, stop=True)
                    col += n
                    kx += n
                    left -= n
                c_t = p.inc("tensor", "t0")
                p.wait("vector", "t0", c_t)
                p.v("scalar_tensor_tensor", Sm[:, 0:nk], p.ps[5][:, 0:nk], 1.0,
                    maskadd[:, mask_c0:mask_c0 + nk], ALU.mult, ALU.add)
                p.v("reduce_max", mcols[:, h:h + 1], Sm[:, 0:nk], axis=AX.X,
                    negate=True)
                c_v = p.inc("vector", "v0")
                p.wait("scalar", "v0", c_v)
                p.act(Pb[:, 0:nk], Sm[:, 0:nk], AF.Exp, bias=mcols[:, h:h + 1],
                      accum_out=zcols[:, h:h + 1])
                c_s = p.inc("scalar", "s0")
                p.wait("tensor", "s0", c_s)
                p.wait("vector", "s0", c_s)
            p.act(mbb[:, :], mcols[:, :], AF.Copy)
            p.act(mbf[:, :], mbb[:, :], AF.Copy)
            c_s = p.inc("scalar", "s0")
            p.wait("vector", "s0", c_s)
            p.v("tensor_tensor", ccols[:, :], mcols[:, :], mbf[:, :], ALU.subtract)
            p.v("reciprocal", zcols[:, :], zcols[:, :])
            c_v = p.inc("vector", "v0")
            p.wait("scalar", "v0", c_v)
            p.act(ccols[:, :], ccols[:, :], AF.Exp)
            c_s = p.inc("scalar", "s0")
            p.wait("vector", "s0", c_s)
            p.v("tensor_tensor", ccols[:, :], ccols[:, :], zcols[:, :], ALU.mult)
            c_v = p.inc("vector", "v0")
            p.wait("scalar", "v0", c_v)
            p.act(ccb[:, :], ccols[:, :], AF.Copy)
            c_s = p.inc("scalar", "s0")
            p.wait("tensor", "s0", c_s)
            p.trans(p.psT[0:16, 0:128], mbb[:, :], identb[:, :])
            p.trans(p.psT[0:16, 128:256], ccb[:, :], identb[:, :])
            c_t = p.inc("tensor", "t0")
            p.wait("scalar", "t0", c_t)
            p.act(mrows[0:16, :], p.psT[0:16, 0:128], AF.Copy)
            p.act(crows[0:16, :], p.psT[0:16, 128:256], AF.Copy)
            c_s = p.inc("scalar", "s0")
            p.wait("tensor", "s0", c_s)
            for h in range(16):
                base = 64 * (h % 2)
                rq = qt[h // 2][base:base + 64, 128 * qb:128 * qb + 128]
                col = 0
                kx = koff - SW * s
                left = nk
                while left > 0:
                    if kx < 0:
                        n = min(-kx, left)
                        lhs = ktail[h // 2][base:base + 64, 256 + kx:256 + kx + n]
                    else:
                        n = min(512 - kx % 512, left)
                        lhs = kt[h // 2][base:base + 64, kx % 512:kx % 512 + n]
                    off = 0
                    while off < n:
                        m = min(128, n - off)
                        p.mm(p.psT[0:m, 512 + col + off:512 + col + off + 128]
                             if False else p.ps[5][0:m, 0:128],
                             lhs[:, off:off + m], rq, start=True, stop=False)
                        p.mm(p.ps[5][0:m, 0:128],
                             hsel[0:16, 128 * h:128 * h + m],
                             mrows[0:16, :], start=False, stop=True)
                        c_t = p.inc("tensor", "t0")
                        p.wait("scalar", "t0", c_t)
                        p.act(PT[0:m, col + off:col + off + 128]
                              if False else PT[0:128, col + off:col + off + 128][0:m, :],
                              p.ps[5][0:m, 0:128], AF.Exp)
                        c_s = p.inc("scalar", "s0")
                        p.wait("tensor", "s0", c_s)
                        off += 128
                    col += n
                    kx += n
                    left -= n
                p.wait("vector", "s0", p.cnt["s0"])
                p.v("tensor_tensor", PT[:, 0:nk], PT[:, 0:nk],
                    maskmulT[:, mask_c0:mask_c0 + nk], ALU.mult)
                c_v = p.inc("vector", "v0")
                p.wait("tensor", "v0", c_v)
                p.mm(p.ps[6][0:64, 0:128], hsel[0:16, 128 * h:128 * h + 64],
                     crows[0:16, :], start=True, stop=True)
                col = 0
                kx = koff - SW * s
                left = nk
                nmm = 0
                while left > 0:
                    n = min(128, left)
                    if kx < 0:
                        vtile = vtail[(256 + kx) // 128]
                        vrow0 = 0
                    else:
                        vtile = vt[kx // 128]
                        vrow0 = 0
                    p.mm(p.ps[4][0:64, 0:128],
                         vtile[0:128, 64 * h:64 * h + 64][0:n, :]
                         if n < 128 else vtile[:, 64 * h:64 * h + 64],
                         PT[0:n, col:col + 128],
                         start=(nmm == 0), stop=(left - n == 0))
                    nmm += 1
                    col += n
                    kx += n
                    left -= n
                c_t = p.inc("tensor", "t0")
                p.wait("scalar", "t0", c_t)
                p.act(crep[0:64, :], p.ps[6][0:64, 0:128], AF.Copy)
                c_s = p.inc("scalar", "s0")
                p.wait("vector", "s0", c_s)
                p.v("tensor_tensor", ot[h // 2][base:base + 64,
                                                128 * qb:128 * qb + 128],
                    p.ps[4][0:64, 0:128], crep[0:64, :], ALU.mult)
                c_v = p.inc("vector", "v0")
                p.wait("tensor", "v0", c_v)
                p.wait("scalar", "v0", c_v)
        for half in range(2):
            for dm in range(4):
                mc = 4 * half + dm
                for kc in range(8):
                    p.mm(p.ps[dm][:, 0:SW], wo[kc][:, 128 * mc:128 * mc + 128],
                         ot[kc][:, :], start=(kc == 0), stop=(kc == 7))
            c_t = p.inc("tensor", "t0")
            p.wait("vector", "t0", c_t)
            for dm in range(4):
                mc = 4 * half + dm
                p.v("tensor_tensor", wres[mc][:, :], wres[mc][:, :],
                    p.ps[dm][:, 0:SW], ALU.add)
            c_v = p.inc("vector", "v0")
            p.wait("tensor", "v0", c_v)
        c_v = p.inc("vector", "v0")
        p.wait("gpsimd", "v0", c_v)
        for i in range(8):
            p.dma("gpsimd", nxt[128 * i:128 * i + 128, cs], wres[i][:, :])
        p.wait("scalar", "v0", c_v)
        for i in range(8):
            p.act(ktail[i][:, :], kt[i][:, SW - 256:SW], AF.Copy)
        for j in range(2):
            p.act(vtail[j][:, :], vt[2 + j][:, :], AF.Copy)
        c_s = p.inc("scalar", "s0")
        p.barrier()
    p.reset(pref + "as")
    p.reset(pref + "aw")


def ffn_layer(p, li, cur, nxt, W, consts, lgb, b1c, b2c, xnf_hbm, hf_hbm, pref):
    NS = 8
    SW = T // NS
    DIH = 2816
    gcol = [lgb[:, 16 * li + i:16 * li + i + 1] for i in range(8)]
    bcol = [lgb[:, 16 * li + 8 + i:16 * li + 8 + i + 1] for i in range(8)]
    for halfw in range(2):
        mc0, mc1 = (0, 11) if halfw == 0 else (11, 22)
        p.mark(pref + "fw")
        w1 = [p.take("wbf", 256 * (mc1 - mc0)) for _ in range(8)]
        for k in range(8):
            p.dma("gpsimd", w1[k][:, :],
                  W["W1p"][li, 128 * k:128 * k + 128, 256 * mc0:256 * mc1])
        p.barrier()
        p.mark(pref + "f1")
        for s in range(NS):
            p.reset(pref + "f1")
            cs = slice(SW * s, SW * s + SW)
            xb = [p.take("abf", SW) for _ in range(8)]
            if halfw == 0:
                ws = [p.take("af32", SW) for _ in range(8)]
                load_shifted(p, ws, cur, s, SW)
                p.barrier()
                emit_ln(p, ws, xb, SW, consts, xb, p.ps[0], p.ps[1],
                        gcol=gcol, bcol=bcol)
                c_v = p.inc("vector", "v0")
                p.wait("gpsimd", "v0", c_v)
                for i in range(8):
                    p.dma("gpsimd", xnf_hbm[128 * i:128 * i + 128, cs],
                          xb[i][:, :])
                p.wait("tensor", "v0", c_v)
            else:
                for i in range(8):
                    p.dma("sync", xb[i][:, :], xnf_hbm[128 * i:128 * i + 128, cs])
                p.barrier()
            gg = p.take("abf", SW)
            hh = p.take("abf", SW)
            for mc in range(mc0, mc1):
                wof = 256 * (mc - mc0)
                for kc in range(8):
                    p.mm(p.ps[2][:, 0:SW], w1[kc][:, wof + 128:wof + 256],
                         xb[kc][:, :], start=(kc == 0), stop=(kc == 7))
                c_t = p.inc("tensor", "t0")
                p.wait("scalar", "t0", c_t)
                p.act(gg[:, :], p.ps[2][:, 0:SW], AF.Gelu,
                      bias=b1c[:, 44 * li + 2 * mc + 1:44 * li + 2 * mc + 2])
                c_s = p.inc("scalar", "s0")
                for kc in range(8):
                    p.mm(p.ps[3][:, 0:SW], w1[kc][:, wof:wof + 128],
                         xb[kc][:, :], start=(kc == 0), stop=(kc == 7))
                c_t = p.inc("tensor", "t0")
                p.wait("vector", "t0", c_t)
                p.wait("vector", "s0", p.cnt["s0"])
                p.v("scalar_tensor_tensor", hh[:, :], p.ps[3][:, 0:SW],
                    b1c[:, 44 * li + 2 * mc:44 * li + 2 * mc + 1], gg[:, :],
                    ALU.add, ALU.mult)
                c_v = p.inc("vector", "v0")
                p.wait("gpsimd", "v0", c_v)
                p.dma("gpsimd", hf_hbm[128 * mc:128 * mc + 128, cs], hh[:, :])
                p.wait("scalar", "dma_gpsimd", p.cnt["dma_gpsimd"])
                p.wait("vector", "dma_gpsimd", p.cnt["dma_gpsimd"])
                p.wait("tensor", "dma_gpsimd", p.cnt["dma_gpsimd"])
            p.barrier()
        p.reset(pref + "f1")
        p.reset(pref + "fw")
    p.mark(pref + "fw2")
    w2 = [p.take("wbf", 1024) for _ in range(22)]
    for k in range(22):
        p.dma("gpsimd", w2[k][:, :], W["W2p"][li, 128 * k:128 * k + 128, :])
    p.barrier()
    p.mark(pref + "f2")
    for s in range(NS):
        p.reset(pref + "f2")
        cs = slice(SW * s, SW * s + SW)
        hb = [p.take("abf", SW) for _ in range(22)]
        wres = [p.take("af32", SW) for _ in range(8)]
        for k in range(22):
            p.dma("sync", hb[k][:, :], hf_hbm[128 * k:128 * k + 128, cs])
        for i in range(8):
            p.dma("sync", wres[i][:, :], cur[128 * i:128 * i + 128, cs])
        p.barrier()
        for half in range(2):
            for dm in range(4):
                mc = 4 * half + dm
                for kc in range(22):
                    p.mm(p.ps[dm][:, 0:SW], w2[kc][:, 128 * mc:128 * mc + 128],
                         hb[kc][:, :], start=(kc == 0), stop=(kc == 21))
            c_t = p.inc("tensor", "t0")
            p.wait("vector", "t0", c_t)
            for dm in range(4):
                mc = 4 * half + dm
                p.v("scalar_tensor_tensor", wres[mc][:, :], p.ps[dm][:, 0:SW],
                    b2c[:, 8 * li + mc:8 * li + mc + 1], wres[mc][:, :],
                    ALU.add, ALU.add)
            c_v = p.inc("vector", "v0")
            p.wait("tensor", "v0", c_v)
        c_v = p.inc("vector", "v0")
        p.wait("gpsimd", "v0", c_v)
        for i in range(8):
            p.dma("gpsimd", nxt[128 * i:128 * i + 128, cs], wres[i][:, :])
        p.barrier()
    p.reset(pref + "f2")
    p.reset(pref + "fw2")


def host_l0ffn_weights(inputs):
    DEPTH, DIM, DH, W_ = 6, 1024, 64, 256
    INNER = 1024
    Wqkv_p = np.ascontiguousarray(np.array(inputs["L0_ffn_Wqkv"]))
    rdim = np.arange(128) % 64
    sign = np.where((np.arange(128) % 64) < 32, -1.0, 1.0)
    half = np.arange(0, DH, 2, dtype=np.float32)
    inv_freq = (1.0 / 10000.0 ** (half / DH)).astype(np.float32)
    t = np.arange(T, dtype=np.float32)
    freqs = t[:, None] * inv_freq[None, :]
    base_scale = ((half + 0.4 * DH) / (1.4 * DH)).astype(np.float32)
    power = ((t - T // 2) / (W_ // 2)).astype(np.float32)
    scale = base_scale[None, :] ** power[:, None]
    cos = np.cos(freqs)
    sin = np.sin(freqs)
    hd = rdim % 32
    cs2p = np.ascontiguousarray((cos[:, hd] * scale[:, hd]).T.astype(np.float32))
    snq = np.ascontiguousarray(
        ((sin[:, hd] * scale[:, hd]).T * sign[:, None]).astype(np.float32))
    ci2p = np.ascontiguousarray((cos[:, hd] / scale[:, hd]).T.astype(np.float32))
    snk = np.ascontiguousarray(
        ((sin[:, hd] / scale[:, hd]).T * sign[:, None]).astype(np.float32))
    rh = (np.arange(128) // 64) * 64 + ((np.arange(128) + 32) % 64)
    qk = np.zeros((128, 4 * DEPTH), dtype=np.float32)
    for li in range(DEPTH):
        qs = inputs["L0_ffn_q_scale"][li]
        ks = inputs["L0_ffn_k_scale"][li]
        qk[:, 4 * li + 0] = qs[rdim]
        qk[:, 4 * li + 1] = qs[rdim[rh]]
        qk[:, 4 * li + 2] = ks[rdim]
        qk[:, 4 * li + 3] = ks[rdim[rh]]
    assert np.all(rdim[rh] == (rdim + 32) % 64)
    maskadd = np.zeros((128, 384), dtype=np.float32)
    r = np.arange(128)[:, None]
    j = np.arange(128)[None, :]
    maskadd[:, 0:128] = np.where(j >= r, 0.0, NEG_BIG)
    maskadd[:, 256:384] = np.where(j <= r, 0.0, NEG_BIG)
    maskmulT = np.ones((128, 384), dtype=np.float32)
    maskmulT[:, 0:128] = (np.arange(128)[:, None] >= np.arange(128)[None, :])
    maskmulT[:, 256:384] = (np.arange(128)[:, None] <= np.arange(128)[None, :])
    ingrp = np.zeros((128, 2), dtype=np.float32)
    ingrp[0:64, 0] = 1
    ingrp[64:128, 1] = 1
    ones16 = np.zeros((128, 128), dtype=np.float32)
    sel16 = np.zeros((16, 1024), dtype=np.float32)
    for c in range(8):
        ones16[:, 16 * c + 2 * c:16 * c + 2 * c + 2] = ingrp
        sel16[2 * c:2 * c + 2, 128 * c:128 * c + 128] = ingrp.T
    hsel = np.zeros((16, 2048), dtype=np.float32)
    for h in range(16):
        hsel[h, 128 * h:128 * h + 128] = 1.0
    lngb = np.zeros((128, 16 * DEPTH), dtype=np.float32)
    for li in range(DEPTH):
        lngb[:, 16 * li:16 * li + 8] = inputs["L0_ffn_ln_g"][li].reshape(8, 128).T
        lngb[:, 16 * li + 8:16 * li + 16] = \
            inputs["L0_ffn_ln_b"][li].reshape(8, 128).T
    W1 = np.array(inputs["L0_ffn_W1"])
    DI0 = 2730
    W1a = np.zeros((DEPTH, DIM, 2816), dtype=np.float32)
    W1g = np.zeros((DEPTH, DIM, 2816), dtype=np.float32)
    W1a[:, :, :DI0] = W1[:, :, :DI0]
    W1g[:, :, :DI0] = W1[:, :, DI0:]
    W1p = np.zeros((DEPTH, DIM, 5632), dtype=np.float32)
    for mc in range(22):
        W1p[:, :, 256 * mc:256 * mc + 128] = W1a[:, :, 128 * mc:128 * mc + 128]
        W1p[:, :, 256 * mc + 128:256 * mc + 256] = \
            W1g[:, :, 128 * mc:128 * mc + 128]
    W1p = np.ascontiguousarray(W1p)
    b1 = np.array(inputs["L0_ffn_b1"])
    b1r = np.zeros((DEPTH, 128, 44), dtype=np.float32)
    b1a = np.zeros((DEPTH, 2816), dtype=np.float32)
    b1g = np.zeros((DEPTH, 2816), dtype=np.float32)
    b1a[:, :DI0] = b1[:, :DI0]
    b1g[:, :DI0] = b1[:, DI0:]
    for mc in range(22):
        b1r[:, :, 2 * mc] = b1a[:, 128 * mc:128 * mc + 128]
        b1r[:, :, 2 * mc + 1] = b1g[:, 128 * mc:128 * mc + 128]
    b1rc = np.zeros((128, 44 * DEPTH), dtype=np.float32)
    for li in range(DEPTH):
        b1rc[:, 44 * li:44 * li + 44] = b1r[li]
    W2 = np.array(inputs["L0_ffn_W2"])
    W2p = np.zeros((DEPTH, 2816, 1024), dtype=np.float32)
    W2p[:, :DI0, :] = W2
    W2p = np.ascontiguousarray(W2p)
    b2r = np.zeros((128, 8 * DEPTH), dtype=np.float32)
    for li in range(DEPTH):
        b2r[:, 8 * li:8 * li + 8] = inputs["L0_ffn_b2"][li].reshape(8, 128).T
    return {
        "Wqkv": Wqkv_p,
        "Wo": np.ascontiguousarray(inputs["L0_ffn_Wo"]),
        "W1p": W1p,
        "W2p": W2p,
        "cs2p": cs2p, "snq": snq, "ci2p": ci2p, "snk": snk,
        "qk_scales": qk,
        "maskadd": maskadd, "maskmulT": maskmulT,
        "ones16": ones16, "sel16": sel16, "hsel": hsel,
        "ln_gb": lngb, "b1r": b1rc, "b2r": b2r,
        "ident_bf": bf16(np.eye(128, dtype=np.float32)),
    }


def build_l0ffn_prog(debug=False, depth=6, do_attn=True, do_ffn=True):
    p = Prog("trf")
    p.arena("cst", 128, F32)
    p.arena("cbf", 10240, BF16)
    p.arena("cf32", 2048, F32)
    p.arena("wbf", 8 * 4096, BF16)
    p.arena("abf", 30720, BF16)
    p.arena("af32", 11264, F32)
    xin = p.inp("xin", (D, T), F32)
    xout = p.out("xout", (D, T), F32)
    W = {
        "Wqkv": p.inp("Wqkv", (6, 1024, 3072), F32),
        "Wo": p.inp("Wo", (6, 1024, 1024), F32),
        "W1p": p.inp("W1p", (6, 1024, 5632), F32),
        "W2p": p.inp("W2p", (6, 2816, 1024), F32),
        "cs2p": p.inp("cs2p", (128, T), F32),
        "snq": p.inp("snq", (128, T), F32),
        "ci2p": p.inp("ci2p", (128, T), F32),
        "snk": p.inp("snk", (128, T), F32),
        "qk_scales": p.inp("qk_scales", (128, 24), F32),
        "maskadd": p.inp("maskadd", (128, 384), F32),
        "maskmulT": p.inp("maskmulT", (128, 384), F32),
        "ones16": p.inp("ones16", (128, 128), F32),
        "sel16": p.inp("sel16", (16, 1024), F32),
        "hsel": p.inp("hsel", (16, 2048), F32),
        "ln_gb": p.inp("ln_gb", (128, 96), F32),
        "b1r": p.inp("b1r", (128, 264), F32),
        "b2r": p.inp("b2r", (128, 48), F32),
        "ident_bf": p.inp("ident_bf", (128, 128), BF16),
    }
    sec_l0ffn(p, xin, xout, W, debug=debug, depth=depth, do_attn=do_attn,
              do_ffn=do_ffn)
    return p.build()


MAMBA_WS = {
    "W_inT": ((1024, 4096), F32), "W_xprojT": ((2048, 96), F32),
    "W_dtT": ((64, 2048), F32), "W_outT": ((2048, 1024), F32),
    "conv_wr": ((128, 64), F32), "conv_br": ((128, 16), F32),
    "b_dtr": ((128, 16), F32), "D_skipr": ((128, 16), F32),
    "A_logr": ((128, 256), F32), "bcsel": ((32, 4096), F32),
}
TRF_WS = {
    "Wqkv": ((6, 1024, 3072), F32), "Wo": ((6, 1024, 1024), F32),
    "W1p": ((6, 1024, 5632), F32), "W2p": ((6, 2816, 1024), F32),
    "cs2p": ((128, T), F32), "snq": ((128, T), F32),
    "ci2p": ((128, T), F32), "snk": ((128, T), F32),
    "qk_scales": ((128, 24), F32), "maskadd": ((128, 384), F32),
    "maskmulT": ((128, 384), F32), "ones16": ((128, 128), F32),
    "sel16": ((16, 1024), F32), "hsel": ((16, 2048), F32),
    "ln_gb": ((128, 96), F32), "b1r": ((128, 264), F32),
    "b2r": ((128, 48), F32), "ident_bf": ((128, 128), BF16),
}
GLA_WS = {
    "Wq": ((1024, 1024), F32), "Wk": ((1024, 1024), F32),
    "Wv": ((1024, 2048), F32), "Wg": ((1024, 2048), F32),
    "Wgk1": ((1024, 16), F32), "Wgk2": ((16, 1024), F32),
    "Wo": ((2048, 1024), F32), "bgk2r": ((128, 8), F32),
    "gnwr": ((128, 4), F32), "ident_bf": ((128, 128), BF16),
    "maskT": ((64, 64), F32),
}
MOE_WS = {
    "gate_W": ((1024, 8), F32), "W1": ((8, 1024, 2048), F32),
    "b1r": ((8, 128, 16), F32), "W2": ((8, 2048, 1024), F32),
    "B2": ((8, 1024), F32), "esel": ((8, 1024), F32),
    "ident_f32": ((128, 128), F32),
}


def build_fused_prog(debug=False):
    p = Prog("fused")
    p.arena("cst", 128, F32)
    p.arena("cbf", 7168, BF16)
    p.arena("bf", 68224, BF16)
    p.arena("ff", 14080, F32)
    for nm in ("wbf", "abf", "dbl", "obf"):
        p.alias(nm, "bf")
    for nm in ("af32", "cf32", "wf32", "sf32", "ybig"):
        p.alias(nm, "ff")
    xin = p.inp("xin", (D, T), F32)
    xout = p.out("xout", (D, T), F32)
    x1 = p.dram("x1", (D, T), F32)
    x2 = p.dram("x2", (D, T), F32)
    x3 = p.dram("x3", (D, T), F32)

    def wmap(prefix, spec):
        return {k: p.inp(prefix + k, s, dt) for k, (s, dt) in spec.items()}

    Wm = wmap("m_", MAMBA_WS)
    Wt = wmap("t_", TRF_WS)
    Wg = wmap("g_", GLA_WS)
    We = wmap("e_", MOE_WS)

    std_consts(p)
    p.mark("sect")
    sec_mamba(p, xin, x1, Wm, debug=debug)
    p.barrier()
    p.reset("sect")
    sec_l0ffn(p, x1, x2, Wt, debug=debug)
    p.barrier()
    p.reset("sect")
    sec_gla(p, x2, x3, Wg, debug=debug)
    p.barrier()
    p.reset("sect")
    sec_moe(p, x3, xout, We, debug=debug)
    return p.build()


_PROGS = {}
FUSED = True


def _get_prog(name, builder):
    if name not in _PROGS:
        _PROGS[name] = builder()
    return _PROGS[name]


def kernel(**inputs):
    x = np.asarray(inputs["L0_attn_x"])
    xT = [np.ascontiguousarray(x[b].T) for b in range(NCORES)]

    hw_m = host_mamba_weights(inputs)
    hw_t = host_l0ffn_weights(inputs)
    hw_g = host_gla_weights({k: inputs[k] for k in inputs if k.startswith("L1_attn")})
    hw_e = host_moe_weights({k: inputs[k] for k in inputs if k.startswith("L1_ffn")})

    def launch(progname, builder, hw, xs):
        nc = _get_prog(progname, builder)
        in_maps = [dict({"xin": xs[b]}, **hw) for b in range(NCORES)]
        res = run_bass_kernel_spmd(nc, in_maps, core_ids=list(range(NCORES)))
        return [np.ascontiguousarray(np.asarray(res.results[b]["xout"]))
                for b in range(NCORES)]

    if FUSED:
        hw = {}
        for pref, d in (("m_", hw_m), ("t_", hw_t), ("g_", hw_g), ("e_", hw_e)):
            for k, v in d.items():
                hw[pref + k] = v
        x4 = launch("fused", build_fused_prog, hw, xT)
    else:
        x1 = launch("mamba", build_mamba_prog, hw_m, xT)
        x2 = launch("l0ffn", build_l0ffn_prog, hw_t, x1)
        x3 = launch("gla", build_gla_prog, hw_g, x2)
        x4 = launch("moe", build_moe_prog, hw_e, x3)
    return np.stack([x4[b].T for b in range(NCORES)], axis=0).astype(np.float32)


# version 04
# speedup vs baseline: 1.1494x; 1.1494x over previous
import numpy as np
from contextlib import ExitStack
import concourse.bass as bass
import concourse.mybir as mybir
from concourse.bass_utils import run_bass_kernel_spmd

F32 = mybir.dt.float32
BF16 = mybir.dt.bfloat16
AF = mybir.ActivationFunctionType
ALU = mybir.AluOpType
AX = mybir.AxisListType

NCORES = 8
T = 4096
D = 1024
ENGINES = ("tensor", "vector", "scalar", "gpsimd", "sync")
NEG_BIG = -1.0e30


def bf16(x):
    import ml_dtypes
    return np.asarray(x, dtype=ml_dtypes.bfloat16)


class Prog:
    def __init__(self, name="prog"):
        self.nc = bass.Bass("TRN2", target_bir_lowering=False, debug=False)
        self.stack = ExitStack()
        self.ops = []
        self.sem = {}
        self.cnt = {}
        self.bar_n = 0
        self.bar_dma_seen = {}
        self._uid = 0
        for e in ENGINES:
            self.new_sem("bar_" + e)
        for q in ("sync", "gpsimd", "scalar", "vector"):
            self.new_sem("dma_" + q)
            self.bar_dma_seen[q] = 0
        for s in ("t0", "t1", "v0", "v1", "s0", "s1", "g0"):
            self.new_sem(s)
        self.ps = [self.stack.enter_context(
            self.nc.psum_tensor(f"ps{i}", [128, 512], F32)) for i in range(7)]
        self.psT = self.stack.enter_context(
            self.nc.psum_tensor("psT", [128, 1024], BF16))
        self.arenas = {}
        self.marks = {}
        self.cvals = {}
        self.peaks = {}

    def uid(self):
        self._uid += 1
        return self._uid

    def new_sem(self, name):
        h = self.stack.enter_context(self.nc.semaphore(name))
        self.sem[name] = h
        self.cnt[name] = 0
        return h

    def arena(self, name, cols, dtype=F32):
        t = self.stack.enter_context(self.nc.sbuf_tensor(name, [128, cols], dtype))
        self.arenas[name] = [t, 0, cols]
        return t

    def take(self, name, cols, rows=128):
        a = self.arenas[name]
        off = (a[1] + 7) & ~7
        assert off + cols <= a[2], f"arena {name} overflow: {off}+{cols} > {a[2]}"
        a[1] = off + cols
        pk = self.peaks.get(a[0].name, 0)
        if a[1] > pk:
            self.peaks[a[0].name] = a[1]
        return a[0][0:rows, off:off + cols]

    def alias(self, name, target):
        self.arenas[name] = self.arenas[target]

    def mark(self, name):
        self.marks[name] = {k: v[1] for k, v in self.arenas.items()}

    def reset(self, name):
        for k, v in self.marks[name].items():
            self.arenas[k][1] = v

    def constv(self, value):
        if value in self.cvals:
            return value
        tile = self.take("cst", 8)
        self.v("memset", tile[:, 0:1], value)
        self.nc.const_aps.register(F32, value, tile[:, 0:1])
        self.cvals[value] = tile
        return value

    def dram(self, name, shape, dtype=F32, kind="Internal"):
        return self.nc.dram_tensor(name, list(shape), dtype, kind=kind).ap()

    def inp(self, name, shape, dtype=F32):
        return self.dram(name, shape, dtype, kind="ExternalInput")

    def out(self, name, shape, dtype=F32):
        return self.dram(name, shape, dtype, kind="ExternalOutput")

    AUTODRAIN = ("vector", "scalar", "gpsimd")

    def op(self, engine, method, *args, inc=None, nodrain=False, **kwargs):
        if inc is not None:
            sname, n = inc
            self.cnt[sname] += n
        drain = (engine in self.AUTODRAIN and not nodrain
                 and method != "dma_start")

        def fn(eng, method=method, args=args, kwargs=kwargs, inc=inc, drain=drain):
            r = getattr(eng, method)(*args, **kwargs)
            if inc is not None:
                r.then_inc(self.sem[inc[0]], inc[1])
            if drain:
                eng.drain()
            return r
        self.ops.append((engine, fn))
        if inc is not None:
            return self.cnt[inc[0]]

    def dma(self, queue, out_ap, in_ap, **kw):
        return self.op(queue, "dma_start", inc=("dma_" + queue, 16),
                       out=out_ap, in_=in_ap, **kw)

    def wait(self, engine, semname, val):
        self.ops.append((engine,
                         lambda eng, s=semname, v=val: eng.wait_ge(self.sem[s], v)))

    def wait_dma(self, engine, queue, val=None):
        self.wait(engine, "dma_" + queue, self.cnt["dma_" + queue] if val is None else val)

    def inc(self, engine, semname, n=1, drain=True):
        self.cnt[semname] += n
        if not drain:
            eng0, fn0 = self.ops[-1]
            assert eng0 == engine, f"last op is {eng0}, not {engine}"

            def fnw(eng, fn0=fn0, s=semname, n=n):
                r = fn0(eng)
                r.then_inc(self.sem[s], n)
                return r
            self.ops[-1] = (engine, fnw)
            return self.cnt[semname]

        def fn(eng, s=semname, n=n):
            return eng.drain().then_inc(self.sem[s], n)
        self.ops.append((engine, fn))
        return self.cnt[semname]

    def barrier(self):
        for q in ("sync", "gpsimd", "scalar", "vector"):
            c = self.cnt["dma_" + q]
            if c > self.bar_dma_seen[q]:
                self.wait(q, "dma_" + q, c)
                self.bar_dma_seen[q] = c
        self.bar_n += 1
        for e in ENGINES:
            self.inc(e, "bar_" + e, 1)
        for e in ENGINES:
            for o in ENGINES:
                if o != e:
                    self.wait(e, "bar_" + o, self.bar_n)

    def mm(self, ps, lhsT, rhs, start=True, stop=True, inc=None):
        return self.op("tensor", "matmul", ps, lhsT, rhs,
                       start=start, stop=stop, inc=inc)

    def trans(self, ps, in_, ident, inc=None):
        return self.op("tensor", "transpose", ps, in_, ident, inc=inc)

    def act(self, out, in_, func, bias=0.0, scale=1.0, accum_out=None, inc=None,
            eng="scalar"):
        return self.op(eng, "activation", out, in_, func,
                       bias=bias, scale=scale, accum_out=accum_out, inc=inc)

    def v(self, method, *args, inc=None, eng="vector", **kw):
        return self.op(eng, method, *args, inc=inc, **kw)

    def build(self):
        with self.nc.Block() as block:
            for ename in ENGINES:
                def make_runner(ename):
                    def runner(eng):
                        for e, fn in self.ops:
                            if e == ename:
                                fn(eng)
                    return runner
                getattr(block, ename)(make_runner(ename))
        self.stack.close()
        return self.nc


def emit_ln(p, x_tiles_f32, xb_tiles, width, consts, out_bf_tiles, psA, psB,
            t_tiles=None, out_f32_tiles=None, gcol=None, bcol=None, eps=1e-5):
    nch = len(x_tiles_f32)
    N = nch * 128
    for i in range(nch):
        p.act(xb_tiles[i][:, :width], x_tiles_f32[i][:, :width], AF.Copy)
    c_s = p.inc("scalar", "s0")
    p.wait("tensor", "s0", c_s)
    for i in range(nch):
        p.mm(psA[0:1, :width], consts["ones_col"][:, 0:1], xb_tiles[i][:, :width],
             start=(i == 0), stop=(i == nch - 1))
    c_t = p.inc("tensor", "t0")
    p.wait("scalar", "t0", c_t)
    p.act(consts["rowbf"][0:1, :width], psA[0:1, :width], AF.Copy, scale=1.0 / N)
    c_s = p.inc("scalar", "s0")
    p.wait("tensor", "s0", c_s)
    p.mm(psB[:, :width], consts["ones_row"], consts["rowbf"][0:1, :width])
    c_t = p.inc("tensor", "t0")
    p.wait("scalar", "t0", c_t)
    p.act(consts["rep_bf"][:, :width], psB[:, :width], AF.Copy)
    c_s = p.inc("scalar", "s0")
    p.wait("vector", "s0", c_s)
    tt = t_tiles if t_tiles is not None else x_tiles_f32
    for i in range(nch):
        p.v("tensor_tensor", tt[i][:, :width], x_tiles_f32[i][:, :width],
            consts["rep_bf"][:, :width], ALU.subtract)
    c_v = p.inc("vector", "v0")
    p.wait("scalar", "v0", c_v)
    for i in range(nch):
        p.act(xb_tiles[i][:, :width], tt[i][:, :width], AF.Square)
    c_s = p.inc("scalar", "s0")
    p.wait("tensor", "s0", c_s)
    for i in range(nch):
        p.mm(psA[0:1, :width], consts["ones_col"][:, 0:1], xb_tiles[i][:, :width],
             start=(i == 0), stop=(i == nch - 1))
    c_t = p.inc("tensor", "t0")
    p.wait("scalar", "t0", c_t)
    p.act(consts["row_a"][0:1, :width], psA[0:1, :width], AF.Ln,
          scale=1.0 / N, bias=eps)
    p.act(consts["rowbf"][0:1, :width], consts["row_a"][0:1, :width], AF.Exp,
          scale=-0.5)
    c_s = p.inc("scalar", "s0")
    p.wait("tensor", "s0", c_s)
    p.mm(psB[:, :width], consts["ones_row"], consts["rowbf"][0:1, :width])
    c_t = p.inc("tensor", "t0")
    p.wait("scalar", "t0", c_t)
    p.act(consts["rep2_bf"][:, :width], psB[:, :width], AF.Copy)
    c_s = p.inc("scalar", "s0")
    p.wait("vector", "s0", c_s)
    for i in range(nch):
        p.v("tensor_tensor", out_bf_tiles[i][:, :width], tt[i][:, :width],
            consts["rep2_bf"][:, :width], ALU.mult)
    if out_f32_tiles is not None:
        for i in range(nch):
            p.v("tensor_tensor", out_f32_tiles[i][:, :width], tt[i][:, :width],
                consts["rep2_bf"][:, :width], ALU.mult)
    if gcol is not None:
        for i in range(nch):
            p.v("tensor_scalar", out_bf_tiles[i][:, :width],
                out_bf_tiles[i][:, :width], gcol[i], None, ALU.mult)
            p.v("tensor_scalar", out_bf_tiles[i][:, :width],
                out_bf_tiles[i][:, :width], bcol[i], None, ALU.add)


def sec_moe(p, xin, xout, W, pref="moe", debug=False):
    nc = p.nc
    NS = 8
    SW = T // NS
    E, DH = 8, 2048

    kind = "ExternalOutput" if debug else "Internal"
    xn_hbm = p.dram(pref + "_xn", (D, T), BF16, kind=kind)
    y_hbm = p.dram(pref + "_y", (D, T), F32, kind=kind)
    cw_dbg = p.out(pref + "_cw", (8, T), F32) if debug else None

    consts = std_consts(p)

    gw = p.take("wbf", 64)
    for k in range(8):
        p.dma("gpsimd", gw[:, 8 * k:8 * k + 8], W["gate_W"][128 * k:128 * k + 128, :])

    cw_rows_b = p.take("cbf", T, rows=8)
    B2t = p.take("wbf", 1024, rows=8)
    p.dma("gpsimd", B2t[0:8, :], W["B2"][:, :])
    esel = p.take("wbf", 1024, rows=8)
    p.dma("gpsimd", esel[0:8, :], W["esel"][:, :])
    identf = p.take("cf32", 128)
    p.dma("sync", identf[:, :], W["ident_f32"][:, :])
    p.barrier()

    p.mark(pref + "s1")
    for s in range(NS):
        p.reset(pref + "s1")
        xf = [p.take("af32", SW) for _ in range(8)]
        xb = [p.take("abf", SW) for _ in range(8)]
        for i in range(8):
            p.dma("sync", xf[i][:, :], xin[128 * i:128 * i + 128, SW * s:SW * s + SW])
        p.barrier()
        emit_ln(p, xf, xb, SW, consts, xb, p.ps[0], p.ps[1])
        c_v = p.inc("vector", "v0")
        p.wait("gpsimd", "v0", c_v)
        for i in range(8):
            p.dma("gpsimd", xn_hbm[128 * i:128 * i + 128, SW * s:SW * s + SW],
                  xb[i][:, :])
        p.wait("tensor", "v0", c_v)
        lg = p.take("af32", 8 * 4)
        pcw = p.take("af32", 8 * 4)
        for tc in range(4):
            for k in range(8):
                p.mm(p.ps[2][:, 8 * tc:8 * tc + 8],
                     xb[k][:, 128 * tc:128 * tc + 128], gw[:, 8 * k:8 * k + 8],
                     start=(k == 0), stop=(k == 7))
        c_t = p.inc("tensor", "t0")
        p.wait("vector", "t0", c_t)
        p.v("tensor_copy", lg[:, :], p.ps[2][:, 0:32])
        mx = p.take("af32", 4)
        sm = p.take("af32", 4)
        for tc in range(4):
            p.v("reduce_max", mx[:, tc:tc + 1], lg[:, 8 * tc:8 * tc + 8], axis=AX.X,
                negate=True)
        c_v = p.inc("vector", "v0")
        p.wait("scalar", "v0", c_v)
        for tc in range(4):
            p.act(pcw[:, 8 * tc:8 * tc + 8], lg[:, 8 * tc:8 * tc + 8], AF.Exp,
                  bias=mx[:, tc:tc + 1], accum_out=sm[:, tc:tc + 1])
        c_s = p.inc("scalar", "s0")
        p.wait("vector", "s0", c_s)
        m1 = p.take("af32", 4)
        m2 = p.take("af32", 4)
        t1 = p.take("af32", 8 * 4)
        w12 = p.take("af32", 8 * 4)
        rs = p.take("af32", 4)
        for tc in range(4):
            a = lg[:, 8 * tc:8 * tc + 8]
            pp = pcw[:, 8 * tc:8 * tc + 8]
            tt = t1[:, 8 * tc:8 * tc + 8]
            ww = w12[:, 8 * tc:8 * tc + 8]
            p.v("reduce_max", m1[:, tc:tc + 1], pp, axis=AX.X)
            p.v("tensor_scalar", ww, pp, m1[:, tc:tc + 1], None, ALU.is_ge)
            p.v("scalar_tensor_tensor", tt, ww, NEG_BIG, pp, ALU.mult, ALU.add)
            p.v("reduce_max", m2[:, tc:tc + 1], tt, axis=AX.X)
            p.v("tensor_scalar", tt, tt, m2[:, tc:tc + 1], None, ALU.is_ge)
            p.v("tensor_tensor", ww, ww, tt, ALU.add)
            p.v("tensor_tensor", ww, ww, pp, ALU.mult)
            p.v("tensor_tensor", rs[:, tc:tc + 1], m1[:, tc:tc + 1],
                m2[:, tc:tc + 1], ALU.add)
            p.v("scalar_tensor_tensor", rs[:, tc:tc + 1], sm[:, tc:tc + 1], 1e-8,
                rs[:, tc:tc + 1], ALU.mult, ALU.add)
            p.v("reciprocal", rs[:, tc:tc + 1], rs[:, tc:tc + 1])
            p.v("tensor_scalar", ww, ww, rs[:, tc:tc + 1], None, ALU.mult)
        c_v = p.inc("vector", "v0")
        p.wait("tensor", "v0", c_v)
        for tc in range(4):
            p.trans(p.ps[3][0:8, 128 * tc:128 * tc + 128],
                    w12[:, 8 * tc:8 * tc + 8], identf[:, :])
        c_t = p.inc("tensor", "t0")
        p.wait("scalar", "t0", c_t)
        p.act(cw_rows_b[0:8, SW * s:SW * s + SW], p.ps[3][0:8, 0:SW], AF.Copy)
        if debug:
            cwf = p.take("af32", SW, rows=8)
            p.act(cwf[0:8, :], p.ps[3][0:8, 0:SW], AF.Copy)
            c_s = p.inc("scalar", "s0")
            p.wait("gpsimd", "s0", c_s)
            p.dma("gpsimd", cw_dbg[0:8, SW * s:SW * s + SW], cwf[0:8, :])
        p.barrier()
    p.reset(pref + "s1")

    p.mark(pref + "s2")
    for s in range(NS):
        p.reset(pref + "s2")
        xb = [p.take("abf", SW) for _ in range(8)]
        yf = [p.take("af32", SW) for _ in range(8)]
        for i in range(8):
            p.dma("sync", xb[i][:, :], xn_hbm[128 * i:128 * i + 128, SW * s:SW * s + SW])
        p.wait_dma("tensor", "sync")
        for i in range(8):
            p.mm(p.ps[i % 4][:, 0:SW], B2t[0:8, 128 * i:128 * i + 128],
                 cw_rows_b[0:8, SW * s:SW * s + SW],
                 start=True, stop=True)
            if i % 4 == 3 or i == 7:
                c_t = p.inc("tensor", "t0")
                p.wait("vector", "t0", c_t)
                for j in range(4 * (i // 4), i + 1):
                    p.v("tensor_tensor", yf[j][:, :],
                        p.ps[j % 4][:, 0:SW], xb[j][:, :], ALU.add)
                c_v = p.inc("vector", "v0")
                p.wait("tensor", "v0", c_v)
        p.wait("gpsimd", "v0", p.cnt["v0"])
        for j in range(8):
            p.dma("gpsimd", y_hbm[128 * j:128 * j + 128, SW * s:SW * s + SW],
                  yf[j][:, :])
        p.barrier()
    p.reset(pref + "s2")

    b1c = p.take("wf32", 16)
    p.mark(pref + "s3")
    for e in range(E):
        p.reset(pref + "s3")
        w1 = [p.take("wbf", DH) for _ in range(8)]
        w2 = [p.take("wbf", D) for _ in range(16)]
        p.dma("gpsimd", b1c[:, :], W["b1r"][e])
        for k in range(8):
            p.dma("gpsimd", w1[k][:, :], W["W1"][e, 128 * k:128 * k + 128, :])
        for k in range(16):
            p.dma("gpsimd", w2[k][:, :], W["W2"][e, 128 * k:128 * k + 128, :])
        p.barrier()
        for s in range(NS):
            p.mark(pref + "s3s")
            xb = [p.take("abf", SW) for _ in range(8)]
            yf = [p.take("af32", SW) for _ in range(8)]
            p.wait_dma("sync", "gpsimd")
            for i in range(8):
                p.dma("sync", xb[i][:, :],
                      xn_hbm[128 * i:128 * i + 128, SW * s:SW * s + SW])
                p.dma("sync", yf[i][:, :],
                      y_hbm[128 * i:128 * i + 128, SW * s:SW * s + SW])
            cwrep = p.take("abf", SW)
            p.wait_dma("tensor", "sync")
            p.mm(p.ps[6][:, 0:SW], esel[0:8, 128 * e:128 * e + 128],
                 cw_rows_b[0:8, SW * s:SW * s + SW])
            c_t = p.inc("tensor", "t0")
            p.wait("scalar", "t0", c_t)
            p.act(cwrep[:, :], p.ps[6][:, 0:SW], AF.Copy)
            c_s = p.inc("scalar", "s0")
            p.wait("vector", "s0", c_s)
            p.wait("tensor", "s0", c_s)
            ht = [p.take("abf", SW) for _ in range(16)]
            hr = p.take("abf", SW)
            for mc in range(16):
                pb = p.ps[mc % 2]
                for k in range(8):
                    p.mm(pb[:, 0:SW], w1[k][:, 128 * mc:128 * mc + 128], xb[k][:, :],
                         start=(k == 0), stop=(k == 7))
                c_t = p.inc("tensor", "t0")
                p.wait("scalar", "t0", c_t)
                p.act(hr[:, :], pb[:, 0:SW], AF.Relu, bias=b1c[:, mc:mc + 1])
                c_s = p.inc("scalar", "s0")
                p.wait("vector", "s0", c_s)
                p.v("tensor_tensor", ht[mc][:, :], hr[:, :], cwrep[:, :], ALU.mult)
                c_v = p.inc("vector", "v0")
                p.wait("scalar", "v0", c_v)
                p.wait("tensor", "v0", c_v)
            for half in range(2):
                for dm in range(4):
                    j = 4 * half + dm
                    for mc in range(16):
                        p.mm(p.ps[2 + dm][:, 0:SW],
                             w2[mc][:, 128 * j:128 * j + 128], ht[mc][:, :],
                             start=(mc == 0), stop=(mc == 15))
                c_t = p.inc("tensor", "t0")
                p.wait("vector", "t0", c_t)
                for dm in range(4):
                    j = 4 * half + dm
                    p.v("tensor_tensor", yf[j][:, :],
                        yf[j][:, :], p.ps[2 + dm][:, 0:SW], ALU.add)
                c_v = p.inc("vector", "v0")
                p.wait("tensor", "v0", c_v)
            p.wait("gpsimd", "v0", p.cnt["v0"])
            for j in range(8):
                p.dma("gpsimd", y_hbm[128 * j:128 * j + 128, SW * s:SW * s + SW],
                      yf[j][:, :])
            p.reset(pref + "s3s")
        p.barrier()
    p.reset(pref + "s3")

    p.mark(pref + "s4")
    for s in range(NS):
        p.reset(pref + "s4")
        yv = [p.take("af32", SW) for _ in range(8)]
        yb = [p.take("abf", SW) for _ in range(8)]
        ob = [p.take("abf", SW) for _ in range(8)]
        xf = [p.take("af32", SW) for _ in range(8)]
        for i in range(8):
            p.dma("sync", xf[i][:, :], xin[128 * i:128 * i + 128, SW * s:SW * s + SW])
            p.dma("sync", yv[i][:, :], y_hbm[128 * i:128 * i + 128, SW * s:SW * s + SW])
        p.barrier()
        emit_ln(p, yv, yb, SW, consts, ob, p.ps[0], p.ps[1])
        c_v = p.inc("vector", "v0")
        p.wait("vector", "v0", c_v)
        for i in range(8):
            p.v("tensor_tensor", xf[i][:, :], xf[i][:, :], ob[i][:, :], ALU.add)
        c_v = p.inc("vector", "v0")
        p.wait("gpsimd", "v0", c_v)
        for i in range(8):
            p.dma("gpsimd", xout[128 * i:128 * i + 128, SW * s:SW * s + SW],
                  xf[i][:, :])
        p.barrier()


def host_moe_weights(inputs):
    W1 = np.ascontiguousarray(inputs["L1_ffn_W1"])
    W2 = np.ascontiguousarray(inputs["L1_ffn_W2"])
    b1 = inputs["L1_ffn_b1"]
    b1r = np.ascontiguousarray(b1.reshape(8, 16, 128).transpose(0, 2, 1))
    B2 = np.ascontiguousarray(inputs["L1_ffn_b2"])
    gate_W = np.ascontiguousarray(inputs["L1_ffn_gate_W"])
    ident = np.eye(128, dtype=np.float32)
    esel = np.zeros((8, 1024), dtype=np.float32)
    for e in range(8):
        esel[e, 128 * e:128 * e + 128] = 1.0
    return dict(W1=W1, W2=W2, b1r=b1r, B2=B2, gate_W=gate_W, ident_f32=ident,
                esel=esel)


def build_moe_prog(debug=False):
    p = Prog("moe")
    p.arena("cst", 128, F32)
    p.arena("cbf", 10240, BF16)
    p.arena("cf32", 1024, F32)
    p.arena("wbf", 8 * 2048 + 16 * 1024 + 4096, BF16)
    p.arena("wf32", 64, F32)
    p.arena("abf", 15360, BF16)
    p.arena("af32", 18 * 512, F32)
    xin = p.inp("xin", (D, T), F32)
    xout = p.out("xout", (D, T), F32)
    W = {
        "gate_W": p.inp("gate_W", (1024, 8), F32),
        "W1": p.inp("W1", (8, 1024, 2048), F32),
        "b1r": p.inp("b1r", (8, 128, 16), F32),
        "W2": p.inp("W2", (8, 2048, 1024), F32),
        "B2": p.inp("B2", (8, 1024), F32),
        "esel": p.inp("esel", (8, 1024), F32),
        "ident_f32": p.inp("ident_f32", (128, 128), F32),
    }
    sec_moe(p, xin, xout, W, debug=debug)
    return p.build()


def std_consts(p):
    if hasattr(p, "_stdconsts"):
        return p._stdconsts
    consts = {
        "ones_col": p.take("cbf", 8),
        "ones_row": p.take("cbf", 128, rows=1),
        "rowbf": p.take("cbf", 1024, rows=1),
        "row_a": p.take("cf32", 512, rows=1),
        "rep_bf": p.take("cbf", 512),
        "rep2_bf": p.take("cbf", 512),
    }
    p.v("memset", consts["ones_col"][:, 0:1], 1.0)
    p.v("memset", consts["ones_row"][0:1, :], 1.0)
    p.constv(1e-5)
    p.constv(1.0)
    p._stdconsts = consts
    return consts


def sec_gla(p, xin, xout, W, pref="gla", debug=False):
    NS = 8
    SW = T // NS
    NH, HDK, HDV, CH = 4, 256, 512, 64
    kind = "ExternalOutput" if debug else "Internal"

    xn_hbm = p.dram(pref + "_xn", (D, T), BF16, kind=kind)
    q_hbm = p.dram(pref + "_q", (D, T), BF16, kind=kind)
    k_hbm = p.dram(pref + "_k", (D, T), BF16, kind=kind)
    gkt_hbm = p.dram(pref + "_gkt", (D, T), F32, kind=kind)
    v_hbm = p.dram(pref + "_v", (T, 2048), BF16, kind=kind)
    sg_hbm = p.dram(pref + "_sg", (2048, T), BF16, kind=kind)
    og_hbm = p.dram(pref + "_og", (2048, T), BF16, kind=kind)
    op_hbm = p.dram(pref + "_op", (2048, T), BF16, kind=kind) if debug else None
    if debug:
        dbg_sT = p.out(pref + "_dbg_sT", (64, 64), BF16)
        dbg_q = p.out(pref + "_dbg_q", (128, 64), BF16)
        dbg_k = p.out(pref + "_dbg_k", (128, 64), BF16)
        dbg_b = p.out(pref + "_dbg_b", (128, 64), F32)
        dbg_eb = p.out(pref + "_dbg_eb", (128, 64), BF16)
        dbg_qraw = p.out(pref + "_dbg_qraw", (128, 64), BF16)

    consts = std_consts(p)
    consts["maskT"] = p.take("cf32", 64, rows=64)
    p.dma("sync", consts["maskT"][0:64, :], W["maskT"][:, :])
    identb = p.take("cbf", 128)
    p.dma("sync", identb[:, :], W["ident_bf"][:, :])
    zer = p.take("cf32", 64)
    p.v("memset", zer[:, :], 0.0)

    bgk = p.take("cf32", 8)
    nbgk = p.take("cf32", 8)
    gnw = p.take("cf32", 4)
    p.dma("sync", bgk[:, :], W["bgk2r"][:, :])
    p.dma("sync", gnw[:, :], W["gnwr"][:, :])
    p.wait_dma("vector", "sync")
    p.v("tensor_scalar_mul", nbgk[:, :], bgk[:, :], -1.0)

    p.mark(pref + "w")
    wq = [p.take("wbf", 1024) for _ in range(8)]
    wk = [p.take("wbf", 1024) for _ in range(8)]
    wgk1 = p.take("wbf", 128)
    wgk2 = p.take("wbf", 1024, rows=16)
    for k in range(8):
        r = slice(128 * k, 128 * k + 128)
        p.dma("gpsimd", wq[k][:, :], W["Wq"][r, :])
        p.dma("gpsimd", wk[k][:, :], W["Wk"][r, :])
        p.dma("gpsimd", wgk1[:, 16 * k:16 * k + 16], W["Wgk1"][r, :])
    p.dma("gpsimd", wgk2[0:16, :], W["Wgk2"][:, :])

    p.barrier()
    p.mark(pref + "p1")
    for s in range(NS):
        p.reset(pref + "p1")
        cs = slice(SW * s, SW * s + SW)
        xf = [p.take("af32", SW) for _ in range(8)]
        xb = [p.take("abf", SW) for _ in range(8)]
        for i in range(8):
            p.dma("sync", xf[i][:, :], xin[128 * i:128 * i + 128, cs])
        p.barrier()
        emit_ln(p, xf, xb, SW, consts, xb, p.ps[0], p.ps[1])
        c_v = p.inc("vector", "v0")
        p.wait("gpsimd", "v0", c_v)
        for i in range(8):
            p.dma("gpsimd", xn_hbm[128 * i:128 * i + 128, cs], xb[i][:, :])
        p.wait("tensor", "v0", c_v)
        ev = p.take("abf", SW)
        for (wt, dst, scl) in ((wq, q_hbm, 1.0 / 16.0), (wk, k_hbm, 1.0)):
            for mc in range(8):
                pb = p.ps[2 + mc % 2]
                for kc in range(8):
                    p.mm(pb[:, 0:SW], wt[kc][:, 128 * mc:128 * mc + 128],
                         xb[kc][:, :], start=(kc == 0), stop=(kc == 7))
                c_t = p.inc("tensor", "t0")
                p.wait("scalar", "t0", c_t)
                p.act(ev[:, :], pb[:, 0:SW], AF.Copy, scale=scl)
                c_s = p.inc("scalar", "s0")
                p.wait("gpsimd", "s0", c_s)
                p.dma("gpsimd", dst[128 * mc:128 * mc + 128, cs], ev[:, :])
                p.wait("tensor", "dma_gpsimd", p.cnt["dma_gpsimd"])
        t1b = p.take("abf", SW, rows=16)
        for kc in range(8):
            p.mm(p.ps[4][0:16, 0:SW], wgk1[:, 16 * kc:16 * kc + 16],
                 xb[kc][:, :], start=(kc == 0), stop=(kc == 7))
        c_t = p.inc("tensor", "t0")
        p.wait("scalar", "t0", c_t)
        p.act(t1b[0:16, :], p.ps[4][0:16, 0:SW], AF.Copy)
        c_s = p.inc("scalar", "s0")
        p.wait("tensor", "s0", c_s)
        sa = p.take("af32", SW)
        sb = p.take("af32", SW)
        gk_o = p.take("af32", SW)
        for mc in range(8):
            pb = p.ps[2 + mc % 2]
            p.mm(pb[:, 0:SW], wgk2[0:16, 128 * mc:128 * mc + 128], t1b[0:16, :])
            c_t = p.inc("tensor", "t0")
            p.wait("scalar", "t0", c_t)
            bcol = bgk[:, mc:mc + 1]
            p.act(sa[:, :], pb[:, 0:SW], AF.Abs, bias=bcol)
            p.act(sa[:, :], sa[:, :], AF.Exp, scale=-1.0)
            p.act(sa[:, :], sa[:, :], AF.Ln, bias=1.0)
            p.act(sb[:, :], pb[:, 0:SW], AF.Relu, scale=-1.0,
                  bias=nbgk[:, mc:mc + 1])
            c_s = p.inc("scalar", "s0")
            p.wait("vector", "s0", c_s)
            p.v("tensor_tensor", gk_o[:, :], sa[:, :], sb[:, :], ALU.add)
            c_v = p.inc("vector", "v0")
            p.wait("gpsimd", "v0", c_v)
            p.dma("gpsimd", gkt_hbm[128 * mc:128 * mc + 128, cs], gk_o[:, :])
            p.wait("scalar", "dma_gpsimd", p.cnt["dma_gpsimd"])
            p.wait("tensor", "dma_gpsimd", p.cnt["dma_gpsimd"])
        p.barrier()
    p.reset(pref + "p1")
    p.reset(pref + "w")

    wv = [p.take("wbf", 2048) for _ in range(8)]
    wg = [p.take("wbf", 2048) for _ in range(8)]
    for k in range(8):
        r = slice(128 * k, 128 * k + 128)
        p.dma("gpsimd", wv[k][:, :], W["Wv"][r, :])
        p.dma("gpsimd", wg[k][:, :], W["Wg"][r, :])
    p.barrier()
    p.mark(pref + "p1b")
    for s in range(NS):
        p.reset(pref + "p1b")
        cs = slice(SW * s, SW * s + SW)
        xb = [p.take("abf", SW) for _ in range(8)]
        for i in range(8):
            p.dma("sync", xb[i][:, :], xn_hbm[128 * i:128 * i + 128, cs])
        p.barrier()
        ev = p.take("abf", SW)
        for mc in range(16):
            pb = p.ps[2 + mc % 2]
            for kc in range(8):
                p.mm(pb[:, 0:SW], wg[kc][:, 128 * mc:128 * mc + 128],
                     xb[kc][:, :], start=(kc == 0), stop=(kc == 7))
            c_t = p.inc("tensor", "t0")
            p.wait("scalar", "t0", c_t)
            p.act(ev[:, :], pb[:, 0:SW], AF.Silu)
            c_s = p.inc("scalar", "s0")
            p.wait("gpsimd", "s0", c_s)
            p.dma("gpsimd", sg_hbm[128 * mc:128 * mc + 128, cs], ev[:, :])
            p.wait("tensor", "dma_gpsimd", p.cnt["dma_gpsimd"])
        vb = p.take("abf", 512)
        for tc in range(4):
            for dvc in range(4):
                pb = p.ps[2 + (tc * 4 + dvc) % 2]
                for kc in range(8):
                    p.mm(pb[:, 0:512], xb[kc][:, 128 * tc:128 * tc + 128],
                         wv[kc][:, 512 * dvc:512 * dvc + 512],
                         start=(kc == 0), stop=(kc == 7))
                c_t = p.inc("tensor", "t0")
                p.wait("scalar", "t0", c_t)
                p.act(vb[:, :], pb[:, 0:512], AF.Copy)
                c_s = p.inc("scalar", "s0")
                p.wait("gpsimd", "s0", c_s)
                p.dma("gpsimd",
                      v_hbm[SW * s + 128 * tc:SW * s + 128 * tc + 128,
                            512 * dvc:512 * dvc + 512], vb[:, :])
                p.wait("tensor", "dma_gpsimd", p.cnt["dma_gpsimd"])
        p.barrier()
    p.reset(pref + "p1b")
    p.reset(pref + "w")

    p.mark(pref + "p2")
    NG = 8
    for h in range(NH):
        p.reset(pref + "p2")
        hr = slice(256 * h, 256 * h + 256)
        S0 = p.take("sf32", 512)
        S1 = p.take("sf32", 512)
        Sb0 = p.take("abf", 512)
        Sb1 = p.take("abf", 512)
        o_head = [p.take("obf", T) for _ in range(4)]
        for tl in (S0, S1):
            p.v("memset", tl[:, :], 0.0)
        for tl in (Sb0, Sb1):
            p.v("memset", tl[:, :], 0.0)
        p.mark(pref + "p2g")
        for cg in range(NG):
            p.reset(pref + "p2g")
            gs = slice(SW * cg, SW * cg + SW)
            qg_t = [p.take("abf", SW) for _ in range(2)]
            kg_t = [p.take("abf", SW) for _ in range(2)]
            gk_t = [p.take("af32", SW) for _ in range(2)]
            vg_t = [p.take("abf", 512, rows=64) for _ in range(8)]
            for d2 in range(2):
                rr = slice(256 * h + 128 * d2, 256 * h + 128 * d2 + 128)
                p.dma("sync", qg_t[d2][:, :], q_hbm[rr, gs])
                p.dma("sync", kg_t[d2][:, :], k_hbm[rr, gs])
                p.dma("sync", gk_t[d2][:, :], gkt_hbm[rr, gs])
            for tc in range(8):
                p.dma("sync", vg_t[tc][0:64, :],
                      v_hbm[SW * cg + 64 * tc:SW * cg + 64 * tc + 64,
                            512 * h:512 * h + 512])
            p.barrier()
            for c in range(8):
                ci = slice(64 * c, 64 * c + 64)
                vt = vg_t[c]
                vrows = slice(0, 64)
                bb = [p.take("af32", 64) for _ in range(2)]
                eb = p.take("abf", 64)
                qgc = [p.take("abf", 64) for _ in range(2)]
                kic = [p.take("abf", 64) for _ in range(2)]
                ksc = [p.take("abf", 64) for _ in range(2)]
                blc = p.take("af32", 8)
                sT = p.take("abf", 64, rows=64)
                kst = p.take("abf", 256, rows=64)
                for d2 in range(2):
                    p.v("tensor_tensor_scan", bb[d2][:, :], gk_t[d2][:, ci],
                        zer[:, :], 0.0, ALU.add, ALU.add)
                c_v = p.inc("vector", "v0")
                p.wait("scalar", "v0", c_v)
                for d2 in range(2):
                    p.act(blc[:, d2:d2 + 1], bb[d2][:, 63:64], AF.Copy,
                          scale=-1.0 / 16.0)
                    p.act(blc[:, 2 + d2:3 + d2], bb[d2][:, 63:64], AF.Exp,
                          scale=-1.0 / 16.0)
                c_s = p.inc("scalar", "s0")
                p.wait("vector", "s0", c_s)
                for d2 in range(2):
                    p.act(eb[:, :], bb[d2][:, :], AF.Exp, scale=-1.0 / 16.0,
                          eng="scalar")
                    c_s = p.inc("scalar", "s0")
                    p.wait("vector", "s0", c_s)
                    p.v("tensor_tensor", qgc[d2][:, :], qg_t[d2][:, ci], eb[:, :],
                        ALU.mult)
                    c_v = p.inc("vector", "v0")
                    p.wait("scalar", "v0", c_v)
                    p.act(eb[:, :], bb[d2][:, :], AF.Exp, scale=1.0 / 16.0)
                    c_s = p.inc("scalar", "s0")
                    p.wait("vector", "s0", c_s)
                    p.v("tensor_tensor", kic[d2][:, :], kg_t[d2][:, ci], eb[:, :],
                        ALU.mult)
                    c_v = p.inc("vector", "v0")
                    p.wait("scalar", "v0", c_v)
                    p.act(eb[:, :], bb[d2][:, :], AF.Exp, scale=1.0 / 16.0,
                          bias=blc[:, d2:d2 + 1])
                    c_s = p.inc("scalar", "s0")
                    p.wait("vector", "s0", c_s)
                    p.v("tensor_tensor", ksc[d2][:, :], kg_t[d2][:, ci], eb[:, :],
                        ALU.mult)
                    c_v = p.inc("vector", "v0")
                    p.wait("scalar", "v0", c_v)
                c_v = p.inc("vector", "v0")
                p.wait("tensor", "v0", c_v)
                for d2 in range(2):
                    p.mm(p.ps[4][0:64, 0:64], kic[d2][:, :], qgc[d2][:, :],
                         start=(d2 == 0), stop=(d2 == 1))
                c_t = p.inc("tensor", "t0")
                p.wait("vector", "t0", c_t)
                p.v("tensor_tensor", sT[0:64, :], p.ps[4][0:64, 0:64],
                    consts["maskT"][0:64, :], ALU.mult)
                c_v = p.inc("vector", "v0")
                if debug and h == 0 and cg == 0 and c == 0:
                    p.wait("gpsimd", "v0", c_v)
                    p.dma("gpsimd", dbg_sT[0:64, :], sT[0:64, :])
                    p.dma("gpsimd", dbg_q[:, :], qgc[0][:, :])
                    p.dma("gpsimd", dbg_k[:, :], kic[0][:, :])
                    p.dma("gpsimd", dbg_b[:, :], bb[0][:, :])
                    p.dma("gpsimd", dbg_eb[:, :], eb[:, :])
                    p.dma("gpsimd", dbg_qraw[:, :], qg_t[0][:, 0:64])
                    p.wait_dma("vector", "gpsimd")
                p.wait("tensor", "v0", c_v)
                for dvc in range(4):
                    p.mm(p.ps[dvc][:, 0:64],
                         vt[vrows, 128 * dvc:128 * dvc + 128], sT[0:64, :],
                         start=True, stop=False)
                    for d2, Sb in ((0, Sb0), (1, Sb1)):
                        p.mm(p.ps[dvc][:, 0:64],
                             Sb[:, 128 * dvc:128 * dvc + 128], qgc[d2][:, :],
                             start=False, stop=(d2 == 1))
                c_t = p.inc("tensor", "t0")
                p.wait("scalar", "t0", c_t)
                for dvc in range(4):
                    p.act(o_head[dvc][:, SW * cg + 64 * c:SW * cg + 64 * c + 64],
                          p.ps[dvc][:, 0:64], AF.Copy)
                c_s = p.inc("scalar", "s0")
                p.wait("tensor", "s0", c_s)
                for d2 in range(2):
                    p.trans(p.psT[0:64, 128 * d2:128 * d2 + 128], ksc[d2][:, :],
                            identb[:, :])
                c_t = p.inc("tensor", "t0")
                p.wait("scalar", "t0", c_t)
                p.act(kst[0:64, :], p.psT[0:64, 0:256], AF.Copy)
                c_s = p.inc("scalar", "s0")
                p.wait("tensor", "s0", c_s)
                for d2 in range(2):
                    p.mm(p.ps[5 + d2][:, :], kst[0:64, 128 * d2:128 * d2 + 128],
                         vt[vrows, :], start=True, stop=True)
                c_t = p.inc("tensor", "t0")
                p.wait("vector", "t0", c_t)
                for d2, Sf in ((0, S0), (1, S1)):
                    p.v("scalar_tensor_tensor", Sf[:, :], Sf[:, :],
                        blc[:, 2 + d2:3 + d2], p.ps[5 + d2][:, :],
                        ALU.mult, ALU.add)
                c_v = p.inc("vector", "v0")
                p.wait("scalar", "v0", c_v)
                for d2, (Sf, Sb) in ((0, (S0, Sb0)), (1, (S1, Sb1))):
                    p.act(Sb[:, :], Sf[:, :], AF.Copy)
                c_s = p.inc("scalar", "s0")
                p.wait("tensor", "s0", c_s)
                p.wait("vector", "s0", c_s)
            p.barrier()
        p.mark(pref + "p3")
        for s in range(NS):
            p.reset(pref + "p3")
            cs = slice(SW * s, SW * s + SW)
            if debug:
                for dvc in range(4):
                    p.dma("gpsimd", op_hbm[512 * h + 128 * dvc:512 * h + 128 * dvc + 128, cs],
                          o_head[dvc][:, cs])
            sqb = p.take("abf", SW)
            sgb = [p.take("abf", SW) for _ in range(4)]
            ob = p.take("abf", SW)
            for dvc in range(4):
                p.dma("sync", sgb[dvc][:, :],
                      sg_hbm[512 * h + 128 * dvc:512 * h + 128 * dvc + 128, cs])
            p.barrier()
            for dvc in range(4):
                p.act(sqb[:, :], o_head[dvc][:, cs], AF.Square)
                c_s = p.inc("scalar", "s0")
                p.wait("tensor", "s0", c_s)
                p.mm(p.ps[0][0:1, 0:SW], consts["ones_col"][:, 0:1], sqb[:, :],
                     start=(dvc == 0), stop=(dvc == 3))
                c_t = p.inc("tensor", "t0")
                p.wait("scalar", "t0", c_t)
            p.act(consts["row_a"][0:1, :SW], p.ps[0][0:1, 0:SW], AF.Ln,
                  scale=1.0 / 512.0, bias=1e-5)
            p.act(consts["rowbf"][0:1, :SW], consts["row_a"][0:1, :SW], AF.Exp,
                  scale=-0.5)
            c_s = p.inc("scalar", "s0")
            p.wait("tensor", "s0", c_s)
            p.mm(p.ps[1][:, 0:SW], consts["ones_row"], consts["rowbf"][0:1, :SW])
            c_t = p.inc("tensor", "t0")
            p.wait("vector", "t0", c_t)
            for dvc in range(4):
                p.v("tensor_tensor", ob[:, :], o_head[dvc][:, cs], p.ps[1][:, 0:SW],
                    ALU.mult)
                c_v = p.inc("vector", "v0")
                p.wait("scalar", "v0", c_v)
                p.act(ob[:, :], ob[:, :], AF.Copy, scale=gnw[:, dvc:dvc + 1])
                c_s = p.inc("scalar", "s0")
                p.wait("vector", "s0", c_s)
                p.v("tensor_tensor", ob[:, :], ob[:, :], sgb[dvc][:, :], ALU.mult)
                c_v = p.inc("vector", "v0")
                p.wait("gpsimd", "v0", c_v)
                p.dma("gpsimd", og_hbm[512 * h + 128 * dvc:512 * h + 128 * dvc + 128,
                                       cs], ob[:, :])
                p.wait("vector", "dma_gpsimd", p.cnt["dma_gpsimd"])
                p.wait("scalar", "dma_gpsimd", p.cnt["dma_gpsimd"])
            p.barrier()
        p.reset(pref + "p3")
    p.reset(pref + "p2")

    p.mark(pref + "p4w")
    wo = [p.take("wbf", 1024) for _ in range(16)]
    for k in range(16):
        p.dma("gpsimd", wo[k][:, :], W["Wo"][128 * k:128 * k + 128, :])
    p.barrier()
    p.mark(pref + "p4")
    for s in range(NS):
        p.reset(pref + "p4")
        cs = slice(SW * s, SW * s + SW)
        ogb = [p.take("abf", SW) for _ in range(16)]
        yv = [p.take("af32", SW) for _ in range(8)]
        yb = [p.take("abf", SW) for _ in range(8)]
        obb = [p.take("abf", SW) for _ in range(8)]
        xf = [p.take("af32", SW) for _ in range(8)]
        for k in range(16):
            p.dma("sync", ogb[k][:, :], og_hbm[128 * k:128 * k + 128, cs])
        for i in range(8):
            p.dma("sync", xf[i][:, :], xin[128 * i:128 * i + 128, cs])
        p.barrier()
        for half in range(2):
            for dm in range(4):
                mc = 4 * half + dm
                for kc in range(16):
                    p.mm(p.ps[dm][:, 0:SW], wo[kc][:, 128 * mc:128 * mc + 128],
                         ogb[kc][:, :], start=(kc == 0), stop=(kc == 15))
            c_t = p.inc("tensor", "t0")
            p.wait("scalar", "t0", c_t)
            for dm in range(4):
                mc = 4 * half + dm
                p.act(yv[mc][:, :], p.ps[dm][:, 0:SW], AF.Copy)
            c_s = p.inc("scalar", "s0")
            p.wait("tensor", "s0", c_s)
        p.wait("vector", "s0", p.cnt["s0"])
        emit_ln(p, yv, yb, SW, consts, obb, p.ps[4], p.ps[5])
        c_v = p.inc("vector", "v0")
        p.wait("vector", "v0", c_v)
        for i in range(8):
            p.v("tensor_tensor", xf[i][:, :], xf[i][:, :], obb[i][:, :], ALU.add)
        c_v = p.inc("vector", "v0")
        p.wait("gpsimd", "v0", c_v)
        for i in range(8):
            p.dma("gpsimd", xout[128 * i:128 * i + 128, cs], xf[i][:, :])
        p.barrier()
    p.reset(pref + "p4")
    p.reset(pref + "p4w")


def host_gla_weights(inputs):
    gnwr = np.zeros((128, 4), dtype=np.float32)
    gnw = inputs["L1_attn_g_norm_w"]
    for c in range(4):
        gnwr[:, c] = gnw[128 * c:128 * c + 128]
    bgk = inputs["L1_attn_bgk2"]
    bgk2r = np.ascontiguousarray(bgk.reshape(8, 128).T)
    identb = bf16(np.eye(128, dtype=np.float32))
    maskT = np.zeros((64, 64), dtype=np.float32)
    for j in range(64):
        maskT[j, j:] = 1.0
    return {
        "Wq": np.ascontiguousarray(inputs["L1_attn_Wq"]),
        "Wk": np.ascontiguousarray(inputs["L1_attn_Wk"]),
        "Wv": np.ascontiguousarray(inputs["L1_attn_Wv"]),
        "Wg": np.ascontiguousarray(inputs["L1_attn_Wg"]),
        "Wgk1": np.ascontiguousarray(inputs["L1_attn_Wgk1"]),
        "Wgk2": np.ascontiguousarray(inputs["L1_attn_Wgk2"]),
        "Wo": np.ascontiguousarray(inputs["L1_attn_Wo"]),
        "bgk2r": bgk2r,
        "gnwr": gnwr,
        "ident_bf": identb,
        "maskT": maskT,
    }


def build_gla_prog(debug=False):
    p = Prog("gla")
    p.arena("cst", 128, F32)
    p.arena("cbf", 11264, BF16)
    p.arena("cf32", 1152, F32)
    p.arena("wbf", 33280, BF16)
    p.arena("abf", 18432, BF16)
    p.arena("af32", 19 * 512, F32)
    p.arena("sf32", 1024, F32)
    p.arena("obf", 4 * T, BF16)
    xin = p.inp("xin", (D, T), F32)
    xout = p.out("xout", (D, T), F32)
    W = {
        "Wq": p.inp("Wq", (1024, 1024), F32),
        "Wk": p.inp("Wk", (1024, 1024), F32),
        "Wv": p.inp("Wv", (1024, 2048), F32),
        "Wg": p.inp("Wg", (1024, 2048), F32),
        "Wgk1": p.inp("Wgk1", (1024, 16), F32),
        "Wgk2": p.inp("Wgk2", (16, 1024), F32),
        "Wo": p.inp("Wo", (2048, 1024), F32),
        "bgk2r": p.inp("bgk2r", (128, 8), F32),
        "gnwr": p.inp("gnwr", (128, 4), F32),
        "ident_bf": p.inp("ident_bf", (128, 128), BF16),
        "maskT": p.inp("maskT", (64, 64), F32),
    }
    sec_gla(p, xin, xout, W, debug=debug)
    return p.build()


def sec_mamba(p, xin, xout, W, pref="mmb", debug=False):
    NS = 8
    SW = T // NS
    DI = 2048
    kind = "ExternalOutput" if debug else "Internal"

    xn_hbm = p.dram(pref + "_xn", (D, T), BF16, kind=kind)
    u_hbm = p.dram(pref + "_u", (DI, T), BF16, kind=kind)
    sz_hbm = p.dram(pref + "_sz", (DI, T), BF16, kind=kind)
    yh_hbm = p.dram(pref + "_yh", (DI, T), BF16, kind=kind)
    if debug:
        dt_dbg = p.out(pref + "_dt", (DI, T), F32)
        dbl_dbg = p.out(pref + "_dbl", (128, T), F32)
        d_hda = p.out(pref + "_hda", (128, 512), F32)
        d_hb = p.out(pref + "_hb", (128, 512), F32)
        d_hh = p.out(pref + "_hh", (128, 512), F32)
        d_Br = p.out(pref + "_Br", (128, 512), F32)
        d_Cr = p.out(pref + "_Cr", (128, 512), F32)

    consts = std_consts(p)
    p.constv(1.0)

    cwv = p.take("cf32", 64)
    cbv = p.take("cf32", 16)
    bdt = p.take("cf32", 16)
    dsk = p.take("cf32", 16)
    Atl = p.take("cf32", 256)
    bcsel = p.take("cbf", 4096, rows=32)
    p.dma("sync", cwv[:, :], W["conv_wr"][:, :])
    p.dma("sync", cbv[:, :], W["conv_br"][:, :])
    p.dma("sync", bdt[:, :], W["b_dtr"][:, :])
    p.dma("sync", dsk[:, :], W["D_skipr"][:, :])
    p.dma("sync", Atl[:, :], W["A_logr"][:, :])
    p.dma("gpsimd", bcsel[0:32, :], W["bcsel"][:, :])
    p.barrier()
    p.act(Atl[:, :], Atl[:, :], AF.Exp)
    c_s = p.inc("scalar", "s0")
    p.wait("vector", "s0", c_s)
    p.v("tensor_scalar_mul", Atl[:, :], Atl[:, :], -1.0)
    p.barrier()

    dbl = p.take("dbl", T, rows=96)

    p.mark(pref + "w1")
    winT = [p.take("wbf", 2048) for _ in range(8)]
    wxT = [p.take("wbf", 96) for _ in range(16)]
    for k in range(8):
        p.dma("gpsimd", winT[k][:, :], W["W_inT"][128 * k:128 * k + 128, 0:2048])
    for k in range(16):
        p.dma("gpsimd", wxT[k][:, :], W["W_xprojT"][128 * k:128 * k + 128, :])
    ptail = p.take("af32", 48)
    p.v("memset", ptail[:, :], 0.0)
    p.barrier()
    p.mark(pref + "m1")
    for s in range(NS):
        p.reset(pref + "m1")
        cs = slice(SW * s, SW * s + SW)
        xf = [p.take("af32", SW) for _ in range(8)]
        xb = [p.take("abf", SW) for _ in range(8)]
        for i in range(8):
            p.dma("sync", xf[i][:, :], xin[128 * i:128 * i + 128, cs])
        p.barrier()
        emit_ln(p, xf, xb, SW, consts, xb, p.ps[0], p.ps[1])
        c_v = p.inc("vector", "v0")
        p.wait("tensor", "v0", c_v)
        p.wait("gpsimd", "v0", c_v)
        for i in range(8):
            p.dma("gpsimd", xn_hbm[128 * i:128 * i + 128, cs], xb[i][:, :])
        ub = [p.take("abf", SW) for _ in range(16)]
        xi = p.take("af32", 520)
        xc = p.take("af32", SW)
        for mc in range(16):
            pb = p.ps[2 + mc % 2]
            for kc in range(8):
                p.mm(pb[:, 0:SW], winT[kc][:, 128 * mc:128 * mc + 128],
                     xb[kc][:, :], start=(kc == 0), stop=(kc == 7))
            c_t = p.inc("tensor", "t0")
            p.wait("scalar", "t0", c_t)
            cb = mc
            p.act(xi[:, 3:3 + SW], pb[:, 0:SW], AF.Copy)
            p.act(xi[:, 0:3], ptail[:, 3 * cb:3 * cb + 3], AF.Copy)
            p.act(ptail[:, 3 * cb:3 * cb + 3], xi[:, SW:SW + 3], AF.Copy)
            p.act(xc[:, :], xi[:, 0:SW], AF.Copy, scale=cwv[:, 4 * cb:4 * cb + 1])
            c_s = p.inc("scalar", "s0")
            p.wait("vector", "s0", c_s)
            for k in (1, 2, 3):
                p.v("scalar_tensor_tensor", xc[:, :], xi[:, k:k + SW],
                    cwv[:, 4 * cb + k:4 * cb + k + 1], xc[:, :],
                    ALU.mult, ALU.add)
            c_v = p.inc("vector", "v0")
            p.wait("scalar", "v0", c_v)
            p.act(ub[cb][:, :], xc[:, :], AF.Silu, bias=cbv[:, cb:cb + 1])
            c_s = p.inc("scalar", "s0")
            p.wait("gpsimd", "s0", c_s)
            p.dma("gpsimd", u_hbm[128 * cb:128 * cb + 128, cs], ub[cb][:, :])
            p.wait("tensor", "s0", c_s)
        for kc in range(16):
            p.mm(p.ps[4][0:96, 0:SW], wxT[kc][:, :], ub[kc][:, :],
                 start=(kc == 0), stop=(kc == 15))
        c_t = p.inc("tensor", "t0")
        p.wait("scalar", "t0", c_t)
        p.act(dbl[0:96, cs], p.ps[4][0:96, 0:SW], AF.Copy)
        c_s = p.inc("scalar", "s0")
        p.barrier()
    p.reset(pref + "m1")
    p.reset(pref + "w1")

    wz = [p.take("wbf", 2048) for _ in range(8)]
    for k in range(8):
        p.dma("gpsimd", wz[k][:, :], W["W_inT"][128 * k:128 * k + 128, 2048:4096])
    p.barrier()
    p.mark(pref + "m1b")
    for s in range(NS):
        p.reset(pref + "m1b")
        cs = slice(SW * s, SW * s + SW)
        xb = [p.take("abf", SW) for _ in range(8)]
        ev = p.take("abf", SW)
        for i in range(8):
            p.dma("sync", xb[i][:, :], xn_hbm[128 * i:128 * i + 128, cs])
        p.barrier()
        for mc in range(16):
            pb = p.ps[2 + mc % 2]
            for kc in range(8):
                p.mm(pb[:, 0:SW], wz[kc][:, 128 * mc:128 * mc + 128],
                     xb[kc][:, :], start=(kc == 0), stop=(kc == 7))
            c_t = p.inc("tensor", "t0")
            p.wait("scalar", "t0", c_t)
            p.act(ev[:, :], pb[:, 0:SW], AF.Silu)
            c_s = p.inc("scalar", "s0")
            p.wait("gpsimd", "s0", c_s)
            p.dma("gpsimd", sz_hbm[128 * mc:128 * mc + 128, cs], ev[:, :])
            p.wait("tensor", "dma_gpsimd", p.cnt["dma_gpsimd"])
            p.wait("scalar", "dma_gpsimd", p.cnt["dma_gpsimd"])
        p.barrier()
    p.reset(pref + "m1b")
    p.reset(pref + "w1")
    if debug:
        dbf = p.take("af32", T, rows=96)
        p.act(dbf[0:96, :], dbl[0:96, :], AF.Copy)
        c_s = p.inc("scalar", "s0")
        p.wait("gpsimd", "s0", c_s)
        p.dma("gpsimd", dbl_dbg[0:96, :], dbf[0:96, :])
        p.barrier()
        p.reset(pref + "w1")

    wdtT = p.take("wbf", 2048, rows=64)
    p.dma("gpsimd", wdtT[0:64, :], W["W_dtT"][:, :])
    hcar = p.take("cf32", 256)
    p.v("memset", hcar[:, :], 0.0)
    zer = p.take("cf32", SW)
    p.v("memset", zer[:, :], 0.0)
    p.barrier()
    p.mark(pref + "m2")
    for tc in range(NS):
        p.reset(pref + "m2")
        cs = slice(SW * tc, SW * tc + SW)
        ub = [p.take("abf", SW) for _ in range(16)]
        for cb in range(16):
            p.dma("sync", ub[cb][:, :], u_hbm[128 * cb:128 * cb + 128, cs])
        dt_t = [p.take("abf", SW) for _ in range(16)]
        dtu = [p.take("abf", SW) for _ in range(16)]
        sa = p.take("af32", SW)
        sb2 = p.take("af32", SW)
        p.wait_dma("tensor", "sync")
        for cb in range(16):
            pb = p.ps[2 + cb % 2]
            p.mm(pb[:, 0:SW], wdtT[0:64, 128 * cb:128 * cb + 128],
                 dbl[0:64, cs], start=True, stop=True)
            c_t = p.inc("tensor", "t0")
            p.wait("scalar", "t0", c_t)
            bcol = bdt[:, cb:cb + 1]
            p.act(sa[:, :], pb[:, 0:SW], AF.Abs, bias=bcol)
            p.act(sa[:, :], sa[:, :], AF.Exp, scale=-1.0)
            p.act(sa[:, :], sa[:, :], AF.Ln, bias=1.0)
            p.act(sb2[:, :], pb[:, 0:SW], AF.Relu, bias=bcol)
            c_s = p.inc("scalar", "s0")
            p.wait("vector", "s0", c_s)
            p.v("tensor_tensor", dt_t[cb][:, :], sa[:, :], sb2[:, :], ALU.add)
            p.v("tensor_tensor", dtu[cb][:, :], dt_t[cb][:, :], ub[cb][:, :],
                ALU.mult)
            c_v = p.inc("vector", "v0")
            p.wait("scalar", "v0", c_v)
            p.wait("tensor", "v0", c_v)
            if debug:
                p.wait("gpsimd", "v0", c_v)
                p.dma("gpsimd", dt_dbg[128 * cb:128 * cb + 128, cs],
                      dt_t[cb][:, :])
        bc0 = p.take("abf", SW, rows=32)
        p.act(bc0[0:32, :], dbl[64:96, cs], AF.Copy)
        c_s = p.inc("scalar", "s0")
        p.wait("tensor", "s0", c_s)
        y_t = [p.take("af32", SW) for _ in range(16)]
        for cb in range(16):
            p.v("memset", y_t[cb][:, :], 0.0)
        c_v = p.inc("vector", "v0")
        hda = p.take("af32", SW)
        hb = p.take("af32", SW)
        hh = p.take("af32", SW)
        ht2 = p.take("af32", SW)
        Brep = p.take("abf", SW)
        Crep = p.take("af32", SW)
        for s in range(16):
            p.wait("tensor", "v0", p.cnt["v0"])
            p.mm(p.ps[4][:, 0:SW], bcsel[0:32, 128 * s:128 * s + 128],
                 bc0[0:32, :], start=True, stop=True)
            p.mm(p.ps[5][:, 0:SW], bcsel[0:32, 2048 + 128 * s:2048 + 128 * s + 128],
                 bc0[0:32, :], start=True, stop=True)
            c_t = p.inc("tensor", "t0")
            p.wait("scalar", "t0", c_t)
            p.act(Brep[:, :], p.ps[4][:, 0:SW], AF.Copy)
            p.act(Crep[:, :], p.ps[5][:, 0:SW], AF.Copy)
            c_s = p.inc("scalar", "s0")
            p.wait("vector", "s0", c_s)
            for cb in range(16):
                p.act(hda[:, :], dt_t[cb][:, :], AF.Exp,
                      scale=Atl[:, 16 * cb + s:16 * cb + s + 1])
                c_s = p.inc("scalar", "s0")
                p.wait("vector", "s0", c_s)
                p.v("tensor_tensor", hb[:, :], dtu[cb][:, :], Brep[:, :], ALU.mult)
                p.v("tensor_tensor_scan", hh[:, :], hda[:, :], hb[:, :],
                    hcar[:, 16 * s + cb:16 * s + cb + 1], ALU.mult, ALU.add)
                p.v("tensor_tensor", ht2[:, :], hh[:, :], Crep[:, :], ALU.mult)
                p.v("tensor_tensor", y_t[cb][:, :], y_t[cb][:, :], ht2[:, :],
                    ALU.add)
                c_v = p.inc("vector", "v0")
                p.wait("scalar", "v0", c_v)
                if debug and tc == 0 and s == 0 and cb == 0:
                    p.wait("gpsimd", "v0", c_v)
                    p.dma("gpsimd", d_hda[:, :], hda[:, :])
                    p.dma("gpsimd", d_hb[:, :], hb[:, :])
                    p.dma("gpsimd", d_hh[:, :], hh[:, :])
                    p.dma("gpsimd", d_Br[:, :], Brep[:, :])
                    p.dma("gpsimd", d_Cr[:, :], Crep[:, :])
                    p.wait("scalar", "dma_gpsimd", p.cnt["dma_gpsimd"])
                    p.wait("vector", "dma_gpsimd", p.cnt["dma_gpsimd"])
                p.act(hcar[:, 16 * s + cb:16 * s + cb + 1], hh[:, 511:512],
                      AF.Copy)
                c_s = p.inc("scalar", "s0")
                p.wait("vector", "s0", c_s)
        if debug and tc == 0:
            p.wait("gpsimd", "v0", p.cnt["v0"])
            p.dma("gpsimd", d_hb[:, :], y_t[0][:, :])
            p.wait("vector", "dma_gpsimd", p.cnt["dma_gpsimd"])
        szb = p.take("abf", SW)
        yhb = p.take("abf", SW)
        for cb in range(16):
            if cb > 0:
                p.wait("sync", "v0", p.cnt["v0"])
            p.dma("sync", szb[:, :], sz_hbm[128 * cb:128 * cb + 128, cs])
            p.act(sa[:, :], ub[cb][:, :], AF.Copy, scale=dsk[:, cb:cb + 1])
            c_s = p.inc("scalar", "s0")
            p.wait("vector", "s0", c_s)
            p.wait_dma("vector", "sync")
            p.v("tensor_tensor", y_t[cb][:, :], y_t[cb][:, :], sa[:, :], ALU.add)
            p.v("tensor_tensor", yhb[:, :], y_t[cb][:, :], szb[:, :], ALU.mult)
            c_v = p.inc("vector", "v0")
            p.wait("gpsimd", "v0", c_v)
            p.dma("gpsimd", yh_hbm[128 * cb:128 * cb + 128, cs], yhb[:, :])
            p.wait("scalar", "dma_gpsimd", p.cnt["dma_gpsimd"])
            p.wait("vector", "dma_gpsimd", p.cnt["dma_gpsimd"])
        p.barrier()
    p.reset(pref + "m2")

    woT = [p.take("wbf", 1024) for _ in range(16)]
    for k in range(16):
        p.dma("gpsimd", woT[k][:, :], W["W_outT"][128 * k:128 * k + 128, :])
    p.barrier()
    p.mark(pref + "m3")
    for s in range(NS):
        p.reset(pref + "m3")
        cs = slice(SW * s, SW * s + SW)
        yhb = [p.take("abf", SW) for _ in range(16)]
        yv = [p.take("af32", SW) for _ in range(8)]
        yb = [p.take("abf", SW) for _ in range(8)]
        obb = [p.take("abf", SW) for _ in range(8)]
        xf = [p.take("af32", SW) for _ in range(8)]
        for k in range(16):
            p.dma("sync", yhb[k][:, :], yh_hbm[128 * k:128 * k + 128, cs])
        for i in range(8):
            p.dma("sync", xf[i][:, :], xin[128 * i:128 * i + 128, cs])
        p.barrier()
        for half in range(2):
            for dm in range(4):
                mc = 4 * half + dm
                for kc in range(16):
                    p.mm(p.ps[dm][:, 0:SW], woT[kc][:, 128 * mc:128 * mc + 128],
                         yhb[kc][:, :], start=(kc == 0), stop=(kc == 15))
            c_t = p.inc("tensor", "t0")
            p.wait("scalar", "t0", c_t)
            for dm in range(4):
                mc = 4 * half + dm
                p.act(yv[mc][:, :], p.ps[dm][:, 0:SW], AF.Copy)
            c_s = p.inc("scalar", "s0")
            p.wait("tensor", "s0", c_s)
        p.wait("vector", "s0", p.cnt["s0"])
        emit_ln(p, yv, yb, SW, consts, obb, p.ps[4], p.ps[5])
        c_v = p.inc("vector", "v0")
        p.wait("vector", "v0", c_v)
        for i in range(8):
            p.v("tensor_tensor", xf[i][:, :], xf[i][:, :], obb[i][:, :], ALU.add)
        c_v = p.inc("vector", "v0")
        p.wait("gpsimd", "v0", c_v)
        for i in range(8):
            p.dma("gpsimd", xout[128 * i:128 * i + 128, cs], xf[i][:, :])
        p.barrier()
    p.reset(pref + "m3")


def host_mamba_weights(inputs):
    conv_w = inputs["L0_attn_conv_w"]
    conv_wr = np.ascontiguousarray(
        conv_w.reshape(16, 128, 4).transpose(1, 0, 2).reshape(128, 64))
    def cols(v):
        return np.ascontiguousarray(v.reshape(16, 128).T)
    A_logr = np.ascontiguousarray(
        inputs["L0_attn_A_log"].reshape(16, 128, 16).transpose(1, 0, 2)
        .reshape(128, 256))
    bcsel = np.zeros((32, 4096), dtype=np.float32)
    for s in range(16):
        bcsel[s, 128 * s:128 * s + 128] = 1.0
        bcsel[16 + s, 2048 + 128 * s:2048 + 128 * s + 128] = 1.0
    return {
        "W_inT": np.ascontiguousarray(inputs["L0_attn_W_in"].T),
        "W_xprojT": np.ascontiguousarray(inputs["L0_attn_W_xproj"].T),
        "W_dtT": np.ascontiguousarray(inputs["L0_attn_W_dt"].T),
        "W_outT": np.ascontiguousarray(inputs["L0_attn_W_out"].T),
        "conv_wr": conv_wr,
        "conv_br": cols(inputs["L0_attn_conv_b"]),
        "b_dtr": cols(inputs["L0_attn_b_dt"]),
        "D_skipr": cols(inputs["L0_attn_D_skip"]),
        "A_logr": A_logr,
        "bcsel": bcsel,
    }


def build_mamba_prog(debug=False):
    p = Prog("mmb")
    p.arena("cst", 128, F32)
    p.arena("cbf", 10240, BF16)
    p.arena("cf32", 2304, F32)
    p.arena("dbl", T, BF16)
    p.arena("wbf", 18432, BF16)
    p.arena("abf", 27648, BF16)
    p.arena("af32", 13312, F32)
    xin = p.inp("xin", (D, T), F32)
    xout = p.out("xout", (D, T), F32)
    W = {
        "W_inT": p.inp("W_inT", (1024, 4096), F32),
        "W_xprojT": p.inp("W_xprojT", (2048, 96), F32),
        "W_dtT": p.inp("W_dtT", (64, 2048), F32),
        "W_outT": p.inp("W_outT", (2048, 1024), F32),
        "conv_wr": p.inp("conv_wr", (128, 64), F32),
        "conv_br": p.inp("conv_br", (128, 16), F32),
        "b_dtr": p.inp("b_dtr", (128, 16), F32),
        "D_skipr": p.inp("D_skipr", (128, 16), F32),
        "A_logr": p.inp("A_logr", (128, 256), F32),
        "bcsel": p.inp("bcsel", (32, 4096), F32),
    }
    sec_mamba(p, xin, xout, W, debug=debug)
    return p.build()


def sec_l0ffn(p, xin, xout, W, pref="trf", debug=False, depth=6,
              do_attn=True, do_ffn=True):
    NS = 8
    SW = T // NS
    DEPTH = depth
    DIH = 2816
    kind = "ExternalOutput" if debug else "Internal"

    ra = p.dram(pref + "_ra", (D, T), F32, kind=kind)
    rb = p.dram(pref + "_rb", (D, T), F32, kind=kind)
    xnf_hbm = p.dram(pref + "_xnf", (D, T), BF16, kind=kind)
    hf_hbm = p.dram(pref + "_hf", (DIH, T), BF16, kind=kind)

    consts = std_consts(p)
    identb = p.take("cbf", 128)
    p.dma("sync", identb[:, :], W["ident_bf"][:, :])
    maskadd = p.take("cf32", 384)
    p.dma("sync", maskadd[:, :], W["maskadd"][:, :])
    maskmulT = p.take("cbf", 384)
    p.dma("gpsimd", maskmulT[:, :], W["maskmulT"][:, :])
    ones16 = p.take("cbf", 128)
    p.dma("gpsimd", ones16[:, :], W["ones16"][:, :])
    sel16 = p.take("cbf", 1024, rows=16)
    p.dma("gpsimd", sel16[0:16, :], W["sel16"][:, :])
    hsel = p.take("cbf", 2048, rows=16)
    p.dma("gpsimd", hsel[0:16, :], W["hsel"][:, :])
    qsc = p.take("cf32", 24)
    p.dma("sync", qsc[:, :], W["qk_scales"][:, :])
    lgb = p.take("cf32", 96)
    p.dma("sync", lgb[:, :], W["ln_gb"][:, :])
    b2c = p.take("cf32", 48)
    p.dma("sync", b2c[:, :], W["b2r"][:, :])
    b1c = p.take("cf32", 264)
    p.dma("sync", b1c[:, :], W["b1r"][:, :])
    p.barrier()

    p.mark(pref + "i0")
    for s in range(NS):
        p.reset(pref + "i0")
        cs = slice(SW * s, SW * s + SW)
        xf = [p.take("af32", SW) for _ in range(8)]
        xb = [p.take("abf", SW) for _ in range(8)]
        of = [p.take("af32", SW) for _ in range(8)]
        for i in range(8):
            p.dma("sync", xf[i][:, :], xin[128 * i:128 * i + 128, cs])
        p.barrier()
        emit_ln(p, xf, xb, SW, consts, xb, p.ps[0], p.ps[1],
                out_f32_tiles=of)
        c_v = p.inc("vector", "v0")
        p.wait("gpsimd", "v0", c_v)
        for i in range(8):
            p.dma("gpsimd", ra[128 * i:128 * i + 128, cs], of[i][:, :])
        p.barrier()
    p.reset(pref + "i0")

    cur, nxt = ra, rb
    for li in range(DEPTH):
        if do_attn:
            attn_layer(p, li, cur, nxt, W, consts, identb, maskadd, maskmulT,
                       ones16, sel16, hsel, qsc, pref)
            cur, nxt = nxt, cur
        if do_ffn:
            ffn_layer(p, li, cur, nxt, W, consts, lgb, b1c, b2c, xnf_hbm, hf_hbm,
                      pref)
            cur, nxt = nxt, cur

    p.mark(pref + "f0")
    for s in range(NS):
        p.reset(pref + "f0")
        cs = slice(SW * s, SW * s + SW)
        yv = [p.take("af32", SW) for _ in range(8)]
        yb = [p.take("abf", SW) for _ in range(8)]
        ob = [p.take("abf", SW) for _ in range(8)]
        xf = [p.take("af32", SW) for _ in range(8)]
        for i in range(8):
            p.dma("sync", xf[i][:, :], xin[128 * i:128 * i + 128, cs])
            p.dma("sync", yv[i][:, :], cur[128 * i:128 * i + 128, cs])
        p.barrier()
        emit_ln(p, yv, yb, SW, consts, ob, p.ps[0], p.ps[1])
        c_v = p.inc("vector", "v0")
        p.wait("vector", "v0", c_v)
        for i in range(8):
            p.v("tensor_tensor", xf[i][:, :], xf[i][:, :], ob[i][:, :], ALU.add)
        c_v = p.inc("vector", "v0")
        p.wait("gpsimd", "v0", c_v)
        for i in range(8):
            p.dma("gpsimd", xout[128 * i:128 * i + 128, cs], xf[i][:, :])
        p.barrier()
    p.reset(pref + "f0")


def load_shifted(p, tiles, src, s, SW):
    cs = slice(SW * s, SW * s + SW)
    for i in range(4):
        p.dma("sync", tiles[i][:, :], src[128 * i:128 * i + 128, cs])
    for i in range(4, 8):
        r = slice(128 * i, 128 * i + 128)
        if s == 0:
            p.v("memset", tiles[i][:, 0:1], 0.0)
            p.dma("sync", tiles[i][:, 1:SW], src[r, 0:SW - 1])
        else:
            p.dma("sync", tiles[i][:, :], src[r, SW * s - 1:SW * s + SW - 1])


def attn_layer(p, li, cur, nxt, W, consts, identb, maskadd, maskmulT,
               ones16, sel16, hsel, qsc, pref):
    NS = 8
    SW = T // NS
    p.mark(pref + "aw")
    wqkv = [p.take("wbf", 3072) for _ in range(8)]
    wo = [p.take("wbf", 1024) for _ in range(8)]
    for k in range(8):
        p.dma("gpsimd", wqkv[k][:, :], W["Wqkv"][li, 128 * k:128 * k + 128, :])
        p.dma("gpsimd", wo[k][:, :], W["Wo"][li, 128 * k:128 * k + 128, :])
    qsp = qsc[:, 4 * li:4 * li + 1]
    qsrh = qsc[:, 4 * li + 1:4 * li + 2]
    ksp = qsc[:, 4 * li + 2:4 * li + 3]
    ksrh = qsc[:, 4 * li + 3:4 * li + 4]
    ktail = [p.take("abf", 256) for _ in range(8)]
    vtail = [p.take("abf", 1024) for _ in range(2)]
    p.barrier()
    p.mark(pref + "as")
    for s in range(NS):
        p.reset(pref + "as")
        cs = slice(SW * s, SW * s + SW)
        ws = [p.take("af32", SW) for _ in range(8)]
        wres = [p.take("af32", SW) for _ in range(8)]
        xb = [p.take("abf", SW) for _ in range(8)]
        load_shifted(p, ws, cur, s, SW)
        for i in range(8):
            p.dma("sync", wres[i][:, :], cur[128 * i:128 * i + 128, cs])
        tb_f = p.take("af32", SW)
        Aq = p.take("abf", SW)
        Bq = p.take("abf", SW)
        Ak = p.take("abf", SW)
        Bk = p.take("abf", SW)
        p.barrier()
        for i in range(8):
            p.act(xb[i][:, :], ws[i][:, :], AF.Copy)
        c_s = p.inc("scalar", "s0")
        for (tb, hb, col, scl) in ((Aq, "cs2p", qsp, 8.0), (Bq, "snq", qsrh, 8.0),
                                   (Ak, "ci2p", ksp, 1.0), (Bk, "snk", ksrh, 1.0)):
            p.dma("sync", tb_f[:, :], W[hb][:, cs])
            p.wait_dma("scalar", "sync")
            p.act(tb[:, :], tb_f[:, :], AF.Copy, scale=col)
            c_s = p.inc("scalar", "s0")
            if scl != 1.0:
                p.wait("vector", "s0", c_s)
                p.v("tensor_scalar_mul", tb[:, :], tb[:, :], scl)
                c_v = p.inc("vector", "v0")
                p.wait("sync", "v0", c_v)
            else:
                p.wait("sync", "s0", c_s)
        p.wait("tensor", "s0", p.cnt["s0"])
        p.wait("gpsimd", "s0", p.cnt["s0"])
        qt = [p.take("abf", SW) for _ in range(8)]
        kt = [p.take("abf", SW) for _ in range(8)]
        vt = [p.take("abf", 1024) for _ in range(4)]
        q2t = [p.take("abf", SW) for _ in range(3)]
        sdone = [0] * 16

        def sumsq_mm(j):
            p.mm(p.ps[4 + j // 8][0:16, 0:SW],
                 ones16[:, 16 * (j % 8):16 * (j % 8) + 16],
                 q2t[j % 3][:, :], start=(j % 8 == 0), stop=(j % 8 == 7))
        for i in range(16):
            part, mc = i // 8, i % 8
            tiles = qt if part == 0 else kt
            pb = p.ps[2 + i % 2]
            if i >= 2:
                p.wait("tensor", "s0", sdone[i - 2])
                sumsq_mm(i - 2)
            for kc in range(8):
                p.mm(pb[:, 0:SW],
                     wqkv[kc][:, 1024 * part + 128 * mc:1024 * part + 128 * mc + 128],
                     xb[kc][:, :], start=(kc == 0), stop=(kc == 7))
            td = p.inc("tensor", "t0", drain=False)
            p.wait("scalar", "t0", td)
            p.act(tiles[mc][:, :], pb[:, 0:SW], AF.Copy)
            p.act(q2t[i % 3][:, :], pb[:, 0:SW], AF.Square)
            sdone[i] = p.inc("scalar", "s0")
        nqq = p.take("af32", 512, rows=16)
        nqk = p.take("af32", 512, rows=16)
        nqbq = p.take("abf", 512, rows=16)
        nqbk = p.take("abf", 512, rows=16)
        for j in (14, 15):
            p.wait("tensor", "s0", sdone[j])
            sumsq_mm(j)
        td = p.inc("tensor", "t0", drain=False)
        p.wait("scalar", "t0", td)
        p.act(nqq[0:16, :], p.ps[4][0:16, 0:SW], AF.Sqrt)
        p.act(nqk[0:16, :], p.ps[5][0:16, 0:SW], AF.Sqrt)
        c_s = p.inc("scalar", "s0")
        p.wait("vector", "s0", c_s)
        for nq in (nqq, nqk):
            p.v("tensor_scalar_max", nq[0:16, :], nq[0:16, :], 1e-12)
            p.v("reciprocal", nq[0:16, :], nq[0:16, :])
        c_v = p.inc("vector", "v0")
        p.wait("scalar", "v0", c_v)
        p.act(nqbq[0:16, :], nqq[0:16, :], AF.Copy)
        p.act(nqbk[0:16, :], nqk[0:16, :], AF.Copy)
        c_s = p.inc("scalar", "s0")
        p.wait("tensor", "s0", c_s)
        tmpb = [p.take("abf", SW) for _ in range(2)]
        rhb = [p.take("abf", SW) for _ in range(2)]
        t1b = [p.take("abf", SW) for _ in range(2)]
        t2b = [p.take("abf", SW) for _ in range(2)]
        vmul = [0] * 16
        gdone = [0] * 16
        vfin = [0] * 16
        for i in range(16):
            part, mc = i // 8, i % 8
            tiles = qt if part == 0 else kt
            nqb = nqbq if part == 0 else nqbk
            Atb, Btb = (Aq, Bq) if part == 0 else (Ak, Bk)
            b = i % 2
            pb = p.ps[4 + b]
            if i >= 2:
                p.wait("tensor", "v0", vmul[i - 2])
            p.mm(pb[:, 0:SW], sel16[0:16, 128 * mc:128 * mc + 128],
                 nqb[0:16, :], start=True, stop=True)
            td = p.inc("tensor", "t0", drain=False)
            p.wait("vector", "t0", td)
            if i >= 2:
                p.wait("vector", "g0", gdone[i - 2])
                p.wait("vector", "v0", vfin[i - 2])
            p.v("tensor_tensor", tmpb[b][:, :], tiles[mc][:, :], pb[:, 0:SW],
                ALU.mult)
            vmul[i] = p.inc("vector", "v0")
            p.wait("gpsimd", "v0", vmul[i])
            p.v("tensor_copy", rhb[b][0:32, :], tmpb[b][32:64, :], eng="gpsimd")
            p.v("tensor_copy", rhb[b][32:64, :], tmpb[b][0:32, :], eng="gpsimd")
            p.v("tensor_copy", rhb[b][64:96, :], tmpb[b][96:128, :], eng="gpsimd")
            p.v("tensor_copy", rhb[b][96:128, :], tmpb[b][64:96, :], eng="gpsimd")
            gdone[i] = p.inc("gpsimd", "g0")
            p.wait("vector", "g0", gdone[i])
            p.v("tensor_tensor", t1b[b][:, :], tmpb[b][:, :], Atb[:, :], ALU.mult)
            p.v("tensor_tensor", t2b[b][:, :], rhb[b][:, :], Btb[:, :], ALU.mult)
            p.v("tensor_tensor", tiles[mc][:, :], t1b[b][:, :], t2b[b][:, :],
                ALU.add)
            vfin[i] = p.inc("vector", "v0")
        vsd = [0] * 8
        for i in range(8):
            tc, nv = i // 2, i % 2
            pb = p.ps[i % 2]
            if i >= 2:
                p.wait("tensor", "s0", vsd[i - 2])
            for kc in range(8):
                p.mm(pb[:, 0:SW], xb[kc][:, 128 * tc:128 * tc + 128],
                     wqkv[kc][:, 2048 + 512 * nv:2048 + 512 * nv + 512],
                     start=(kc == 0), stop=(kc == 7))
            td = p.inc("tensor", "t0", drain=False)
            p.wait("scalar", "t0", td)
            p.act(vt[tc][:, 512 * nv:512 * nv + 512], pb[:, 0:SW], AF.Copy)
            vsd[i] = p.inc("scalar", "s0")
        p.wait("tensor", "s0", vsd[7])
        p.wait("vector", "s0", vsd[7])
        p.wait("vector", "v0", vfin[15])
        p.wait("scalar", "v0", vfin[15])
        p.wait("tensor", "v0", vfin[15])
        ot = [p.take("abf", SW) for _ in range(8)]
        mcols = p.take("af32", 16)
        zcols = p.take("af32", 16)
        mbb = p.take("abf", 16)
        mbf = p.take("af32", 16)
        ccols = p.take("af32", 16)
        ccb = p.take("abf", 16)
        mrows = p.take("abf", 128, rows=16)
        crows = p.take("abf", 128, rows=16)
        Smb = [p.take("af32", 384) for _ in range(2)]
        Pbb = [p.take("abf", 384) for _ in range(2)]
        PTb = [p.take("abf", 384) for _ in range(2)]
        crepb = [p.take("abf", 128, rows=64) for _ in range(2)]
        for qb in range(4):
            qbg = 4 * s + qb
            nk = min(384, 128 * (qbg + 1))
            koff = 128 * (qbg + 1) - nk
            mask_c0 = 384 - nk

            def ksrc(h, kx, left):
                base = 64 * (h % 2)
                if kx < 0:
                    n = min(-kx, left)
                    return (ktail[h // 2][base:base + 64, 256 + kx:256 + kx + n], n)
                n = min(512 - kx % 512, left)
                return (kt[h // 2][base:base + 64, kx % 512:kx % 512 + n], n)
            p1v = [0] * 16
            p1s = [0] * 16
            for h in range(16):
                base = 64 * (h % 2)
                b = h % 2
                lq = qt[h // 2][base:base + 64, 128 * qb:128 * qb + 128]
                if h >= 2:
                    p.wait("tensor", "v0", p1v[h - 2])
                col = 0
                kx = koff - SW * s
                left = nk
                while left > 0:
                    rhs, n = ksrc(h, kx, left)
                    p.mm(p.ps[5 + b][:, col:col + n], lq, rhs,
                         start=True, stop=True)
                    col += n
                    kx += n
                    left -= n
                td = p.inc("tensor", "t0", drain=False)
                p.wait("vector", "t0", td)
                if h >= 2:
                    p.wait("vector", "s1", p1s[h - 2])
                p.v("scalar_tensor_tensor", Smb[b][:, 0:nk],
                    p.ps[5 + b][:, 0:nk], 1.0,
                    maskadd[:, mask_c0:mask_c0 + nk], ALU.mult, ALU.add)
                p.v("reduce_max", mcols[:, h:h + 1], Smb[b][:, 0:nk], axis=AX.X,
                    negate=True)
                p1v[h] = p.inc("vector", "v0")
                p.wait("scalar", "v0", p1v[h])
                p.act(Pbb[b][:, 0:nk], Smb[b][:, 0:nk], AF.Exp,
                      bias=mcols[:, h:h + 1], accum_out=zcols[:, h:h + 1])
                p1s[h] = p.inc("scalar", "s1")
            p.wait("scalar", "s1", p1s[15])
            p.act(mbb[:, :], mcols[:, :], AF.Copy)
            p.act(mbf[:, :], mbb[:, :], AF.Copy)
            c_s = p.inc("scalar", "s0")
            p.wait("vector", "s0", c_s)
            p.v("tensor_tensor", ccols[:, :], mcols[:, :], mbf[:, :], ALU.subtract)
            p.v("reciprocal", zcols[:, :], zcols[:, :])
            c_v = p.inc("vector", "v0")
            p.wait("scalar", "v0", c_v)
            p.act(ccols[:, :], ccols[:, :], AF.Exp)
            c_s = p.inc("scalar", "s0")
            p.wait("vector", "s0", c_s)
            p.v("tensor_tensor", ccols[:, :], ccols[:, :], zcols[:, :], ALU.mult)
            c_v = p.inc("vector", "v0")
            p.wait("scalar", "v0", c_v)
            p.act(ccb[:, :], ccols[:, :], AF.Copy)
            c_s = p.inc("scalar", "s0")
            p.wait("tensor", "s0", c_s)
            p.trans(p.psT[0:16, 0:128], mbb[:, :], identb[:, :])
            p.trans(p.psT[0:16, 128:256], ccb[:, :], identb[:, :])
            td = p.inc("tensor", "t0", drain=False)
            p.wait("scalar", "t0", td)
            p.act(mrows[0:16, :], p.psT[0:16, 0:128], AF.Copy)
            p.act(crows[0:16, :], p.psT[0:16, 128:256], AF.Copy)
            c_s = p.inc("scalar", "s0")
            p.wait("tensor", "s0", c_s)
            p2e = [0] * 16
            p2m = [0] * 16
            p2f = [0] * 16
            p2t = [0] * 16
            for h in range(16):
                base = 64 * (h % 2)
                b = h % 2
                rq = qt[h // 2][base:base + 64, 128 * qb:128 * qb + 128]
                if h >= 2:
                    p.wait("tensor", "s1", p2e[h - 2])
                col = 0
                kx = koff - SW * s
                left = nk
                while left > 0:
                    lhs, n = ksrc(h, kx, left)
                    off = 0
                    while off < n:
                        m = min(128, n - off)
                        p.mm(p.ps[5 + b][0:m, col + off:col + off + 128],
                             lhs[:, off:off + m], rq, start=True, stop=False)
                        p.mm(p.ps[5 + b][0:m, col + off:col + off + 128],
                             hsel[0:16, 128 * h:128 * h + m],
                             mrows[0:16, :], start=False, stop=True)
                        off += 128
                    col += n
                    kx += n
                    left -= n
                td = p.inc("tensor", "t0", drain=False)
                p.wait("scalar", "t0", td)
                if h >= 2:
                    p.wait("scalar", "t0", p2t[h - 2])
                p.act(PTb[b][:, 0:nk], p.ps[5 + b][:, 0:nk], AF.Exp)
                p2e[h] = p.inc("scalar", "s1")
                p.wait("vector", "s1", p2e[h])
                p.v("tensor_tensor", PTb[b][:, 0:nk], PTb[b][:, 0:nk],
                    maskmulT[:, mask_c0:mask_c0 + nk], ALU.mult)
                p2m[h] = p.inc("vector", "v0")
                p.wait("tensor", "v0", p2m[h])
                if h >= 2:
                    p.wait("tensor", "v0", p2f[h - 2])
                p.mm(p.ps[0 + b][0:64, 0:128], hsel[0:16, 128 * h:128 * h + 64],
                     crows[0:16, :], start=True, stop=True)
                col = 0
                kx = koff - SW * s
                left = nk
                nmm = 0
                nblk = nk // 128
                while left > 0:
                    n = min(128, left)
                    if kx < 0:
                        vtile = vtail[(256 + kx) // 128]
                    else:
                        vtile = vt[kx // 128]
                    p.mm(p.ps[2 + b][0:64, 0:128], vtile[:, 64 * h:64 * h + 64],
                         PTb[b][0:128, col:col + 128],
                         start=(nmm == 0), stop=(nmm == nblk - 1))
                    nmm += 1
                    col += n
                    kx += n
                    left -= n
                p2t[h] = p.inc("tensor", "t0", drain=False)
                p.wait("scalar", "t0", p2t[h])
                p.act(crepb[b][0:64, :], p.ps[0 + b][0:64, 0:128], AF.Copy)
                cse = p.inc("scalar", "s0")
                p.wait("vector", "s0", cse)
                p.v("tensor_tensor", ot[h // 2][base:base + 64,
                                                128 * qb:128 * qb + 128],
                    p.ps[2 + b][0:64, 0:128], crepb[b][0:64, :], ALU.mult)
                p2f[h] = p.inc("vector", "v0")
            p.wait("tensor", "v0", p2f[15])
            p.wait("scalar", "v0", p2f[15])
        for half in range(2):
            for dm in range(4):
                mc = 4 * half + dm
                for kc in range(8):
                    p.mm(p.ps[dm][:, 0:SW], wo[kc][:, 128 * mc:128 * mc + 128],
                         ot[kc][:, :], start=(kc == 0), stop=(kc == 7))
            td = p.inc("tensor", "t0", drain=False)
            p.wait("vector", "t0", td)
            for dm in range(4):
                mc = 4 * half + dm
                p.v("tensor_tensor", wres[mc][:, :], wres[mc][:, :],
                    p.ps[dm][:, 0:SW], ALU.add)
            c_v = p.inc("vector", "v0")
            p.wait("tensor", "v0", c_v)
        c_v = p.inc("vector", "v0")
        p.wait("gpsimd", "v0", c_v)
        for i in range(8):
            p.dma("gpsimd", nxt[128 * i:128 * i + 128, cs], wres[i][:, :])
        p.wait("scalar", "v0", c_v)
        for i in range(8):
            p.act(ktail[i][:, :], kt[i][:, SW - 256:SW], AF.Copy)
        for j in range(2):
            p.act(vtail[j][:, :], vt[2 + j][:, :], AF.Copy)
        c_s = p.inc("scalar", "s0")
        p.barrier()
    p.reset(pref + "as")
    p.reset(pref + "aw")


def ffn_layer(p, li, cur, nxt, W, consts, lgb, b1c, b2c, xnf_hbm, hf_hbm, pref):
    NS = 8
    SW = T // NS
    DIH = 2816
    gcol = [lgb[:, 16 * li + i:16 * li + i + 1] for i in range(8)]
    bcol = [lgb[:, 16 * li + 8 + i:16 * li + 8 + i + 1] for i in range(8)]
    for halfw in range(2):
        mc0, mc1 = (0, 11) if halfw == 0 else (11, 22)
        p.mark(pref + "fw")
        w1 = [p.take("wbf", 256 * (mc1 - mc0)) for _ in range(8)]
        for k in range(8):
            p.dma("gpsimd", w1[k][:, :],
                  W["W1p"][li, 128 * k:128 * k + 128, 256 * mc0:256 * mc1])
        p.barrier()
        p.mark(pref + "f1")
        for s in range(NS):
            p.reset(pref + "f1")
            cs = slice(SW * s, SW * s + SW)
            xb = [p.take("abf", SW) for _ in range(8)]
            if halfw == 0:
                ws = [p.take("af32", SW) for _ in range(8)]
                load_shifted(p, ws, cur, s, SW)
                p.barrier()
                emit_ln(p, ws, xb, SW, consts, xb, p.ps[0], p.ps[1],
                        gcol=gcol, bcol=bcol)
                c_v = p.inc("vector", "v0")
                p.wait("gpsimd", "v0", c_v)
                for i in range(8):
                    p.dma("gpsimd", xnf_hbm[128 * i:128 * i + 128, cs],
                          xb[i][:, :])
                p.wait("tensor", "v0", c_v)
            else:
                for i in range(8):
                    p.dma("sync", xb[i][:, :], xnf_hbm[128 * i:128 * i + 128, cs])
                p.barrier()
            gg = p.take("abf", SW)
            hh = p.take("abf", SW)
            for mc in range(mc0, mc1):
                wof = 256 * (mc - mc0)
                for kc in range(8):
                    p.mm(p.ps[2][:, 0:SW], w1[kc][:, wof + 128:wof + 256],
                         xb[kc][:, :], start=(kc == 0), stop=(kc == 7))
                c_t = p.inc("tensor", "t0")
                p.wait("scalar", "t0", c_t)
                p.act(gg[:, :], p.ps[2][:, 0:SW], AF.Gelu,
                      bias=b1c[:, 44 * li + 2 * mc + 1:44 * li + 2 * mc + 2])
                c_s = p.inc("scalar", "s0")
                for kc in range(8):
                    p.mm(p.ps[3][:, 0:SW], w1[kc][:, wof:wof + 128],
                         xb[kc][:, :], start=(kc == 0), stop=(kc == 7))
                c_t = p.inc("tensor", "t0")
                p.wait("vector", "t0", c_t)
                p.wait("vector", "s0", p.cnt["s0"])
                p.v("scalar_tensor_tensor", hh[:, :], p.ps[3][:, 0:SW],
                    b1c[:, 44 * li + 2 * mc:44 * li + 2 * mc + 1], gg[:, :],
                    ALU.add, ALU.mult)
                c_v = p.inc("vector", "v0")
                p.wait("gpsimd", "v0", c_v)
                p.dma("gpsimd", hf_hbm[128 * mc:128 * mc + 128, cs], hh[:, :])
                p.wait("scalar", "dma_gpsimd", p.cnt["dma_gpsimd"])
                p.wait("vector", "dma_gpsimd", p.cnt["dma_gpsimd"])
                p.wait("tensor", "dma_gpsimd", p.cnt["dma_gpsimd"])
            p.barrier()
        p.reset(pref + "f1")
        p.reset(pref + "fw")
    p.mark(pref + "fw2")
    w2 = [p.take("wbf", 1024) for _ in range(22)]
    for k in range(22):
        p.dma("gpsimd", w2[k][:, :], W["W2p"][li, 128 * k:128 * k + 128, :])
    p.barrier()
    p.mark(pref + "f2")
    for s in range(NS):
        p.reset(pref + "f2")
        cs = slice(SW * s, SW * s + SW)
        hb = [p.take("abf", SW) for _ in range(22)]
        wres = [p.take("af32", SW) for _ in range(8)]
        for k in range(22):
            p.dma("sync", hb[k][:, :], hf_hbm[128 * k:128 * k + 128, cs])
        for i in range(8):
            p.dma("sync", wres[i][:, :], cur[128 * i:128 * i + 128, cs])
        p.barrier()
        for half in range(2):
            for dm in range(4):
                mc = 4 * half + dm
                for kc in range(22):
                    p.mm(p.ps[dm][:, 0:SW], w2[kc][:, 128 * mc:128 * mc + 128],
                         hb[kc][:, :], start=(kc == 0), stop=(kc == 21))
            c_t = p.inc("tensor", "t0")
            p.wait("vector", "t0", c_t)
            for dm in range(4):
                mc = 4 * half + dm
                p.v("scalar_tensor_tensor", wres[mc][:, :], p.ps[dm][:, 0:SW],
                    b2c[:, 8 * li + mc:8 * li + mc + 1], wres[mc][:, :],
                    ALU.add, ALU.add)
            c_v = p.inc("vector", "v0")
            p.wait("tensor", "v0", c_v)
        c_v = p.inc("vector", "v0")
        p.wait("gpsimd", "v0", c_v)
        for i in range(8):
            p.dma("gpsimd", nxt[128 * i:128 * i + 128, cs], wres[i][:, :])
        p.barrier()
    p.reset(pref + "f2")
    p.reset(pref + "fw2")


def host_l0ffn_weights(inputs):
    DEPTH, DIM, DH, W_ = 6, 1024, 64, 256
    INNER = 1024
    Wqkv_p = np.ascontiguousarray(np.array(inputs["L0_ffn_Wqkv"]))
    rdim = np.arange(128) % 64
    sign = np.where((np.arange(128) % 64) < 32, -1.0, 1.0)
    half = np.arange(0, DH, 2, dtype=np.float32)
    inv_freq = (1.0 / 10000.0 ** (half / DH)).astype(np.float32)
    t = np.arange(T, dtype=np.float32)
    freqs = t[:, None] * inv_freq[None, :]
    base_scale = ((half + 0.4 * DH) / (1.4 * DH)).astype(np.float32)
    power = ((t - T // 2) / (W_ // 2)).astype(np.float32)
    scale = base_scale[None, :] ** power[:, None]
    cos = np.cos(freqs)
    sin = np.sin(freqs)
    hd = rdim % 32
    cs2p = np.ascontiguousarray((cos[:, hd] * scale[:, hd]).T.astype(np.float32))
    snq = np.ascontiguousarray(
        ((sin[:, hd] * scale[:, hd]).T * sign[:, None]).astype(np.float32))
    ci2p = np.ascontiguousarray((cos[:, hd] / scale[:, hd]).T.astype(np.float32))
    snk = np.ascontiguousarray(
        ((sin[:, hd] / scale[:, hd]).T * sign[:, None]).astype(np.float32))
    rh = (np.arange(128) // 64) * 64 + ((np.arange(128) + 32) % 64)
    qk = np.zeros((128, 4 * DEPTH), dtype=np.float32)
    for li in range(DEPTH):
        qs = inputs["L0_ffn_q_scale"][li]
        ks = inputs["L0_ffn_k_scale"][li]
        qk[:, 4 * li + 0] = qs[rdim]
        qk[:, 4 * li + 1] = qs[rdim[rh]]
        qk[:, 4 * li + 2] = ks[rdim]
        qk[:, 4 * li + 3] = ks[rdim[rh]]
    assert np.all(rdim[rh] == (rdim + 32) % 64)
    maskadd = np.zeros((128, 384), dtype=np.float32)
    r = np.arange(128)[:, None]
    j = np.arange(128)[None, :]
    maskadd[:, 0:128] = np.where(j >= r, 0.0, NEG_BIG)
    maskadd[:, 256:384] = np.where(j <= r, 0.0, NEG_BIG)
    maskmulT = np.ones((128, 384), dtype=np.float32)
    maskmulT[:, 0:128] = (np.arange(128)[:, None] >= np.arange(128)[None, :])
    maskmulT[:, 256:384] = (np.arange(128)[:, None] <= np.arange(128)[None, :])
    ingrp = np.zeros((128, 2), dtype=np.float32)
    ingrp[0:64, 0] = 1
    ingrp[64:128, 1] = 1
    ones16 = np.zeros((128, 128), dtype=np.float32)
    sel16 = np.zeros((16, 1024), dtype=np.float32)
    for c in range(8):
        ones16[:, 16 * c + 2 * c:16 * c + 2 * c + 2] = ingrp
        sel16[2 * c:2 * c + 2, 128 * c:128 * c + 128] = ingrp.T
    hsel = np.zeros((16, 2048), dtype=np.float32)
    for h in range(16):
        hsel[h, 128 * h:128 * h + 128] = 1.0
    lngb = np.zeros((128, 16 * DEPTH), dtype=np.float32)
    for li in range(DEPTH):
        lngb[:, 16 * li:16 * li + 8] = inputs["L0_ffn_ln_g"][li].reshape(8, 128).T
        lngb[:, 16 * li + 8:16 * li + 16] = \
            inputs["L0_ffn_ln_b"][li].reshape(8, 128).T
    W1 = np.array(inputs["L0_ffn_W1"])
    DI0 = 2730
    W1a = np.zeros((DEPTH, DIM, 2816), dtype=np.float32)
    W1g = np.zeros((DEPTH, DIM, 2816), dtype=np.float32)
    W1a[:, :, :DI0] = W1[:, :, :DI0]
    W1g[:, :, :DI0] = W1[:, :, DI0:]
    W1p = np.zeros((DEPTH, DIM, 5632), dtype=np.float32)
    for mc in range(22):
        W1p[:, :, 256 * mc:256 * mc + 128] = W1a[:, :, 128 * mc:128 * mc + 128]
        W1p[:, :, 256 * mc + 128:256 * mc + 256] = \
            W1g[:, :, 128 * mc:128 * mc + 128]
    W1p = np.ascontiguousarray(W1p)
    b1 = np.array(inputs["L0_ffn_b1"])
    b1r = np.zeros((DEPTH, 128, 44), dtype=np.float32)
    b1a = np.zeros((DEPTH, 2816), dtype=np.float32)
    b1g = np.zeros((DEPTH, 2816), dtype=np.float32)
    b1a[:, :DI0] = b1[:, :DI0]
    b1g[:, :DI0] = b1[:, DI0:]
    for mc in range(22):
        b1r[:, :, 2 * mc] = b1a[:, 128 * mc:128 * mc + 128]
        b1r[:, :, 2 * mc + 1] = b1g[:, 128 * mc:128 * mc + 128]
    b1rc = np.zeros((128, 44 * DEPTH), dtype=np.float32)
    for li in range(DEPTH):
        b1rc[:, 44 * li:44 * li + 44] = b1r[li]
    W2 = np.array(inputs["L0_ffn_W2"])
    W2p = np.zeros((DEPTH, 2816, 1024), dtype=np.float32)
    W2p[:, :DI0, :] = W2
    W2p = np.ascontiguousarray(W2p)
    b2r = np.zeros((128, 8 * DEPTH), dtype=np.float32)
    for li in range(DEPTH):
        b2r[:, 8 * li:8 * li + 8] = inputs["L0_ffn_b2"][li].reshape(8, 128).T
    return {
        "Wqkv": Wqkv_p,
        "Wo": np.ascontiguousarray(inputs["L0_ffn_Wo"]),
        "W1p": W1p,
        "W2p": W2p,
        "cs2p": cs2p, "snq": snq, "ci2p": ci2p, "snk": snk,
        "qk_scales": qk,
        "maskadd": maskadd, "maskmulT": maskmulT,
        "ones16": ones16, "sel16": sel16, "hsel": hsel,
        "ln_gb": lngb, "b1r": b1rc, "b2r": b2r,
        "ident_bf": bf16(np.eye(128, dtype=np.float32)),
    }


def build_l0ffn_prog(debug=False, depth=6, do_attn=True, do_ffn=True):
    p = Prog("trf")
    p.arena("cst", 128, F32)
    p.arena("cbf", 6144, BF16)
    p.arena("cf32", 2048, F32)
    p.arena("wbf", 8 * 4096, BF16)
    p.arena("abf", 36864, BF16)
    p.arena("af32", 11264, F32)
    xin = p.inp("xin", (D, T), F32)
    xout = p.out("xout", (D, T), F32)
    W = {
        "Wqkv": p.inp("Wqkv", (6, 1024, 3072), F32),
        "Wo": p.inp("Wo", (6, 1024, 1024), F32),
        "W1p": p.inp("W1p", (6, 1024, 5632), F32),
        "W2p": p.inp("W2p", (6, 2816, 1024), F32),
        "cs2p": p.inp("cs2p", (128, T), F32),
        "snq": p.inp("snq", (128, T), F32),
        "ci2p": p.inp("ci2p", (128, T), F32),
        "snk": p.inp("snk", (128, T), F32),
        "qk_scales": p.inp("qk_scales", (128, 24), F32),
        "maskadd": p.inp("maskadd", (128, 384), F32),
        "maskmulT": p.inp("maskmulT", (128, 384), F32),
        "ones16": p.inp("ones16", (128, 128), F32),
        "sel16": p.inp("sel16", (16, 1024), F32),
        "hsel": p.inp("hsel", (16, 2048), F32),
        "ln_gb": p.inp("ln_gb", (128, 96), F32),
        "b1r": p.inp("b1r", (128, 264), F32),
        "b2r": p.inp("b2r", (128, 48), F32),
        "ident_bf": p.inp("ident_bf", (128, 128), BF16),
    }
    sec_l0ffn(p, xin, xout, W, debug=debug, depth=depth, do_attn=do_attn,
              do_ffn=do_ffn)
    return p.build()


MAMBA_WS = {
    "W_inT": ((1024, 4096), F32), "W_xprojT": ((2048, 96), F32),
    "W_dtT": ((64, 2048), F32), "W_outT": ((2048, 1024), F32),
    "conv_wr": ((128, 64), F32), "conv_br": ((128, 16), F32),
    "b_dtr": ((128, 16), F32), "D_skipr": ((128, 16), F32),
    "A_logr": ((128, 256), F32), "bcsel": ((32, 4096), F32),
}
TRF_WS = {
    "Wqkv": ((6, 1024, 3072), F32), "Wo": ((6, 1024, 1024), F32),
    "W1p": ((6, 1024, 5632), F32), "W2p": ((6, 2816, 1024), F32),
    "cs2p": ((128, T), F32), "snq": ((128, T), F32),
    "ci2p": ((128, T), F32), "snk": ((128, T), F32),
    "qk_scales": ((128, 24), F32), "maskadd": ((128, 384), F32),
    "maskmulT": ((128, 384), F32), "ones16": ((128, 128), F32),
    "sel16": ((16, 1024), F32), "hsel": ((16, 2048), F32),
    "ln_gb": ((128, 96), F32), "b1r": ((128, 264), F32),
    "b2r": ((128, 48), F32), "ident_bf": ((128, 128), BF16),
}
GLA_WS = {
    "Wq": ((1024, 1024), F32), "Wk": ((1024, 1024), F32),
    "Wv": ((1024, 2048), F32), "Wg": ((1024, 2048), F32),
    "Wgk1": ((1024, 16), F32), "Wgk2": ((16, 1024), F32),
    "Wo": ((2048, 1024), F32), "bgk2r": ((128, 8), F32),
    "gnwr": ((128, 4), F32), "ident_bf": ((128, 128), BF16),
    "maskT": ((64, 64), F32),
}
MOE_WS = {
    "gate_W": ((1024, 8), F32), "W1": ((8, 1024, 2048), F32),
    "b1r": ((8, 128, 16), F32), "W2": ((8, 2048, 1024), F32),
    "B2": ((8, 1024), F32), "esel": ((8, 1024), F32),
    "ident_f32": ((128, 128), F32),
}


def build_fused_prog(debug=False):
    p = Prog("fused")
    p.arena("cst", 128, F32)
    p.arena("cbf", 7168, BF16)
    p.arena("bf", 68224, BF16)
    p.arena("ff", 14080, F32)
    for nm in ("wbf", "abf", "dbl", "obf"):
        p.alias(nm, "bf")
    for nm in ("af32", "cf32", "wf32", "sf32", "ybig"):
        p.alias(nm, "ff")
    xin = p.inp("xin", (D, T), F32)
    xout = p.out("xout", (D, T), F32)
    x1 = p.dram("x1", (D, T), F32)
    x2 = p.dram("x2", (D, T), F32)
    x3 = p.dram("x3", (D, T), F32)

    def wmap(prefix, spec):
        return {k: p.inp(prefix + k, s, dt) for k, (s, dt) in spec.items()}

    Wm = wmap("m_", MAMBA_WS)
    Wt = wmap("t_", TRF_WS)
    Wg = wmap("g_", GLA_WS)
    We = wmap("e_", MOE_WS)

    std_consts(p)
    p.mark("sect")
    sec_mamba(p, xin, x1, Wm, debug=debug)
    p.barrier()
    p.reset("sect")
    sec_l0ffn(p, x1, x2, Wt, debug=debug)
    p.barrier()
    p.reset("sect")
    sec_gla(p, x2, x3, Wg, debug=debug)
    p.barrier()
    p.reset("sect")
    sec_moe(p, x3, xout, We, debug=debug)
    return p.build()


_PROGS = {}
FUSED = True


def _get_prog(name, builder):
    if name not in _PROGS:
        _PROGS[name] = builder()
    return _PROGS[name]


def kernel(**inputs):
    x = np.asarray(inputs["L0_attn_x"])
    xT = [np.ascontiguousarray(x[b].T) for b in range(NCORES)]

    hw_m = host_mamba_weights(inputs)
    hw_t = host_l0ffn_weights(inputs)
    hw_g = host_gla_weights({k: inputs[k] for k in inputs if k.startswith("L1_attn")})
    hw_e = host_moe_weights({k: inputs[k] for k in inputs if k.startswith("L1_ffn")})

    def launch(progname, builder, hw, xs):
        nc = _get_prog(progname, builder)
        in_maps = [dict({"xin": xs[b]}, **hw) for b in range(NCORES)]
        res = run_bass_kernel_spmd(nc, in_maps, core_ids=list(range(NCORES)))
        return [np.ascontiguousarray(np.asarray(res.results[b]["xout"]))
                for b in range(NCORES)]

    if FUSED:
        hw = {}
        for pref, d in (("m_", hw_m), ("t_", hw_t), ("g_", hw_g), ("e_", hw_e)):
            for k, v in d.items():
                hw[pref + k] = v
        x4 = launch("fused", build_fused_prog, hw, xT)
    else:
        x1 = launch("mamba", build_mamba_prog, hw_m, xT)
        x2 = launch("l0ffn", build_l0ffn_prog, hw_t, x1)
        x3 = launch("gla", build_gla_prog, hw_g, x2)
        x4 = launch("moe", build_moe_prog, hw_e, x3)
    return np.stack([x4[b].T for b in range(NCORES)], axis=0).astype(np.float32)


# version 06
# speedup vs baseline: 1.5983x; 1.3906x over previous
import numpy as np
from contextlib import ExitStack
import concourse.bass as bass
import concourse.mybir as mybir
from concourse.bass_utils import run_bass_kernel_spmd

F32 = mybir.dt.float32
BF16 = mybir.dt.bfloat16
AF = mybir.ActivationFunctionType
ALU = mybir.AluOpType
AX = mybir.AxisListType

NCORES = 8
T = 4096
D = 1024
ENGINES = ("tensor", "vector", "scalar", "gpsimd", "sync")
NEG_BIG = -1.0e30


def bf16(x):
    import ml_dtypes
    return np.asarray(x, dtype=ml_dtypes.bfloat16)


class Prog:
    def __init__(self, name="prog"):
        self.nc = bass.Bass("TRN2", target_bir_lowering=False, debug=False)
        self.stack = ExitStack()
        self.ops = []
        self.sem = {}
        self.cnt = {}
        self.bar_n = 0
        self.bar_dma_seen = {}
        self._uid = 0
        for e in ENGINES:
            self.new_sem("bar_" + e)
        for q in ("sync", "gpsimd", "scalar", "vector"):
            self.new_sem("dma_" + q)
            self.bar_dma_seen[q] = 0
        for s in ("t0", "t1", "v0", "v1", "s0", "s1", "g0"):
            self.new_sem(s)
        self.ps = [self.stack.enter_context(
            self.nc.psum_tensor(f"ps{i}", [128, 512], F32)) for i in range(7)]
        self.psT = self.stack.enter_context(
            self.nc.psum_tensor("psT", [128, 1024], BF16))
        self.arenas = {}
        self.marks = {}
        self.cvals = {}
        self.peaks = {}

    def uid(self):
        self._uid += 1
        return self._uid

    def new_sem(self, name):
        h = self.stack.enter_context(self.nc.semaphore(name))
        self.sem[name] = h
        self.cnt[name] = 0
        return h

    def arena(self, name, cols, dtype=F32):
        t = self.stack.enter_context(self.nc.sbuf_tensor(name, [128, cols], dtype))
        self.arenas[name] = [t, 0, cols]
        return t

    def take(self, name, cols, rows=128):
        a = self.arenas[name]
        off = (a[1] + 7) & ~7
        assert off + cols <= a[2], f"arena {name} overflow: {off}+{cols} > {a[2]}"
        a[1] = off + cols
        pk = self.peaks.get(a[0].name, 0)
        if a[1] > pk:
            self.peaks[a[0].name] = a[1]
        return a[0][0:rows, off:off + cols]

    def alias(self, name, target):
        self.arenas[name] = self.arenas[target]

    def mark(self, name):
        self.marks[name] = {k: v[1] for k, v in self.arenas.items()}

    def reset(self, name):
        for k, v in self.marks[name].items():
            self.arenas[k][1] = v

    def constv(self, value):
        if value in self.cvals:
            return value
        tile = self.take("cst", 8)
        self.v("memset", tile[:, 0:1], value)
        self.nc.const_aps.register(F32, value, tile[:, 0:1])
        self.cvals[value] = tile
        return value

    def dram(self, name, shape, dtype=F32, kind="Internal"):
        return self.nc.dram_tensor(name, list(shape), dtype, kind=kind).ap()

    def inp(self, name, shape, dtype=F32):
        return self.dram(name, shape, dtype, kind="ExternalInput")

    def out(self, name, shape, dtype=F32):
        return self.dram(name, shape, dtype, kind="ExternalOutput")

    AUTODRAIN = ("vector", "scalar", "gpsimd")

    def op(self, engine, method, *args, inc=None, nodrain=False, **kwargs):
        if inc is not None:
            sname, n = inc
            self.cnt[sname] += n
        drain = (engine in self.AUTODRAIN and not nodrain
                 and method != "dma_start")

        def fn(eng, method=method, args=args, kwargs=kwargs, inc=inc, drain=drain):
            r = getattr(eng, method)(*args, **kwargs)
            if inc is not None:
                r.then_inc(self.sem[inc[0]], inc[1])
            if drain:
                eng.drain()
            return r
        self.ops.append((engine, fn))
        if inc is not None:
            return self.cnt[inc[0]]

    def dma(self, queue, out_ap, in_ap, **kw):
        return self.op(queue, "dma_start", inc=("dma_" + queue, 16),
                       out=out_ap, in_=in_ap, **kw)

    def wait(self, engine, semname, val):
        self.ops.append((engine,
                         lambda eng, s=semname, v=val: eng.wait_ge(self.sem[s], v)))

    def wait_dma(self, engine, queue, val=None):
        self.wait(engine, "dma_" + queue, self.cnt["dma_" + queue] if val is None else val)

    def inc(self, engine, semname, n=1, drain=True):
        self.cnt[semname] += n
        if not drain:
            eng0, fn0 = self.ops[-1]
            assert eng0 == engine, f"last op is {eng0}, not {engine}"

            def fnw(eng, fn0=fn0, s=semname, n=n):
                r = fn0(eng)
                r.then_inc(self.sem[s], n)
                return r
            self.ops[-1] = (engine, fnw)
            return self.cnt[semname]

        def fn(eng, s=semname, n=n):
            return eng.drain().then_inc(self.sem[s], n)
        self.ops.append((engine, fn))
        return self.cnt[semname]

    def barrier(self):
        for q in ("sync", "gpsimd", "scalar", "vector"):
            c = self.cnt["dma_" + q]
            if c > self.bar_dma_seen[q]:
                self.wait(q, "dma_" + q, c)
                self.bar_dma_seen[q] = c
        self.bar_n += 1
        for e in ENGINES:
            self.inc(e, "bar_" + e, 1)
        for e in ENGINES:
            for o in ENGINES:
                if o != e:
                    self.wait(e, "bar_" + o, self.bar_n)

    def mm(self, ps, lhsT, rhs, start=True, stop=True, inc=None):
        return self.op("tensor", "matmul", ps, lhsT, rhs,
                       start=start, stop=stop, inc=inc)

    def trans(self, ps, in_, ident, inc=None):
        return self.op("tensor", "transpose", ps, in_, ident, inc=inc)

    def act(self, out, in_, func, bias=0.0, scale=1.0, accum_out=None, inc=None,
            eng="scalar"):
        return self.op(eng, "activation", out, in_, func,
                       bias=bias, scale=scale, accum_out=accum_out, inc=inc)

    def v(self, method, *args, inc=None, eng="vector", **kw):
        return self.op(eng, method, *args, inc=inc, **kw)

    def build(self):
        with self.nc.Block() as block:
            for ename in ENGINES:
                def make_runner(ename):
                    def runner(eng):
                        for e, fn in self.ops:
                            if e == ename:
                                fn(eng)
                    return runner
                getattr(block, ename)(make_runner(ename))
        self.stack.close()
        return self.nc


def emit_ln(p, x_tiles_f32, xb_tiles, width, consts, out_bf_tiles, psA, psB,
            t_tiles=None, out_f32_tiles=None, gcol=None, bcol=None, eps=1e-5):
    nch = len(x_tiles_f32)
    N = nch * 128
    for i in range(nch):
        p.act(xb_tiles[i][:, :width], x_tiles_f32[i][:, :width], AF.Copy)
    c_s = p.inc("scalar", "s0")
    p.wait("tensor", "s0", c_s)
    for i in range(nch):
        p.mm(psA[0:1, :width], consts["ones_col"][:, 0:1], xb_tiles[i][:, :width],
             start=(i == 0), stop=(i == nch - 1))
    c_t = p.inc("tensor", "t0")
    p.wait("scalar", "t0", c_t)
    p.act(consts["rowbf"][0:1, :width], psA[0:1, :width], AF.Copy, scale=1.0 / N)
    c_s = p.inc("scalar", "s0")
    p.wait("tensor", "s0", c_s)
    p.mm(psB[:, :width], consts["ones_row"], consts["rowbf"][0:1, :width])
    c_t = p.inc("tensor", "t0")
    p.wait("scalar", "t0", c_t)
    p.act(consts["rep_bf"][:, :width], psB[:, :width], AF.Copy)
    c_s = p.inc("scalar", "s0")
    p.wait("vector", "s0", c_s)
    tt = t_tiles if t_tiles is not None else x_tiles_f32
    for i in range(nch):
        p.v("tensor_tensor", tt[i][:, :width], x_tiles_f32[i][:, :width],
            consts["rep_bf"][:, :width], ALU.subtract)
    c_v = p.inc("vector", "v0")
    p.wait("scalar", "v0", c_v)
    for i in range(nch):
        p.act(xb_tiles[i][:, :width], tt[i][:, :width], AF.Square)
    c_s = p.inc("scalar", "s0")
    p.wait("tensor", "s0", c_s)
    for i in range(nch):
        p.mm(psA[0:1, :width], consts["ones_col"][:, 0:1], xb_tiles[i][:, :width],
             start=(i == 0), stop=(i == nch - 1))
    c_t = p.inc("tensor", "t0")
    p.wait("scalar", "t0", c_t)
    p.act(consts["row_a"][0:1, :width], psA[0:1, :width], AF.Ln,
          scale=1.0 / N, bias=eps)
    p.act(consts["rowbf"][0:1, :width], consts["row_a"][0:1, :width], AF.Exp,
          scale=-0.5)
    c_s = p.inc("scalar", "s0")
    p.wait("tensor", "s0", c_s)
    p.mm(psB[:, :width], consts["ones_row"], consts["rowbf"][0:1, :width])
    c_t = p.inc("tensor", "t0")
    p.wait("scalar", "t0", c_t)
    p.act(consts["rep2_bf"][:, :width], psB[:, :width], AF.Copy)
    c_s = p.inc("scalar", "s0")
    p.wait("vector", "s0", c_s)
    for i in range(nch):
        p.v("tensor_tensor", out_bf_tiles[i][:, :width], tt[i][:, :width],
            consts["rep2_bf"][:, :width], ALU.mult)
    if out_f32_tiles is not None:
        for i in range(nch):
            p.v("tensor_tensor", out_f32_tiles[i][:, :width], tt[i][:, :width],
                consts["rep2_bf"][:, :width], ALU.mult)
    if gcol is not None:
        for i in range(nch):
            p.v("tensor_scalar", out_bf_tiles[i][:, :width],
                out_bf_tiles[i][:, :width], gcol[i], None, ALU.mult)
            p.v("tensor_scalar", out_bf_tiles[i][:, :width],
                out_bf_tiles[i][:, :width], bcol[i], None, ALU.add)


def sec_moe(p, xin, xout, W, pref="moe", debug=False):
    nc = p.nc
    NS = 8
    SW = T // NS
    E, DH = 8, 2048

    kind = "ExternalOutput" if debug else "Internal"
    xn_hbm = p.dram(pref + "_xn", (D, T), BF16, kind=kind)
    y_hbm = p.dram(pref + "_y", (D, T), F32, kind=kind)
    cw_dbg = p.out(pref + "_cw", (8, T), F32) if debug else None

    consts = std_consts(p)

    gw = p.take("wbf", 64)
    for k in range(8):
        p.dma("gpsimd", gw[:, 8 * k:8 * k + 8], W["gate_W"][128 * k:128 * k + 128, :])

    cw_rows_b = p.take("cbf", T, rows=8)
    B2t = p.take("wbf", 1024, rows=8)
    p.dma("gpsimd", B2t[0:8, :], W["B2"][:, :])
    esel = p.take("wbf", 1024, rows=8)
    p.dma("gpsimd", esel[0:8, :], W["esel"][:, :])
    identf = p.take("cf32", 128)
    p.dma("sync", identf[:, :], W["ident_f32"][:, :])
    p.barrier()

    p.mark(pref + "s1")
    for s in range(NS):
        p.reset(pref + "s1")
        xf = [p.take("af32", SW) for _ in range(8)]
        xb = [p.take("abf", SW) for _ in range(8)]
        for i in range(8):
            p.dma("sync", xf[i][:, :], xin[128 * i:128 * i + 128, SW * s:SW * s + SW])
        p.barrier()
        emit_ln(p, xf, xb, SW, consts, xb, p.ps[0], p.ps[1])
        c_v = p.inc("vector", "v0")
        p.wait("gpsimd", "v0", c_v)
        for i in range(8):
            p.dma("gpsimd", xn_hbm[128 * i:128 * i + 128, SW * s:SW * s + SW],
                  xb[i][:, :])
        p.wait("tensor", "v0", c_v)
        lg = p.take("af32", 8 * 4)
        pcw = p.take("af32", 8 * 4)
        for tc in range(4):
            for k in range(8):
                p.mm(p.ps[2][:, 8 * tc:8 * tc + 8],
                     xb[k][:, 128 * tc:128 * tc + 128], gw[:, 8 * k:8 * k + 8],
                     start=(k == 0), stop=(k == 7))
        c_t = p.inc("tensor", "t0")
        p.wait("vector", "t0", c_t)
        p.v("tensor_copy", lg[:, :], p.ps[2][:, 0:32])
        mx = p.take("af32", 4)
        sm = p.take("af32", 4)
        for tc in range(4):
            p.v("reduce_max", mx[:, tc:tc + 1], lg[:, 8 * tc:8 * tc + 8], axis=AX.X,
                negate=True)
        c_v = p.inc("vector", "v0")
        p.wait("scalar", "v0", c_v)
        for tc in range(4):
            p.act(pcw[:, 8 * tc:8 * tc + 8], lg[:, 8 * tc:8 * tc + 8], AF.Exp,
                  bias=mx[:, tc:tc + 1], accum_out=sm[:, tc:tc + 1])
        c_s = p.inc("scalar", "s0")
        p.wait("vector", "s0", c_s)
        m1 = p.take("af32", 4)
        m2 = p.take("af32", 4)
        t1 = p.take("af32", 8 * 4)
        w12 = p.take("af32", 8 * 4)
        rs = p.take("af32", 4)
        for tc in range(4):
            a = lg[:, 8 * tc:8 * tc + 8]
            pp = pcw[:, 8 * tc:8 * tc + 8]
            tt = t1[:, 8 * tc:8 * tc + 8]
            ww = w12[:, 8 * tc:8 * tc + 8]
            p.v("reduce_max", m1[:, tc:tc + 1], pp, axis=AX.X)
            p.v("tensor_scalar", ww, pp, m1[:, tc:tc + 1], None, ALU.is_ge)
            p.v("scalar_tensor_tensor", tt, ww, NEG_BIG, pp, ALU.mult, ALU.add)
            p.v("reduce_max", m2[:, tc:tc + 1], tt, axis=AX.X)
            p.v("tensor_scalar", tt, tt, m2[:, tc:tc + 1], None, ALU.is_ge)
            p.v("tensor_tensor", ww, ww, tt, ALU.add)
            p.v("tensor_tensor", ww, ww, pp, ALU.mult)
            p.v("tensor_tensor", rs[:, tc:tc + 1], m1[:, tc:tc + 1],
                m2[:, tc:tc + 1], ALU.add)
            p.v("scalar_tensor_tensor", rs[:, tc:tc + 1], sm[:, tc:tc + 1], 1e-8,
                rs[:, tc:tc + 1], ALU.mult, ALU.add)
            p.v("reciprocal", rs[:, tc:tc + 1], rs[:, tc:tc + 1])
            p.v("tensor_scalar", ww, ww, rs[:, tc:tc + 1], None, ALU.mult)
        c_v = p.inc("vector", "v0")
        p.wait("tensor", "v0", c_v)
        for tc in range(4):
            p.trans(p.ps[3][0:8, 128 * tc:128 * tc + 128],
                    w12[:, 8 * tc:8 * tc + 8], identf[:, :])
        c_t = p.inc("tensor", "t0")
        p.wait("scalar", "t0", c_t)
        p.act(cw_rows_b[0:8, SW * s:SW * s + SW], p.ps[3][0:8, 0:SW], AF.Copy)
        if debug:
            cwf = p.take("af32", SW, rows=8)
            p.act(cwf[0:8, :], p.ps[3][0:8, 0:SW], AF.Copy)
            c_s = p.inc("scalar", "s0")
            p.wait("gpsimd", "s0", c_s)
            p.dma("gpsimd", cw_dbg[0:8, SW * s:SW * s + SW], cwf[0:8, :])
        p.barrier()
    p.reset(pref + "s1")

    p.mark(pref + "s2")
    for s in range(NS):
        p.reset(pref + "s2")
        xb = [p.take("abf", SW) for _ in range(8)]
        yf = [p.take("af32", SW) for _ in range(8)]
        for i in range(8):
            p.dma("sync", xb[i][:, :], xn_hbm[128 * i:128 * i + 128, SW * s:SW * s + SW])
        p.wait_dma("tensor", "sync")
        for i in range(8):
            p.mm(p.ps[i % 4][:, 0:SW], B2t[0:8, 128 * i:128 * i + 128],
                 cw_rows_b[0:8, SW * s:SW * s + SW],
                 start=True, stop=True)
            if i % 4 == 3 or i == 7:
                c_t = p.inc("tensor", "t0")
                p.wait("vector", "t0", c_t)
                for j in range(4 * (i // 4), i + 1):
                    p.v("tensor_tensor", yf[j][:, :],
                        p.ps[j % 4][:, 0:SW], xb[j][:, :], ALU.add)
                c_v = p.inc("vector", "v0")
                p.wait("tensor", "v0", c_v)
        p.wait("gpsimd", "v0", p.cnt["v0"])
        for j in range(8):
            p.dma("gpsimd", y_hbm[128 * j:128 * j + 128, SW * s:SW * s + SW],
                  yf[j][:, :])
        p.barrier()
    p.reset(pref + "s2")

    b1c = p.take("wf32", 16)
    p.mark(pref + "s3")
    for e in range(E):
        p.reset(pref + "s3")
        w1 = [p.take("wbf", DH) for _ in range(8)]
        w2 = [p.take("wbf", D) for _ in range(16)]
        p.dma("gpsimd", b1c[:, :], W["b1r"][e])
        for k in range(8):
            p.dma("gpsimd", w1[k][:, :], W["W1"][e, 128 * k:128 * k + 128, :])
        for k in range(16):
            p.dma("gpsimd", w2[k][:, :], W["W2"][e, 128 * k:128 * k + 128, :])
        p.barrier()
        for s in range(NS):
            p.mark(pref + "s3s")
            xb = [p.take("abf", SW) for _ in range(8)]
            yf = [p.take("af32", SW) for _ in range(8)]
            p.wait_dma("sync", "gpsimd")
            for i in range(8):
                p.dma("sync", xb[i][:, :],
                      xn_hbm[128 * i:128 * i + 128, SW * s:SW * s + SW])
                p.dma("sync", yf[i][:, :],
                      y_hbm[128 * i:128 * i + 128, SW * s:SW * s + SW])
            cwrep = p.take("abf", SW)
            p.wait_dma("tensor", "sync")
            p.mm(p.ps[6][:, 0:SW], esel[0:8, 128 * e:128 * e + 128],
                 cw_rows_b[0:8, SW * s:SW * s + SW])
            c_t = p.inc("tensor", "t0")
            p.wait("scalar", "t0", c_t)
            p.act(cwrep[:, :], p.ps[6][:, 0:SW], AF.Copy)
            c_s = p.inc("scalar", "s0")
            p.wait("vector", "s0", c_s)
            p.wait("tensor", "s0", c_s)
            ht = [p.take("abf", SW) for _ in range(16)]
            hrb = [p.take("abf", SW) for _ in range(2)]
            vdn = [0] * 16
            for mc in range(16):
                pb = p.ps[mc % 2]
                if mc >= 2:
                    p.wait("tensor", "s0", p.cnt["s0"] if False else sdn[mc - 2])
                for k in range(8):
                    p.mm(pb[:, 0:SW], w1[k][:, 128 * mc:128 * mc + 128], xb[k][:, :],
                         start=(k == 0), stop=(k == 7))
                c_t = p.inc("tensor", "t0", drain=False)
                p.wait("scalar", "t0", c_t)
                if mc >= 2:
                    p.wait("scalar", "v0", vdn[mc - 2])
                p.act(hrb[mc % 2][:, :], pb[:, 0:SW], AF.Relu, bias=b1c[:, mc:mc + 1])
                c_s = p.inc("scalar", "s0")
                if mc == 0:
                    sdn = [0] * 16
                sdn[mc] = c_s
                p.wait("vector", "s0", c_s)
                p.v("tensor_tensor", ht[mc][:, :], hrb[mc % 2][:, :], cwrep[:, :],
                    ALU.mult)
                vdn[mc] = p.inc("vector", "v0")
            p.wait("tensor", "v0", vdn[15])
            for half in range(2):
                for dm in range(4):
                    j = 4 * half + dm
                    for mc in range(16):
                        p.mm(p.ps[2 + dm][:, 0:SW],
                             w2[mc][:, 128 * j:128 * j + 128], ht[mc][:, :],
                             start=(mc == 0), stop=(mc == 15))
                c_t = p.inc("tensor", "t0")
                p.wait("vector", "t0", c_t)
                for dm in range(4):
                    j = 4 * half + dm
                    p.v("tensor_tensor", yf[j][:, :],
                        yf[j][:, :], p.ps[2 + dm][:, 0:SW], ALU.add)
                c_v = p.inc("vector", "v0")
                p.wait("tensor", "v0", c_v)
            p.wait("gpsimd", "v0", p.cnt["v0"])
            for j in range(8):
                p.dma("gpsimd", y_hbm[128 * j:128 * j + 128, SW * s:SW * s + SW],
                      yf[j][:, :])
            p.reset(pref + "s3s")
        p.barrier()
    p.reset(pref + "s3")

    p.mark(pref + "s4")
    for s in range(NS):
        p.reset(pref + "s4")
        yv = [p.take("af32", SW) for _ in range(8)]
        yb = [p.take("abf", SW) for _ in range(8)]
        ob = [p.take("abf", SW) for _ in range(8)]
        xf = [p.take("af32", SW) for _ in range(8)]
        for i in range(8):
            p.dma("sync", xf[i][:, :], xin[128 * i:128 * i + 128, SW * s:SW * s + SW])
            p.dma("sync", yv[i][:, :], y_hbm[128 * i:128 * i + 128, SW * s:SW * s + SW])
        p.barrier()
        emit_ln(p, yv, yb, SW, consts, ob, p.ps[0], p.ps[1])
        c_v = p.inc("vector", "v0")
        p.wait("vector", "v0", c_v)
        for i in range(8):
            p.v("tensor_tensor", xf[i][:, :], xf[i][:, :], ob[i][:, :], ALU.add)
        c_v = p.inc("vector", "v0")
        p.wait("gpsimd", "v0", c_v)
        for i in range(8):
            p.dma("gpsimd", xout[128 * i:128 * i + 128, SW * s:SW * s + SW],
                  xf[i][:, :])
        p.barrier()


def host_moe_weights(inputs):
    W1 = np.ascontiguousarray(inputs["L1_ffn_W1"])
    W2 = np.ascontiguousarray(inputs["L1_ffn_W2"])
    b1 = inputs["L1_ffn_b1"]
    b1r = np.ascontiguousarray(b1.reshape(8, 16, 128).transpose(0, 2, 1))
    B2 = np.ascontiguousarray(inputs["L1_ffn_b2"])
    gate_W = np.ascontiguousarray(inputs["L1_ffn_gate_W"])
    ident = np.eye(128, dtype=np.float32)
    esel = np.zeros((8, 1024), dtype=np.float32)
    for e in range(8):
        esel[e, 128 * e:128 * e + 128] = 1.0
    return dict(W1=W1, W2=W2, b1r=b1r, B2=B2, gate_W=gate_W, ident_f32=ident,
                esel=esel)


def build_moe_prog(debug=False):
    p = Prog("moe")
    p.arena("cst", 128, F32)
    p.arena("cbf", 10240, BF16)
    p.arena("cf32", 1024, F32)
    p.arena("wbf", 8 * 2048 + 16 * 1024 + 4096, BF16)
    p.arena("wf32", 64, F32)
    p.arena("abf", 15360, BF16)
    p.arena("af32", 18 * 512, F32)
    xin = p.inp("xin", (D, T), F32)
    xout = p.out("xout", (D, T), F32)
    W = {
        "gate_W": p.inp("gate_W", (1024, 8), F32),
        "W1": p.inp("W1", (8, 1024, 2048), F32),
        "b1r": p.inp("b1r", (8, 128, 16), F32),
        "W2": p.inp("W2", (8, 2048, 1024), F32),
        "B2": p.inp("B2", (8, 1024), F32),
        "esel": p.inp("esel", (8, 1024), F32),
        "ident_f32": p.inp("ident_f32", (128, 128), F32),
    }
    sec_moe(p, xin, xout, W, debug=debug)
    return p.build()


def std_consts(p):
    if hasattr(p, "_stdconsts"):
        return p._stdconsts
    consts = {
        "ones_col": p.take("cbf", 8),
        "ones_row": p.take("cbf", 128, rows=1),
        "rowbf": p.take("cbf", 1024, rows=1),
        "row_a": p.take("cf32", 512, rows=1),
        "rep_bf": p.take("cbf", 512),
        "rep2_bf": p.take("cbf", 512),
    }
    p.v("memset", consts["ones_col"][:, 0:1], 1.0)
    p.v("memset", consts["ones_row"][0:1, :], 1.0)
    p.constv(1e-5)
    p.constv(1.0)
    p._stdconsts = consts
    return consts


def sec_gla(p, xin, xout, W, pref="gla", debug=False):
    NS = 8
    SW = T // NS
    NH, HDK, HDV, CH = 4, 256, 512, 64
    kind = "ExternalOutput" if debug else "Internal"

    xn_hbm = p.dram(pref + "_xn", (D, T), BF16, kind=kind)
    q_hbm = p.dram(pref + "_q", (D, T), BF16, kind=kind)
    k_hbm = p.dram(pref + "_k", (D, T), BF16, kind=kind)
    gkt_hbm = p.dram(pref + "_gkt", (D, T), F32, kind=kind)
    v_hbm = p.dram(pref + "_v", (T, 2048), BF16, kind=kind)
    sg_hbm = p.dram(pref + "_sg", (2048, T), BF16, kind=kind)
    og_hbm = p.dram(pref + "_og", (2048, T), BF16, kind=kind)
    op_hbm = p.dram(pref + "_op", (2048, T), BF16, kind=kind) if debug else None
    if debug:
        dbg_sT = p.out(pref + "_dbg_sT", (64, 64), BF16)
        dbg_q = p.out(pref + "_dbg_q", (128, 64), BF16)
        dbg_k = p.out(pref + "_dbg_k", (128, 64), BF16)
        dbg_b = p.out(pref + "_dbg_b", (128, 64), F32)
        dbg_eb = p.out(pref + "_dbg_eb", (128, 64), BF16)
        dbg_qraw = p.out(pref + "_dbg_qraw", (128, 64), BF16)

    consts = std_consts(p)
    consts["maskT"] = p.take("cf32", 64, rows=64)
    p.dma("sync", consts["maskT"][0:64, :], W["maskT"][:, :])
    identb = p.take("cbf", 128)
    p.dma("sync", identb[:, :], W["ident_bf"][:, :])
    zer = p.take("cf32", 64)
    p.v("memset", zer[:, :], 0.0)

    bgk = p.take("cf32", 8)
    nbgk = p.take("cf32", 8)
    gnw = p.take("cf32", 4)
    p.dma("sync", bgk[:, :], W["bgk2r"][:, :])
    p.dma("sync", gnw[:, :], W["gnwr"][:, :])
    p.wait_dma("vector", "sync")
    p.v("tensor_scalar_mul", nbgk[:, :], bgk[:, :], -1.0)

    p.mark(pref + "w")
    wq = [p.take("wbf", 1024) for _ in range(8)]
    wk = [p.take("wbf", 1024) for _ in range(8)]
    wgk1 = p.take("wbf", 128)
    wgk2 = p.take("wbf", 1024, rows=16)
    for k in range(8):
        r = slice(128 * k, 128 * k + 128)
        p.dma("gpsimd", wq[k][:, :], W["Wq"][r, :])
        p.dma("gpsimd", wk[k][:, :], W["Wk"][r, :])
        p.dma("gpsimd", wgk1[:, 16 * k:16 * k + 16], W["Wgk1"][r, :])
    p.dma("gpsimd", wgk2[0:16, :], W["Wgk2"][:, :])

    p.barrier()
    p.mark(pref + "p1")
    for s in range(NS):
        p.reset(pref + "p1")
        cs = slice(SW * s, SW * s + SW)
        xf = [p.take("af32", SW) for _ in range(8)]
        xb = [p.take("abf", SW) for _ in range(8)]
        for i in range(8):
            p.dma("sync", xf[i][:, :], xin[128 * i:128 * i + 128, cs])
        p.barrier()
        emit_ln(p, xf, xb, SW, consts, xb, p.ps[0], p.ps[1])
        c_v = p.inc("vector", "v0")
        p.wait("gpsimd", "v0", c_v)
        for i in range(8):
            p.dma("gpsimd", xn_hbm[128 * i:128 * i + 128, cs], xb[i][:, :])
        p.wait("tensor", "v0", c_v)
        evb = [p.take("abf", SW) for _ in range(2)]
        sdn = [0] * 16
        ddn = [0] * 16
        for i in range(16):
            wt, dst, scl = ((wq, q_hbm, 1.0 / 16.0) if i < 8 else (wk, k_hbm, 1.0))
            mc = i % 8
            pb = p.ps[2 + i % 2]
            if i >= 2:
                p.wait("tensor", "s0", sdn[i - 2])
            for kc in range(8):
                p.mm(pb[:, 0:SW], wt[kc][:, 128 * mc:128 * mc + 128],
                     xb[kc][:, :], start=(kc == 0), stop=(kc == 7))
            c_t = p.inc("tensor", "t0", drain=False)
            p.wait("scalar", "t0", c_t)
            if i >= 2:
                p.wait("scalar", "dma_gpsimd", ddn[i - 2])
            p.act(evb[i % 2][:, :], pb[:, 0:SW], AF.Copy, scale=scl)
            sdn[i] = p.inc("scalar", "s0")
            p.wait("gpsimd", "s0", sdn[i])
            p.dma("gpsimd", dst[128 * mc:128 * mc + 128, cs], evb[i % 2][:, :])
            ddn[i] = p.cnt["dma_gpsimd"]
        t1b = p.take("abf", SW, rows=16)
        for kc in range(8):
            p.mm(p.ps[4][0:16, 0:SW], wgk1[:, 16 * kc:16 * kc + 16],
                 xb[kc][:, :], start=(kc == 0), stop=(kc == 7))
        c_t = p.inc("tensor", "t0")
        p.wait("scalar", "t0", c_t)
        p.act(t1b[0:16, :], p.ps[4][0:16, 0:SW], AF.Copy)
        c_s = p.inc("scalar", "s0")
        p.wait("tensor", "s0", c_s)
        sa = p.take("af32", SW)
        sb = p.take("af32", SW)
        gk_o = p.take("af32", SW)
        for mc in range(8):
            pb = p.ps[2 + mc % 2]
            p.mm(pb[:, 0:SW], wgk2[0:16, 128 * mc:128 * mc + 128], t1b[0:16, :])
            c_t = p.inc("tensor", "t0")
            p.wait("scalar", "t0", c_t)
            bcol = bgk[:, mc:mc + 1]
            p.act(sa[:, :], pb[:, 0:SW], AF.Abs, bias=bcol)
            p.act(sa[:, :], sa[:, :], AF.Exp, scale=-1.0)
            p.act(sa[:, :], sa[:, :], AF.Ln, bias=1.0)
            p.act(sb[:, :], pb[:, 0:SW], AF.Relu, scale=-1.0,
                  bias=nbgk[:, mc:mc + 1])
            c_s = p.inc("scalar", "s0")
            p.wait("vector", "s0", c_s)
            p.v("tensor_tensor", gk_o[:, :], sa[:, :], sb[:, :], ALU.add)
            c_v = p.inc("vector", "v0")
            p.wait("gpsimd", "v0", c_v)
            p.dma("gpsimd", gkt_hbm[128 * mc:128 * mc + 128, cs], gk_o[:, :])
            p.wait("scalar", "dma_gpsimd", p.cnt["dma_gpsimd"])
            p.wait("tensor", "dma_gpsimd", p.cnt["dma_gpsimd"])
        p.barrier()
    p.reset(pref + "p1")
    p.reset(pref + "w")

    wv = [p.take("wbf", 2048) for _ in range(8)]
    wg = [p.take("wbf", 2048) for _ in range(8)]
    for k in range(8):
        r = slice(128 * k, 128 * k + 128)
        p.dma("gpsimd", wv[k][:, :], W["Wv"][r, :])
        p.dma("gpsimd", wg[k][:, :], W["Wg"][r, :])
    p.barrier()
    p.mark(pref + "p1b")
    for s in range(NS):
        p.reset(pref + "p1b")
        cs = slice(SW * s, SW * s + SW)
        xb = [p.take("abf", SW) for _ in range(8)]
        for i in range(8):
            p.dma("sync", xb[i][:, :], xn_hbm[128 * i:128 * i + 128, cs])
        p.barrier()
        evb = [p.take("abf", SW) for _ in range(2)]
        sdn = [0] * 16
        ddn = [0] * 16
        for mc in range(16):
            pb = p.ps[2 + mc % 2]
            if mc >= 2:
                p.wait("tensor", "s0", sdn[mc - 2])
            for kc in range(8):
                p.mm(pb[:, 0:SW], wg[kc][:, 128 * mc:128 * mc + 128],
                     xb[kc][:, :], start=(kc == 0), stop=(kc == 7))
            c_t = p.inc("tensor", "t0", drain=False)
            p.wait("scalar", "t0", c_t)
            if mc >= 2:
                p.wait("scalar", "dma_gpsimd", ddn[mc - 2])
            p.act(evb[mc % 2][:, :], pb[:, 0:SW], AF.Silu)
            sdn[mc] = p.inc("scalar", "s0")
            p.wait("gpsimd", "s0", sdn[mc])
            p.dma("gpsimd", sg_hbm[128 * mc:128 * mc + 128, cs], evb[mc % 2][:, :])
            ddn[mc] = p.cnt["dma_gpsimd"]
        vbb = [p.take("abf", 512) for _ in range(2)]
        vsd = [0] * 16
        vdd = [0] * 16
        for i in range(16):
            tc, dvc = i // 4, i % 4
            pb = p.ps[4 + i % 2]
            if i >= 2:
                p.wait("tensor", "s0", vsd[i - 2])
            for kc in range(8):
                p.mm(pb[:, 0:512], xb[kc][:, 128 * tc:128 * tc + 128],
                     wv[kc][:, 512 * dvc:512 * dvc + 512],
                     start=(kc == 0), stop=(kc == 7))
            c_t = p.inc("tensor", "t0", drain=False)
            p.wait("scalar", "t0", c_t)
            if i >= 2:
                p.wait("scalar", "dma_gpsimd", vdd[i - 2])
            p.act(vbb[i % 2][:, :], pb[:, 0:512], AF.Copy)
            vsd[i] = p.inc("scalar", "s0")
            p.wait("gpsimd", "s0", vsd[i])
            p.dma("gpsimd",
                  v_hbm[SW * s + 128 * tc:SW * s + 128 * tc + 128,
                        512 * dvc:512 * dvc + 512], vbb[i % 2][:, :])
            vdd[i] = p.cnt["dma_gpsimd"]
        p.barrier()
    p.reset(pref + "p1b")
    p.reset(pref + "w")

    p.mark(pref + "p2")
    NG = 8
    for h in range(NH):
        p.reset(pref + "p2")
        hr = slice(256 * h, 256 * h + 256)
        S0 = p.take("sf32", 512)
        S1 = p.take("sf32", 512)
        Sb0 = p.take("abf", 512)
        Sb1 = p.take("abf", 512)
        o_head = [p.take("obf", T) for _ in range(4)]
        for tl in (S0, S1):
            p.v("memset", tl[:, :], 0.0)
        for tl in (Sb0, Sb1):
            p.v("memset", tl[:, :], 0.0)
        p.mark(pref + "p2g")
        for cg in range(NG):
            p.reset(pref + "p2g")
            gs = slice(SW * cg, SW * cg + SW)
            qg_t = [p.take("abf", SW) for _ in range(2)]
            kg_t = [p.take("abf", SW) for _ in range(2)]
            gk_t = [p.take("af32", SW) for _ in range(2)]
            vg_t = [p.take("abf", 512, rows=64) for _ in range(8)]
            for d2 in range(2):
                rr = slice(256 * h + 128 * d2, 256 * h + 128 * d2 + 128)
                p.dma("sync", qg_t[d2][:, :], q_hbm[rr, gs])
                p.dma("sync", kg_t[d2][:, :], k_hbm[rr, gs])
                p.dma("sync", gk_t[d2][:, :], gkt_hbm[rr, gs])
            for tc in range(8):
                p.dma("sync", vg_t[tc][0:64, :],
                      v_hbm[SW * cg + 64 * tc:SW * cg + 64 * tc + 64,
                            512 * h:512 * h + 512])
            p.barrier()
            for c in range(8):
                ci = slice(64 * c, 64 * c + 64)
                vt = vg_t[c]
                vrows = slice(0, 64)
                bb = [p.take("af32", 64) for _ in range(2)]
                eb = p.take("abf", 64)
                qgc = [p.take("abf", 64) for _ in range(2)]
                kic = [p.take("abf", 64) for _ in range(2)]
                ksc = [p.take("abf", 64) for _ in range(2)]
                blc = p.take("af32", 8)
                sT = p.take("abf", 64, rows=64)
                kst = p.take("abf", 256, rows=64)
                for d2 in range(2):
                    p.v("tensor_tensor_scan", bb[d2][:, :], gk_t[d2][:, ci],
                        zer[:, :], 0.0, ALU.add, ALU.add)
                c_v = p.inc("vector", "v0")
                p.wait("scalar", "v0", c_v)
                for d2 in range(2):
                    p.act(blc[:, d2:d2 + 1], bb[d2][:, 63:64], AF.Copy,
                          scale=-1.0 / 16.0)
                    p.act(blc[:, 2 + d2:3 + d2], bb[d2][:, 63:64], AF.Exp,
                          scale=-1.0 / 16.0)
                c_s = p.inc("scalar", "s0")
                p.wait("vector", "s0", c_s)
                for d2 in range(2):
                    p.act(eb[:, :], bb[d2][:, :], AF.Exp, scale=-1.0 / 16.0,
                          eng="scalar")
                    c_s = p.inc("scalar", "s0")
                    p.wait("vector", "s0", c_s)
                    p.v("tensor_tensor", qgc[d2][:, :], qg_t[d2][:, ci], eb[:, :],
                        ALU.mult)
                    c_v = p.inc("vector", "v0")
                    p.wait("scalar", "v0", c_v)
                    p.act(eb[:, :], bb[d2][:, :], AF.Exp, scale=1.0 / 16.0)
                    c_s = p.inc("scalar", "s0")
                    p.wait("vector", "s0", c_s)
                    p.v("tensor_tensor", kic[d2][:, :], kg_t[d2][:, ci], eb[:, :],
                        ALU.mult)
                    c_v = p.inc("vector", "v0")
                    p.wait("scalar", "v0", c_v)
                    p.act(eb[:, :], bb[d2][:, :], AF.Exp, scale=1.0 / 16.0,
                          bias=blc[:, d2:d2 + 1])
                    c_s = p.inc("scalar", "s0")
                    p.wait("vector", "s0", c_s)
                    p.v("tensor_tensor", ksc[d2][:, :], kg_t[d2][:, ci], eb[:, :],
                        ALU.mult)
                    c_v = p.inc("vector", "v0")
                    p.wait("scalar", "v0", c_v)
                c_v = p.inc("vector", "v0")
                p.wait("tensor", "v0", c_v)
                for d2 in range(2):
                    p.mm(p.ps[4][0:64, 0:64], kic[d2][:, :], qgc[d2][:, :],
                         start=(d2 == 0), stop=(d2 == 1))
                c_t = p.inc("tensor", "t0")
                p.wait("vector", "t0", c_t)
                p.v("tensor_tensor", sT[0:64, :], p.ps[4][0:64, 0:64],
                    consts["maskT"][0:64, :], ALU.mult)
                c_v = p.inc("vector", "v0")
                if debug and h == 0 and cg == 0 and c == 0:
                    p.wait("gpsimd", "v0", c_v)
                    p.dma("gpsimd", dbg_sT[0:64, :], sT[0:64, :])
                    p.dma("gpsimd", dbg_q[:, :], qgc[0][:, :])
                    p.dma("gpsimd", dbg_k[:, :], kic[0][:, :])
                    p.dma("gpsimd", dbg_b[:, :], bb[0][:, :])
                    p.dma("gpsimd", dbg_eb[:, :], eb[:, :])
                    p.dma("gpsimd", dbg_qraw[:, :], qg_t[0][:, 0:64])
                    p.wait_dma("vector", "gpsimd")
                p.wait("tensor", "v0", c_v)
                for dvc in range(4):
                    p.mm(p.ps[dvc][:, 0:64],
                         vt[vrows, 128 * dvc:128 * dvc + 128], sT[0:64, :],
                         start=True, stop=False)
                    for d2, Sb in ((0, Sb0), (1, Sb1)):
                        p.mm(p.ps[dvc][:, 0:64],
                             Sb[:, 128 * dvc:128 * dvc + 128], qgc[d2][:, :],
                             start=False, stop=(d2 == 1))
                c_t = p.inc("tensor", "t0")
                p.wait("scalar", "t0", c_t)
                for dvc in range(4):
                    p.act(o_head[dvc][:, SW * cg + 64 * c:SW * cg + 64 * c + 64],
                          p.ps[dvc][:, 0:64], AF.Copy)
                c_s = p.inc("scalar", "s0")
                p.wait("tensor", "s0", c_s)
                for d2 in range(2):
                    p.trans(p.psT[0:64, 128 * d2:128 * d2 + 128], ksc[d2][:, :],
                            identb[:, :])
                c_t = p.inc("tensor", "t0")
                p.wait("scalar", "t0", c_t)
                p.act(kst[0:64, :], p.psT[0:64, 0:256], AF.Copy)
                c_s = p.inc("scalar", "s0")
                p.wait("tensor", "s0", c_s)
                for d2 in range(2):
                    p.mm(p.ps[5 + d2][:, :], kst[0:64, 128 * d2:128 * d2 + 128],
                         vt[vrows, :], start=True, stop=True)
                c_t = p.inc("tensor", "t0")
                p.wait("vector", "t0", c_t)
                for d2, Sf in ((0, S0), (1, S1)):
                    p.v("scalar_tensor_tensor", Sf[:, :], Sf[:, :],
                        blc[:, 2 + d2:3 + d2], p.ps[5 + d2][:, :],
                        ALU.mult, ALU.add)
                c_v = p.inc("vector", "v0")
                p.wait("scalar", "v0", c_v)
                for d2, (Sf, Sb) in ((0, (S0, Sb0)), (1, (S1, Sb1))):
                    p.act(Sb[:, :], Sf[:, :], AF.Copy)
                c_s = p.inc("scalar", "s0")
                p.wait("tensor", "s0", c_s)
                p.wait("vector", "s0", c_s)
            p.barrier()
        p.mark(pref + "p3")
        for s in range(NS):
            p.reset(pref + "p3")
            cs = slice(SW * s, SW * s + SW)
            if debug:
                for dvc in range(4):
                    p.dma("gpsimd", op_hbm[512 * h + 128 * dvc:512 * h + 128 * dvc + 128, cs],
                          o_head[dvc][:, cs])
            sqb = p.take("abf", SW)
            sgb = [p.take("abf", SW) for _ in range(4)]
            ob = p.take("abf", SW)
            for dvc in range(4):
                p.dma("sync", sgb[dvc][:, :],
                      sg_hbm[512 * h + 128 * dvc:512 * h + 128 * dvc + 128, cs])
            p.barrier()
            for dvc in range(4):
                p.act(sqb[:, :], o_head[dvc][:, cs], AF.Square)
                c_s = p.inc("scalar", "s0")
                p.wait("tensor", "s0", c_s)
                p.mm(p.ps[0][0:1, 0:SW], consts["ones_col"][:, 0:1], sqb[:, :],
                     start=(dvc == 0), stop=(dvc == 3))
                c_t = p.inc("tensor", "t0")
                p.wait("scalar", "t0", c_t)
            p.act(consts["row_a"][0:1, :SW], p.ps[0][0:1, 0:SW], AF.Ln,
                  scale=1.0 / 512.0, bias=1e-5)
            p.act(consts["rowbf"][0:1, :SW], consts["row_a"][0:1, :SW], AF.Exp,
                  scale=-0.5)
            c_s = p.inc("scalar", "s0")
            p.wait("tensor", "s0", c_s)
            p.mm(p.ps[1][:, 0:SW], consts["ones_row"], consts["rowbf"][0:1, :SW])
            c_t = p.inc("tensor", "t0")
            p.wait("vector", "t0", c_t)
            for dvc in range(4):
                p.v("tensor_tensor", ob[:, :], o_head[dvc][:, cs], p.ps[1][:, 0:SW],
                    ALU.mult)
                c_v = p.inc("vector", "v0")
                p.wait("scalar", "v0", c_v)
                p.act(ob[:, :], ob[:, :], AF.Copy, scale=gnw[:, dvc:dvc + 1])
                c_s = p.inc("scalar", "s0")
                p.wait("vector", "s0", c_s)
                p.v("tensor_tensor", ob[:, :], ob[:, :], sgb[dvc][:, :], ALU.mult)
                c_v = p.inc("vector", "v0")
                p.wait("gpsimd", "v0", c_v)
                p.dma("gpsimd", og_hbm[512 * h + 128 * dvc:512 * h + 128 * dvc + 128,
                                       cs], ob[:, :])
                p.wait("vector", "dma_gpsimd", p.cnt["dma_gpsimd"])
                p.wait("scalar", "dma_gpsimd", p.cnt["dma_gpsimd"])
            p.barrier()
        p.reset(pref + "p3")
    p.reset(pref + "p2")

    p.mark(pref + "p4w")
    wo = [p.take("wbf", 1024) for _ in range(16)]
    for k in range(16):
        p.dma("gpsimd", wo[k][:, :], W["Wo"][128 * k:128 * k + 128, :])
    p.barrier()
    p.mark(pref + "p4")
    for s in range(NS):
        p.reset(pref + "p4")
        cs = slice(SW * s, SW * s + SW)
        ogb = [p.take("abf", SW) for _ in range(16)]
        yv = [p.take("af32", SW) for _ in range(8)]
        yb = [p.take("abf", SW) for _ in range(8)]
        obb = [p.take("abf", SW) for _ in range(8)]
        xf = [p.take("af32", SW) for _ in range(8)]
        for k in range(16):
            p.dma("sync", ogb[k][:, :], og_hbm[128 * k:128 * k + 128, cs])
        for i in range(8):
            p.dma("sync", xf[i][:, :], xin[128 * i:128 * i + 128, cs])
        p.barrier()
        for half in range(2):
            for dm in range(4):
                mc = 4 * half + dm
                for kc in range(16):
                    p.mm(p.ps[dm][:, 0:SW], wo[kc][:, 128 * mc:128 * mc + 128],
                         ogb[kc][:, :], start=(kc == 0), stop=(kc == 15))
            c_t = p.inc("tensor", "t0")
            p.wait("scalar", "t0", c_t)
            for dm in range(4):
                mc = 4 * half + dm
                p.act(yv[mc][:, :], p.ps[dm][:, 0:SW], AF.Copy)
            c_s = p.inc("scalar", "s0")
            p.wait("tensor", "s0", c_s)
        p.wait("vector", "s0", p.cnt["s0"])
        emit_ln(p, yv, yb, SW, consts, obb, p.ps[4], p.ps[5])
        c_v = p.inc("vector", "v0")
        p.wait("vector", "v0", c_v)
        for i in range(8):
            p.v("tensor_tensor", xf[i][:, :], xf[i][:, :], obb[i][:, :], ALU.add)
        c_v = p.inc("vector", "v0")
        p.wait("gpsimd", "v0", c_v)
        for i in range(8):
            p.dma("gpsimd", xout[128 * i:128 * i + 128, cs], xf[i][:, :])
        p.barrier()
    p.reset(pref + "p4")
    p.reset(pref + "p4w")


def host_gla_weights(inputs):
    gnwr = np.zeros((128, 4), dtype=np.float32)
    gnw = inputs["L1_attn_g_norm_w"]
    for c in range(4):
        gnwr[:, c] = gnw[128 * c:128 * c + 128]
    bgk = inputs["L1_attn_bgk2"]
    bgk2r = np.ascontiguousarray(bgk.reshape(8, 128).T)
    identb = bf16(np.eye(128, dtype=np.float32))
    maskT = np.zeros((64, 64), dtype=np.float32)
    for j in range(64):
        maskT[j, j:] = 1.0
    return {
        "Wq": np.ascontiguousarray(inputs["L1_attn_Wq"]),
        "Wk": np.ascontiguousarray(inputs["L1_attn_Wk"]),
        "Wv": np.ascontiguousarray(inputs["L1_attn_Wv"]),
        "Wg": np.ascontiguousarray(inputs["L1_attn_Wg"]),
        "Wgk1": np.ascontiguousarray(inputs["L1_attn_Wgk1"]),
        "Wgk2": np.ascontiguousarray(inputs["L1_attn_Wgk2"]),
        "Wo": np.ascontiguousarray(inputs["L1_attn_Wo"]),
        "bgk2r": bgk2r,
        "gnwr": gnwr,
        "ident_bf": identb,
        "maskT": maskT,
    }


def build_gla_prog(debug=False):
    p = Prog("gla")
    p.arena("cst", 128, F32)
    p.arena("cbf", 11264, BF16)
    p.arena("cf32", 1152, F32)
    p.arena("wbf", 33280, BF16)
    p.arena("abf", 18432, BF16)
    p.arena("af32", 19 * 512, F32)
    p.arena("sf32", 1024, F32)
    p.arena("obf", 4 * T, BF16)
    xin = p.inp("xin", (D, T), F32)
    xout = p.out("xout", (D, T), F32)
    W = {
        "Wq": p.inp("Wq", (1024, 1024), F32),
        "Wk": p.inp("Wk", (1024, 1024), F32),
        "Wv": p.inp("Wv", (1024, 2048), F32),
        "Wg": p.inp("Wg", (1024, 2048), F32),
        "Wgk1": p.inp("Wgk1", (1024, 16), F32),
        "Wgk2": p.inp("Wgk2", (16, 1024), F32),
        "Wo": p.inp("Wo", (2048, 1024), F32),
        "bgk2r": p.inp("bgk2r", (128, 8), F32),
        "gnwr": p.inp("gnwr", (128, 4), F32),
        "ident_bf": p.inp("ident_bf", (128, 128), BF16),
        "maskT": p.inp("maskT", (64, 64), F32),
    }
    sec_gla(p, xin, xout, W, debug=debug)
    return p.build()


def sec_mamba(p, xin, xout, W, pref="mmb", debug=False):
    NS = 8
    SW = T // NS
    DI = 2048
    kind = "ExternalOutput" if debug else "Internal"

    xn_hbm = p.dram(pref + "_xn", (D, T), BF16, kind=kind)
    u_hbm = p.dram(pref + "_u", (DI, T), BF16, kind=kind)
    sz_hbm = p.dram(pref + "_sz", (DI, T), BF16, kind=kind)
    yh_hbm = p.dram(pref + "_yh", (DI, T), BF16, kind=kind)
    if debug:
        dt_dbg = p.out(pref + "_dt", (DI, T), F32)
        dbl_dbg = p.out(pref + "_dbl", (128, T), F32)
        d_hda = p.out(pref + "_hda", (128, 512), F32)
        d_hb = p.out(pref + "_hb", (128, 512), F32)
        d_hh = p.out(pref + "_hh", (128, 512), F32)
        d_Br = p.out(pref + "_Br", (128, 512), F32)
        d_Cr = p.out(pref + "_Cr", (128, 512), F32)

    consts = std_consts(p)
    p.constv(1.0)

    cwv = p.take("cf32", 64)
    cbv = p.take("cf32", 16)
    bdt = p.take("cf32", 16)
    dsk = p.take("cf32", 16)
    Atl = p.take("cf32", 256)
    bcsel = p.take("cbf", 4096, rows=32)
    p.dma("sync", cwv[:, :], W["conv_wr"][:, :])
    p.dma("sync", cbv[:, :], W["conv_br"][:, :])
    p.dma("sync", bdt[:, :], W["b_dtr"][:, :])
    p.dma("sync", dsk[:, :], W["D_skipr"][:, :])
    p.dma("sync", Atl[:, :], W["A_logr"][:, :])
    p.dma("gpsimd", bcsel[0:32, :], W["bcsel"][:, :])
    p.barrier()
    p.act(Atl[:, :], Atl[:, :], AF.Exp)
    c_s = p.inc("scalar", "s0")
    p.wait("vector", "s0", c_s)
    p.v("tensor_scalar_mul", Atl[:, :], Atl[:, :], -1.0)
    p.barrier()

    dbl = p.take("dbl", T, rows=96)

    p.mark(pref + "w1")
    winT = [p.take("wbf", 2048) for _ in range(8)]
    wxT = [p.take("wbf", 96) for _ in range(16)]
    for k in range(8):
        p.dma("gpsimd", winT[k][:, :], W["W_inT"][128 * k:128 * k + 128, 0:2048])
    for k in range(16):
        p.dma("gpsimd", wxT[k][:, :], W["W_xprojT"][128 * k:128 * k + 128, :])
    ptail = p.take("af32", 48)
    p.v("memset", ptail[:, :], 0.0)
    p.barrier()
    p.mark(pref + "m1")
    for s in range(NS):
        p.reset(pref + "m1")
        cs = slice(SW * s, SW * s + SW)
        xf = [p.take("af32", SW) for _ in range(8)]
        xb = [p.take("abf", SW) for _ in range(8)]
        for i in range(8):
            p.dma("sync", xf[i][:, :], xin[128 * i:128 * i + 128, cs])
        p.barrier()
        emit_ln(p, xf, xb, SW, consts, xb, p.ps[0], p.ps[1])
        c_v = p.inc("vector", "v0")
        p.wait("tensor", "v0", c_v)
        p.wait("gpsimd", "v0", c_v)
        for i in range(8):
            p.dma("gpsimd", xn_hbm[128 * i:128 * i + 128, cs], xb[i][:, :])
        ub = [p.take("abf", SW) for _ in range(16)]
        xib = [p.take("af32", 520) for _ in range(2)]
        xcb = [p.take("af32", SW) for _ in range(2)]
        msd = [0] * 16
        mvd = [0] * 16
        for mc in range(16):
            b = mc % 2
            pb = p.ps[2 + b]
            xi = xib[b]
            xc = xcb[b]
            if mc >= 2:
                p.wait("tensor", "s0", msd[mc - 2])
            for kc in range(8):
                p.mm(pb[:, 0:SW], winT[kc][:, 128 * mc:128 * mc + 128],
                     xb[kc][:, :], start=(kc == 0), stop=(kc == 7))
            c_t = p.inc("tensor", "t0", drain=False)
            p.wait("scalar", "t0", c_t)
            cb = mc
            if mc >= 2:
                p.wait("scalar", "v0", mvd[mc - 2])
            p.act(xi[:, 3:3 + SW], pb[:, 0:SW], AF.Copy)
            p.act(xi[:, 0:3], ptail[:, 3 * cb:3 * cb + 3], AF.Copy)
            p.act(ptail[:, 3 * cb:3 * cb + 3], xi[:, SW:SW + 3], AF.Copy)
            p.act(xc[:, :], xi[:, 0:SW], AF.Copy, scale=cwv[:, 4 * cb:4 * cb + 1])
            msd[mc] = p.inc("scalar", "s0")
            p.wait("vector", "s0", msd[mc])
            for k in (1, 2, 3):
                p.v("scalar_tensor_tensor", xc[:, :], xi[:, k:k + SW],
                    cwv[:, 4 * cb + k:4 * cb + k + 1], xc[:, :],
                    ALU.mult, ALU.add)
            mvd[mc] = p.inc("vector", "v0")
            p.wait("scalar", "v0", mvd[mc])
            p.act(ub[cb][:, :], xc[:, :], AF.Silu, bias=cbv[:, cb:cb + 1])
            c_s = p.inc("scalar", "s0")
            p.wait("gpsimd", "s0", c_s)
            p.dma("gpsimd", u_hbm[128 * cb:128 * cb + 128, cs], ub[cb][:, :])
        p.wait("tensor", "s0", p.cnt["s0"])
        for kc in range(16):
            p.mm(p.ps[4][0:96, 0:SW], wxT[kc][:, :], ub[kc][:, :],
                 start=(kc == 0), stop=(kc == 15))
        c_t = p.inc("tensor", "t0")
        p.wait("scalar", "t0", c_t)
        p.act(dbl[0:96, cs], p.ps[4][0:96, 0:SW], AF.Copy)
        c_s = p.inc("scalar", "s0")
        p.barrier()
    p.reset(pref + "m1")
    p.reset(pref + "w1")

    wz = [p.take("wbf", 2048) for _ in range(8)]
    for k in range(8):
        p.dma("gpsimd", wz[k][:, :], W["W_inT"][128 * k:128 * k + 128, 2048:4096])
    p.barrier()
    p.mark(pref + "m1b")
    for s in range(NS):
        p.reset(pref + "m1b")
        cs = slice(SW * s, SW * s + SW)
        xb = [p.take("abf", SW) for _ in range(8)]
        ev = p.take("abf", SW)
        for i in range(8):
            p.dma("sync", xb[i][:, :], xn_hbm[128 * i:128 * i + 128, cs])
        p.barrier()
        zsd = [0] * 16
        zdd = [0] * 16
        evb2 = [p.take("abf", SW) for _ in range(2)]
        for mc in range(16):
            pb = p.ps[2 + mc % 2]
            if mc >= 2:
                p.wait("tensor", "s0", zsd[mc - 2])
            for kc in range(8):
                p.mm(pb[:, 0:SW], wz[kc][:, 128 * mc:128 * mc + 128],
                     xb[kc][:, :], start=(kc == 0), stop=(kc == 7))
            c_t = p.inc("tensor", "t0", drain=False)
            p.wait("scalar", "t0", c_t)
            if mc >= 2:
                p.wait("scalar", "dma_gpsimd", zdd[mc - 2])
            p.act(evb2[mc % 2][:, :], pb[:, 0:SW], AF.Silu)
            zsd[mc] = p.inc("scalar", "s0")
            p.wait("gpsimd", "s0", zsd[mc])
            p.dma("gpsimd", sz_hbm[128 * mc:128 * mc + 128, cs], evb2[mc % 2][:, :])
            zdd[mc] = p.cnt["dma_gpsimd"]
        p.barrier()
    p.reset(pref + "m1b")
    p.reset(pref + "w1")
    if debug:
        dbf = p.take("af32", T, rows=96)
        p.act(dbf[0:96, :], dbl[0:96, :], AF.Copy)
        c_s = p.inc("scalar", "s0")
        p.wait("gpsimd", "s0", c_s)
        p.dma("gpsimd", dbl_dbg[0:96, :], dbf[0:96, :])
        p.barrier()
        p.reset(pref + "w1")

    wdtT = p.take("wbf", 2048, rows=64)
    p.dma("gpsimd", wdtT[0:64, :], W["W_dtT"][:, :])
    hcar = p.take("cf32", 256)
    p.v("memset", hcar[:, :], 0.0)
    zer = p.take("cf32", SW)
    p.v("memset", zer[:, :], 0.0)
    p.barrier()
    p.mark(pref + "m2")
    for tc in range(NS):
        p.reset(pref + "m2")
        cs = slice(SW * tc, SW * tc + SW)
        ub = [p.take("abf", SW) for _ in range(16)]
        for cb in range(16):
            p.dma("sync", ub[cb][:, :], u_hbm[128 * cb:128 * cb + 128, cs])
        dt_t = [p.take("abf", SW) for _ in range(16)]
        dtu = [p.take("abf", SW) for _ in range(16)]
        sa = p.take("af32", SW)
        sb2 = p.take("af32", SW)
        p.wait_dma("tensor", "sync")
        sabb = [p.take("af32", SW) for _ in range(2)]
        sbbb = [p.take("af32", SW) for _ in range(2)]
        dsd = [0] * 16
        dvd = [0] * 16
        for cb in range(16):
            b = cb % 2
            pb = p.ps[2 + b]
            if cb >= 2:
                p.wait("tensor", "s0", dsd[cb - 2])
            p.mm(pb[:, 0:SW], wdtT[0:64, 128 * cb:128 * cb + 128],
                 dbl[0:64, cs], start=True, stop=True)
            c_t = p.inc("tensor", "t0", drain=False)
            p.wait("scalar", "t0", c_t)
            if cb >= 2:
                p.wait("scalar", "v0", dvd[cb - 2])
            bcol = bdt[:, cb:cb + 1]
            p.act(sabb[b][:, :], pb[:, 0:SW], AF.Abs, bias=bcol)
            p.act(sabb[b][:, :], sabb[b][:, :], AF.Exp, scale=-1.0)
            p.act(sabb[b][:, :], sabb[b][:, :], AF.Ln, bias=1.0)
            p.act(sbbb[b][:, :], pb[:, 0:SW], AF.Relu, bias=bcol)
            dsd[cb] = p.inc("scalar", "s0")
            p.wait("vector", "s0", dsd[cb])
            p.v("tensor_tensor", dt_t[cb][:, :], sabb[b][:, :], sbbb[b][:, :],
                ALU.add)
            p.v("tensor_tensor", dtu[cb][:, :], dt_t[cb][:, :], ub[cb][:, :],
                ALU.mult)
            dvd[cb] = p.inc("vector", "v0")
            if debug:
                p.wait("gpsimd", "v0", dvd[cb])
                p.dma("gpsimd", dt_dbg[128 * cb:128 * cb + 128, cs],
                      dt_t[cb][:, :])
        p.wait("tensor", "v0", dvd[15])
        bc0 = p.take("abf", SW, rows=32)
        p.act(bc0[0:32, :], dbl[64:96, cs], AF.Copy)
        c_s = p.inc("scalar", "s0")
        p.wait("tensor", "s0", c_s)
        y_t = [p.take("af32", SW) for _ in range(16)]
        for cb in range(16):
            p.v("memset", y_t[cb][:, :], 0.0)
        c_v = p.inc("vector", "v0")
        hdab = [p.take("af32", SW) for _ in range(2)]
        hb = p.take("af32", SW)
        hh = p.take("af32", SW)
        ht2 = hb
        Brep = p.take("abf", SW)
        Crep = p.take("af32", SW)
        for s in range(16):
            p.wait("tensor", "v0", p.cnt["v0"])
            p.mm(p.ps[4][:, 0:SW], bcsel[0:32, 128 * s:128 * s + 128],
                 bc0[0:32, :], start=True, stop=True)
            p.mm(p.ps[5][:, 0:SW], bcsel[0:32, 2048 + 128 * s:2048 + 128 * s + 128],
                 bc0[0:32, :], start=True, stop=True)
            c_t = p.inc("tensor", "t0")
            p.wait("scalar", "t0", c_t)
            p.act(Brep[:, :], p.ps[4][:, 0:SW], AF.Copy)
            p.act(Crep[:, :], p.ps[5][:, 0:SW], AF.Copy)
            c_s = p.inc("scalar", "s0")
            p.wait("vector", "s0", c_s)
            scv = [0] * 16
            scs = [0] * 16
            for cb in range(16):
                bq = cb % 2
                if cb >= 2:
                    p.wait("scalar", "v0", scv[cb - 2])
                p.act(hdab[bq][:, :], dt_t[cb][:, :], AF.Exp,
                      scale=Atl[:, 16 * cb + s:16 * cb + s + 1])
                scs[cb] = p.inc("scalar", "s0")
                p.wait("vector", "s0", scs[cb])
                p.v("tensor_tensor", hb[:, :], dtu[cb][:, :], Brep[:, :], ALU.mult)
                p.v("tensor_tensor_scan", hh[:, :], hdab[bq][:, :], hb[:, :],
                    hcar[:, 16 * s + cb:16 * s + cb + 1], ALU.mult, ALU.add)
                p.v("tensor_tensor", ht2[:, :], hh[:, :], Crep[:, :], ALU.mult)
                p.v("tensor_tensor", y_t[cb][:, :], y_t[cb][:, :], ht2[:, :],
                    ALU.add)
                p.v("tensor_copy", hcar[:, 16 * s + cb:16 * s + cb + 1],
                    hh[:, 511:512])
                c_v = scv[cb] = p.inc("vector", "v0")
                if debug and tc == 0 and s == 0 and cb == 0:
                    p.wait("gpsimd", "v0", c_v)
                    p.dma("gpsimd", d_hda[:, :], hda[:, :])
                    p.dma("gpsimd", d_hb[:, :], hb[:, :])
                    p.dma("gpsimd", d_hh[:, :], hh[:, :])
                    p.dma("gpsimd", d_Br[:, :], Brep[:, :])
                    p.dma("gpsimd", d_Cr[:, :], Crep[:, :])
                    p.wait("scalar", "dma_gpsimd", p.cnt["dma_gpsimd"])
                    p.wait("vector", "dma_gpsimd", p.cnt["dma_gpsimd"])
        if debug and tc == 0:
            p.wait("gpsimd", "v0", p.cnt["v0"])
            p.dma("gpsimd", d_hb[:, :], y_t[0][:, :])
            p.wait("vector", "dma_gpsimd", p.cnt["dma_gpsimd"])
        szb = p.take("abf", SW)
        yhb = p.take("abf", SW)
        for cb in range(16):
            if cb > 0:
                p.wait("sync", "v0", p.cnt["v0"])
            p.dma("sync", szb[:, :], sz_hbm[128 * cb:128 * cb + 128, cs])
            p.act(sa[:, :], ub[cb][:, :], AF.Copy, scale=dsk[:, cb:cb + 1])
            c_s = p.inc("scalar", "s0")
            p.wait("vector", "s0", c_s)
            p.wait_dma("vector", "sync")
            p.v("tensor_tensor", y_t[cb][:, :], y_t[cb][:, :], sa[:, :], ALU.add)
            p.v("tensor_tensor", yhb[:, :], y_t[cb][:, :], szb[:, :], ALU.mult)
            c_v = p.inc("vector", "v0")
            p.wait("gpsimd", "v0", c_v)
            p.dma("gpsimd", yh_hbm[128 * cb:128 * cb + 128, cs], yhb[:, :])
            p.wait("scalar", "dma_gpsimd", p.cnt["dma_gpsimd"])
            p.wait("vector", "dma_gpsimd", p.cnt["dma_gpsimd"])
        p.barrier()
    p.reset(pref + "m2")

    woT = [p.take("wbf", 1024) for _ in range(16)]
    for k in range(16):
        p.dma("gpsimd", woT[k][:, :], W["W_outT"][128 * k:128 * k + 128, :])
    p.barrier()
    p.mark(pref + "m3")
    for s in range(NS):
        p.reset(pref + "m3")
        cs = slice(SW * s, SW * s + SW)
        yhb = [p.take("abf", SW) for _ in range(16)]
        yv = [p.take("af32", SW) for _ in range(8)]
        yb = [p.take("abf", SW) for _ in range(8)]
        obb = [p.take("abf", SW) for _ in range(8)]
        xf = [p.take("af32", SW) for _ in range(8)]
        for k in range(16):
            p.dma("sync", yhb[k][:, :], yh_hbm[128 * k:128 * k + 128, cs])
        for i in range(8):
            p.dma("sync", xf[i][:, :], xin[128 * i:128 * i + 128, cs])
        p.barrier()
        for half in range(2):
            for dm in range(4):
                mc = 4 * half + dm
                for kc in range(16):
                    p.mm(p.ps[dm][:, 0:SW], woT[kc][:, 128 * mc:128 * mc + 128],
                         yhb[kc][:, :], start=(kc == 0), stop=(kc == 15))
            c_t = p.inc("tensor", "t0")
            p.wait("scalar", "t0", c_t)
            for dm in range(4):
                mc = 4 * half + dm
                p.act(yv[mc][:, :], p.ps[dm][:, 0:SW], AF.Copy)
            c_s = p.inc("scalar", "s0")
            p.wait("tensor", "s0", c_s)
        p.wait("vector", "s0", p.cnt["s0"])
        emit_ln(p, yv, yb, SW, consts, obb, p.ps[4], p.ps[5])
        c_v = p.inc("vector", "v0")
        p.wait("vector", "v0", c_v)
        for i in range(8):
            p.v("tensor_tensor", xf[i][:, :], xf[i][:, :], obb[i][:, :], ALU.add)
        c_v = p.inc("vector", "v0")
        p.wait("gpsimd", "v0", c_v)
        for i in range(8):
            p.dma("gpsimd", xout[128 * i:128 * i + 128, cs], xf[i][:, :])
        p.barrier()
    p.reset(pref + "m3")


def host_mamba_weights(inputs):
    conv_w = inputs["L0_attn_conv_w"]
    conv_wr = np.ascontiguousarray(
        conv_w.reshape(16, 128, 4).transpose(1, 0, 2).reshape(128, 64))
    def cols(v):
        return np.ascontiguousarray(v.reshape(16, 128).T)
    A_logr = np.ascontiguousarray(
        inputs["L0_attn_A_log"].reshape(16, 128, 16).transpose(1, 0, 2)
        .reshape(128, 256))
    bcsel = np.zeros((32, 4096), dtype=np.float32)
    for s in range(16):
        bcsel[s, 128 * s:128 * s + 128] = 1.0
        bcsel[16 + s, 2048 + 128 * s:2048 + 128 * s + 128] = 1.0
    return {
        "W_inT": np.ascontiguousarray(inputs["L0_attn_W_in"].T),
        "W_xprojT": np.ascontiguousarray(inputs["L0_attn_W_xproj"].T),
        "W_dtT": np.ascontiguousarray(inputs["L0_attn_W_dt"].T),
        "W_outT": np.ascontiguousarray(inputs["L0_attn_W_out"].T),
        "conv_wr": conv_wr,
        "conv_br": cols(inputs["L0_attn_conv_b"]),
        "b_dtr": cols(inputs["L0_attn_b_dt"]),
        "D_skipr": cols(inputs["L0_attn_D_skip"]),
        "A_logr": A_logr,
        "bcsel": bcsel,
    }


def build_mamba_prog(debug=False):
    p = Prog("mmb")
    p.arena("cst", 128, F32)
    p.arena("cbf", 10240, BF16)
    p.arena("cf32", 2304, F32)
    p.arena("dbl", T, BF16)
    p.arena("wbf", 18432, BF16)
    p.arena("abf", 28672, BF16)
    p.arena("af32", 16384, F32)
    xin = p.inp("xin", (D, T), F32)
    xout = p.out("xout", (D, T), F32)
    W = {
        "W_inT": p.inp("W_inT", (1024, 4096), F32),
        "W_xprojT": p.inp("W_xprojT", (2048, 96), F32),
        "W_dtT": p.inp("W_dtT", (64, 2048), F32),
        "W_outT": p.inp("W_outT", (2048, 1024), F32),
        "conv_wr": p.inp("conv_wr", (128, 64), F32),
        "conv_br": p.inp("conv_br", (128, 16), F32),
        "b_dtr": p.inp("b_dtr", (128, 16), F32),
        "D_skipr": p.inp("D_skipr", (128, 16), F32),
        "A_logr": p.inp("A_logr", (128, 256), F32),
        "bcsel": p.inp("bcsel", (32, 4096), F32),
    }
    sec_mamba(p, xin, xout, W, debug=debug)
    return p.build()


def sec_l0ffn(p, xin, xout, W, pref="trf", debug=False, depth=6,
              do_attn=True, do_ffn=True):
    NS = 8
    SW = T // NS
    DEPTH = depth
    DIH = 2816
    kind = "ExternalOutput" if debug else "Internal"

    ra = p.dram(pref + "_ra", (D, T), F32, kind=kind)
    rb = p.dram(pref + "_rb", (D, T), F32, kind=kind)
    xnf_hbm = p.dram(pref + "_xnf", (D, T), BF16, kind=kind)
    hf_hbm = p.dram(pref + "_hf", (DIH, T), BF16, kind=kind)

    consts = std_consts(p)
    identb = p.take("cbf", 128)
    p.dma("sync", identb[:, :], W["ident_bf"][:, :])
    maskadd = p.take("cf32", 384)
    p.dma("sync", maskadd[:, :], W["maskadd"][:, :])
    maskmulT = p.take("cbf", 384)
    p.dma("gpsimd", maskmulT[:, :], W["maskmulT"][:, :])
    ones16 = p.take("cbf", 128)
    p.dma("gpsimd", ones16[:, :], W["ones16"][:, :])
    sel16 = p.take("cbf", 1024, rows=16)
    p.dma("gpsimd", sel16[0:16, :], W["sel16"][:, :])
    hsel = p.take("cbf", 2048, rows=16)
    p.dma("gpsimd", hsel[0:16, :], W["hsel"][:, :])
    qsc = p.take("cf32", 24)
    p.dma("sync", qsc[:, :], W["qk_scales"][:, :])
    lgb = p.take("cf32", 96)
    p.dma("sync", lgb[:, :], W["ln_gb"][:, :])
    b2c = p.take("cf32", 48)
    p.dma("sync", b2c[:, :], W["b2r"][:, :])
    b1c = p.take("cf32", 264)
    p.dma("sync", b1c[:, :], W["b1r"][:, :])
    p.barrier()

    p.mark(pref + "i0")
    for s in range(NS):
        p.reset(pref + "i0")
        cs = slice(SW * s, SW * s + SW)
        xf = [p.take("af32", SW) for _ in range(8)]
        xb = [p.take("abf", SW) for _ in range(8)]
        of = [p.take("af32", SW) for _ in range(8)]
        for i in range(8):
            p.dma("sync", xf[i][:, :], xin[128 * i:128 * i + 128, cs])
        p.barrier()
        emit_ln(p, xf, xb, SW, consts, xb, p.ps[0], p.ps[1],
                out_f32_tiles=of)
        c_v = p.inc("vector", "v0")
        p.wait("gpsimd", "v0", c_v)
        for i in range(8):
            p.dma("gpsimd", ra[128 * i:128 * i + 128, cs], of[i][:, :])
        p.barrier()
    p.reset(pref + "i0")

    cur, nxt = ra, rb
    for li in range(DEPTH):
        if do_attn:
            attn_layer(p, li, cur, nxt, W, consts, identb, maskadd, maskmulT,
                       ones16, sel16, hsel, qsc, pref)
            cur, nxt = nxt, cur
        if do_ffn:
            ffn_layer(p, li, cur, nxt, W, consts, lgb, b1c, b2c, xnf_hbm, hf_hbm,
                      pref)
            cur, nxt = nxt, cur

    p.mark(pref + "f0")
    for s in range(NS):
        p.reset(pref + "f0")
        cs = slice(SW * s, SW * s + SW)
        yv = [p.take("af32", SW) for _ in range(8)]
        yb = [p.take("abf", SW) for _ in range(8)]
        ob = [p.take("abf", SW) for _ in range(8)]
        xf = [p.take("af32", SW) for _ in range(8)]
        for i in range(8):
            p.dma("sync", xf[i][:, :], xin[128 * i:128 * i + 128, cs])
            p.dma("sync", yv[i][:, :], cur[128 * i:128 * i + 128, cs])
        p.barrier()
        emit_ln(p, yv, yb, SW, consts, ob, p.ps[0], p.ps[1])
        c_v = p.inc("vector", "v0")
        p.wait("vector", "v0", c_v)
        for i in range(8):
            p.v("tensor_tensor", xf[i][:, :], xf[i][:, :], ob[i][:, :], ALU.add)
        c_v = p.inc("vector", "v0")
        p.wait("gpsimd", "v0", c_v)
        for i in range(8):
            p.dma("gpsimd", xout[128 * i:128 * i + 128, cs], xf[i][:, :])
        p.barrier()
    p.reset(pref + "f0")


def load_shifted(p, tiles, src, s, SW):
    cs = slice(SW * s, SW * s + SW)
    for i in range(4):
        p.dma("sync", tiles[i][:, :], src[128 * i:128 * i + 128, cs])
    for i in range(4, 8):
        r = slice(128 * i, 128 * i + 128)
        if s == 0:
            p.v("memset", tiles[i][:, 0:1], 0.0)
            p.dma("sync", tiles[i][:, 1:SW], src[r, 0:SW - 1])
        else:
            p.dma("sync", tiles[i][:, :], src[r, SW * s - 1:SW * s + SW - 1])


def attn_layer(p, li, cur, nxt, W, consts, identb, maskadd, maskmulT,
               ones16, sel16, hsel, qsc, pref):
    NS = 8
    SW = T // NS
    p.mark(pref + "aw")
    wqkv = [p.take("wbf", 3072) for _ in range(8)]
    wo = [p.take("wbf", 1024) for _ in range(8)]
    for k in range(8):
        p.dma("gpsimd", wqkv[k][:, :], W["Wqkv"][li, 128 * k:128 * k + 128, :])
        p.dma("gpsimd", wo[k][:, :], W["Wo"][li, 128 * k:128 * k + 128, :])
    qsp = qsc[:, 4 * li:4 * li + 1]
    qsrh = qsc[:, 4 * li + 1:4 * li + 2]
    ksp = qsc[:, 4 * li + 2:4 * li + 3]
    ksrh = qsc[:, 4 * li + 3:4 * li + 4]
    ktail = [p.take("abf", 256) for _ in range(8)]
    vtail = [p.take("abf", 1024) for _ in range(2)]
    p.barrier()
    p.mark(pref + "as")
    for s in range(NS):
        p.reset(pref + "as")
        cs = slice(SW * s, SW * s + SW)
        ws = [p.take("af32", SW) for _ in range(8)]
        wres = [p.take("af32", SW) for _ in range(8)]
        xb = [p.take("abf", SW) for _ in range(8)]
        load_shifted(p, ws, cur, s, SW)
        for i in range(8):
            p.dma("sync", wres[i][:, :], cur[128 * i:128 * i + 128, cs])
        tb_f = p.take("af32", SW)
        Aq = p.take("abf", SW)
        Bq = p.take("abf", SW)
        Ak = p.take("abf", SW)
        Bk = p.take("abf", SW)
        p.barrier()
        for i in range(8):
            p.act(xb[i][:, :], ws[i][:, :], AF.Copy)
        c_s = p.inc("scalar", "s0")
        for (tb, hb, col, scl) in ((Aq, "cs2p", qsp, 8.0), (Bq, "snq", qsrh, 8.0),
                                   (Ak, "ci2p", ksp, 1.0), (Bk, "snk", ksrh, 1.0)):
            p.dma("sync", tb_f[:, :], W[hb][:, cs])
            p.wait_dma("scalar", "sync")
            p.act(tb[:, :], tb_f[:, :], AF.Copy, scale=col)
            c_s = p.inc("scalar", "s0")
            if scl != 1.0:
                p.wait("vector", "s0", c_s)
                p.v("tensor_scalar_mul", tb[:, :], tb[:, :], scl)
                c_v = p.inc("vector", "v0")
                p.wait("sync", "v0", c_v)
            else:
                p.wait("sync", "s0", c_s)
        p.wait("tensor", "s0", p.cnt["s0"])
        p.wait("gpsimd", "s0", p.cnt["s0"])
        qt = [p.take("abf", SW) for _ in range(8)]
        kt = [p.take("abf", SW) for _ in range(8)]
        vt = [p.take("abf", 1024) for _ in range(4)]
        q2t = [p.take("abf", SW) for _ in range(3)]
        sdone = [0] * 16

        def sumsq_mm(j):
            p.mm(p.ps[4 + j // 8][0:16, 0:SW],
                 ones16[:, 16 * (j % 8):16 * (j % 8) + 16],
                 q2t[j % 3][:, :], start=(j % 8 == 0), stop=(j % 8 == 7))
        for i in range(16):
            part, mc = i // 8, i % 8
            tiles = qt if part == 0 else kt
            pb = p.ps[2 + i % 2]
            if i >= 2:
                p.wait("tensor", "s0", sdone[i - 2])
                sumsq_mm(i - 2)
            for kc in range(8):
                p.mm(pb[:, 0:SW],
                     wqkv[kc][:, 1024 * part + 128 * mc:1024 * part + 128 * mc + 128],
                     xb[kc][:, :], start=(kc == 0), stop=(kc == 7))
            td = p.inc("tensor", "t0", drain=False)
            p.wait("scalar", "t0", td)
            p.act(tiles[mc][:, :], pb[:, 0:SW], AF.Copy)
            p.act(q2t[i % 3][:, :], pb[:, 0:SW], AF.Square)
            sdone[i] = p.inc("scalar", "s0")
        nqq = p.take("af32", 512, rows=16)
        nqk = p.take("af32", 512, rows=16)
        nqbq = p.take("abf", 512, rows=16)
        nqbk = p.take("abf", 512, rows=16)
        for j in (14, 15):
            p.wait("tensor", "s0", sdone[j])
            sumsq_mm(j)
        td = p.inc("tensor", "t0", drain=False)
        p.wait("scalar", "t0", td)
        p.act(nqq[0:16, :], p.ps[4][0:16, 0:SW], AF.Sqrt)
        p.act(nqk[0:16, :], p.ps[5][0:16, 0:SW], AF.Sqrt)
        c_s = p.inc("scalar", "s0")
        p.wait("vector", "s0", c_s)
        for nq in (nqq, nqk):
            p.v("tensor_scalar_max", nq[0:16, :], nq[0:16, :], 1e-12)
            p.v("reciprocal", nq[0:16, :], nq[0:16, :])
        c_v = p.inc("vector", "v0")
        p.wait("scalar", "v0", c_v)
        p.act(nqbq[0:16, :], nqq[0:16, :], AF.Copy)
        p.act(nqbk[0:16, :], nqk[0:16, :], AF.Copy)
        c_s = p.inc("scalar", "s0")
        p.wait("tensor", "s0", c_s)
        tmpb = [p.take("abf", SW) for _ in range(2)]
        rhb = [p.take("abf", SW) for _ in range(2)]
        t1b = [p.take("abf", SW) for _ in range(2)]
        t2b = [p.take("abf", SW) for _ in range(2)]
        vmul = [0] * 16
        gdone = [0] * 16
        vfin = [0] * 16
        for i in range(16):
            part, mc = i // 8, i % 8
            tiles = qt if part == 0 else kt
            nqb = nqbq if part == 0 else nqbk
            Atb, Btb = (Aq, Bq) if part == 0 else (Ak, Bk)
            b = i % 2
            pb = p.ps[4 + b]
            if i >= 2:
                p.wait("tensor", "v0", vmul[i - 2])
            p.mm(pb[:, 0:SW], sel16[0:16, 128 * mc:128 * mc + 128],
                 nqb[0:16, :], start=True, stop=True)
            td = p.inc("tensor", "t0", drain=False)
            p.wait("vector", "t0", td)
            if i >= 2:
                p.wait("vector", "g0", gdone[i - 2])
                p.wait("vector", "v0", vfin[i - 2])
            p.v("tensor_tensor", tmpb[b][:, :], tiles[mc][:, :], pb[:, 0:SW],
                ALU.mult)
            vmul[i] = p.inc("vector", "v0")
            p.wait("gpsimd", "v0", vmul[i])
            p.v("tensor_copy", rhb[b][0:32, :], tmpb[b][32:64, :], eng="gpsimd")
            p.v("tensor_copy", rhb[b][32:64, :], tmpb[b][0:32, :], eng="gpsimd")
            p.v("tensor_copy", rhb[b][64:96, :], tmpb[b][96:128, :], eng="gpsimd")
            p.v("tensor_copy", rhb[b][96:128, :], tmpb[b][64:96, :], eng="gpsimd")
            gdone[i] = p.inc("gpsimd", "g0")
            p.wait("vector", "g0", gdone[i])
            p.v("tensor_tensor", t1b[b][:, :], tmpb[b][:, :], Atb[:, :], ALU.mult)
            p.v("tensor_tensor", t2b[b][:, :], rhb[b][:, :], Btb[:, :], ALU.mult)
            p.v("tensor_tensor", tiles[mc][:, :], t1b[b][:, :], t2b[b][:, :],
                ALU.add)
            vfin[i] = p.inc("vector", "v0")
        vsd = [0] * 8
        for i in range(8):
            tc, nv = i // 2, i % 2
            pb = p.ps[i % 2]
            if i >= 2:
                p.wait("tensor", "s0", vsd[i - 2])
            for kc in range(8):
                p.mm(pb[:, 0:SW], xb[kc][:, 128 * tc:128 * tc + 128],
                     wqkv[kc][:, 2048 + 512 * nv:2048 + 512 * nv + 512],
                     start=(kc == 0), stop=(kc == 7))
            td = p.inc("tensor", "t0", drain=False)
            p.wait("scalar", "t0", td)
            p.act(vt[tc][:, 512 * nv:512 * nv + 512], pb[:, 0:SW], AF.Copy)
            vsd[i] = p.inc("scalar", "s0")
        p.wait("tensor", "s0", vsd[7])
        p.wait("vector", "s0", vsd[7])
        p.wait("vector", "v0", vfin[15])
        p.wait("scalar", "v0", vfin[15])
        p.wait("tensor", "v0", vfin[15])
        ot = [p.take("abf", SW) for _ in range(8)]
        mcols = p.take("af32", 16)
        zcols = p.take("af32", 16)
        mbb = p.take("abf", 16)
        mbf = p.take("af32", 16)
        ccols = p.take("af32", 16)
        ccb = p.take("abf", 16)
        mrows = p.take("abf", 128, rows=16)
        crows = p.take("abf", 128, rows=16)
        Smb = [p.take("af32", 384) for _ in range(2)]
        Pbb = [p.take("abf", 384) for _ in range(2)]
        PTb = [p.take("abf", 384) for _ in range(2)]
        crepb = [p.take("abf", 128, rows=64) for _ in range(2)]
        for qb in range(4):
            qbg = 4 * s + qb
            nk = min(384, 128 * (qbg + 1))
            koff = 128 * (qbg + 1) - nk
            mask_c0 = 384 - nk

            def ksrc(h, kx, left):
                base = 64 * (h % 2)
                if kx < 0:
                    n = min(-kx, left)
                    return (ktail[h // 2][base:base + 64, 256 + kx:256 + kx + n], n)
                n = min(512 - kx % 512, left)
                return (kt[h // 2][base:base + 64, kx % 512:kx % 512 + n], n)
            p1v = [0] * 16
            p1s = [0] * 16
            for h in range(16):
                base = 64 * (h % 2)
                b = h % 2
                lq = qt[h // 2][base:base + 64, 128 * qb:128 * qb + 128]
                if h >= 2:
                    p.wait("tensor", "v0", p1v[h - 2])
                col = 0
                kx = koff - SW * s
                left = nk
                while left > 0:
                    rhs, n = ksrc(h, kx, left)
                    p.mm(p.ps[5 + b][:, col:col + n], lq, rhs,
                         start=True, stop=True)
                    col += n
                    kx += n
                    left -= n
                td = p.inc("tensor", "t0", drain=False)
                p.wait("vector", "t0", td)
                if h >= 2:
                    p.wait("vector", "s1", p1s[h - 2])
                p.v("scalar_tensor_tensor", Smb[b][:, 0:nk],
                    p.ps[5 + b][:, 0:nk], 1.0,
                    maskadd[:, mask_c0:mask_c0 + nk], ALU.mult, ALU.add)
                p.v("reduce_max", mcols[:, h:h + 1], Smb[b][:, 0:nk], axis=AX.X,
                    negate=True)
                p1v[h] = p.inc("vector", "v0")
                p.wait("scalar", "v0", p1v[h])
                p.act(Pbb[b][:, 0:nk], Smb[b][:, 0:nk], AF.Exp,
                      bias=mcols[:, h:h + 1], accum_out=zcols[:, h:h + 1])
                p1s[h] = p.inc("scalar", "s1")
            p.wait("scalar", "s1", p1s[15])
            p.act(mbb[:, :], mcols[:, :], AF.Copy)
            p.act(mbf[:, :], mbb[:, :], AF.Copy)
            c_s = p.inc("scalar", "s0")
            p.wait("vector", "s0", c_s)
            p.v("tensor_tensor", ccols[:, :], mcols[:, :], mbf[:, :], ALU.subtract)
            p.v("reciprocal", zcols[:, :], zcols[:, :])
            c_v = p.inc("vector", "v0")
            p.wait("scalar", "v0", c_v)
            p.act(ccols[:, :], ccols[:, :], AF.Exp)
            c_s = p.inc("scalar", "s0")
            p.wait("vector", "s0", c_s)
            p.v("tensor_tensor", ccols[:, :], ccols[:, :], zcols[:, :], ALU.mult)
            c_v = p.inc("vector", "v0")
            p.wait("scalar", "v0", c_v)
            p.act(ccb[:, :], ccols[:, :], AF.Copy)
            c_s = p.inc("scalar", "s0")
            p.wait("tensor", "s0", c_s)
            p.trans(p.psT[0:16, 0:128], mbb[:, :], identb[:, :])
            p.trans(p.psT[0:16, 128:256], ccb[:, :], identb[:, :])
            td = p.inc("tensor", "t0", drain=False)
            p.wait("scalar", "t0", td)
            p.act(mrows[0:16, :], p.psT[0:16, 0:128], AF.Copy)
            p.act(crows[0:16, :], p.psT[0:16, 128:256], AF.Copy)
            c_s = p.inc("scalar", "s0")
            p.wait("tensor", "s0", c_s)
            p2e = [0] * 16
            p2m = [0] * 16
            p2f = [0] * 16
            p2t = [0] * 16
            for h in range(16):
                base = 64 * (h % 2)
                b = h % 2
                rq = qt[h // 2][base:base + 64, 128 * qb:128 * qb + 128]
                if h >= 2:
                    p.wait("tensor", "s1", p2e[h - 2])
                col = 0
                kx = koff - SW * s
                left = nk
                while left > 0:
                    lhs, n = ksrc(h, kx, left)
                    off = 0
                    while off < n:
                        m = min(128, n - off)
                        p.mm(p.ps[5 + b][0:m, col + off:col + off + 128],
                             lhs[:, off:off + m], rq, start=True, stop=False)
                        p.mm(p.ps[5 + b][0:m, col + off:col + off + 128],
                             hsel[0:16, 128 * h:128 * h + m],
                             mrows[0:16, :], start=False, stop=True)
                        off += 128
                    col += n
                    kx += n
                    left -= n
                td = p.inc("tensor", "t0", drain=False)
                p.wait("scalar", "t0", td)
                if h >= 2:
                    p.wait("scalar", "t0", p2t[h - 2])
                p.act(PTb[b][:, 0:nk], p.ps[5 + b][:, 0:nk], AF.Exp)
                p2e[h] = p.inc("scalar", "s1")
                p.wait("vector", "s1", p2e[h])
                p.v("tensor_tensor", PTb[b][:, 0:nk], PTb[b][:, 0:nk],
                    maskmulT[:, mask_c0:mask_c0 + nk], ALU.mult)
                p2m[h] = p.inc("vector", "v0")
                p.wait("tensor", "v0", p2m[h])
                if h >= 2:
                    p.wait("tensor", "v0", p2f[h - 2])
                p.mm(p.ps[0 + b][0:64, 0:128], hsel[0:16, 128 * h:128 * h + 64],
                     crows[0:16, :], start=True, stop=True)
                col = 0
                kx = koff - SW * s
                left = nk
                nmm = 0
                nblk = nk // 128
                while left > 0:
                    n = min(128, left)
                    if kx < 0:
                        vtile = vtail[(256 + kx) // 128]
                    else:
                        vtile = vt[kx // 128]
                    p.mm(p.ps[2 + b][0:64, 0:128], vtile[:, 64 * h:64 * h + 64],
                         PTb[b][0:128, col:col + 128],
                         start=(nmm == 0), stop=(nmm == nblk - 1))
                    nmm += 1
                    col += n
                    kx += n
                    left -= n
                p2t[h] = p.inc("tensor", "t0", drain=False)
                p.wait("scalar", "t0", p2t[h])
                p.act(crepb[b][0:64, :], p.ps[0 + b][0:64, 0:128], AF.Copy)
                cse = p.inc("scalar", "s0")
                p.wait("vector", "s0", cse)
                p.v("tensor_tensor", ot[h // 2][base:base + 64,
                                                128 * qb:128 * qb + 128],
                    p.ps[2 + b][0:64, 0:128], crepb[b][0:64, :], ALU.mult)
                p2f[h] = p.inc("vector", "v0")
            p.wait("tensor", "v0", p2f[15])
            p.wait("scalar", "v0", p2f[15])
        for half in range(2):
            for dm in range(4):
                mc = 4 * half + dm
                for kc in range(8):
                    p.mm(p.ps[dm][:, 0:SW], wo[kc][:, 128 * mc:128 * mc + 128],
                         ot[kc][:, :], start=(kc == 0), stop=(kc == 7))
            td = p.inc("tensor", "t0", drain=False)
            p.wait("vector", "t0", td)
            for dm in range(4):
                mc = 4 * half + dm
                p.v("tensor_tensor", wres[mc][:, :], wres[mc][:, :],
                    p.ps[dm][:, 0:SW], ALU.add)
            c_v = p.inc("vector", "v0")
            p.wait("tensor", "v0", c_v)
        c_v = p.inc("vector", "v0")
        p.wait("gpsimd", "v0", c_v)
        for i in range(8):
            p.dma("gpsimd", nxt[128 * i:128 * i + 128, cs], wres[i][:, :])
        p.wait("scalar", "v0", c_v)
        for i in range(8):
            p.act(ktail[i][:, :], kt[i][:, SW - 256:SW], AF.Copy)
        for j in range(2):
            p.act(vtail[j][:, :], vt[2 + j][:, :], AF.Copy)
        c_s = p.inc("scalar", "s0")
        p.barrier()
    p.reset(pref + "as")
    p.reset(pref + "aw")


def ffn_layer(p, li, cur, nxt, W, consts, lgb, b1c, b2c, xnf_hbm, hf_hbm, pref):
    NS = 8
    SW = T // NS
    DIH = 2816
    gcol = [lgb[:, 16 * li + i:16 * li + i + 1] for i in range(8)]
    bcol = [lgb[:, 16 * li + 8 + i:16 * li + 8 + i + 1] for i in range(8)]
    for halfw in range(2):
        mc0, mc1 = (0, 11) if halfw == 0 else (11, 22)
        p.mark(pref + "fw")
        w1 = [p.take("wbf", 256 * (mc1 - mc0)) for _ in range(8)]
        for k in range(8):
            p.dma("gpsimd", w1[k][:, :],
                  W["W1p"][li, 128 * k:128 * k + 128, 256 * mc0:256 * mc1])
        p.barrier()
        p.mark(pref + "f1")
        for s in range(NS):
            p.reset(pref + "f1")
            cs = slice(SW * s, SW * s + SW)
            xb = [p.take("abf", SW) for _ in range(8)]
            if halfw == 0:
                ws = [p.take("af32", SW) for _ in range(8)]
                load_shifted(p, ws, cur, s, SW)
                p.barrier()
                emit_ln(p, ws, xb, SW, consts, xb, p.ps[0], p.ps[1],
                        gcol=gcol, bcol=bcol)
                c_v = p.inc("vector", "v0")
                p.wait("gpsimd", "v0", c_v)
                for i in range(8):
                    p.dma("gpsimd", xnf_hbm[128 * i:128 * i + 128, cs],
                          xb[i][:, :])
                p.wait("tensor", "v0", c_v)
            else:
                for i in range(8):
                    p.dma("sync", xb[i][:, :], xnf_hbm[128 * i:128 * i + 128, cs])
                p.barrier()
            ggb = [p.take("abf", SW) for _ in range(2)]
            hhb = [p.take("abf", SW) for _ in range(2)]
            nmc = mc1 - mc0
            sgd = [0] * nmc
            vhd = [0] * nmc
            dhd = [0] * nmc
            for j in range(nmc):
                mc = mc0 + j
                wof = 256 * j
                b = j % 2
                if j >= 2:
                    p.wait("tensor", "s0", sgd[j - 2])
                for kc in range(8):
                    p.mm(p.ps[2 + 2 * b][:, 0:SW], w1[kc][:, wof + 128:wof + 256],
                         xb[kc][:, :], start=(kc == 0), stop=(kc == 7))
                c_t = p.inc("tensor", "t0", drain=False)
                if j >= 2:
                    p.wait("tensor", "v0", vhd[j - 2])
                for kc in range(8):
                    p.mm(p.ps[3 + 2 * b][:, 0:SW], w1[kc][:, wof:wof + 128],
                         xb[kc][:, :], start=(kc == 0), stop=(kc == 7))
                c_t2 = p.inc("tensor", "t0", drain=False)
                p.wait("scalar", "t0", c_t)
                if j >= 2:
                    p.wait("scalar", "v0", vhd[j - 2])
                p.act(ggb[b][:, :], p.ps[2 + 2 * b][:, 0:SW], AF.Gelu,
                      bias=b1c[:, 44 * li + 2 * mc + 1:44 * li + 2 * mc + 2])
                sgd[j] = p.inc("scalar", "s0")
                p.wait("vector", "t0", c_t2)
                p.wait("vector", "s0", sgd[j])
                if j >= 2:
                    p.wait("vector", "dma_gpsimd", dhd[j - 2])
                p.v("scalar_tensor_tensor", hhb[b][:, :], p.ps[3 + 2 * b][:, 0:SW],
                    b1c[:, 44 * li + 2 * mc:44 * li + 2 * mc + 1], ggb[b][:, :],
                    ALU.add, ALU.mult)
                vhd[j] = p.inc("vector", "v0")
                p.wait("gpsimd", "v0", vhd[j])
                p.dma("gpsimd", hf_hbm[128 * mc:128 * mc + 128, cs], hhb[b][:, :])
                dhd[j] = p.cnt["dma_gpsimd"]
            p.barrier()
        p.reset(pref + "f1")
        p.reset(pref + "fw")
    p.mark(pref + "fw2")
    w2 = [p.take("wbf", 1024) for _ in range(22)]
    for k in range(22):
        p.dma("gpsimd", w2[k][:, :], W["W2p"][li, 128 * k:128 * k + 128, :])
    p.barrier()
    p.mark(pref + "f2")
    for s in range(NS):
        p.reset(pref + "f2")
        cs = slice(SW * s, SW * s + SW)
        hb = [p.take("abf", SW) for _ in range(22)]
        wres = [p.take("af32", SW) for _ in range(8)]
        for k in range(22):
            p.dma("sync", hb[k][:, :], hf_hbm[128 * k:128 * k + 128, cs])
        for i in range(8):
            p.dma("sync", wres[i][:, :], cur[128 * i:128 * i + 128, cs])
        p.barrier()
        for half in range(2):
            for dm in range(4):
                mc = 4 * half + dm
                for kc in range(22):
                    p.mm(p.ps[dm][:, 0:SW], w2[kc][:, 128 * mc:128 * mc + 128],
                         hb[kc][:, :], start=(kc == 0), stop=(kc == 21))
            c_t = p.inc("tensor", "t0")
            p.wait("vector", "t0", c_t)
            for dm in range(4):
                mc = 4 * half + dm
                p.v("scalar_tensor_tensor", wres[mc][:, :], p.ps[dm][:, 0:SW],
                    b2c[:, 8 * li + mc:8 * li + mc + 1], wres[mc][:, :],
                    ALU.add, ALU.add)
            c_v = p.inc("vector", "v0")
            p.wait("tensor", "v0", c_v)
        c_v = p.inc("vector", "v0")
        p.wait("gpsimd", "v0", c_v)
        for i in range(8):
            p.dma("gpsimd", nxt[128 * i:128 * i + 128, cs], wres[i][:, :])
        p.barrier()
    p.reset(pref + "f2")
    p.reset(pref + "fw2")


def host_l0ffn_weights(inputs):
    DEPTH, DIM, DH, W_ = 6, 1024, 64, 256
    INNER = 1024
    Wqkv_p = np.ascontiguousarray(np.array(inputs["L0_ffn_Wqkv"]))
    rdim = np.arange(128) % 64
    sign = np.where((np.arange(128) % 64) < 32, -1.0, 1.0)
    half = np.arange(0, DH, 2, dtype=np.float32)
    inv_freq = (1.0 / 10000.0 ** (half / DH)).astype(np.float32)
    t = np.arange(T, dtype=np.float32)
    freqs = t[:, None] * inv_freq[None, :]
    base_scale = ((half + 0.4 * DH) / (1.4 * DH)).astype(np.float32)
    power = ((t - T // 2) / (W_ // 2)).astype(np.float32)
    scale = base_scale[None, :] ** power[:, None]
    cos = np.cos(freqs)
    sin = np.sin(freqs)
    hd = rdim % 32
    cs2p = np.ascontiguousarray((cos[:, hd] * scale[:, hd]).T.astype(np.float32))
    snq = np.ascontiguousarray(
        ((sin[:, hd] * scale[:, hd]).T * sign[:, None]).astype(np.float32))
    ci2p = np.ascontiguousarray((cos[:, hd] / scale[:, hd]).T.astype(np.float32))
    snk = np.ascontiguousarray(
        ((sin[:, hd] / scale[:, hd]).T * sign[:, None]).astype(np.float32))
    rh = (np.arange(128) // 64) * 64 + ((np.arange(128) + 32) % 64)
    qk = np.zeros((128, 4 * DEPTH), dtype=np.float32)
    for li in range(DEPTH):
        qs = inputs["L0_ffn_q_scale"][li]
        ks = inputs["L0_ffn_k_scale"][li]
        qk[:, 4 * li + 0] = qs[rdim]
        qk[:, 4 * li + 1] = qs[rdim[rh]]
        qk[:, 4 * li + 2] = ks[rdim]
        qk[:, 4 * li + 3] = ks[rdim[rh]]
    assert np.all(rdim[rh] == (rdim + 32) % 64)
    maskadd = np.zeros((128, 384), dtype=np.float32)
    r = np.arange(128)[:, None]
    j = np.arange(128)[None, :]
    maskadd[:, 0:128] = np.where(j >= r, 0.0, NEG_BIG)
    maskadd[:, 256:384] = np.where(j <= r, 0.0, NEG_BIG)
    maskmulT = np.ones((128, 384), dtype=np.float32)
    maskmulT[:, 0:128] = (np.arange(128)[:, None] >= np.arange(128)[None, :])
    maskmulT[:, 256:384] = (np.arange(128)[:, None] <= np.arange(128)[None, :])
    ingrp = np.zeros((128, 2), dtype=np.float32)
    ingrp[0:64, 0] = 1
    ingrp[64:128, 1] = 1
    ones16 = np.zeros((128, 128), dtype=np.float32)
    sel16 = np.zeros((16, 1024), dtype=np.float32)
    for c in range(8):
        ones16[:, 16 * c + 2 * c:16 * c + 2 * c + 2] = ingrp
        sel16[2 * c:2 * c + 2, 128 * c:128 * c + 128] = ingrp.T
    hsel = np.zeros((16, 2048), dtype=np.float32)
    for h in range(16):
        hsel[h, 128 * h:128 * h + 128] = 1.0
    lngb = np.zeros((128, 16 * DEPTH), dtype=np.float32)
    for li in range(DEPTH):
        lngb[:, 16 * li:16 * li + 8] = inputs["L0_ffn_ln_g"][li].reshape(8, 128).T
        lngb[:, 16 * li + 8:16 * li + 16] = \
            inputs["L0_ffn_ln_b"][li].reshape(8, 128).T
    W1 = np.array(inputs["L0_ffn_W1"])
    DI0 = 2730
    W1a = np.zeros((DEPTH, DIM, 2816), dtype=np.float32)
    W1g = np.zeros((DEPTH, DIM, 2816), dtype=np.float32)
    W1a[:, :, :DI0] = W1[:, :, :DI0]
    W1g[:, :, :DI0] = W1[:, :, DI0:]
    W1p = np.zeros((DEPTH, DIM, 5632), dtype=np.float32)
    for mc in range(22):
        W1p[:, :, 256 * mc:256 * mc + 128] = W1a[:, :, 128 * mc:128 * mc + 128]
        W1p[:, :, 256 * mc + 128:256 * mc + 256] = \
            W1g[:, :, 128 * mc:128 * mc + 128]
    W1p = np.ascontiguousarray(W1p)
    b1 = np.array(inputs["L0_ffn_b1"])
    b1r = np.zeros((DEPTH, 128, 44), dtype=np.float32)
    b1a = np.zeros((DEPTH, 2816), dtype=np.float32)
    b1g = np.zeros((DEPTH, 2816), dtype=np.float32)
    b1a[:, :DI0] = b1[:, :DI0]
    b1g[:, :DI0] = b1[:, DI0:]
    for mc in range(22):
        b1r[:, :, 2 * mc] = b1a[:, 128 * mc:128 * mc + 128]
        b1r[:, :, 2 * mc + 1] = b1g[:, 128 * mc:128 * mc + 128]
    b1rc = np.zeros((128, 44 * DEPTH), dtype=np.float32)
    for li in range(DEPTH):
        b1rc[:, 44 * li:44 * li + 44] = b1r[li]
    W2 = np.array(inputs["L0_ffn_W2"])
    W2p = np.zeros((DEPTH, 2816, 1024), dtype=np.float32)
    W2p[:, :DI0, :] = W2
    W2p = np.ascontiguousarray(W2p)
    b2r = np.zeros((128, 8 * DEPTH), dtype=np.float32)
    for li in range(DEPTH):
        b2r[:, 8 * li:8 * li + 8] = inputs["L0_ffn_b2"][li].reshape(8, 128).T
    return {
        "Wqkv": Wqkv_p,
        "Wo": np.ascontiguousarray(inputs["L0_ffn_Wo"]),
        "W1p": W1p,
        "W2p": W2p,
        "cs2p": cs2p, "snq": snq, "ci2p": ci2p, "snk": snk,
        "qk_scales": qk,
        "maskadd": maskadd, "maskmulT": maskmulT,
        "ones16": ones16, "sel16": sel16, "hsel": hsel,
        "ln_gb": lngb, "b1r": b1rc, "b2r": b2r,
        "ident_bf": bf16(np.eye(128, dtype=np.float32)),
    }


def build_l0ffn_prog(debug=False, depth=6, do_attn=True, do_ffn=True):
    p = Prog("trf")
    p.arena("cst", 128, F32)
    p.arena("cbf", 6144, BF16)
    p.arena("cf32", 2048, F32)
    p.arena("wbf", 8 * 4096, BF16)
    p.arena("abf", 36864, BF16)
    p.arena("af32", 11264, F32)
    xin = p.inp("xin", (D, T), F32)
    xout = p.out("xout", (D, T), F32)
    W = {
        "Wqkv": p.inp("Wqkv", (6, 1024, 3072), F32),
        "Wo": p.inp("Wo", (6, 1024, 1024), F32),
        "W1p": p.inp("W1p", (6, 1024, 5632), F32),
        "W2p": p.inp("W2p", (6, 2816, 1024), F32),
        "cs2p": p.inp("cs2p", (128, T), F32),
        "snq": p.inp("snq", (128, T), F32),
        "ci2p": p.inp("ci2p", (128, T), F32),
        "snk": p.inp("snk", (128, T), F32),
        "qk_scales": p.inp("qk_scales", (128, 24), F32),
        "maskadd": p.inp("maskadd", (128, 384), F32),
        "maskmulT": p.inp("maskmulT", (128, 384), F32),
        "ones16": p.inp("ones16", (128, 128), F32),
        "sel16": p.inp("sel16", (16, 1024), F32),
        "hsel": p.inp("hsel", (16, 2048), F32),
        "ln_gb": p.inp("ln_gb", (128, 96), F32),
        "b1r": p.inp("b1r", (128, 264), F32),
        "b2r": p.inp("b2r", (128, 48), F32),
        "ident_bf": p.inp("ident_bf", (128, 128), BF16),
    }
    sec_l0ffn(p, xin, xout, W, debug=debug, depth=depth, do_attn=do_attn,
              do_ffn=do_ffn)
    return p.build()


MAMBA_WS = {
    "W_inT": ((1024, 4096), F32), "W_xprojT": ((2048, 96), F32),
    "W_dtT": ((64, 2048), F32), "W_outT": ((2048, 1024), F32),
    "conv_wr": ((128, 64), F32), "conv_br": ((128, 16), F32),
    "b_dtr": ((128, 16), F32), "D_skipr": ((128, 16), F32),
    "A_logr": ((128, 256), F32), "bcsel": ((32, 4096), F32),
}
TRF_WS = {
    "Wqkv": ((6, 1024, 3072), F32), "Wo": ((6, 1024, 1024), F32),
    "W1p": ((6, 1024, 5632), F32), "W2p": ((6, 2816, 1024), F32),
    "cs2p": ((128, T), F32), "snq": ((128, T), F32),
    "ci2p": ((128, T), F32), "snk": ((128, T), F32),
    "qk_scales": ((128, 24), F32), "maskadd": ((128, 384), F32),
    "maskmulT": ((128, 384), F32), "ones16": ((128, 128), F32),
    "sel16": ((16, 1024), F32), "hsel": ((16, 2048), F32),
    "ln_gb": ((128, 96), F32), "b1r": ((128, 264), F32),
    "b2r": ((128, 48), F32), "ident_bf": ((128, 128), BF16),
}
GLA_WS = {
    "Wq": ((1024, 1024), F32), "Wk": ((1024, 1024), F32),
    "Wv": ((1024, 2048), F32), "Wg": ((1024, 2048), F32),
    "Wgk1": ((1024, 16), F32), "Wgk2": ((16, 1024), F32),
    "Wo": ((2048, 1024), F32), "bgk2r": ((128, 8), F32),
    "gnwr": ((128, 4), F32), "ident_bf": ((128, 128), BF16),
    "maskT": ((64, 64), F32),
}
MOE_WS = {
    "gate_W": ((1024, 8), F32), "W1": ((8, 1024, 2048), F32),
    "b1r": ((8, 128, 16), F32), "W2": ((8, 2048, 1024), F32),
    "B2": ((8, 1024), F32), "esel": ((8, 1024), F32),
    "ident_f32": ((128, 128), F32),
}


def build_fused_prog(debug=False):
    p = Prog("fused")
    p.arena("cst", 128, F32)
    p.arena("cbf", 6400, BF16)
    p.arena("bf", 68160, BF16)
    p.arena("ff", 15616, F32)
    for nm in ("wbf", "abf", "dbl", "obf"):
        p.alias(nm, "bf")
    for nm in ("af32", "cf32", "wf32", "sf32", "ybig"):
        p.alias(nm, "ff")
    xin = p.inp("xin", (D, T), F32)
    xout = p.out("xout", (D, T), F32)
    x1 = p.dram("x1", (D, T), F32)
    x2 = p.dram("x2", (D, T), F32)
    x3 = p.dram("x3", (D, T), F32)

    def wmap(prefix, spec):
        return {k: p.inp(prefix + k, s, dt) for k, (s, dt) in spec.items()}

    Wm = wmap("m_", MAMBA_WS)
    Wt = wmap("t_", TRF_WS)
    Wg = wmap("g_", GLA_WS)
    We = wmap("e_", MOE_WS)

    std_consts(p)
    p.mark("sect")
    sec_mamba(p, xin, x1, Wm, debug=debug)
    p.barrier()
    p.reset("sect")
    sec_l0ffn(p, x1, x2, Wt, debug=debug)
    p.barrier()
    p.reset("sect")
    sec_gla(p, x2, x3, Wg, debug=debug)
    p.barrier()
    p.reset("sect")
    sec_moe(p, x3, xout, We, debug=debug)
    return p.build()


_PROGS = {}
FUSED = True


def _get_prog(name, builder):
    if name not in _PROGS:
        _PROGS[name] = builder()
    return _PROGS[name]


def kernel(**inputs):
    x = np.asarray(inputs["L0_attn_x"])
    xT = [np.ascontiguousarray(x[b].T) for b in range(NCORES)]

    hw_m = host_mamba_weights(inputs)
    hw_t = host_l0ffn_weights(inputs)
    hw_g = host_gla_weights({k: inputs[k] for k in inputs if k.startswith("L1_attn")})
    hw_e = host_moe_weights({k: inputs[k] for k in inputs if k.startswith("L1_ffn")})

    def launch(progname, builder, hw, xs):
        nc = _get_prog(progname, builder)
        in_maps = [dict({"xin": xs[b]}, **hw) for b in range(NCORES)]
        res = run_bass_kernel_spmd(nc, in_maps, core_ids=list(range(NCORES)))
        return [np.ascontiguousarray(np.asarray(res.results[b]["xout"]))
                for b in range(NCORES)]

    if FUSED:
        hw = {}
        for pref, d in (("m_", hw_m), ("t_", hw_t), ("g_", hw_g), ("e_", hw_e)):
            for k, v in d.items():
                hw[pref + k] = v
        x4 = launch("fused", build_fused_prog, hw, xT)
    else:
        x1 = launch("mamba", build_mamba_prog, hw_m, xT)
        x2 = launch("l0ffn", build_l0ffn_prog, hw_t, x1)
        x3 = launch("gla", build_gla_prog, hw_g, x2)
        x4 = launch("moe", build_moe_prog, hw_e, x3)
    return np.stack([x4[b].T for b in range(NCORES)], axis=0).astype(np.float32)


# version 07
# speedup vs baseline: 1.7185x; 1.0752x over previous
import numpy as np
from contextlib import ExitStack
import concourse.bass as bass
import concourse.mybir as mybir
from concourse.bass_utils import run_bass_kernel_spmd

F32 = mybir.dt.float32
BF16 = mybir.dt.bfloat16
AF = mybir.ActivationFunctionType
ALU = mybir.AluOpType
AX = mybir.AxisListType

NCORES = 8
T = 4096
D = 1024
ENGINES = ("tensor", "vector", "scalar", "gpsimd", "sync")
NEG_BIG = -1.0e30


def bf16(x):
    import ml_dtypes
    return np.asarray(x, dtype=ml_dtypes.bfloat16)


class Prog:
    def __init__(self, name="prog"):
        self.nc = bass.Bass("TRN2", target_bir_lowering=False, debug=False)
        self.stack = ExitStack()
        self.ops = []
        self.sem = {}
        self.cnt = {}
        self.bar_n = 0
        self.bar_dma_seen = {}
        self._uid = 0
        for e in ENGINES:
            self.new_sem("bar_" + e)
        for q in ("sync", "gpsimd", "scalar", "vector"):
            self.new_sem("dma_" + q)
            self.bar_dma_seen[q] = 0
        for s in ("t0", "t1", "v0", "v1", "s0", "s1", "g0"):
            self.new_sem(s)
        self.ps = [self.stack.enter_context(
            self.nc.psum_tensor(f"ps{i}", [128, 512], F32)) for i in range(7)]
        self.psT = self.stack.enter_context(
            self.nc.psum_tensor("psT", [128, 1024], BF16))
        self.arenas = {}
        self.marks = {}
        self.cvals = {}
        self.peaks = {}

    def uid(self):
        self._uid += 1
        return self._uid

    def new_sem(self, name):
        h = self.stack.enter_context(self.nc.semaphore(name))
        self.sem[name] = h
        self.cnt[name] = 0
        return h

    def arena(self, name, cols, dtype=F32):
        t = self.stack.enter_context(self.nc.sbuf_tensor(name, [128, cols], dtype))
        self.arenas[name] = [t, 0, cols]
        return t

    def take(self, name, cols, rows=128):
        a = self.arenas[name]
        off = (a[1] + 7) & ~7
        assert off + cols <= a[2], f"arena {name} overflow: {off}+{cols} > {a[2]}"
        a[1] = off + cols
        pk = self.peaks.get(a[0].name, 0)
        if a[1] > pk:
            self.peaks[a[0].name] = a[1]
        return a[0][0:rows, off:off + cols]

    def alias(self, name, target):
        self.arenas[name] = self.arenas[target]

    def mark(self, name):
        self.marks[name] = {k: v[1] for k, v in self.arenas.items()}

    def reset(self, name):
        for k, v in self.marks[name].items():
            self.arenas[k][1] = v

    def constv(self, value):
        if value in self.cvals:
            return value
        tile = self.take("cst", 8)
        self.v("memset", tile[:, 0:1], value)
        self.nc.const_aps.register(F32, value, tile[:, 0:1])
        self.cvals[value] = tile
        return value

    def dram(self, name, shape, dtype=F32, kind="Internal"):
        return self.nc.dram_tensor(name, list(shape), dtype, kind=kind).ap()

    def inp(self, name, shape, dtype=F32):
        return self.dram(name, shape, dtype, kind="ExternalInput")

    def out(self, name, shape, dtype=F32):
        return self.dram(name, shape, dtype, kind="ExternalOutput")

    AUTODRAIN = ("vector", "scalar", "gpsimd")

    def op(self, engine, method, *args, inc=None, nodrain=False, **kwargs):
        if inc is not None:
            sname, n = inc
            self.cnt[sname] += n
        drain = (engine in self.AUTODRAIN and not nodrain
                 and method != "dma_start")

        def fn(eng, method=method, args=args, kwargs=kwargs, inc=inc, drain=drain):
            r = getattr(eng, method)(*args, **kwargs)
            if inc is not None:
                r.then_inc(self.sem[inc[0]], inc[1])
            if drain:
                eng.drain()
            return r
        self.ops.append((engine, fn))
        if inc is not None:
            return self.cnt[inc[0]]

    def dma(self, queue, out_ap, in_ap, **kw):
        return self.op(queue, "dma_start", inc=("dma_" + queue, 16),
                       out=out_ap, in_=in_ap, **kw)

    def wait(self, engine, semname, val):
        self.ops.append((engine,
                         lambda eng, s=semname, v=val: eng.wait_ge(self.sem[s], v)))

    def wait_dma(self, engine, queue, val=None):
        self.wait(engine, "dma_" + queue, self.cnt["dma_" + queue] if val is None else val)

    def inc(self, engine, semname, n=1, drain=True):
        self.cnt[semname] += n
        if not drain:
            eng0, fn0 = self.ops[-1]
            assert eng0 == engine, f"last op is {eng0}, not {engine}"

            def fnw(eng, fn0=fn0, s=semname, n=n):
                r = fn0(eng)
                r.then_inc(self.sem[s], n)
                return r
            self.ops[-1] = (engine, fnw)
            return self.cnt[semname]

        def fn(eng, s=semname, n=n):
            return eng.drain().then_inc(self.sem[s], n)
        self.ops.append((engine, fn))
        return self.cnt[semname]

    def barrier(self):
        for q in ("sync", "gpsimd", "scalar", "vector"):
            c = self.cnt["dma_" + q]
            if c > self.bar_dma_seen[q]:
                self.wait(q, "dma_" + q, c)
                self.bar_dma_seen[q] = c
        self.bar_n += 1
        for e in ENGINES:
            self.inc(e, "bar_" + e, 1)
        for e in ENGINES:
            for o in ENGINES:
                if o != e:
                    self.wait(e, "bar_" + o, self.bar_n)

    def mm(self, ps, lhsT, rhs, start=True, stop=True, inc=None):
        return self.op("tensor", "matmul", ps, lhsT, rhs,
                       start=start, stop=stop, inc=inc)

    def trans(self, ps, in_, ident, inc=None):
        return self.op("tensor", "transpose", ps, in_, ident, inc=inc)

    def act(self, out, in_, func, bias=0.0, scale=1.0, accum_out=None, inc=None,
            eng="scalar"):
        return self.op(eng, "activation", out, in_, func,
                       bias=bias, scale=scale, accum_out=accum_out, inc=inc)

    def v(self, method, *args, inc=None, eng="vector", **kw):
        return self.op(eng, method, *args, inc=inc, **kw)

    def build(self):
        with self.nc.Block() as block:
            for ename in ENGINES:
                def make_runner(ename):
                    def runner(eng):
                        for e, fn in self.ops:
                            if e == ename:
                                fn(eng)
                    return runner
                getattr(block, ename)(make_runner(ename))
        self.stack.close()
        return self.nc


def emit_ln(p, x_tiles_f32, xb_tiles, width, consts, out_bf_tiles, psA, psB,
            t_tiles=None, out_f32_tiles=None, gcol=None, bcol=None, eps=1e-5):
    nch = len(x_tiles_f32)
    N = nch * 128
    for i in range(nch):
        p.act(xb_tiles[i][:, :width], x_tiles_f32[i][:, :width], AF.Copy)
    c_s = p.inc("scalar", "s0")
    p.wait("tensor", "s0", c_s)
    for i in range(nch):
        p.mm(psA[0:1, :width], consts["ones_col"][:, 0:1], xb_tiles[i][:, :width],
             start=(i == 0), stop=(i == nch - 1))
    c_t = p.inc("tensor", "t0")
    p.wait("scalar", "t0", c_t)
    p.act(consts["rowbf"][0:1, :width], psA[0:1, :width], AF.Copy, scale=1.0 / N)
    c_s = p.inc("scalar", "s0")
    p.wait("tensor", "s0", c_s)
    p.mm(psB[:, :width], consts["ones_row"], consts["rowbf"][0:1, :width])
    c_t = p.inc("tensor", "t0")
    p.wait("scalar", "t0", c_t)
    p.act(consts["rep_bf"][:, :width], psB[:, :width], AF.Copy)
    c_s = p.inc("scalar", "s0")
    p.wait("vector", "s0", c_s)
    tt = t_tiles if t_tiles is not None else x_tiles_f32
    for i in range(nch):
        p.v("tensor_tensor", tt[i][:, :width], x_tiles_f32[i][:, :width],
            consts["rep_bf"][:, :width], ALU.subtract)
    c_v = p.inc("vector", "v0")
    p.wait("scalar", "v0", c_v)
    for i in range(nch):
        p.act(xb_tiles[i][:, :width], tt[i][:, :width], AF.Square)
    c_s = p.inc("scalar", "s0")
    p.wait("tensor", "s0", c_s)
    for i in range(nch):
        p.mm(psA[0:1, :width], consts["ones_col"][:, 0:1], xb_tiles[i][:, :width],
             start=(i == 0), stop=(i == nch - 1))
    c_t = p.inc("tensor", "t0")
    p.wait("scalar", "t0", c_t)
    p.act(consts["row_a"][0:1, :width], psA[0:1, :width], AF.Ln,
          scale=1.0 / N, bias=eps)
    p.act(consts["rowbf"][0:1, :width], consts["row_a"][0:1, :width], AF.Exp,
          scale=-0.5)
    c_s = p.inc("scalar", "s0")
    p.wait("tensor", "s0", c_s)
    p.mm(psB[:, :width], consts["ones_row"], consts["rowbf"][0:1, :width])
    c_t = p.inc("tensor", "t0")
    p.wait("scalar", "t0", c_t)
    p.act(consts["rep2_bf"][:, :width], psB[:, :width], AF.Copy)
    c_s = p.inc("scalar", "s0")
    p.wait("vector", "s0", c_s)
    for i in range(nch):
        p.v("tensor_tensor", out_bf_tiles[i][:, :width], tt[i][:, :width],
            consts["rep2_bf"][:, :width], ALU.mult)
    if out_f32_tiles is not None:
        for i in range(nch):
            p.v("tensor_tensor", out_f32_tiles[i][:, :width], tt[i][:, :width],
                consts["rep2_bf"][:, :width], ALU.mult)
    if gcol is not None:
        for i in range(nch):
            p.v("tensor_scalar", out_bf_tiles[i][:, :width],
                out_bf_tiles[i][:, :width], gcol[i], None, ALU.mult)
            p.v("tensor_scalar", out_bf_tiles[i][:, :width],
                out_bf_tiles[i][:, :width], bcol[i], None, ALU.add)


def sec_moe(p, xin, xout, W, pref="moe", debug=False):
    nc = p.nc
    NS = 8
    SW = T // NS
    E, DH = 8, 2048

    kind = "ExternalOutput" if debug else "Internal"
    xn_hbm = p.dram(pref + "_xn", (D, T), BF16, kind=kind)
    y_hbm = p.dram(pref + "_y", (D, T), F32, kind=kind)
    cw_dbg = p.out(pref + "_cw", (8, T), F32) if debug else None

    consts = std_consts(p)

    gw = p.take("wbf", 64)
    for k in range(8):
        p.dma("gpsimd", gw[:, 8 * k:8 * k + 8], W["gate_W"][128 * k:128 * k + 128, :])

    cw_rows_b = p.take("cbf", T, rows=8)
    B2t = p.take("wbf", 1024, rows=8)
    p.dma("gpsimd", B2t[0:8, :], W["B2"][:, :])
    esel = p.take("wbf", 1024, rows=8)
    p.dma("gpsimd", esel[0:8, :], W["esel"][:, :])
    identf = p.take("cf32", 128)
    p.dma("sync", identf[:, :], W["ident_f32"][:, :])
    p.barrier()

    p.mark(pref + "s1")
    for s in range(NS):
        p.reset(pref + "s1")
        xf = [p.take("af32", SW) for _ in range(8)]
        xb = [p.take("abf", SW) for _ in range(8)]
        for i in range(8):
            p.dma("sync", xf[i][:, :], xin[128 * i:128 * i + 128, SW * s:SW * s + SW])
        p.barrier()
        emit_ln(p, xf, xb, SW, consts, xb, p.ps[0], p.ps[1])
        c_v = p.inc("vector", "v0")
        p.wait("gpsimd", "v0", c_v)
        for i in range(8):
            p.dma("gpsimd", xn_hbm[128 * i:128 * i + 128, SW * s:SW * s + SW],
                  xb[i][:, :])
        p.wait("tensor", "v0", c_v)
        lg = p.take("af32", 8 * 4)
        pcw = p.take("af32", 8 * 4)
        for tc in range(4):
            for k in range(8):
                p.mm(p.ps[2][:, 8 * tc:8 * tc + 8],
                     xb[k][:, 128 * tc:128 * tc + 128], gw[:, 8 * k:8 * k + 8],
                     start=(k == 0), stop=(k == 7))
        c_t = p.inc("tensor", "t0")
        p.wait("vector", "t0", c_t)
        p.v("tensor_copy", lg[:, :], p.ps[2][:, 0:32])
        mx = p.take("af32", 4)
        sm = p.take("af32", 4)
        for tc in range(4):
            p.v("reduce_max", mx[:, tc:tc + 1], lg[:, 8 * tc:8 * tc + 8], axis=AX.X,
                negate=True)
        c_v = p.inc("vector", "v0")
        p.wait("scalar", "v0", c_v)
        for tc in range(4):
            p.act(pcw[:, 8 * tc:8 * tc + 8], lg[:, 8 * tc:8 * tc + 8], AF.Exp,
                  bias=mx[:, tc:tc + 1], accum_out=sm[:, tc:tc + 1])
        c_s = p.inc("scalar", "s0")
        p.wait("vector", "s0", c_s)
        m1 = p.take("af32", 4)
        m2 = p.take("af32", 4)
        t1 = p.take("af32", 8 * 4)
        w12 = p.take("af32", 8 * 4)
        rs = p.take("af32", 4)
        for tc in range(4):
            a = lg[:, 8 * tc:8 * tc + 8]
            pp = pcw[:, 8 * tc:8 * tc + 8]
            tt = t1[:, 8 * tc:8 * tc + 8]
            ww = w12[:, 8 * tc:8 * tc + 8]
            p.v("reduce_max", m1[:, tc:tc + 1], pp, axis=AX.X)
            p.v("tensor_scalar", ww, pp, m1[:, tc:tc + 1], None, ALU.is_ge)
            p.v("scalar_tensor_tensor", tt, ww, NEG_BIG, pp, ALU.mult, ALU.add)
            p.v("reduce_max", m2[:, tc:tc + 1], tt, axis=AX.X)
            p.v("tensor_scalar", tt, tt, m2[:, tc:tc + 1], None, ALU.is_ge)
            p.v("tensor_tensor", ww, ww, tt, ALU.add)
            p.v("tensor_tensor", ww, ww, pp, ALU.mult)
            p.v("tensor_tensor", rs[:, tc:tc + 1], m1[:, tc:tc + 1],
                m2[:, tc:tc + 1], ALU.add)
            p.v("scalar_tensor_tensor", rs[:, tc:tc + 1], sm[:, tc:tc + 1], 1e-8,
                rs[:, tc:tc + 1], ALU.mult, ALU.add)
            p.v("reciprocal", rs[:, tc:tc + 1], rs[:, tc:tc + 1])
            p.v("tensor_scalar", ww, ww, rs[:, tc:tc + 1], None, ALU.mult)
        c_v = p.inc("vector", "v0")
        p.wait("tensor", "v0", c_v)
        for tc in range(4):
            p.trans(p.ps[3][0:8, 128 * tc:128 * tc + 128],
                    w12[:, 8 * tc:8 * tc + 8], identf[:, :])
        c_t = p.inc("tensor", "t0")
        p.wait("scalar", "t0", c_t)
        p.act(cw_rows_b[0:8, SW * s:SW * s + SW], p.ps[3][0:8, 0:SW], AF.Copy)
        if debug:
            cwf = p.take("af32", SW, rows=8)
            p.act(cwf[0:8, :], p.ps[3][0:8, 0:SW], AF.Copy)
            c_s = p.inc("scalar", "s0")
            p.wait("gpsimd", "s0", c_s)
            p.dma("gpsimd", cw_dbg[0:8, SW * s:SW * s + SW], cwf[0:8, :])
        p.barrier()
    p.reset(pref + "s1")

    p.mark(pref + "s2")
    for s in range(NS):
        p.reset(pref + "s2")
        xb = [p.take("abf", SW) for _ in range(8)]
        yf = [p.take("af32", SW) for _ in range(8)]
        for i in range(8):
            p.dma("sync", xb[i][:, :], xn_hbm[128 * i:128 * i + 128, SW * s:SW * s + SW])
        p.wait_dma("tensor", "sync")
        for i in range(8):
            p.mm(p.ps[i % 4][:, 0:SW], B2t[0:8, 128 * i:128 * i + 128],
                 cw_rows_b[0:8, SW * s:SW * s + SW],
                 start=True, stop=True)
            if i % 4 == 3 or i == 7:
                c_t = p.inc("tensor", "t0")
                p.wait("vector", "t0", c_t)
                for j in range(4 * (i // 4), i + 1):
                    p.v("tensor_tensor", yf[j][:, :],
                        p.ps[j % 4][:, 0:SW], xb[j][:, :], ALU.add)
                c_v = p.inc("vector", "v0")
                p.wait("tensor", "v0", c_v)
        p.wait("gpsimd", "v0", p.cnt["v0"])
        for j in range(8):
            p.dma("gpsimd", y_hbm[128 * j:128 * j + 128, SW * s:SW * s + SW],
                  yf[j][:, :])
        p.barrier()
    p.reset(pref + "s2")

    b1c = p.take("wf32", 16)
    p.mark(pref + "s3")
    for e in range(E):
        p.reset(pref + "s3")
        w1 = [p.take("wbf", DH) for _ in range(8)]
        w2 = [p.take("wbf", D) for _ in range(16)]
        p.dma("gpsimd", b1c[:, :], W["b1r"][e])
        for k in range(8):
            p.dma("gpsimd", w1[k][:, :], W["W1"][e, 128 * k:128 * k + 128, :])
        for k in range(16):
            p.dma("gpsimd", w2[k][:, :], W["W2"][e, 128 * k:128 * k + 128, :])
        p.barrier()
        for s in range(NS):
            p.mark(pref + "s3s")
            xb = [p.take("abf", SW) for _ in range(8)]
            yf = [p.take("af32", SW) for _ in range(8)]
            p.wait_dma("sync", "gpsimd")
            for i in range(8):
                p.dma("sync", xb[i][:, :],
                      xn_hbm[128 * i:128 * i + 128, SW * s:SW * s + SW])
                p.dma("sync", yf[i][:, :],
                      y_hbm[128 * i:128 * i + 128, SW * s:SW * s + SW])
            cwrep = p.take("abf", SW)
            p.wait_dma("tensor", "sync")
            p.mm(p.ps[6][:, 0:SW], esel[0:8, 128 * e:128 * e + 128],
                 cw_rows_b[0:8, SW * s:SW * s + SW])
            c_t = p.inc("tensor", "t0")
            p.wait("scalar", "t0", c_t)
            p.act(cwrep[:, :], p.ps[6][:, 0:SW], AF.Copy)
            c_s = p.inc("scalar", "s0")
            p.wait("vector", "s0", c_s)
            p.wait("tensor", "s0", c_s)
            ht = [p.take("abf", SW) for _ in range(16)]
            hrb = [p.take("abf", SW) for _ in range(2)]
            vdn = [0] * 16
            for mc in range(16):
                pb = p.ps[mc % 2]
                if mc >= 2:
                    p.wait("tensor", "s0", p.cnt["s0"] if False else sdn[mc - 2])
                for k in range(8):
                    p.mm(pb[:, 0:SW], w1[k][:, 128 * mc:128 * mc + 128], xb[k][:, :],
                         start=(k == 0), stop=(k == 7))
                c_t = p.inc("tensor", "t0", drain=False)
                p.wait("scalar", "t0", c_t)
                if mc >= 2:
                    p.wait("scalar", "v0", vdn[mc - 2])
                p.act(hrb[mc % 2][:, :], pb[:, 0:SW], AF.Relu, bias=b1c[:, mc:mc + 1])
                c_s = p.inc("scalar", "s0")
                if mc == 0:
                    sdn = [0] * 16
                sdn[mc] = c_s
                p.wait("vector", "s0", c_s)
                p.v("tensor_tensor", ht[mc][:, :], hrb[mc % 2][:, :], cwrep[:, :],
                    ALU.mult)
                vdn[mc] = p.inc("vector", "v0")
            p.wait("tensor", "v0", vdn[15])
            for half in range(2):
                for dm in range(4):
                    j = 4 * half + dm
                    for mc in range(16):
                        p.mm(p.ps[2 + dm][:, 0:SW],
                             w2[mc][:, 128 * j:128 * j + 128], ht[mc][:, :],
                             start=(mc == 0), stop=(mc == 15))
                c_t = p.inc("tensor", "t0")
                p.wait("vector", "t0", c_t)
                for dm in range(4):
                    j = 4 * half + dm
                    p.v("tensor_tensor", yf[j][:, :],
                        yf[j][:, :], p.ps[2 + dm][:, 0:SW], ALU.add)
                c_v = p.inc("vector", "v0")
                p.wait("tensor", "v0", c_v)
            p.wait("gpsimd", "v0", p.cnt["v0"])
            for j in range(8):
                p.dma("gpsimd", y_hbm[128 * j:128 * j + 128, SW * s:SW * s + SW],
                      yf[j][:, :])
            p.reset(pref + "s3s")
        p.barrier()
    p.reset(pref + "s3")

    p.mark(pref + "s4")
    for s in range(NS):
        p.reset(pref + "s4")
        yv = [p.take("af32", SW) for _ in range(8)]
        yb = [p.take("abf", SW) for _ in range(8)]
        ob = [p.take("abf", SW) for _ in range(8)]
        xf = [p.take("af32", SW) for _ in range(8)]
        for i in range(8):
            p.dma("sync", xf[i][:, :], xin[128 * i:128 * i + 128, SW * s:SW * s + SW])
            p.dma("sync", yv[i][:, :], y_hbm[128 * i:128 * i + 128, SW * s:SW * s + SW])
        p.barrier()
        emit_ln(p, yv, yb, SW, consts, ob, p.ps[0], p.ps[1])
        c_v = p.inc("vector", "v0")
        p.wait("vector", "v0", c_v)
        for i in range(8):
            p.v("tensor_tensor", xf[i][:, :], xf[i][:, :], ob[i][:, :], ALU.add)
        c_v = p.inc("vector", "v0")
        p.wait("gpsimd", "v0", c_v)
        for i in range(8):
            p.dma("gpsimd", xout[128 * i:128 * i + 128, SW * s:SW * s + SW],
                  xf[i][:, :])
        p.barrier()


def host_moe_weights(inputs):
    W1 = np.ascontiguousarray(inputs["L1_ffn_W1"])
    W2 = np.ascontiguousarray(inputs["L1_ffn_W2"])
    b1 = inputs["L1_ffn_b1"]
    b1r = np.ascontiguousarray(b1.reshape(8, 16, 128).transpose(0, 2, 1))
    B2 = np.ascontiguousarray(inputs["L1_ffn_b2"])
    gate_W = np.ascontiguousarray(inputs["L1_ffn_gate_W"])
    ident = np.eye(128, dtype=np.float32)
    esel = np.zeros((8, 1024), dtype=np.float32)
    for e in range(8):
        esel[e, 128 * e:128 * e + 128] = 1.0
    return dict(W1=W1, W2=W2, b1r=b1r, B2=B2, gate_W=gate_W, ident_f32=ident,
                esel=esel)


def build_moe_prog(debug=False):
    p = Prog("moe")
    p.arena("cst", 128, F32)
    p.arena("cbf", 10240, BF16)
    p.arena("cf32", 1024, F32)
    p.arena("wbf", 8 * 2048 + 16 * 1024 + 4096, BF16)
    p.arena("wf32", 64, F32)
    p.arena("abf", 15360, BF16)
    p.arena("af32", 18 * 512, F32)
    xin = p.inp("xin", (D, T), F32)
    xout = p.out("xout", (D, T), F32)
    W = {
        "gate_W": p.inp("gate_W", (1024, 8), F32),
        "W1": p.inp("W1", (8, 1024, 2048), F32),
        "b1r": p.inp("b1r", (8, 128, 16), F32),
        "W2": p.inp("W2", (8, 2048, 1024), F32),
        "B2": p.inp("B2", (8, 1024), F32),
        "esel": p.inp("esel", (8, 1024), F32),
        "ident_f32": p.inp("ident_f32", (128, 128), F32),
    }
    sec_moe(p, xin, xout, W, debug=debug)
    return p.build()


def std_consts(p):
    if hasattr(p, "_stdconsts"):
        return p._stdconsts
    consts = {
        "ones_col": p.take("cbf", 8),
        "ones_row": p.take("cbf", 128, rows=1),
        "rowbf": p.take("cbf", 1024, rows=1),
        "row_a": p.take("cf32", 512, rows=1),
        "rep_bf": p.take("cbf", 512),
        "rep2_bf": p.take("cbf", 512),
    }
    p.v("memset", consts["ones_col"][:, 0:1], 1.0)
    p.v("memset", consts["ones_row"][0:1, :], 1.0)
    p.constv(1e-5)
    p.constv(1.0)
    p._stdconsts = consts
    return consts


def sec_gla(p, xin, xout, W, pref="gla", debug=False):
    NS = 8
    SW = T // NS
    NH, HDK, HDV, CH = 4, 256, 512, 64
    kind = "ExternalOutput" if debug else "Internal"

    xn_hbm = p.dram(pref + "_xn", (D, T), BF16, kind=kind)
    q_hbm = p.dram(pref + "_q", (D, T), BF16, kind=kind)
    k_hbm = p.dram(pref + "_k", (D, T), BF16, kind=kind)
    gkt_hbm = p.dram(pref + "_gkt", (D, T), F32, kind=kind)
    v_hbm = p.dram(pref + "_v", (T, 2048), BF16, kind=kind)
    sg_hbm = p.dram(pref + "_sg", (2048, T), BF16, kind=kind)
    og_hbm = p.dram(pref + "_og", (2048, T), BF16, kind=kind)
    op_hbm = p.dram(pref + "_op", (2048, T), BF16, kind=kind) if debug else None
    if debug:
        dbg_sT = p.out(pref + "_dbg_sT", (64, 64), BF16)
        dbg_q = p.out(pref + "_dbg_q", (128, 64), BF16)
        dbg_k = p.out(pref + "_dbg_k", (128, 64), BF16)
        dbg_b = p.out(pref + "_dbg_b", (128, 64), F32)
        dbg_eb = p.out(pref + "_dbg_eb", (128, 64), BF16)
        dbg_qraw = p.out(pref + "_dbg_qraw", (128, 64), BF16)

    consts = std_consts(p)
    consts["maskT"] = p.take("cf32", 64, rows=64)
    p.dma("sync", consts["maskT"][0:64, :], W["maskT"][:, :])
    identb = p.take("cbf", 128)
    p.dma("sync", identb[:, :], W["ident_bf"][:, :])
    zer = p.take("cf32", 64)
    p.v("memset", zer[:, :], 0.0)

    bgk = p.take("cf32", 8)
    nbgk = p.take("cf32", 8)
    gnw = p.take("cf32", 4)
    p.dma("sync", bgk[:, :], W["bgk2r"][:, :])
    p.dma("sync", gnw[:, :], W["gnwr"][:, :])
    p.wait_dma("vector", "sync")
    p.v("tensor_scalar_mul", nbgk[:, :], bgk[:, :], -1.0)

    p.mark(pref + "w")
    wq = [p.take("wbf", 1024) for _ in range(8)]
    wk = [p.take("wbf", 1024) for _ in range(8)]
    wgk1 = p.take("wbf", 128)
    wgk2 = p.take("wbf", 1024, rows=16)
    for k in range(8):
        r = slice(128 * k, 128 * k + 128)
        p.dma("gpsimd", wq[k][:, :], W["Wq"][r, :])
        p.dma("gpsimd", wk[k][:, :], W["Wk"][r, :])
        p.dma("gpsimd", wgk1[:, 16 * k:16 * k + 16], W["Wgk1"][r, :])
    p.dma("gpsimd", wgk2[0:16, :], W["Wgk2"][:, :])

    p.barrier()
    p.mark(pref + "p1")
    for s in range(NS):
        p.reset(pref + "p1")
        cs = slice(SW * s, SW * s + SW)
        xf = [p.take("af32", SW) for _ in range(8)]
        xb = [p.take("abf", SW) for _ in range(8)]
        for i in range(8):
            p.dma("sync", xf[i][:, :], xin[128 * i:128 * i + 128, cs])
        p.barrier()
        emit_ln(p, xf, xb, SW, consts, xb, p.ps[0], p.ps[1])
        c_v = p.inc("vector", "v0")
        p.wait("gpsimd", "v0", c_v)
        for i in range(8):
            p.dma("gpsimd", xn_hbm[128 * i:128 * i + 128, cs], xb[i][:, :])
        p.wait("tensor", "v0", c_v)
        evb = [p.take("abf", SW) for _ in range(2)]
        sdn = [0] * 16
        ddn = [0] * 16
        for i in range(16):
            wt, dst, scl = ((wq, q_hbm, 1.0 / 16.0) if i < 8 else (wk, k_hbm, 1.0))
            mc = i % 8
            pb = p.ps[2 + i % 2]
            if i >= 2:
                p.wait("tensor", "s0", sdn[i - 2])
            for kc in range(8):
                p.mm(pb[:, 0:SW], wt[kc][:, 128 * mc:128 * mc + 128],
                     xb[kc][:, :], start=(kc == 0), stop=(kc == 7))
            c_t = p.inc("tensor", "t0", drain=False)
            p.wait("scalar", "t0", c_t)
            if i >= 2:
                p.wait("scalar", "dma_gpsimd", ddn[i - 2])
            p.act(evb[i % 2][:, :], pb[:, 0:SW], AF.Copy, scale=scl)
            sdn[i] = p.inc("scalar", "s0")
            p.wait("gpsimd", "s0", sdn[i])
            p.dma("gpsimd", dst[128 * mc:128 * mc + 128, cs], evb[i % 2][:, :])
            ddn[i] = p.cnt["dma_gpsimd"]
        t1b = p.take("abf", SW, rows=16)
        for kc in range(8):
            p.mm(p.ps[4][0:16, 0:SW], wgk1[:, 16 * kc:16 * kc + 16],
                 xb[kc][:, :], start=(kc == 0), stop=(kc == 7))
        c_t = p.inc("tensor", "t0")
        p.wait("scalar", "t0", c_t)
        p.act(t1b[0:16, :], p.ps[4][0:16, 0:SW], AF.Copy)
        c_s = p.inc("scalar", "s0")
        p.wait("tensor", "s0", c_s)
        sa = p.take("af32", SW)
        sb = p.take("af32", SW)
        gk_o = p.take("af32", SW)
        for mc in range(8):
            pb = p.ps[2 + mc % 2]
            p.mm(pb[:, 0:SW], wgk2[0:16, 128 * mc:128 * mc + 128], t1b[0:16, :])
            c_t = p.inc("tensor", "t0")
            p.wait("scalar", "t0", c_t)
            bcol = bgk[:, mc:mc + 1]
            p.act(sa[:, :], pb[:, 0:SW], AF.Abs, bias=bcol)
            p.act(sa[:, :], sa[:, :], AF.Exp, scale=-1.0)
            p.act(sa[:, :], sa[:, :], AF.Ln, bias=1.0)
            p.act(sb[:, :], pb[:, 0:SW], AF.Relu, scale=-1.0,
                  bias=nbgk[:, mc:mc + 1])
            c_s = p.inc("scalar", "s0")
            p.wait("vector", "s0", c_s)
            p.v("tensor_tensor", gk_o[:, :], sa[:, :], sb[:, :], ALU.add)
            c_v = p.inc("vector", "v0")
            p.wait("gpsimd", "v0", c_v)
            p.dma("gpsimd", gkt_hbm[128 * mc:128 * mc + 128, cs], gk_o[:, :])
            p.wait("scalar", "dma_gpsimd", p.cnt["dma_gpsimd"])
            p.wait("tensor", "dma_gpsimd", p.cnt["dma_gpsimd"])
        p.barrier()
    p.reset(pref + "p1")
    p.reset(pref + "w")

    wv = [p.take("wbf", 2048) for _ in range(8)]
    wg = [p.take("wbf", 2048) for _ in range(8)]
    for k in range(8):
        r = slice(128 * k, 128 * k + 128)
        p.dma("gpsimd", wv[k][:, :], W["Wv"][r, :])
        p.dma("gpsimd", wg[k][:, :], W["Wg"][r, :])
    p.barrier()
    p.mark(pref + "p1b")
    for s in range(NS):
        p.reset(pref + "p1b")
        cs = slice(SW * s, SW * s + SW)
        xb = [p.take("abf", SW) for _ in range(8)]
        for i in range(8):
            p.dma("sync", xb[i][:, :], xn_hbm[128 * i:128 * i + 128, cs])
        p.barrier()
        evb = [p.take("abf", SW) for _ in range(2)]
        sdn = [0] * 16
        ddn = [0] * 16
        for mc in range(16):
            pb = p.ps[2 + mc % 2]
            if mc >= 2:
                p.wait("tensor", "s0", sdn[mc - 2])
            for kc in range(8):
                p.mm(pb[:, 0:SW], wg[kc][:, 128 * mc:128 * mc + 128],
                     xb[kc][:, :], start=(kc == 0), stop=(kc == 7))
            c_t = p.inc("tensor", "t0", drain=False)
            p.wait("scalar", "t0", c_t)
            if mc >= 2:
                p.wait("scalar", "dma_gpsimd", ddn[mc - 2])
            p.act(evb[mc % 2][:, :], pb[:, 0:SW], AF.Silu)
            sdn[mc] = p.inc("scalar", "s0")
            p.wait("gpsimd", "s0", sdn[mc])
            p.dma("gpsimd", sg_hbm[128 * mc:128 * mc + 128, cs], evb[mc % 2][:, :])
            ddn[mc] = p.cnt["dma_gpsimd"]
        vbb = [p.take("abf", 512) for _ in range(2)]
        vsd = [0] * 16
        vdd = [0] * 16
        for i in range(16):
            tc, dvc = i // 4, i % 4
            pb = p.ps[4 + i % 2]
            if i >= 2:
                p.wait("tensor", "s0", vsd[i - 2])
            for kc in range(8):
                p.mm(pb[:, 0:512], xb[kc][:, 128 * tc:128 * tc + 128],
                     wv[kc][:, 512 * dvc:512 * dvc + 512],
                     start=(kc == 0), stop=(kc == 7))
            c_t = p.inc("tensor", "t0", drain=False)
            p.wait("scalar", "t0", c_t)
            if i >= 2:
                p.wait("scalar", "dma_gpsimd", vdd[i - 2])
            p.act(vbb[i % 2][:, :], pb[:, 0:512], AF.Copy)
            vsd[i] = p.inc("scalar", "s0")
            p.wait("gpsimd", "s0", vsd[i])
            p.dma("gpsimd",
                  v_hbm[SW * s + 128 * tc:SW * s + 128 * tc + 128,
                        512 * dvc:512 * dvc + 512], vbb[i % 2][:, :])
            vdd[i] = p.cnt["dma_gpsimd"]
        p.barrier()
    p.reset(pref + "p1b")
    p.reset(pref + "w")

    p.mark(pref + "p2")
    NG = 8
    for h in range(NH):
        p.reset(pref + "p2")
        hr = slice(256 * h, 256 * h + 256)
        S0 = p.take("sf32", 512)
        S1 = p.take("sf32", 512)
        Sb0 = p.take("abf", 512)
        Sb1 = p.take("abf", 512)
        o_head = [p.take("obf", T) for _ in range(4)]
        for tl in (S0, S1):
            p.v("memset", tl[:, :], 0.0)
        for tl in (Sb0, Sb1):
            p.v("memset", tl[:, :], 0.0)
        p.mark(pref + "p2g")
        for cg in range(NG):
            p.reset(pref + "p2g")
            gs = slice(SW * cg, SW * cg + SW)
            qg_t = [p.take("abf", SW) for _ in range(2)]
            kg_t = [p.take("abf", SW) for _ in range(2)]
            gk_t = [p.take("af32", SW) for _ in range(2)]
            vg_t = [p.take("abf", 512, rows=64) for _ in range(8)]
            for d2 in range(2):
                rr = slice(256 * h + 128 * d2, 256 * h + 128 * d2 + 128)
                p.dma("sync", qg_t[d2][:, :], q_hbm[rr, gs])
                p.dma("sync", kg_t[d2][:, :], k_hbm[rr, gs])
                p.dma("sync", gk_t[d2][:, :], gkt_hbm[rr, gs])
            for tc in range(8):
                p.dma("sync", vg_t[tc][0:64, :],
                      v_hbm[SW * cg + 64 * tc:SW * cg + 64 * tc + 64,
                            512 * h:512 * h + 512])
            p.barrier()
            for c in range(8):
                ci = slice(64 * c, 64 * c + 64)
                vt = vg_t[c]
                vrows = slice(0, 64)
                bb = [p.take("af32", 64) for _ in range(2)]
                epb = [p.take("af32", 64) for _ in range(2)]
                enb = p.take("af32", 64)
                qgc = [p.take("abf", 64) for _ in range(2)]
                kic = [p.take("abf", 64) for _ in range(2)]
                ksc = [p.take("abf", 64) for _ in range(2)]
                blc = p.take("af32", 8)
                sT = p.take("abf", 64, rows=64)
                kst = p.take("abf", 256, rows=64)
                for d2 in range(2):
                    p.v("tensor_tensor_scan", bb[d2][:, :], gk_t[d2][:, ci],
                        zer[:, :], 0.0, ALU.add, ALU.add)
                c_v = p.inc("vector", "v0")
                p.wait("scalar", "v0", c_v)
                for d2 in range(2):
                    p.act(epb[d2][:, :], bb[d2][:, :], AF.Exp, scale=1.0 / 16.0)
                    p.act(blc[:, 2 + d2:3 + d2], bb[d2][:, 63:64], AF.Exp,
                          scale=-1.0 / 16.0)
                c_s = p.inc("scalar", "s0")
                p.wait("vector", "s0", c_s)
                for d2 in range(2):
                    p.v("tensor_tensor", kic[d2][:, :], kg_t[d2][:, ci],
                        epb[d2][:, :], ALU.mult)
                    p.v("reciprocal", enb[:, :], epb[d2][:, :])
                    p.v("tensor_tensor", qgc[d2][:, :], qg_t[d2][:, ci],
                        enb[:, :], ALU.mult)
                    p.v("tensor_scalar", ksc[d2][:, :], kic[d2][:, :],
                        blc[:, 2 + d2:3 + d2], None, ALU.mult)
                c_v = p.inc("vector", "v0")
                p.wait("tensor", "v0", c_v)
                for d2 in range(2):
                    p.mm(p.ps[4][0:64, 0:64], kic[d2][:, :], qgc[d2][:, :],
                         start=(d2 == 0), stop=(d2 == 1))
                c_t = p.inc("tensor", "t0")
                p.wait("vector", "t0", c_t)
                p.v("tensor_tensor", sT[0:64, :], p.ps[4][0:64, 0:64],
                    consts["maskT"][0:64, :], ALU.mult)
                c_v = p.inc("vector", "v0")
                if debug and h == 0 and cg == 0 and c == 0:
                    p.wait("gpsimd", "v0", c_v)
                    p.dma("gpsimd", dbg_sT[0:64, :], sT[0:64, :])
                    p.dma("gpsimd", dbg_q[:, :], qgc[0][:, :])
                    p.dma("gpsimd", dbg_k[:, :], kic[0][:, :])
                    p.dma("gpsimd", dbg_b[:, :], bb[0][:, :])
                    p.dma("gpsimd", dbg_eb[:, :], eb[:, :])
                    p.dma("gpsimd", dbg_qraw[:, :], qg_t[0][:, 0:64])
                    p.wait_dma("vector", "gpsimd")
                p.wait("tensor", "v0", c_v)
                for dvc in range(4):
                    p.mm(p.ps[dvc][:, 0:64],
                         vt[vrows, 128 * dvc:128 * dvc + 128], sT[0:64, :],
                         start=True, stop=False)
                    for d2, Sb in ((0, Sb0), (1, Sb1)):
                        p.mm(p.ps[dvc][:, 0:64],
                             Sb[:, 128 * dvc:128 * dvc + 128], qgc[d2][:, :],
                             start=False, stop=(d2 == 1))
                c_t = p.inc("tensor", "t0")
                p.wait("scalar", "t0", c_t)
                for dvc in range(4):
                    p.act(o_head[dvc][:, SW * cg + 64 * c:SW * cg + 64 * c + 64],
                          p.ps[dvc][:, 0:64], AF.Copy)
                c_s = p.inc("scalar", "s0")
                p.wait("tensor", "s0", c_s)
                for d2 in range(2):
                    p.trans(p.psT[0:64, 128 * d2:128 * d2 + 128], ksc[d2][:, :],
                            identb[:, :])
                c_t = p.inc("tensor", "t0")
                p.wait("scalar", "t0", c_t)
                p.act(kst[0:64, :], p.psT[0:64, 0:256], AF.Copy)
                c_s = p.inc("scalar", "s0")
                p.wait("tensor", "s0", c_s)
                for d2 in range(2):
                    p.mm(p.ps[5 + d2][:, :], kst[0:64, 128 * d2:128 * d2 + 128],
                         vt[vrows, :], start=True, stop=True)
                c_t = p.inc("tensor", "t0")
                p.wait("vector", "t0", c_t)
                for d2, Sf in ((0, S0), (1, S1)):
                    p.v("scalar_tensor_tensor", Sf[:, :], Sf[:, :],
                        blc[:, 2 + d2:3 + d2], p.ps[5 + d2][:, :],
                        ALU.mult, ALU.add)
                c_v = p.inc("vector", "v0")
                p.wait("scalar", "v0", c_v)
                for d2, (Sf, Sb) in ((0, (S0, Sb0)), (1, (S1, Sb1))):
                    p.act(Sb[:, :], Sf[:, :], AF.Copy)
                c_s = p.inc("scalar", "s0")
                p.wait("tensor", "s0", c_s)
                p.wait("vector", "s0", c_s)
            p.barrier()
        p.mark(pref + "p3")
        for s in range(NS):
            p.reset(pref + "p3")
            cs = slice(SW * s, SW * s + SW)
            if debug:
                for dvc in range(4):
                    p.dma("gpsimd", op_hbm[512 * h + 128 * dvc:512 * h + 128 * dvc + 128, cs],
                          o_head[dvc][:, cs])
            sqb = p.take("abf", SW)
            sgb = [p.take("abf", SW) for _ in range(4)]
            ob = p.take("abf", SW)
            for dvc in range(4):
                p.dma("sync", sgb[dvc][:, :],
                      sg_hbm[512 * h + 128 * dvc:512 * h + 128 * dvc + 128, cs])
            p.barrier()
            for dvc in range(4):
                p.act(sqb[:, :], o_head[dvc][:, cs], AF.Square)
                c_s = p.inc("scalar", "s0")
                p.wait("tensor", "s0", c_s)
                p.mm(p.ps[0][0:1, 0:SW], consts["ones_col"][:, 0:1], sqb[:, :],
                     start=(dvc == 0), stop=(dvc == 3))
                c_t = p.inc("tensor", "t0")
                p.wait("scalar", "t0", c_t)
            p.act(consts["row_a"][0:1, :SW], p.ps[0][0:1, 0:SW], AF.Ln,
                  scale=1.0 / 512.0, bias=1e-5)
            p.act(consts["rowbf"][0:1, :SW], consts["row_a"][0:1, :SW], AF.Exp,
                  scale=-0.5)
            c_s = p.inc("scalar", "s0")
            p.wait("tensor", "s0", c_s)
            p.mm(p.ps[1][:, 0:SW], consts["ones_row"], consts["rowbf"][0:1, :SW])
            c_t = p.inc("tensor", "t0")
            p.wait("vector", "t0", c_t)
            for dvc in range(4):
                p.v("tensor_tensor", ob[:, :], o_head[dvc][:, cs], p.ps[1][:, 0:SW],
                    ALU.mult)
                c_v = p.inc("vector", "v0")
                p.wait("scalar", "v0", c_v)
                p.act(ob[:, :], ob[:, :], AF.Copy, scale=gnw[:, dvc:dvc + 1])
                c_s = p.inc("scalar", "s0")
                p.wait("vector", "s0", c_s)
                p.v("tensor_tensor", ob[:, :], ob[:, :], sgb[dvc][:, :], ALU.mult)
                c_v = p.inc("vector", "v0")
                p.wait("gpsimd", "v0", c_v)
                p.dma("gpsimd", og_hbm[512 * h + 128 * dvc:512 * h + 128 * dvc + 128,
                                       cs], ob[:, :])
                p.wait("vector", "dma_gpsimd", p.cnt["dma_gpsimd"])
                p.wait("scalar", "dma_gpsimd", p.cnt["dma_gpsimd"])
            p.barrier()
        p.reset(pref + "p3")
    p.reset(pref + "p2")

    p.mark(pref + "p4w")
    wo = [p.take("wbf", 1024) for _ in range(16)]
    for k in range(16):
        p.dma("gpsimd", wo[k][:, :], W["Wo"][128 * k:128 * k + 128, :])
    p.barrier()
    p.mark(pref + "p4")
    for s in range(NS):
        p.reset(pref + "p4")
        cs = slice(SW * s, SW * s + SW)
        ogb = [p.take("abf", SW) for _ in range(16)]
        yv = [p.take("af32", SW) for _ in range(8)]
        yb = [p.take("abf", SW) for _ in range(8)]
        obb = [p.take("abf", SW) for _ in range(8)]
        xf = [p.take("af32", SW) for _ in range(8)]
        for k in range(16):
            p.dma("sync", ogb[k][:, :], og_hbm[128 * k:128 * k + 128, cs])
        for i in range(8):
            p.dma("sync", xf[i][:, :], xin[128 * i:128 * i + 128, cs])
        p.barrier()
        for half in range(2):
            for dm in range(4):
                mc = 4 * half + dm
                for kc in range(16):
                    p.mm(p.ps[dm][:, 0:SW], wo[kc][:, 128 * mc:128 * mc + 128],
                         ogb[kc][:, :], start=(kc == 0), stop=(kc == 15))
            c_t = p.inc("tensor", "t0")
            p.wait("scalar", "t0", c_t)
            for dm in range(4):
                mc = 4 * half + dm
                p.act(yv[mc][:, :], p.ps[dm][:, 0:SW], AF.Copy)
            c_s = p.inc("scalar", "s0")
            p.wait("tensor", "s0", c_s)
        p.wait("vector", "s0", p.cnt["s0"])
        emit_ln(p, yv, yb, SW, consts, obb, p.ps[4], p.ps[5])
        c_v = p.inc("vector", "v0")
        p.wait("vector", "v0", c_v)
        for i in range(8):
            p.v("tensor_tensor", xf[i][:, :], xf[i][:, :], obb[i][:, :], ALU.add)
        c_v = p.inc("vector", "v0")
        p.wait("gpsimd", "v0", c_v)
        for i in range(8):
            p.dma("gpsimd", xout[128 * i:128 * i + 128, cs], xf[i][:, :])
        p.barrier()
    p.reset(pref + "p4")
    p.reset(pref + "p4w")


def host_gla_weights(inputs):
    gnwr = np.zeros((128, 4), dtype=np.float32)
    gnw = inputs["L1_attn_g_norm_w"]
    for c in range(4):
        gnwr[:, c] = gnw[128 * c:128 * c + 128]
    bgk = inputs["L1_attn_bgk2"]
    bgk2r = np.ascontiguousarray(bgk.reshape(8, 128).T)
    identb = bf16(np.eye(128, dtype=np.float32))
    maskT = np.zeros((64, 64), dtype=np.float32)
    for j in range(64):
        maskT[j, j:] = 1.0
    return {
        "Wq": np.ascontiguousarray(inputs["L1_attn_Wq"]),
        "Wk": np.ascontiguousarray(inputs["L1_attn_Wk"]),
        "Wv": np.ascontiguousarray(inputs["L1_attn_Wv"]),
        "Wg": np.ascontiguousarray(inputs["L1_attn_Wg"]),
        "Wgk1": np.ascontiguousarray(inputs["L1_attn_Wgk1"]),
        "Wgk2": np.ascontiguousarray(inputs["L1_attn_Wgk2"]),
        "Wo": np.ascontiguousarray(inputs["L1_attn_Wo"]),
        "bgk2r": bgk2r,
        "gnwr": gnwr,
        "ident_bf": identb,
        "maskT": maskT,
    }


def build_gla_prog(debug=False):
    p = Prog("gla")
    p.arena("cst", 128, F32)
    p.arena("cbf", 11264, BF16)
    p.arena("cf32", 1152, F32)
    p.arena("wbf", 33280, BF16)
    p.arena("abf", 18432, BF16)
    p.arena("af32", 19 * 512, F32)
    p.arena("sf32", 1024, F32)
    p.arena("obf", 4 * T, BF16)
    xin = p.inp("xin", (D, T), F32)
    xout = p.out("xout", (D, T), F32)
    W = {
        "Wq": p.inp("Wq", (1024, 1024), F32),
        "Wk": p.inp("Wk", (1024, 1024), F32),
        "Wv": p.inp("Wv", (1024, 2048), F32),
        "Wg": p.inp("Wg", (1024, 2048), F32),
        "Wgk1": p.inp("Wgk1", (1024, 16), F32),
        "Wgk2": p.inp("Wgk2", (16, 1024), F32),
        "Wo": p.inp("Wo", (2048, 1024), F32),
        "bgk2r": p.inp("bgk2r", (128, 8), F32),
        "gnwr": p.inp("gnwr", (128, 4), F32),
        "ident_bf": p.inp("ident_bf", (128, 128), BF16),
        "maskT": p.inp("maskT", (64, 64), F32),
    }
    sec_gla(p, xin, xout, W, debug=debug)
    return p.build()


def sec_mamba(p, xin, xout, W, pref="mmb", debug=False):
    NS = 8
    SW = T // NS
    DI = 2048
    kind = "ExternalOutput" if debug else "Internal"

    xn_hbm = p.dram(pref + "_xn", (D, T), BF16, kind=kind)
    u_hbm = p.dram(pref + "_u", (DI, T), BF16, kind=kind)
    sz_hbm = p.dram(pref + "_sz", (DI, T), BF16, kind=kind)
    yh_hbm = p.dram(pref + "_yh", (DI, T), BF16, kind=kind)
    if debug:
        dt_dbg = p.out(pref + "_dt", (DI, T), F32)
        dbl_dbg = p.out(pref + "_dbl", (128, T), F32)
        d_hda = p.out(pref + "_hda", (128, 512), F32)
        d_hb = p.out(pref + "_hb", (128, 512), F32)
        d_hh = p.out(pref + "_hh", (128, 512), F32)
        d_Br = p.out(pref + "_Br", (128, 512), F32)
        d_Cr = p.out(pref + "_Cr", (128, 512), F32)

    consts = std_consts(p)
    p.constv(1.0)

    cwv = p.take("cf32", 64)
    cbv = p.take("cf32", 16)
    bdt = p.take("cf32", 16)
    dsk = p.take("cf32", 16)
    Atl = p.take("cf32", 256)
    bcsel = p.take("cbf", 4096, rows=32)
    p.dma("sync", cwv[:, :], W["conv_wr"][:, :])
    p.dma("sync", cbv[:, :], W["conv_br"][:, :])
    p.dma("sync", bdt[:, :], W["b_dtr"][:, :])
    p.dma("sync", dsk[:, :], W["D_skipr"][:, :])
    p.dma("sync", Atl[:, :], W["A_logr"][:, :])
    p.dma("gpsimd", bcsel[0:32, :], W["bcsel"][:, :])
    p.barrier()
    p.act(Atl[:, :], Atl[:, :], AF.Exp)
    c_s = p.inc("scalar", "s0")
    p.wait("vector", "s0", c_s)
    p.v("tensor_scalar_mul", Atl[:, :], Atl[:, :], -1.0)
    p.barrier()

    dbl = p.take("dbl", T, rows=96)

    p.mark(pref + "w1")
    winT = [p.take("wbf", 2048) for _ in range(8)]
    wxT = [p.take("wbf", 96) for _ in range(16)]
    for k in range(8):
        p.dma("gpsimd", winT[k][:, :], W["W_inT"][128 * k:128 * k + 128, 0:2048])
    for k in range(16):
        p.dma("gpsimd", wxT[k][:, :], W["W_xprojT"][128 * k:128 * k + 128, :])
    ptail = p.take("af32", 48)
    p.v("memset", ptail[:, :], 0.0)
    p.barrier()
    p.mark(pref + "m1")
    for s in range(NS):
        p.reset(pref + "m1")
        cs = slice(SW * s, SW * s + SW)
        xf = [p.take("af32", SW) for _ in range(8)]
        xb = [p.take("abf", SW) for _ in range(8)]
        for i in range(8):
            p.dma("sync", xf[i][:, :], xin[128 * i:128 * i + 128, cs])
        p.barrier()
        emit_ln(p, xf, xb, SW, consts, xb, p.ps[0], p.ps[1])
        c_v = p.inc("vector", "v0")
        p.wait("tensor", "v0", c_v)
        p.wait("gpsimd", "v0", c_v)
        for i in range(8):
            p.dma("gpsimd", xn_hbm[128 * i:128 * i + 128, cs], xb[i][:, :])
        ub = [p.take("abf", SW) for _ in range(16)]
        xib = [p.take("af32", 520) for _ in range(2)]
        xcb = [p.take("af32", SW) for _ in range(2)]
        msd = [0] * 16
        mvd = [0] * 16
        for mc in range(16):
            b = mc % 2
            pb = p.ps[2 + b]
            xi = xib[b]
            xc = xcb[b]
            if mc >= 2:
                p.wait("tensor", "s0", msd[mc - 2])
            for kc in range(8):
                p.mm(pb[:, 0:SW], winT[kc][:, 128 * mc:128 * mc + 128],
                     xb[kc][:, :], start=(kc == 0), stop=(kc == 7))
            c_t = p.inc("tensor", "t0", drain=False)
            p.wait("scalar", "t0", c_t)
            cb = mc
            if mc >= 2:
                p.wait("scalar", "v0", mvd[mc - 2])
            p.act(xi[:, 3:3 + SW], pb[:, 0:SW], AF.Copy)
            p.act(xi[:, 0:3], ptail[:, 3 * cb:3 * cb + 3], AF.Copy)
            p.act(ptail[:, 3 * cb:3 * cb + 3], xi[:, SW:SW + 3], AF.Copy)
            p.act(xc[:, :], xi[:, 0:SW], AF.Copy, scale=cwv[:, 4 * cb:4 * cb + 1])
            msd[mc] = p.inc("scalar", "s0")
            p.wait("vector", "s0", msd[mc])
            for k in (1, 2, 3):
                p.v("scalar_tensor_tensor", xc[:, :], xi[:, k:k + SW],
                    cwv[:, 4 * cb + k:4 * cb + k + 1], xc[:, :],
                    ALU.mult, ALU.add)
            mvd[mc] = p.inc("vector", "v0")
            p.wait("scalar", "v0", mvd[mc])
            p.act(ub[cb][:, :], xc[:, :], AF.Silu, bias=cbv[:, cb:cb + 1])
            c_s = p.inc("scalar", "s0")
            p.wait("gpsimd", "s0", c_s)
            p.dma("gpsimd", u_hbm[128 * cb:128 * cb + 128, cs], ub[cb][:, :])
        p.wait("tensor", "s0", p.cnt["s0"])
        for kc in range(16):
            p.mm(p.ps[4][0:96, 0:SW], wxT[kc][:, :], ub[kc][:, :],
                 start=(kc == 0), stop=(kc == 15))
        c_t = p.inc("tensor", "t0")
        p.wait("scalar", "t0", c_t)
        p.act(dbl[0:96, cs], p.ps[4][0:96, 0:SW], AF.Copy)
        c_s = p.inc("scalar", "s0")
        p.barrier()
    p.reset(pref + "m1")
    p.reset(pref + "w1")

    wz = [p.take("wbf", 2048) for _ in range(8)]
    for k in range(8):
        p.dma("gpsimd", wz[k][:, :], W["W_inT"][128 * k:128 * k + 128, 2048:4096])
    p.barrier()
    p.mark(pref + "m1b")
    for s in range(NS):
        p.reset(pref + "m1b")
        cs = slice(SW * s, SW * s + SW)
        xb = [p.take("abf", SW) for _ in range(8)]
        ev = p.take("abf", SW)
        for i in range(8):
            p.dma("sync", xb[i][:, :], xn_hbm[128 * i:128 * i + 128, cs])
        p.barrier()
        zsd = [0] * 16
        zdd = [0] * 16
        evb2 = [p.take("abf", SW) for _ in range(2)]
        for mc in range(16):
            pb = p.ps[2 + mc % 2]
            if mc >= 2:
                p.wait("tensor", "s0", zsd[mc - 2])
            for kc in range(8):
                p.mm(pb[:, 0:SW], wz[kc][:, 128 * mc:128 * mc + 128],
                     xb[kc][:, :], start=(kc == 0), stop=(kc == 7))
            c_t = p.inc("tensor", "t0", drain=False)
            p.wait("scalar", "t0", c_t)
            if mc >= 2:
                p.wait("scalar", "dma_gpsimd", zdd[mc - 2])
            p.act(evb2[mc % 2][:, :], pb[:, 0:SW], AF.Silu)
            zsd[mc] = p.inc("scalar", "s0")
            p.wait("gpsimd", "s0", zsd[mc])
            p.dma("gpsimd", sz_hbm[128 * mc:128 * mc + 128, cs], evb2[mc % 2][:, :])
            zdd[mc] = p.cnt["dma_gpsimd"]
        p.barrier()
    p.reset(pref + "m1b")
    p.reset(pref + "w1")
    if debug:
        dbf = p.take("af32", T, rows=96)
        p.act(dbf[0:96, :], dbl[0:96, :], AF.Copy)
        c_s = p.inc("scalar", "s0")
        p.wait("gpsimd", "s0", c_s)
        p.dma("gpsimd", dbl_dbg[0:96, :], dbf[0:96, :])
        p.barrier()
        p.reset(pref + "w1")

    wdtT = p.take("wbf", 2048, rows=64)
    p.dma("gpsimd", wdtT[0:64, :], W["W_dtT"][:, :])
    hcar = p.take("cf32", 256)
    p.v("memset", hcar[:, :], 0.0)
    zer = p.take("cf32", SW)
    p.v("memset", zer[:, :], 0.0)
    p.barrier()
    p.mark(pref + "m2")
    for tc in range(NS):
        p.reset(pref + "m2")
        cs = slice(SW * tc, SW * tc + SW)
        ub = [p.take("abf", SW) for _ in range(16)]
        for cb in range(16):
            p.dma("sync", ub[cb][:, :], u_hbm[128 * cb:128 * cb + 128, cs])
        dt_t = [p.take("abf", SW) for _ in range(16)]
        dtu = [p.take("abf", SW) for _ in range(16)]
        sa = p.take("af32", SW)
        sb2 = p.take("af32", SW)
        p.wait_dma("tensor", "sync")
        sabb = [p.take("af32", SW) for _ in range(2)]
        sbbb = [p.take("af32", SW) for _ in range(2)]
        dsd = [0] * 16
        dvd = [0] * 16
        for cb in range(16):
            b = cb % 2
            pb = p.ps[2 + b]
            if cb >= 2:
                p.wait("tensor", "s0", dsd[cb - 2])
            p.mm(pb[:, 0:SW], wdtT[0:64, 128 * cb:128 * cb + 128],
                 dbl[0:64, cs], start=True, stop=True)
            c_t = p.inc("tensor", "t0", drain=False)
            p.wait("scalar", "t0", c_t)
            if cb >= 2:
                p.wait("scalar", "v0", dvd[cb - 2])
            bcol = bdt[:, cb:cb + 1]
            p.act(sabb[b][:, :], pb[:, 0:SW], AF.Abs, bias=bcol)
            p.act(sabb[b][:, :], sabb[b][:, :], AF.Exp, scale=-1.0)
            p.act(sabb[b][:, :], sabb[b][:, :], AF.Ln, bias=1.0)
            p.act(sbbb[b][:, :], pb[:, 0:SW], AF.Relu, bias=bcol)
            dsd[cb] = p.inc("scalar", "s0")
            p.wait("vector", "s0", dsd[cb])
            p.v("tensor_tensor", dt_t[cb][:, :], sabb[b][:, :], sbbb[b][:, :],
                ALU.add)
            p.v("tensor_tensor", dtu[cb][:, :], dt_t[cb][:, :], ub[cb][:, :],
                ALU.mult)
            dvd[cb] = p.inc("vector", "v0")
            if debug:
                p.wait("gpsimd", "v0", dvd[cb])
                p.dma("gpsimd", dt_dbg[128 * cb:128 * cb + 128, cs],
                      dt_t[cb][:, :])
        p.wait("tensor", "v0", dvd[15])
        bc0 = p.take("abf", SW, rows=32)
        p.act(bc0[0:32, :], dbl[64:96, cs], AF.Copy)
        c_s = p.inc("scalar", "s0")
        p.wait("tensor", "s0", c_s)
        y_t = [p.take("af32", SW) for _ in range(16)]
        for cb in range(16):
            p.v("memset", y_t[cb][:, :], 0.0)
        c_v = p.inc("vector", "v0")
        hdab = [p.take("af32", SW) for _ in range(2)]
        hb = p.take("af32", SW)
        hh = p.take("af32", SW)
        ht2 = hb
        Brep = p.take("abf", SW)
        Crep = p.take("af32", SW)
        for s in range(16):
            p.wait("tensor", "v0", p.cnt["v0"])
            p.mm(p.ps[4][:, 0:SW], bcsel[0:32, 128 * s:128 * s + 128],
                 bc0[0:32, :], start=True, stop=True)
            p.mm(p.ps[5][:, 0:SW], bcsel[0:32, 2048 + 128 * s:2048 + 128 * s + 128],
                 bc0[0:32, :], start=True, stop=True)
            c_t = p.inc("tensor", "t0")
            p.wait("scalar", "t0", c_t)
            p.act(Brep[:, :], p.ps[4][:, 0:SW], AF.Copy)
            p.act(Crep[:, :], p.ps[5][:, 0:SW], AF.Copy)
            c_s = p.inc("scalar", "s0")
            p.wait("vector", "s0", c_s)
            scv = [0] * 16
            scs = [0] * 16
            for cb in range(16):
                bq = cb % 2
                if cb >= 2:
                    p.wait("scalar", "v0", scv[cb - 2])
                p.act(hdab[bq][:, :], dt_t[cb][:, :], AF.Exp,
                      scale=Atl[:, 16 * cb + s:16 * cb + s + 1])
                scs[cb] = p.inc("scalar", "s0")
                p.wait("vector", "s0", scs[cb])
                p.v("tensor_tensor", hb[:, :], dtu[cb][:, :], Brep[:, :], ALU.mult)
                p.v("tensor_tensor_scan", hh[:, :], hdab[bq][:, :], hb[:, :],
                    hcar[:, 16 * s + cb:16 * s + cb + 1], ALU.mult, ALU.add)
                p.v("tensor_tensor", ht2[:, :], hh[:, :], Crep[:, :], ALU.mult)
                p.v("tensor_tensor", y_t[cb][:, :], y_t[cb][:, :], ht2[:, :],
                    ALU.add)
                p.v("tensor_copy", hcar[:, 16 * s + cb:16 * s + cb + 1],
                    hh[:, 511:512])
                c_v = scv[cb] = p.inc("vector", "v0")
                if debug and tc == 0 and s == 0 and cb == 0:
                    p.wait("gpsimd", "v0", c_v)
                    p.dma("gpsimd", d_hda[:, :], hda[:, :])
                    p.dma("gpsimd", d_hb[:, :], hb[:, :])
                    p.dma("gpsimd", d_hh[:, :], hh[:, :])
                    p.dma("gpsimd", d_Br[:, :], Brep[:, :])
                    p.dma("gpsimd", d_Cr[:, :], Crep[:, :])
                    p.wait("scalar", "dma_gpsimd", p.cnt["dma_gpsimd"])
                    p.wait("vector", "dma_gpsimd", p.cnt["dma_gpsimd"])
        if debug and tc == 0:
            p.wait("gpsimd", "v0", p.cnt["v0"])
            p.dma("gpsimd", d_hb[:, :], y_t[0][:, :])
            p.wait("vector", "dma_gpsimd", p.cnt["dma_gpsimd"])
        szb = p.take("abf", SW)
        yhb = p.take("abf", SW)
        for cb in range(16):
            if cb > 0:
                p.wait("sync", "v0", p.cnt["v0"])
            p.dma("sync", szb[:, :], sz_hbm[128 * cb:128 * cb + 128, cs])
            p.act(sa[:, :], ub[cb][:, :], AF.Copy, scale=dsk[:, cb:cb + 1])
            c_s = p.inc("scalar", "s0")
            p.wait("vector", "s0", c_s)
            p.wait_dma("vector", "sync")
            p.v("tensor_tensor", y_t[cb][:, :], y_t[cb][:, :], sa[:, :], ALU.add)
            p.v("tensor_tensor", yhb[:, :], y_t[cb][:, :], szb[:, :], ALU.mult)
            c_v = p.inc("vector", "v0")
            p.wait("gpsimd", "v0", c_v)
            p.dma("gpsimd", yh_hbm[128 * cb:128 * cb + 128, cs], yhb[:, :])
            p.wait("scalar", "dma_gpsimd", p.cnt["dma_gpsimd"])
            p.wait("vector", "dma_gpsimd", p.cnt["dma_gpsimd"])
        p.barrier()
    p.reset(pref + "m2")

    woT = [p.take("wbf", 1024) for _ in range(16)]
    for k in range(16):
        p.dma("gpsimd", woT[k][:, :], W["W_outT"][128 * k:128 * k + 128, :])
    p.barrier()
    p.mark(pref + "m3")
    for s in range(NS):
        p.reset(pref + "m3")
        cs = slice(SW * s, SW * s + SW)
        yhb = [p.take("abf", SW) for _ in range(16)]
        yv = [p.take("af32", SW) for _ in range(8)]
        yb = [p.take("abf", SW) for _ in range(8)]
        obb = [p.take("abf", SW) for _ in range(8)]
        xf = [p.take("af32", SW) for _ in range(8)]
        for k in range(16):
            p.dma("sync", yhb[k][:, :], yh_hbm[128 * k:128 * k + 128, cs])
        for i in range(8):
            p.dma("sync", xf[i][:, :], xin[128 * i:128 * i + 128, cs])
        p.barrier()
        for half in range(2):
            for dm in range(4):
                mc = 4 * half + dm
                for kc in range(16):
                    p.mm(p.ps[dm][:, 0:SW], woT[kc][:, 128 * mc:128 * mc + 128],
                         yhb[kc][:, :], start=(kc == 0), stop=(kc == 15))
            c_t = p.inc("tensor", "t0")
            p.wait("scalar", "t0", c_t)
            for dm in range(4):
                mc = 4 * half + dm
                p.act(yv[mc][:, :], p.ps[dm][:, 0:SW], AF.Copy)
            c_s = p.inc("scalar", "s0")
            p.wait("tensor", "s0", c_s)
        p.wait("vector", "s0", p.cnt["s0"])
        emit_ln(p, yv, yb, SW, consts, obb, p.ps[4], p.ps[5])
        c_v = p.inc("vector", "v0")
        p.wait("vector", "v0", c_v)
        for i in range(8):
            p.v("tensor_tensor", xf[i][:, :], xf[i][:, :], obb[i][:, :], ALU.add)
        c_v = p.inc("vector", "v0")
        p.wait("gpsimd", "v0", c_v)
        for i in range(8):
            p.dma("gpsimd", xout[128 * i:128 * i + 128, cs], xf[i][:, :])
        p.barrier()
    p.reset(pref + "m3")


def host_mamba_weights(inputs):
    conv_w = inputs["L0_attn_conv_w"]
    conv_wr = np.ascontiguousarray(
        conv_w.reshape(16, 128, 4).transpose(1, 0, 2).reshape(128, 64))
    def cols(v):
        return np.ascontiguousarray(v.reshape(16, 128).T)
    A_logr = np.ascontiguousarray(
        inputs["L0_attn_A_log"].reshape(16, 128, 16).transpose(1, 0, 2)
        .reshape(128, 256))
    bcsel = np.zeros((32, 4096), dtype=np.float32)
    for s in range(16):
        bcsel[s, 128 * s:128 * s + 128] = 1.0
        bcsel[16 + s, 2048 + 128 * s:2048 + 128 * s + 128] = 1.0
    return {
        "W_inT": np.ascontiguousarray(inputs["L0_attn_W_in"].T),
        "W_xprojT": np.ascontiguousarray(inputs["L0_attn_W_xproj"].T),
        "W_dtT": np.ascontiguousarray(inputs["L0_attn_W_dt"].T),
        "W_outT": np.ascontiguousarray(inputs["L0_attn_W_out"].T),
        "conv_wr": conv_wr,
        "conv_br": cols(inputs["L0_attn_conv_b"]),
        "b_dtr": cols(inputs["L0_attn_b_dt"]),
        "D_skipr": cols(inputs["L0_attn_D_skip"]),
        "A_logr": A_logr,
        "bcsel": bcsel,
    }


def build_mamba_prog(debug=False):
    p = Prog("mmb")
    p.arena("cst", 128, F32)
    p.arena("cbf", 10240, BF16)
    p.arena("cf32", 2304, F32)
    p.arena("dbl", T, BF16)
    p.arena("wbf", 18432, BF16)
    p.arena("abf", 28672, BF16)
    p.arena("af32", 16384, F32)
    xin = p.inp("xin", (D, T), F32)
    xout = p.out("xout", (D, T), F32)
    W = {
        "W_inT": p.inp("W_inT", (1024, 4096), F32),
        "W_xprojT": p.inp("W_xprojT", (2048, 96), F32),
        "W_dtT": p.inp("W_dtT", (64, 2048), F32),
        "W_outT": p.inp("W_outT", (2048, 1024), F32),
        "conv_wr": p.inp("conv_wr", (128, 64), F32),
        "conv_br": p.inp("conv_br", (128, 16), F32),
        "b_dtr": p.inp("b_dtr", (128, 16), F32),
        "D_skipr": p.inp("D_skipr", (128, 16), F32),
        "A_logr": p.inp("A_logr", (128, 256), F32),
        "bcsel": p.inp("bcsel", (32, 4096), F32),
    }
    sec_mamba(p, xin, xout, W, debug=debug)
    return p.build()


def sec_l0ffn(p, xin, xout, W, pref="trf", debug=False, depth=6,
              do_attn=True, do_ffn=True):
    NS = 8
    SW = T // NS
    DEPTH = depth
    DIH = 2816
    kind = "ExternalOutput" if debug else "Internal"

    ra = p.dram(pref + "_ra", (D, T), F32, kind=kind)
    rb = p.dram(pref + "_rb", (D, T), F32, kind=kind)
    xnf_hbm = p.dram(pref + "_xnf", (D, T), BF16, kind=kind)
    hf_hbm = p.dram(pref + "_hf", (DIH, T), BF16, kind=kind)

    consts = std_consts(p)
    identb = p.take("cbf", 128)
    p.dma("sync", identb[:, :], W["ident_bf"][:, :])
    W["_prh_tile"] = p.take("cbf", 128)
    p.dma("gpsimd", W["_prh_tile"][:, :], W["Prh"][:, :])
    maskadd = p.take("cf32", 384)
    p.dma("sync", maskadd[:, :], W["maskadd"][:, :])
    maskmulT = p.take("cbf", 384)
    p.dma("gpsimd", maskmulT[:, :], W["maskmulT"][:, :])
    ones16 = p.take("cbf", 128)
    p.dma("gpsimd", ones16[:, :], W["ones16"][:, :])
    sel16 = p.take("cbf", 1024, rows=16)
    p.dma("gpsimd", sel16[0:16, :], W["sel16"][:, :])
    hsel = p.take("cbf", 2048, rows=16)
    p.dma("gpsimd", hsel[0:16, :], W["hsel"][:, :])
    qsc = p.take("cf32", 24)
    p.dma("sync", qsc[:, :], W["qk_scales"][:, :])
    lgb = p.take("cf32", 96)
    p.dma("sync", lgb[:, :], W["ln_gb"][:, :])
    b2c = p.take("cf32", 48)
    p.dma("sync", b2c[:, :], W["b2r"][:, :])
    b1c = p.take("cf32", 264)
    p.dma("sync", b1c[:, :], W["b1r"][:, :])
    p.barrier()

    p.mark(pref + "i0")
    for s in range(NS):
        p.reset(pref + "i0")
        cs = slice(SW * s, SW * s + SW)
        xf = [p.take("af32", SW) for _ in range(8)]
        xb = [p.take("abf", SW) for _ in range(8)]
        of = [p.take("af32", SW) for _ in range(8)]
        for i in range(8):
            p.dma("sync", xf[i][:, :], xin[128 * i:128 * i + 128, cs])
        p.barrier()
        emit_ln(p, xf, xb, SW, consts, xb, p.ps[0], p.ps[1],
                out_f32_tiles=of)
        c_v = p.inc("vector", "v0")
        p.wait("gpsimd", "v0", c_v)
        for i in range(8):
            p.dma("gpsimd", ra[128 * i:128 * i + 128, cs], of[i][:, :])
        p.barrier()
    p.reset(pref + "i0")

    cur, nxt = ra, rb
    for li in range(DEPTH):
        if do_attn:
            attn_layer(p, li, cur, nxt, W, consts, identb, maskadd, maskmulT,
                       ones16, sel16, hsel, qsc, pref)
            cur, nxt = nxt, cur
        if do_ffn:
            ffn_layer(p, li, cur, nxt, W, consts, lgb, b1c, b2c, xnf_hbm, hf_hbm,
                      pref)
            cur, nxt = nxt, cur

    p.mark(pref + "f0")
    for s in range(NS):
        p.reset(pref + "f0")
        cs = slice(SW * s, SW * s + SW)
        yv = [p.take("af32", SW) for _ in range(8)]
        yb = [p.take("abf", SW) for _ in range(8)]
        ob = [p.take("abf", SW) for _ in range(8)]
        xf = [p.take("af32", SW) for _ in range(8)]
        for i in range(8):
            p.dma("sync", xf[i][:, :], xin[128 * i:128 * i + 128, cs])
            p.dma("sync", yv[i][:, :], cur[128 * i:128 * i + 128, cs])
        p.barrier()
        emit_ln(p, yv, yb, SW, consts, ob, p.ps[0], p.ps[1])
        c_v = p.inc("vector", "v0")
        p.wait("vector", "v0", c_v)
        for i in range(8):
            p.v("tensor_tensor", xf[i][:, :], xf[i][:, :], ob[i][:, :], ALU.add)
        c_v = p.inc("vector", "v0")
        p.wait("gpsimd", "v0", c_v)
        for i in range(8):
            p.dma("gpsimd", xout[128 * i:128 * i + 128, cs], xf[i][:, :])
        p.barrier()
    p.reset(pref + "f0")


def load_shifted(p, tiles, src, s, SW):
    cs = slice(SW * s, SW * s + SW)
    for i in range(4):
        p.dma("sync", tiles[i][:, :], src[128 * i:128 * i + 128, cs])
    for i in range(4, 8):
        r = slice(128 * i, 128 * i + 128)
        if s == 0:
            p.v("memset", tiles[i][:, 0:1], 0.0)
            p.dma("sync", tiles[i][:, 1:SW], src[r, 0:SW - 1])
        else:
            p.dma("sync", tiles[i][:, :], src[r, SW * s - 1:SW * s + SW - 1])


def attn_layer(p, li, cur, nxt, W, consts, identb, maskadd, maskmulT,
               ones16, sel16, hsel, qsc, pref):
    NS = 8
    SW = T // NS
    p.mark(pref + "aw")
    wqkv = [p.take("wbf", 3072) for _ in range(8)]
    wo = [p.take("wbf", 1024) for _ in range(8)]
    for k in range(8):
        p.dma("gpsimd", wqkv[k][:, :], W["Wqkv"][li, 128 * k:128 * k + 128, :])
        p.dma("gpsimd", wo[k][:, :], W["Wo"][li, 128 * k:128 * k + 128, :])
    qsp = qsc[:, 4 * li:4 * li + 1]
    qsrh = qsc[:, 4 * li + 1:4 * li + 2]
    ksp = qsc[:, 4 * li + 2:4 * li + 3]
    ksrh = qsc[:, 4 * li + 3:4 * li + 4]
    ktail = [p.take("abf", 256) for _ in range(8)]
    vtail = [p.take("abf", 1024) for _ in range(2)]
    prh = W["_prh_tile"]
    p.barrier()
    p.mark(pref + "as")
    for s in range(NS):
        p.reset(pref + "as")
        cs = slice(SW * s, SW * s + SW)
        ws = [p.take("af32", SW) for _ in range(8)]
        wres = [p.take("af32", SW) for _ in range(8)]
        xb = [p.take("abf", SW) for _ in range(8)]
        load_shifted(p, ws, cur, s, SW)
        for i in range(8):
            p.dma("sync", wres[i][:, :], cur[128 * i:128 * i + 128, cs])
        tb_f = p.take("af32", SW)
        Aq = p.take("abf", SW)
        Bq = p.take("abf", SW)
        Ak = p.take("abf", SW)
        Bk = p.take("abf", SW)
        p.barrier()
        for i in range(8):
            p.act(xb[i][:, :], ws[i][:, :], AF.Copy)
        c_s = p.inc("scalar", "s0")
        for (tb, hb, col, scl) in ((Aq, "cs2p", qsp, 8.0), (Bq, "snq", qsrh, 8.0),
                                   (Ak, "ci2p", ksp, 1.0), (Bk, "snk", ksrh, 1.0)):
            p.dma("sync", tb_f[:, :], W[hb][:, cs])
            p.wait_dma("scalar", "sync")
            p.act(tb[:, :], tb_f[:, :], AF.Copy, scale=col)
            c_s = p.inc("scalar", "s0")
            if scl != 1.0:
                p.wait("vector", "s0", c_s)
                p.v("tensor_scalar_mul", tb[:, :], tb[:, :], scl)
                c_v = p.inc("vector", "v0")
                p.wait("sync", "v0", c_v)
            else:
                p.wait("sync", "s0", c_s)
        p.wait("tensor", "s0", p.cnt["s0"])
        p.wait("gpsimd", "s0", p.cnt["s0"])
        qt = [p.take("abf", SW) for _ in range(8)]
        kt = [p.take("abf", SW) for _ in range(8)]
        vt = [p.take("abf", 1024) for _ in range(4)]
        q2t = [p.take("abf", SW) for _ in range(3)]
        sdone = [0] * 16

        def sumsq_mm(j):
            p.mm(p.ps[4 + j // 8][0:16, 0:SW],
                 ones16[:, 16 * (j % 8):16 * (j % 8) + 16],
                 q2t[j % 3][:, :], start=(j % 8 == 0), stop=(j % 8 == 7))
        for i in range(16):
            part, mc = i // 8, i % 8
            tiles = qt if part == 0 else kt
            pb = p.ps[2 + i % 2]
            if i >= 2:
                p.wait("tensor", "s0", sdone[i - 2])
                sumsq_mm(i - 2)
            for kc in range(8):
                p.mm(pb[:, 0:SW],
                     wqkv[kc][:, 1024 * part + 128 * mc:1024 * part + 128 * mc + 128],
                     xb[kc][:, :], start=(kc == 0), stop=(kc == 7))
            td = p.inc("tensor", "t0", drain=False)
            p.wait("scalar", "t0", td)
            p.act(tiles[mc][:, :], pb[:, 0:SW], AF.Copy)
            p.act(q2t[i % 3][:, :], pb[:, 0:SW], AF.Square)
            sdone[i] = p.inc("scalar", "s0")
        nqq = p.take("af32", 512, rows=16)
        nqk = p.take("af32", 512, rows=16)
        nqbq = p.take("abf", 512, rows=16)
        nqbk = p.take("abf", 512, rows=16)
        for j in (14, 15):
            p.wait("tensor", "s0", sdone[j])
            sumsq_mm(j)
        td = p.inc("tensor", "t0", drain=False)
        p.wait("scalar", "t0", td)
        p.act(nqq[0:16, :], p.ps[4][0:16, 0:SW], AF.Sqrt)
        p.act(nqk[0:16, :], p.ps[5][0:16, 0:SW], AF.Sqrt)
        c_s = p.inc("scalar", "s0")
        p.wait("vector", "s0", c_s)
        for nq in (nqq, nqk):
            p.v("tensor_scalar_max", nq[0:16, :], nq[0:16, :], 1e-12)
            p.v("reciprocal", nq[0:16, :], nq[0:16, :])
        c_v = p.inc("vector", "v0")
        p.wait("scalar", "v0", c_v)
        p.act(nqbq[0:16, :], nqq[0:16, :], AF.Copy)
        p.act(nqbk[0:16, :], nqk[0:16, :], AF.Copy)
        c_s = p.inc("scalar", "s0")
        p.wait("tensor", "s0", c_s)
        tmpb = [p.take("abf", SW) for _ in range(2)]
        t1b = [p.take("abf", SW) for _ in range(2)]
        t2b = [p.take("abf", SW) for _ in range(2)]
        vmul = [0] * 16
        vfin = [0] * 16
        for i in range(16):
            part, mc = i // 8, i % 8
            tiles = qt if part == 0 else kt
            nqb = nqbq if part == 0 else nqbk
            Atb, Btb = (Aq, Bq) if part == 0 else (Ak, Bk)
            b = i % 2
            pb = p.ps[4 + b]
            if i >= 2:
                p.wait("tensor", "v0", vmul[i - 2])
            p.mm(pb[:, 0:SW], sel16[0:16, 128 * mc:128 * mc + 128],
                 nqb[0:16, :], start=True, stop=True)
            td = p.inc("tensor", "t0", drain=False)
            p.wait("vector", "t0", td)
            if i >= 2:
                p.wait("vector", "v0", vfin[i - 2])
            p.v("tensor_tensor", tmpb[b][:, :], tiles[mc][:, :], pb[:, 0:SW],
                ALU.mult)
            vmul[i] = p.inc("vector", "v0")
            p.wait("tensor", "v0", vmul[i])
            p.mm(p.ps[2 + b][:, 0:SW], prh[:, :], tmpb[b][:, :],
                 start=True, stop=True)
            td2 = p.inc("tensor", "t0", drain=False)
            p.wait("vector", "t0", td2)
            p.v("tensor_tensor", t1b[b][:, :], tmpb[b][:, :], Atb[:, :], ALU.mult)
            p.v("tensor_tensor", t2b[b][:, :], p.ps[2 + b][:, 0:SW],
                Btb[:, :], ALU.mult)
            p.v("tensor_tensor", tiles[mc][:, :], t1b[b][:, :], t2b[b][:, :],
                ALU.add)
            vfin[i] = p.inc("vector", "v0")
        vsd = [0] * 8
        for i in range(8):
            tc, nv = i // 2, i % 2
            pb = p.ps[i % 2]
            if i >= 2:
                p.wait("tensor", "s0", vsd[i - 2])
            for kc in range(8):
                p.mm(pb[:, 0:SW], xb[kc][:, 128 * tc:128 * tc + 128],
                     wqkv[kc][:, 2048 + 512 * nv:2048 + 512 * nv + 512],
                     start=(kc == 0), stop=(kc == 7))
            td = p.inc("tensor", "t0", drain=False)
            p.wait("scalar", "t0", td)
            p.act(vt[tc][:, 512 * nv:512 * nv + 512], pb[:, 0:SW], AF.Copy)
            vsd[i] = p.inc("scalar", "s0")
        p.wait("tensor", "s0", vsd[7])
        p.wait("vector", "s0", vsd[7])
        p.wait("vector", "v0", vfin[15])
        p.wait("scalar", "v0", vfin[15])
        p.wait("tensor", "v0", vfin[15])
        ot = [p.take("abf", SW) for _ in range(8)]
        mcols = p.take("af32", 16)
        zcols = p.take("af32", 16)
        mbb = p.take("abf", 16)
        mbf = p.take("af32", 16)
        ccols = p.take("af32", 16)
        ccb = p.take("abf", 16)
        mrows = p.take("abf", 128, rows=16)
        crows = p.take("abf", 128, rows=16)
        Smb = [p.take("af32", 384) for _ in range(2)]
        Pbb = [p.take("abf", 384) for _ in range(2)]
        PTb = [p.take("abf", 384) for _ in range(2)]
        crepb = [p.take("abf", 128, rows=64) for _ in range(2)]
        for qb in range(4):
            qbg = 4 * s + qb
            nk = min(384, 128 * (qbg + 1))
            koff = 128 * (qbg + 1) - nk
            mask_c0 = 384 - nk

            def ksrc(h, kx, left):
                base = 64 * (h % 2)
                if kx < 0:
                    n = min(-kx, left)
                    return (ktail[h // 2][base:base + 64, 256 + kx:256 + kx + n], n)
                n = min(512 - kx % 512, left)
                return (kt[h // 2][base:base + 64, kx % 512:kx % 512 + n], n)
            p1v = [0] * 16
            p1s = [0] * 16
            for h in range(16):
                base = 64 * (h % 2)
                b = h % 2
                lq = qt[h // 2][base:base + 64, 128 * qb:128 * qb + 128]
                if h >= 2:
                    p.wait("tensor", "v0", p1v[h - 2])
                col = 0
                kx = koff - SW * s
                left = nk
                while left > 0:
                    rhs, n = ksrc(h, kx, left)
                    p.mm(p.ps[5 + b][:, col:col + n], lq, rhs,
                         start=True, stop=True)
                    col += n
                    kx += n
                    left -= n
                td = p.inc("tensor", "t0", drain=False)
                p.wait("vector", "t0", td)
                if h >= 2:
                    p.wait("vector", "s1", p1s[h - 2])
                p.v("scalar_tensor_tensor", Smb[b][:, 0:nk],
                    p.ps[5 + b][:, 0:nk], 1.0,
                    maskadd[:, mask_c0:mask_c0 + nk], ALU.mult, ALU.add)
                p.v("reduce_max", mcols[:, h:h + 1], Smb[b][:, 0:nk], axis=AX.X,
                    negate=True)
                p1v[h] = p.inc("vector", "v0")
                p.wait("scalar", "v0", p1v[h])
                p.act(Pbb[b][:, 0:nk], Smb[b][:, 0:nk], AF.Exp,
                      bias=mcols[:, h:h + 1], accum_out=zcols[:, h:h + 1])
                p1s[h] = p.inc("scalar", "s1")
            p.wait("scalar", "s1", p1s[15])
            p.act(mbb[:, :], mcols[:, :], AF.Copy)
            p.act(mbf[:, :], mbb[:, :], AF.Copy)
            c_s = p.inc("scalar", "s0")
            p.wait("vector", "s0", c_s)
            p.v("tensor_tensor", ccols[:, :], mcols[:, :], mbf[:, :], ALU.subtract)
            p.v("reciprocal", zcols[:, :], zcols[:, :])
            c_v = p.inc("vector", "v0")
            p.wait("scalar", "v0", c_v)
            p.act(ccols[:, :], ccols[:, :], AF.Exp)
            c_s = p.inc("scalar", "s0")
            p.wait("vector", "s0", c_s)
            p.v("tensor_tensor", ccols[:, :], ccols[:, :], zcols[:, :], ALU.mult)
            c_v = p.inc("vector", "v0")
            p.wait("scalar", "v0", c_v)
            p.act(ccb[:, :], ccols[:, :], AF.Copy)
            c_s = p.inc("scalar", "s0")
            p.wait("tensor", "s0", c_s)
            p.trans(p.psT[0:16, 0:128], mbb[:, :], identb[:, :])
            p.trans(p.psT[0:16, 128:256], ccb[:, :], identb[:, :])
            td = p.inc("tensor", "t0", drain=False)
            p.wait("scalar", "t0", td)
            p.act(mrows[0:16, :], p.psT[0:16, 0:128], AF.Copy)
            p.act(crows[0:16, :], p.psT[0:16, 128:256], AF.Copy)
            c_s = p.inc("scalar", "s0")
            p.wait("tensor", "s0", c_s)
            p2e = [0] * 16
            p2m = [0] * 16
            p2f = [0] * 16
            p2t = [0] * 16
            for h in range(16):
                base = 64 * (h % 2)
                b = h % 2
                rq = qt[h // 2][base:base + 64, 128 * qb:128 * qb + 128]
                if h >= 2:
                    p.wait("tensor", "s1", p2e[h - 2])
                col = 0
                kx = koff - SW * s
                left = nk
                while left > 0:
                    lhs, n = ksrc(h, kx, left)
                    off = 0
                    while off < n:
                        m = min(128, n - off)
                        p.mm(p.ps[5 + b][0:m, col + off:col + off + 128],
                             lhs[:, off:off + m], rq, start=True, stop=False)
                        p.mm(p.ps[5 + b][0:m, col + off:col + off + 128],
                             hsel[0:16, 128 * h:128 * h + m],
                             mrows[0:16, :], start=False, stop=True)
                        off += 128
                    col += n
                    kx += n
                    left -= n
                td = p.inc("tensor", "t0", drain=False)
                p.wait("scalar", "t0", td)
                if h >= 2:
                    p.wait("scalar", "t0", p2t[h - 2])
                p.act(PTb[b][:, 0:nk], p.ps[5 + b][:, 0:nk], AF.Exp)
                p2e[h] = p.inc("scalar", "s1")
                p.wait("vector", "s1", p2e[h])
                p.v("tensor_tensor", PTb[b][:, 0:nk], PTb[b][:, 0:nk],
                    maskmulT[:, mask_c0:mask_c0 + nk], ALU.mult)
                p2m[h] = p.inc("vector", "v0")
                p.wait("tensor", "v0", p2m[h])
                if h >= 2:
                    p.wait("tensor", "v0", p2f[h - 2])
                p.mm(p.ps[0 + b][0:64, 0:128], hsel[0:16, 128 * h:128 * h + 64],
                     crows[0:16, :], start=True, stop=True)
                col = 0
                kx = koff - SW * s
                left = nk
                nmm = 0
                nblk = nk // 128
                while left > 0:
                    n = min(128, left)
                    if kx < 0:
                        vtile = vtail[(256 + kx) // 128]
                    else:
                        vtile = vt[kx // 128]
                    p.mm(p.ps[2 + b][0:64, 0:128], vtile[:, 64 * h:64 * h + 64],
                         PTb[b][0:128, col:col + 128],
                         start=(nmm == 0), stop=(nmm == nblk - 1))
                    nmm += 1
                    col += n
                    kx += n
                    left -= n
                p2t[h] = p.inc("tensor", "t0", drain=False)
                p.wait("scalar", "t0", p2t[h])
                p.act(crepb[b][0:64, :], p.ps[0 + b][0:64, 0:128], AF.Copy)
                cse = p.inc("scalar", "s0")
                p.wait("vector", "s0", cse)
                p.v("tensor_tensor", ot[h // 2][base:base + 64,
                                                128 * qb:128 * qb + 128],
                    p.ps[2 + b][0:64, 0:128], crepb[b][0:64, :], ALU.mult)
                p2f[h] = p.inc("vector", "v0")
            p.wait("tensor", "v0", p2f[15])
            p.wait("scalar", "v0", p2f[15])
        for half in range(2):
            for dm in range(4):
                mc = 4 * half + dm
                for kc in range(8):
                    p.mm(p.ps[dm][:, 0:SW], wo[kc][:, 128 * mc:128 * mc + 128],
                         ot[kc][:, :], start=(kc == 0), stop=(kc == 7))
            td = p.inc("tensor", "t0", drain=False)
            p.wait("vector", "t0", td)
            for dm in range(4):
                mc = 4 * half + dm
                p.v("tensor_tensor", wres[mc][:, :], wres[mc][:, :],
                    p.ps[dm][:, 0:SW], ALU.add)
            c_v = p.inc("vector", "v0")
            p.wait("tensor", "v0", c_v)
        c_v = p.inc("vector", "v0")
        p.wait("gpsimd", "v0", c_v)
        for i in range(8):
            p.dma("gpsimd", nxt[128 * i:128 * i + 128, cs], wres[i][:, :])
        p.wait("scalar", "v0", c_v)
        for i in range(8):
            p.act(ktail[i][:, :], kt[i][:, SW - 256:SW], AF.Copy)
        for j in range(2):
            p.act(vtail[j][:, :], vt[2 + j][:, :], AF.Copy)
        c_s = p.inc("scalar", "s0")
        p.barrier()
    p.reset(pref + "as")
    p.reset(pref + "aw")


def ffn_layer(p, li, cur, nxt, W, consts, lgb, b1c, b2c, xnf_hbm, hf_hbm, pref):
    NS = 8
    SW = T // NS
    DIH = 2816
    gcol = [lgb[:, 16 * li + i:16 * li + i + 1] for i in range(8)]
    bcol = [lgb[:, 16 * li + 8 + i:16 * li + 8 + i + 1] for i in range(8)]
    for halfw in range(2):
        mc0, mc1 = (0, 11) if halfw == 0 else (11, 22)
        p.mark(pref + "fw")
        w1 = [p.take("wbf", 256 * (mc1 - mc0)) for _ in range(8)]
        for k in range(8):
            p.dma("gpsimd", w1[k][:, :],
                  W["W1p"][li, 128 * k:128 * k + 128, 256 * mc0:256 * mc1])
        p.barrier()
        p.mark(pref + "f1")
        for s in range(NS):
            p.reset(pref + "f1")
            cs = slice(SW * s, SW * s + SW)
            xb = [p.take("abf", SW) for _ in range(8)]
            if halfw == 0:
                ws = [p.take("af32", SW) for _ in range(8)]
                load_shifted(p, ws, cur, s, SW)
                p.barrier()
                emit_ln(p, ws, xb, SW, consts, xb, p.ps[0], p.ps[1],
                        gcol=gcol, bcol=bcol)
                c_v = p.inc("vector", "v0")
                p.wait("gpsimd", "v0", c_v)
                for i in range(8):
                    p.dma("gpsimd", xnf_hbm[128 * i:128 * i + 128, cs],
                          xb[i][:, :])
                p.wait("tensor", "v0", c_v)
            else:
                for i in range(8):
                    p.dma("sync", xb[i][:, :], xnf_hbm[128 * i:128 * i + 128, cs])
                p.barrier()
            ggb = [p.take("abf", SW) for _ in range(2)]
            hhb = [p.take("abf", SW) for _ in range(2)]
            nmc = mc1 - mc0
            sgd = [0] * nmc
            vhd = [0] * nmc
            dhd = [0] * nmc
            for j in range(nmc):
                mc = mc0 + j
                wof = 256 * j
                b = j % 2
                if j >= 2:
                    p.wait("tensor", "s0", sgd[j - 2])
                for kc in range(8):
                    p.mm(p.ps[2 + 2 * b][:, 0:SW], w1[kc][:, wof + 128:wof + 256],
                         xb[kc][:, :], start=(kc == 0), stop=(kc == 7))
                c_t = p.inc("tensor", "t0", drain=False)
                if j >= 2:
                    p.wait("tensor", "v0", vhd[j - 2])
                for kc in range(8):
                    p.mm(p.ps[3 + 2 * b][:, 0:SW], w1[kc][:, wof:wof + 128],
                         xb[kc][:, :], start=(kc == 0), stop=(kc == 7))
                c_t2 = p.inc("tensor", "t0", drain=False)
                p.wait("scalar", "t0", c_t)
                if j >= 2:
                    p.wait("scalar", "v0", vhd[j - 2])
                p.act(ggb[b][:, :], p.ps[2 + 2 * b][:, 0:SW], AF.Gelu,
                      bias=b1c[:, 44 * li + 2 * mc + 1:44 * li + 2 * mc + 2])
                sgd[j] = p.inc("scalar", "s0")
                p.wait("vector", "t0", c_t2)
                p.wait("vector", "s0", sgd[j])
                if j >= 2:
                    p.wait("vector", "dma_gpsimd", dhd[j - 2])
                p.v("scalar_tensor_tensor", hhb[b][:, :], p.ps[3 + 2 * b][:, 0:SW],
                    b1c[:, 44 * li + 2 * mc:44 * li + 2 * mc + 1], ggb[b][:, :],
                    ALU.add, ALU.mult)
                vhd[j] = p.inc("vector", "v0")
                p.wait("gpsimd", "v0", vhd[j])
                p.dma("gpsimd", hf_hbm[128 * mc:128 * mc + 128, cs], hhb[b][:, :])
                dhd[j] = p.cnt["dma_gpsimd"]
            p.barrier()
        p.reset(pref + "f1")
        p.reset(pref + "fw")
    p.mark(pref + "fw2")
    w2 = [p.take("wbf", 1024) for _ in range(22)]
    for k in range(22):
        p.dma("gpsimd", w2[k][:, :], W["W2p"][li, 128 * k:128 * k + 128, :])
    p.barrier()
    p.mark(pref + "f2")
    for s in range(NS):
        p.reset(pref + "f2")
        cs = slice(SW * s, SW * s + SW)
        hb = [p.take("abf", SW) for _ in range(22)]
        wres = [p.take("af32", SW) for _ in range(8)]
        for k in range(22):
            p.dma("sync", hb[k][:, :], hf_hbm[128 * k:128 * k + 128, cs])
        for i in range(8):
            p.dma("sync", wres[i][:, :], cur[128 * i:128 * i + 128, cs])
        p.barrier()
        for half in range(2):
            for dm in range(4):
                mc = 4 * half + dm
                for kc in range(22):
                    p.mm(p.ps[dm][:, 0:SW], w2[kc][:, 128 * mc:128 * mc + 128],
                         hb[kc][:, :], start=(kc == 0), stop=(kc == 21))
            c_t = p.inc("tensor", "t0")
            p.wait("vector", "t0", c_t)
            for dm in range(4):
                mc = 4 * half + dm
                p.v("scalar_tensor_tensor", wres[mc][:, :], p.ps[dm][:, 0:SW],
                    b2c[:, 8 * li + mc:8 * li + mc + 1], wres[mc][:, :],
                    ALU.add, ALU.add)
            c_v = p.inc("vector", "v0")
            p.wait("tensor", "v0", c_v)
        c_v = p.inc("vector", "v0")
        p.wait("gpsimd", "v0", c_v)
        for i in range(8):
            p.dma("gpsimd", nxt[128 * i:128 * i + 128, cs], wres[i][:, :])
        p.barrier()
    p.reset(pref + "f2")
    p.reset(pref + "fw2")


def host_l0ffn_weights(inputs):
    DEPTH, DIM, DH, W_ = 6, 1024, 64, 256
    INNER = 1024
    Wqkv_p = np.ascontiguousarray(np.array(inputs["L0_ffn_Wqkv"]))
    rdim = np.arange(128) % 64
    sign = np.where((np.arange(128) % 64) < 32, -1.0, 1.0)
    half = np.arange(0, DH, 2, dtype=np.float32)
    inv_freq = (1.0 / 10000.0 ** (half / DH)).astype(np.float32)
    t = np.arange(T, dtype=np.float32)
    freqs = t[:, None] * inv_freq[None, :]
    base_scale = ((half + 0.4 * DH) / (1.4 * DH)).astype(np.float32)
    power = ((t - T // 2) / (W_ // 2)).astype(np.float32)
    scale = base_scale[None, :] ** power[:, None]
    cos = np.cos(freqs)
    sin = np.sin(freqs)
    hd = rdim % 32
    cs2p = np.ascontiguousarray((cos[:, hd] * scale[:, hd]).T.astype(np.float32))
    snq = np.ascontiguousarray(
        ((sin[:, hd] * scale[:, hd]).T * sign[:, None]).astype(np.float32))
    ci2p = np.ascontiguousarray((cos[:, hd] / scale[:, hd]).T.astype(np.float32))
    snk = np.ascontiguousarray(
        ((sin[:, hd] / scale[:, hd]).T * sign[:, None]).astype(np.float32))
    rh = (np.arange(128) // 64) * 64 + ((np.arange(128) + 32) % 64)
    qk = np.zeros((128, 4 * DEPTH), dtype=np.float32)
    for li in range(DEPTH):
        qs = inputs["L0_ffn_q_scale"][li]
        ks = inputs["L0_ffn_k_scale"][li]
        qk[:, 4 * li + 0] = qs[rdim]
        qk[:, 4 * li + 1] = qs[rdim[rh]]
        qk[:, 4 * li + 2] = ks[rdim]
        qk[:, 4 * li + 3] = ks[rdim[rh]]
    assert np.all(rdim[rh] == (rdim + 32) % 64)
    maskadd = np.zeros((128, 384), dtype=np.float32)
    r = np.arange(128)[:, None]
    j = np.arange(128)[None, :]
    maskadd[:, 0:128] = np.where(j >= r, 0.0, NEG_BIG)
    maskadd[:, 256:384] = np.where(j <= r, 0.0, NEG_BIG)
    maskmulT = np.ones((128, 384), dtype=np.float32)
    maskmulT[:, 0:128] = (np.arange(128)[:, None] >= np.arange(128)[None, :])
    maskmulT[:, 256:384] = (np.arange(128)[:, None] <= np.arange(128)[None, :])
    ingrp = np.zeros((128, 2), dtype=np.float32)
    ingrp[0:64, 0] = 1
    ingrp[64:128, 1] = 1
    ones16 = np.zeros((128, 128), dtype=np.float32)
    sel16 = np.zeros((16, 1024), dtype=np.float32)
    for c in range(8):
        ones16[:, 16 * c + 2 * c:16 * c + 2 * c + 2] = ingrp
        sel16[2 * c:2 * c + 2, 128 * c:128 * c + 128] = ingrp.T
    hsel = np.zeros((16, 2048), dtype=np.float32)
    for h in range(16):
        hsel[h, 128 * h:128 * h + 128] = 1.0
    lngb = np.zeros((128, 16 * DEPTH), dtype=np.float32)
    for li in range(DEPTH):
        lngb[:, 16 * li:16 * li + 8] = inputs["L0_ffn_ln_g"][li].reshape(8, 128).T
        lngb[:, 16 * li + 8:16 * li + 16] = \
            inputs["L0_ffn_ln_b"][li].reshape(8, 128).T
    W1 = np.array(inputs["L0_ffn_W1"])
    DI0 = 2730
    W1a = np.zeros((DEPTH, DIM, 2816), dtype=np.float32)
    W1g = np.zeros((DEPTH, DIM, 2816), dtype=np.float32)
    W1a[:, :, :DI0] = W1[:, :, :DI0]
    W1g[:, :, :DI0] = W1[:, :, DI0:]
    W1p = np.zeros((DEPTH, DIM, 5632), dtype=np.float32)
    for mc in range(22):
        W1p[:, :, 256 * mc:256 * mc + 128] = W1a[:, :, 128 * mc:128 * mc + 128]
        W1p[:, :, 256 * mc + 128:256 * mc + 256] = \
            W1g[:, :, 128 * mc:128 * mc + 128]
    W1p = np.ascontiguousarray(W1p)
    b1 = np.array(inputs["L0_ffn_b1"])
    b1r = np.zeros((DEPTH, 128, 44), dtype=np.float32)
    b1a = np.zeros((DEPTH, 2816), dtype=np.float32)
    b1g = np.zeros((DEPTH, 2816), dtype=np.float32)
    b1a[:, :DI0] = b1[:, :DI0]
    b1g[:, :DI0] = b1[:, DI0:]
    for mc in range(22):
        b1r[:, :, 2 * mc] = b1a[:, 128 * mc:128 * mc + 128]
        b1r[:, :, 2 * mc + 1] = b1g[:, 128 * mc:128 * mc + 128]
    b1rc = np.zeros((128, 44 * DEPTH), dtype=np.float32)
    for li in range(DEPTH):
        b1rc[:, 44 * li:44 * li + 44] = b1r[li]
    W2 = np.array(inputs["L0_ffn_W2"])
    W2p = np.zeros((DEPTH, 2816, 1024), dtype=np.float32)
    W2p[:, :DI0, :] = W2
    W2p = np.ascontiguousarray(W2p)
    b2r = np.zeros((128, 8 * DEPTH), dtype=np.float32)
    for li in range(DEPTH):
        b2r[:, 8 * li:8 * li + 8] = inputs["L0_ffn_b2"][li].reshape(8, 128).T
    return {
        "Wqkv": Wqkv_p,
        "Wo": np.ascontiguousarray(inputs["L0_ffn_Wo"]),
        "W1p": W1p,
        "W2p": W2p,
        "cs2p": cs2p, "snq": snq, "ci2p": ci2p, "snk": snk,
        "qk_scales": qk,
        "maskadd": maskadd, "maskmulT": maskmulT,
        "ones16": ones16, "sel16": sel16, "hsel": hsel,
        "ln_gb": lngb, "b1r": b1rc, "b2r": b2r,
        "ident_bf": bf16(np.eye(128, dtype=np.float32)),
        "Prh": np.eye(128, dtype=np.float32)[:, [d ^ 32 for d in range(128)]].T.copy()
        if False else np.ascontiguousarray(
            np.array([[1.0 if k == (d ^ 32) else 0.0 for d in range(128)]
                      for k in range(128)], dtype=np.float32)),
    }


def build_l0ffn_prog(debug=False, depth=6, do_attn=True, do_ffn=True):
    p = Prog("trf")
    p.arena("cst", 128, F32)
    p.arena("cbf", 6144, BF16)
    p.arena("cf32", 2048, F32)
    p.arena("wbf", 8 * 4096, BF16)
    p.arena("abf", 36864, BF16)
    p.arena("af32", 11264, F32)
    xin = p.inp("xin", (D, T), F32)
    xout = p.out("xout", (D, T), F32)
    W = {
        "Wqkv": p.inp("Wqkv", (6, 1024, 3072), F32),
        "Wo": p.inp("Wo", (6, 1024, 1024), F32),
        "W1p": p.inp("W1p", (6, 1024, 5632), F32),
        "W2p": p.inp("W2p", (6, 2816, 1024), F32),
        "cs2p": p.inp("cs2p", (128, T), F32),
        "snq": p.inp("snq", (128, T), F32),
        "ci2p": p.inp("ci2p", (128, T), F32),
        "snk": p.inp("snk", (128, T), F32),
        "qk_scales": p.inp("qk_scales", (128, 24), F32),
        "maskadd": p.inp("maskadd", (128, 384), F32),
        "maskmulT": p.inp("maskmulT", (128, 384), F32),
        "ones16": p.inp("ones16", (128, 128), F32),
        "sel16": p.inp("sel16", (16, 1024), F32),
        "hsel": p.inp("hsel", (16, 2048), F32),
        "ln_gb": p.inp("ln_gb", (128, 96), F32),
        "b1r": p.inp("b1r", (128, 264), F32),
        "b2r": p.inp("b2r", (128, 48), F32),
        "ident_bf": p.inp("ident_bf", (128, 128), BF16),
        "Prh": p.inp("Prh", (128, 128), F32),
    }
    sec_l0ffn(p, xin, xout, W, debug=debug, depth=depth, do_attn=do_attn,
              do_ffn=do_ffn)
    return p.build()


MAMBA_WS = {
    "W_inT": ((1024, 4096), F32), "W_xprojT": ((2048, 96), F32),
    "W_dtT": ((64, 2048), F32), "W_outT": ((2048, 1024), F32),
    "conv_wr": ((128, 64), F32), "conv_br": ((128, 16), F32),
    "b_dtr": ((128, 16), F32), "D_skipr": ((128, 16), F32),
    "A_logr": ((128, 256), F32), "bcsel": ((32, 4096), F32),
}
TRF_WS = {
    "Wqkv": ((6, 1024, 3072), F32), "Wo": ((6, 1024, 1024), F32),
    "W1p": ((6, 1024, 5632), F32), "W2p": ((6, 2816, 1024), F32),
    "cs2p": ((128, T), F32), "snq": ((128, T), F32),
    "ci2p": ((128, T), F32), "snk": ((128, T), F32),
    "qk_scales": ((128, 24), F32), "maskadd": ((128, 384), F32),
    "maskmulT": ((128, 384), F32), "ones16": ((128, 128), F32),
    "sel16": ((16, 1024), F32), "hsel": ((16, 2048), F32),
    "ln_gb": ((128, 96), F32), "b1r": ((128, 264), F32),
    "b2r": ((128, 48), F32), "ident_bf": ((128, 128), BF16),
    "Prh": ((128, 128), F32),
}
GLA_WS = {
    "Wq": ((1024, 1024), F32), "Wk": ((1024, 1024), F32),
    "Wv": ((1024, 2048), F32), "Wg": ((1024, 2048), F32),
    "Wgk1": ((1024, 16), F32), "Wgk2": ((16, 1024), F32),
    "Wo": ((2048, 1024), F32), "bgk2r": ((128, 8), F32),
    "gnwr": ((128, 4), F32), "ident_bf": ((128, 128), BF16),
    "maskT": ((64, 64), F32),
}
MOE_WS = {
    "gate_W": ((1024, 8), F32), "W1": ((8, 1024, 2048), F32),
    "b1r": ((8, 128, 16), F32), "W2": ((8, 2048, 1024), F32),
    "B2": ((8, 1024), F32), "esel": ((8, 1024), F32),
    "ident_f32": ((128, 128), F32),
}


def build_fused_prog(debug=False):
    p = Prog("fused")
    p.arena("cst", 128, F32)
    p.arena("cbf", 6400, BF16)
    p.arena("bf", 68160, BF16)
    p.arena("ff", 15616, F32)
    for nm in ("wbf", "abf", "dbl", "obf"):
        p.alias(nm, "bf")
    for nm in ("af32", "cf32", "wf32", "sf32", "ybig"):
        p.alias(nm, "ff")
    xin = p.inp("xin", (D, T), F32)
    xout = p.out("xout", (D, T), F32)
    x1 = p.dram("x1", (D, T), F32)
    x2 = p.dram("x2", (D, T), F32)
    x3 = p.dram("x3", (D, T), F32)

    def wmap(prefix, spec):
        return {k: p.inp(prefix + k, s, dt) for k, (s, dt) in spec.items()}

    Wm = wmap("m_", MAMBA_WS)
    Wt = wmap("t_", TRF_WS)
    Wg = wmap("g_", GLA_WS)
    We = wmap("e_", MOE_WS)

    std_consts(p)
    p.mark("sect")
    sec_mamba(p, xin, x1, Wm, debug=debug)
    p.barrier()
    p.reset("sect")
    sec_l0ffn(p, x1, x2, Wt, debug=debug)
    p.barrier()
    p.reset("sect")
    sec_gla(p, x2, x3, Wg, debug=debug)
    p.barrier()
    p.reset("sect")
    sec_moe(p, x3, xout, We, debug=debug)
    return p.build()


_PROGS = {}
FUSED = True


def _get_prog(name, builder):
    if name not in _PROGS:
        _PROGS[name] = builder()
    return _PROGS[name]


def kernel(**inputs):
    x = np.asarray(inputs["L0_attn_x"])
    xT = [np.ascontiguousarray(x[b].T) for b in range(NCORES)]

    hw_m = host_mamba_weights(inputs)
    hw_t = host_l0ffn_weights(inputs)
    hw_g = host_gla_weights({k: inputs[k] for k in inputs if k.startswith("L1_attn")})
    hw_e = host_moe_weights({k: inputs[k] for k in inputs if k.startswith("L1_ffn")})

    def launch(progname, builder, hw, xs):
        nc = _get_prog(progname, builder)
        in_maps = [dict({"xin": xs[b]}, **hw) for b in range(NCORES)]
        res = run_bass_kernel_spmd(nc, in_maps, core_ids=list(range(NCORES)))
        return [np.ascontiguousarray(np.asarray(res.results[b]["xout"]))
                for b in range(NCORES)]

    if FUSED:
        hw = {}
        for pref, d in (("m_", hw_m), ("t_", hw_t), ("g_", hw_g), ("e_", hw_e)):
            for k, v in d.items():
                hw[pref + k] = v
        x4 = launch("fused", build_fused_prog, hw, xT)
    else:
        x1 = launch("mamba", build_mamba_prog, hw_m, xT)
        x2 = launch("l0ffn", build_l0ffn_prog, hw_t, x1)
        x3 = launch("gla", build_gla_prog, hw_g, x2)
        x4 = launch("moe", build_moe_prog, hw_e, x3)
    return np.stack([x4[b].T for b in range(NCORES)], axis=0).astype(np.float32)
